# version 06
# baseline (speedup 1.0000x reference)
import numpy as np
import ml_dtypes
from contextlib import ExitStack
import concourse.bass as bass
import concourse.mybir as mybir
from concourse.bass_utils import run_bass_kernel_spmd

F32 = mybir.dt.float32
BF16 = mybir.dt.bfloat16
I32 = mybir.dt.int32
AF = mybir.ActivationFunctionType
ALU = mybir.AluOpType
AX = mybir.AxisListType

B, T, D, H, HN, DFF = 8, 4096, 1024, 16, 64, 2816
DC = D // 128
Q = 128
NCH = T // Q
TT = 256
NT = T // TT
CPT = TT // Q
IC = DFF // 128
EPS_LN = 1e-5
EPS_GN = 1e-5 * 64
BIGNEG = 1.0e9
HALFPI = 1.5707963267948966

N_PHASES = 4


class Sem:
    def __init__(self, nc, ctx, name):
        self.s = ctx.enter_context(nc.semaphore(name))
        self.n = 0

    def inc(self, instr, k=1):
        instr.then_inc(self.s, k)
        self.n += k
        return self.n


class E:
    pass


def build(n_phases=N_PHASES, dev_dump=None):
    nc = bass.Bass("TRN2", target_bir_lowering=False)
    e = E()
    e.nc = nc

    xin = nc.dram_tensor("xin", [D, T], F32, kind="ExternalInput").ap()
    WT = {}
    for nm in ["WrT", "WkT", "WvT", "WgT", "WoT"]:
        WT[nm] = nc.dram_tensor(nm, [D, D], F32, kind="ExternalInput").ap()
    maa_in = nc.dram_tensor("maa", [128, DC * 4], F32, kind="ExternalInput").ap()
    decay_row = nc.dram_tensor("decay_row", [1, H], F32, kind="ExternalInput").ap()
    faaaa_row = nc.dram_tensor("faaaa_row", [1, H], F32, kind="ExternalInput").ap()
    lnw_row = nc.dram_tensor("lnw_row", [1, D], F32, kind="ExternalInput").ap()
    lnb_row = nc.dram_tensor("lnb_row", [1, D], F32, kind="ExternalInput").ap()
    c1t_in = nc.dram_tensor("c1t", [128, 128], F32, kind="ExternalInput").ap()
    c2t_in = nc.dram_tensor("c2t", [128, 128], F32, kind="ExternalInput").ap()
    eye_in = nc.dram_tensor("eye", [128, 128], F32, kind="ExternalInput").ap()
    iota_i_in = nc.dram_tensor("iota_i", [128, 1], F32, kind="ExternalInput").ap()
    iota_r_in = nc.dram_tensor("iota_r", [128, 1], F32, kind="ExternalInput").ap()
    ones_in = nc.dram_tensor("ones_in", [128, 1], F32, kind="ExternalInput").ap()
    theta_in = nc.dram_tensor("theta", [D, D], F32, kind="ExternalInput").ap()
    theta_n_in = nc.dram_tensor("theta_n", [D, D], F32, kind="ExternalInput").ap()
    phi_in = nc.dram_tensor("phi", [D, D], F32, kind="ExternalInput").ap()
    phi_n_in = nc.dram_tensor("phi_n", [D, D], F32, kind="ExternalInput").ap()
    WwT_in = nc.dram_tensor("WwT", [D, D], F32, kind="ExternalInput").ap()
    WrdT_in = nc.dram_tensor("WrdT", [D, D], F32, kind="ExternalInput").ap()
    l1s_in = nc.dram_tensor("l1_scal", [1, 4], F32, kind="ExternalInput").ap()
    wg_in = nc.dram_tensor("w_gate", [DFF, D], F32, kind="ExternalInput").ap()
    wu_in = nc.dram_tensor("w_up", [DFF, D], F32, kind="ExternalInput").ap()
    wd_in = nc.dram_tensor("w_down", [D, DFF], F32, kind="ExternalInput").ap()

    out = nc.dram_tensor("out", [T, D], F32, kind="ExternalOutput").ap()
    if dev_dump:
        dbg = nc.dram_tensor("dbg", [D, T], F32, kind="ExternalOutput").ap()

    xs1 = nc.dram_tensor("xs1", [D, T], F32).ap()
    xs2 = nc.dram_tensor("xs2", [D, T], F32).ap()
    x3h_d = nc.dram_tensor("x3h_d", [T, D], BF16).ap()
    x3l_d = nc.dram_tensor("x3l_d", [T, D], BF16).ap()
    h_sp = nc.dram_tensor("h_spill", [DFF, T], F32).ap()
    rowb = nc.dram_tensor("rowbuf", [16, T], F32).ap()
    wsplit = {}
    for nm in ["k", "v", "g"]:
        wsplit[nm] = nc.dram_tensor("ws_" + nm, [2, 2, 128, DC * 512], BF16).ap()
    for nm in ["r", "o"]:
        wsplit[nm] = nc.dram_tensor("ws_" + nm, [2, DC, 128, DC * 128], BF16).ap()
    cc_in1 = nc.dram_tensor("cc_in1", [1, 1], F32)
    cc_out1 = nc.dram_tensor("cc_out1", [1, 1], F32)
    cc_in2 = nc.dram_tensor("cc_in2", [1, 1], F32)
    cc_out2 = nc.dram_tensor("cc_out2", [1, 1], F32)
    scal_d = nc.dram_tensor("scal_d", [1, 16], F32).ap()

    stk = ExitStack()
    with stk:
        def sb(nm, shape, dt=F32, st=stk):
            return st.enter_context(nc.sbuf_tensor(nm, shape, dt))

        e.ones = sb("ones_c", [128, 1])
        e.ioi = sb("ioi", [128, 1])
        e.ior = sb("ior", [128, 1])
        e.eye = sb("eye_t", [128, 128])
        e.maa = sb("maa_t", [128, DC * 4])
        e.Ecol = sb("Ecol", [128, H])
        e.Ucol = sb("Ucol", [128, H])
        e.wb = sb("wb_t", [128, H])
        e.wkd = sb("wkd_t", [128, H])
        e.wsd = sb("wsd_t", [128, H])
        e.scal = sb("scal", [128, 16])
        e.epsln = sb("epsln", [128, 1])
        e.epsgn = sb("epsgn", [128, 1])
        e.m4 = sb("m4_g", [128, NCH])
        e.var4 = sb("var4_g", [128, NCH])
        e.cmx4 = sb("cmx4_g", [128, NCH])
        e.row1 = sb("row1", [1, 512])
        e.ones_row = sb("ones_row", [1, 128])
        e.psA = stk.enter_context(nc.psum_tensor("psA", [128, 512], F32))
        e.psB = stk.enter_context(nc.psum_tensor("psB", [128, 512], F32))
        e.psY = stk.enter_context(nc.psum_tensor("psY", [128, 1024], F32))
        e.psY2 = stk.enter_context(nc.psum_tensor("psY2", [128, 1024], F32))
        e.psC = stk.enter_context(nc.psum_tensor("psC", [128, 512], F32))
        e.psD = stk.enter_context(nc.psum_tensor("psD", [128, 512], F32))

        e.wmatT = None
        e.lnwb = None
        e.lnbb = None
        e.ld = Sem(nc, stk, "ld")
        e.sc = Sem(nc, stk, "scq")
        e.gq = Sem(nc, stk, "gq")
        e.t = Sem(nc, stk, "tse")
        e.v = Sem(nc, stk, "vse")
        e.s = Sem(nc, stk, "sse")
        e.cc = Sem(nc, stk, "ccse")
        e.fin = Sem(nc, stk, "fin")

        blk = stk.enter_context(nc.Block())
        e.blk = blk

        def barrier_vals():
            return dict(ld=e.ld.n, sc=e.sc.n, gq=e.gq.n, t=e.t.n, v=e.v.n, s=e.s.n)

        def wait_barrier(engine, bv):
            engine.wait_ge(e.ld.s, bv["ld"])
            engine.wait_ge(e.sc.s, bv["sc"])
            engine.wait_ge(e.gq.s, bv["gq"])
            engine.wait_ge(e.t.s, bv["t"])
            engine.wait_ge(e.v.s, bv["v"])
            engine.wait_ge(e.s.s, bv["s"])

        psh = ExitStack()
        e.wmatT = sb("wmatT", [128, H * 128], st=psh)
        e.lnwb = sb("lnwb", [128, D], st=psh)
        e.lnbb = sb("lnbb", [128, D], st=psh)
        xt = sb("xt", [128, DC * TT], st=psh)
        ln = sb("ln", [128, DC * (TT + 1)], st=psh)
        sq = sb("sq", [128, DC * TT], st=psh)
        mxh = sb("mxh", [128, DC * TT], BF16, st=psh)
        mxl = sb("mxl", [128, DC * TT], BF16, st=psh)
        wstg = sb("wstg", [128, D], st=psh)
        wsth = sb("wsth", [128, D], BF16, st=psh)
        wstl = sb("wstl", [128, D], BF16, st=psh)
        rw_m = sb("rw_m", [1, TT], st=psh)
        rw_v = sb("rw_v", [1, TT], st=psh)
        rw_t = sb("rw_t", [1, TT], st=psh)
        rw_rs = sb("rw_rs", [1, TT], st=psh)
        rw_mrs = sb("rw_mrs", [1, TT], st=psh)

        def emit_ln(xtile, dst_ln, wait_sem, wait_val, roff):
            @blk.scalar
            def _(s):
                s.wait_ge(wait_sem.s, wait_val)
                for c in range(DC):
                    e.s.inc(s.activation(sq[:, c * TT:(c + 1) * TT], xtile[:, c * TT:(c + 1) * TT], AF.Square))
            sqd = e.s.n

            @blk.tensor
            def _(t):
                t.wait_ge(e.s.s, sqd)
                t.wait_ge(e.v.s, e.v.n)
                for c in range(DC):
                    ins = t.matmul(e.psC[0:1, 0:TT], e.ones[:, :], xtile[:, c * TT:(c + 1) * TT],
                                   start=(c == 0), stop=(c == DC - 1))
                e.t.inc(ins)
                for c in range(DC):
                    ins = t.matmul(e.psD[0:1, 0:TT], e.ones[:, :], sq[:, c * TT:(c + 1) * TT],
                                   start=(c == 0), stop=(c == DC - 1))
                e.t.inc(ins)
            mmd = e.t.n

            @blk.vector
            def _(v):
                v.wait_ge(e.t.s, mmd)
                v.tensor_scalar(rw_m[0:1, 0:TT], e.psC[0:1, 0:TT], 1.0 / D, None, ALU.mult)
                e.v.inc(v.tensor_scalar(rw_v[0:1, 0:TT], e.psD[0:1, 0:TT], 1.0 / D, None, ALU.mult))
                v.wait_ge(e.v.s, e.v.n)
                v.tensor_tensor(rw_t[0:1, 0:TT], rw_m[0:1, 0:TT], rw_m[0:1, 0:TT], ALU.mult)
                e.v.inc(v.tensor_tensor(rw_v[0:1, 0:TT], rw_v[0:1, 0:TT], rw_t[0:1, 0:TT], ALU.subtract))
            vd = e.v.n

            @blk.scalar
            def _(s):
                s.wait_ge(e.v.s, vd)
                e.s.inc(s.activation(rw_t[0:1, 0:TT], rw_v[0:1, 0:TT], AF.Sqrt, bias=e.epsln[0:1, :]))
            sd = e.s.n

            @blk.vector
            def _(v):
                v.wait_ge(e.s.s, sd)
                e.v.inc(v.reciprocal(rw_rs[0:1, 0:TT], rw_t[0:1, 0:TT]))
                v.wait_ge(e.v.s, e.v.n)
                e.v.inc(v.tensor_tensor(rw_mrs[0:1, 0:TT], rw_m[0:1, 0:TT], rw_rs[0:1, 0:TT], ALU.mult))
            vd2 = e.v.n

            @blk.tensor
            def _(t):
                t.wait_ge(e.v.s, vd2)
                e.t.inc(t.matmul(e.psC[:, 0:TT], e.ones_row[0:1, :], rw_rs[0:1, 0:TT],
                                 start=True, stop=True))
                e.t.inc(t.matmul(e.psD[:, 0:TT], e.ones_row[0:1, :], rw_mrs[0:1, 0:TT],
                                 start=True, stop=True))
            bd = e.t.n

            @blk.vector
            def _(v):
                v.wait_ge(e.t.s, bd)
                for c in range(DC):
                    v.tensor_tensor(dst_ln[:, c * (TT + 1) + 1:(c + 1) * (TT + 1)],
                                    xtile[:, c * TT:(c + 1) * TT], e.psC[:, 0:TT], ALU.mult)
                    e.v.inc(v.tensor_tensor(dst_ln[:, c * (TT + 1) + 1:(c + 1) * (TT + 1)],
                                            dst_ln[:, c * (TT + 1) + 1:(c + 1) * (TT + 1)],
                                            e.psD[:, 0:TT], ALU.subtract))
            return e.v.n



        @blk.sync
        def _(sy):
            e.ld.inc(sy.dma_start(out=e.ones[:, :], in_=ones_in), 16)
            e.ld.inc(sy.dma_start(out=e.ones_row[0:1, :], in_=ones_in), 16)
            e.ld.inc(sy.dma_start(out=e.ioi[:, :], in_=iota_i_in), 16)
            e.ld.inc(sy.dma_start(out=e.ior[:, :], in_=iota_r_in), 16)
            e.ld.inc(sy.dma_start(out=e.eye[:, :], in_=eye_in), 16)
            e.ld.inc(sy.dma_start(out=e.maa[:, :], in_=maa_in), 16)
            e.ld.inc(sy.dma_start(out=e.lnwb[:, :], in_=lnw_row.broadcast_to([128, D])), 16)
            e.ld.inc(sy.dma_start(out=e.lnbb[:, :], in_=lnb_row.broadcast_to([128, D])), 16)
            e.ld.inc(sy.dma_start(out=e.Ecol[:, 0:H], in_=decay_row.broadcast_to([128, H])), 16)
            e.ld.inc(sy.dma_start(out=e.Ucol[:, 0:H], in_=faaaa_row.broadcast_to([128, H])), 16)
            for hh in range(H):
                e.ld.inc(sy.dma_start(out=e.wmatT[:, hh * 128:(hh + 1) * 128], in_=c1t_in), 16)
        n0 = e.ld.n

        @blk.scalar
        def _(s):
            s.wait_ge(e.ld.s, n0)
            e.s.inc(s.activation(e.Ecol[:, 0:H], e.Ecol[:, 0:H], AF.Exp))
        n1 = e.s.n

        @blk.vector
        def _(v):
            v.wait_ge(e.s.s, n1)
            v.memset(e.epsln[:, :], EPS_LN)
            v.memset(e.epsgn[:, :], EPS_GN)
            for hh in range(H):
                v.tensor_scalar(e.wb[:, hh:hh + 1], e.ioi[:, :], e.Ecol[:, hh:hh + 1], None, ALU.mult)
                v.tensor_scalar(e.wkd[:, hh:hh + 1], e.ior[:, :], e.Ecol[:, hh:hh + 1], None, ALU.mult)
            e.v.inc(v.tensor_scalar(e.wsd[:, 0:H], e.Ecol[:, 0:H], float(Q), None, ALU.mult))
        n2 = e.v.n

        @blk.scalar
        def _(s):
            s.wait_ge(e.v.s, n2)
            s.activation(e.wb[:, 0:H], e.wb[:, 0:H], AF.Exp, scale=-1.0)
            s.activation(e.wkd[:, 0:H], e.wkd[:, 0:H], AF.Exp, scale=-1.0)
            e.s.inc(s.activation(e.wsd[:, 0:H], e.wsd[:, 0:H], AF.Exp, scale=-1.0))
        n3 = e.s.n

        @blk.vector
        def _(v):
            v.wait_ge(e.s.s, n3)
            for hh in range(H):
                e.v.inc(v.tensor_scalar(e.wmatT[:, hh * 128:(hh + 1) * 128],
                                        e.wmatT[:, hh * 128:(hh + 1) * 128],
                                        e.Ecol[:, hh:hh + 1], None, ALU.mult))
        n4 = e.v.n

        @blk.scalar
        def _(s):
            s.wait_ge(e.v.s, n4)
            for hh in range(H):
                e.s.inc(s.activation(e.wmatT[:, hh * 128:(hh + 1) * 128],
                                     e.wmatT[:, hh * 128:(hh + 1) * 128], AF.Exp, scale=-1.0))
        n5 = e.s.n

        @blk.vector
        def _(v):
            v.wait_ge(e.s.s, n5)
            for hh in range(H):
                e.v.inc(v.scalar_tensor_tensor(e.wmatT[:, hh * 128:(hh + 1) * 128],
                                               e.eye[:, :], e.Ucol[:, hh:hh + 1],
                                               e.wmatT[:, hh * 128:(hh + 1) * 128],
                                               ALU.mult, ALU.add))
        prep_v = e.v.n

        if n_phases >= 1:
            p1 = ExitStack()
            mxh2 = sb("mxh2", [128, DC * TT], BF16, st=p1)
            mxl2 = sb("mxl2", [128, DC * TT], BF16, st=p1)
            wkvg = sb("wkvg", [128, 2 * DC * 512], BF16, st=p1)
            wkvg2 = sb("wkvg2", [128, 2 * DC * 512], BF16, st=p1)
            wro = sb("wro", [128, 2 * DC * 128], BF16, st=p1)
            wro2 = sb("wro2", [128, 2 * DC * 128], BF16, st=p1)
            r_fm = sb("r_fm", [128, DC * TT], st=p1)
            k_tok = sb("k_tok", [128, CPT * D], st=p1)
            v_tok = sb("v_tok", [128, CPT * D], st=p1)
            g_tok = sb("g_tok", [128, CPT * D], st=p1)
            k_fm = sb("k_fm", [128, DC * 128], st=p1)
            kw_tok = sb("kw_tok", [128, D], st=p1)
            attw = sb("attw", [128, 128], st=p1)
            WKB = sb("WKB", [128, D], st=p1)
            WBB = sb("WBB", [128, D], st=p1)
            WSB = sb("WSB", [128, (H // 2) * HN], st=p1)
            attw2 = sb("attw2", [128, 128], st=p1)
            attw3 = sb("attw3", [128, 128], st=p1)
            state = sb("state", [128, (H // 2) * HN], st=p1)
            ysb = sb("ysb", [128, D], st=p1)
            st6 = sb("st6", [128, 6 * H], st=p1)
            gn_m = sb("gn_m", [128, H], st=p1)
            gn_v = sb("gn_v", [128, H], st=p1)
            gn_t = sb("gn_t", [128, H], st=p1)
            gn_rs = sb("gn_rs", [128, H], st=p1)
            zf = sb("zf", [128, D], st=p1)
            zh = sb("zh", [128, D], BF16, st=p1)
            zl = sb("zl", [128, D], BF16, st=p1)
            zfmh = sb("zfmh", [128, DC * TT], BF16, st=p1)
            zfml = sb("zfml", [128, DC * TT], BF16, st=p1)

            wstg2 = sb("wstg2", [128, D], st=p1)
            wsth2 = sb("wsth2", [128, D], BF16, st=p1)
            wstl2 = sb("wstl2", [128, D], BF16, st=p1)
            wsp_stg = [(wstg, wsth, wstl), (wstg2, wsth2, wstl2)]
            wsp_busy = [prep_v, prep_v]
            wsp_st = [0, 0]
            wu_i = 0
            for wi, nm in enumerate(["k", "v", "g", "r", "o"]):
                src = WT["W" + nm + "T"]
                for c in range(DC):
                    par = wu_i % 2
                    wu_i += 1
                    stg_, sth_, stl_ = wsp_stg[par]

                    @blk.sync
                    def _(sy, src=src, c=c, uw=wsp_busy[par], uw2=wsp_st[par], stg_=stg_):
                        sy.wait_ge(e.v.s, uw)
                        sy.wait_ge(e.gq.s, uw2)
                        e.ld.inc(sy.dma_start(out=stg_[:, :], in_=src[c * 128:(c + 1) * 128, :]), 16)
                    w_ld = e.ld.n

                    @blk.scalar
                    def _(s, w_ld=w_ld, sth_=sth_, stg_=stg_, uw2=wsp_st[par]):
                        s.wait_ge(e.ld.s, w_ld)
                        s.wait_ge(e.gq.s, uw2)
                        e.s.inc(s.activation(sth_[:, :], stg_[:, :], AF.Copy))
                    w_s = e.s.n

                    @blk.vector
                    def _(v, w_s=w_s, stl_=stl_, stg_=stg_, sth_=sth_):
                        v.wait_ge(e.s.s, w_s)
                        e.v.inc(v.tensor_tensor(stl_[:, :], stg_[:, :], sth_[:, :], ALU.subtract))
                    w_v = e.v.n
                    wsp_busy[par] = w_v

                    @blk.gpsimd
                    def _(g, nm=nm, c=c, w_v=w_v, wsth=sth_, wstl=stl_):
                        g.wait_ge(e.v.s, w_v)
                        if nm in ("k", "v", "g"):
                            for oh in range(2):
                                e.gq.inc(g.dma_start(out=wsplit[nm][0, oh, :, c * 512:(c + 1) * 512],
                                                     in_=wsth[:, oh * 512:(oh + 1) * 512]), 16)
                                e.gq.inc(g.dma_start(out=wsplit[nm][1, oh, :, c * 512:(c + 1) * 512],
                                                     in_=wstl[:, oh * 512:(oh + 1) * 512]), 16)
                        else:
                            for oc in range(DC):
                                e.gq.inc(g.dma_start(out=wsplit[nm][0, oc, :, c * 128:(c + 1) * 128],
                                                     in_=wsth[:, oc * 128:(oc + 1) * 128]), 16)
                                e.gq.inc(g.dma_start(out=wsplit[nm][1, oc, :, c * 128:(c + 1) * 128],
                                                     in_=wstl[:, oc * 128:(oc + 1) * 128]), 16)
                    wsp_st[par] = e.gq.n
            wsplit_done = e.gq.n

            @blk.vector
            def _(v):
                v.memset(state[:, :], 0.0)
                v.memset(ln[:, :], 0.0)
                v.memset(k_tok[:, :], 0.0)
                v.wait_ge(e.v.s, prep_v)
                e.v.inc(v.memset(zf[:, 0:HN], 1.0))
                v.wait_ge(e.v.s, e.v.n)
                for hh in range(H):
                    hr3, hc3 = (hh % 2) * 64, (hh // 2) * HN
                    v.tensor_scalar(WKB[:, hh * HN:(hh + 1) * HN], zf[:, 0:HN],
                                    e.wkd[:, hh:hh + 1], None, ALU.mult)
                    v.tensor_scalar(WBB[:, hh * HN:(hh + 1) * HN], zf[:, 0:HN],
                                    e.wb[:, hh:hh + 1], None, ALU.mult)
                    e.v.inc(v.tensor_scalar(WSB[hr3:hr3 + 64, hc3:hc3 + HN], zf[hr3:hr3 + 64, 0:HN],
                                            e.wsd[hr3:hr3 + 64, hh:hh + 1], None, ALU.mult))
            p1_init_v = e.v.n

            slot_users = {"wkvg": 0, "wkvg2": 0, "wro": 0, "wro2": 0}
            mix_users = [0, 0]

            def load_kvg(slot, sname, nm, oh):
                @blk.sync
                def _(sy, slot=slot, nm=nm, oh=oh, uw=slot_users[sname]):
                    sy.wait_ge(e.gq.s, wsplit_done)
                    sy.wait_ge(e.t.s, uw)
                    for half in range(2):
                        e.ld.inc(sy.dma_start(out=slot[:, half * DC * 512:(half + 1) * DC * 512],
                                              in_=wsplit[nm][half, oh, :, :]), 16)
                return e.ld.n

            def load_ro_oc(slot, sname, nm, oc):
                @blk.sync
                def _(sy, slot=slot, nm=nm, oc=oc, uw=slot_users[sname]):
                    sy.wait_ge(e.gq.s, wsplit_done)
                    sy.wait_ge(e.t.s, uw)
                    for half in range(2):
                        e.ld.inc(sy.dma_start(out=slot[:, half * DC * 128:(half + 1) * DC * 128],
                                              in_=wsplit[nm][half, oc, :, :]), 16)
                return e.ld.n

            def emit_mix(mcol, ln_ready, mh, ml, uw):
                @blk.vector
                def _(v):
                    v.wait_ge(e.v.s, ln_ready)
                    v.wait_ge(e.t.s, uw)
                    W1 = TT + 1
                    for c in range(DC):
                        lncur = ln[:, c * W1 + 1:(c + 1) * W1]
                        lnsh = ln[:, c * W1:(c + 1) * W1 - 1]
                        xx = sq[:, c * TT:(c + 1) * TT]
                        v.tensor_tensor(xx, lnsh, lncur, ALU.subtract)
                        v.scalar_tensor_tensor(mh[:, c * TT:(c + 1) * TT], xx,
                                               e.maa[:, mcol * DC + c:mcol * DC + c + 1],
                                               lncur, ALU.mult, ALU.add)
                        v.scalar_tensor_tensor(xx, xx,
                                               e.maa[:, mcol * DC + c:mcol * DC + c + 1],
                                               lncur, ALU.mult, ALU.add)
                        e.v.inc(v.tensor_tensor(ml[:, c * TT:(c + 1) * TT], xx,
                                                mh[:, c * TT:(c + 1) * TT], ALU.subtract))
                return e.v.n

            evac_guard = [0, 0]
            x1_store = 0
            for ti in range(1 if dev_dump == 'probe' else NT):
                t0 = ti * TT

                @blk.gpsimd
                def _(g, t0=t0, x1_store=x1_store):
                    g.wait_ge(e.gq.s, max(x1_store, wsplit_done))
                    g.wait_ge(e.v.s, e.v.n)
                    for c in range(DC):
                        e.gq.inc(g.dma_start(out=xt[:, c * TT:(c + 1) * TT],
                                             in_=xin[c * 128:(c + 1) * 128, t0:t0 + TT]), 16)
                x_ld = e.gq.n

                @blk.vector
                def _(v, ti=ti):
                    v.wait_ge(e.gq.s, x_ld)
                    if ti > 0:
                        W1 = TT + 1
                        for c in range(DC):
                            e.v.inc(v.tensor_copy(ln[:, c * W1:c * W1 + 1], ln[:, c * W1 + TT:c * W1 + TT + 1]))
                    else:
                        W1 = TT + 1
                        for c in range(DC):
                            e.v.inc(v.memset(ln[:, c * W1:c * W1 + 1], 0.0))
                guard_v = e.v.n

                ln_done = emit_ln(xt, ln, e.v, guard_v, 0)

                for i, pname in enumerate(["k", "v", "g"]):
                    dst = {"k": k_tok, "v": v_tok, "g": g_tok}[pname]
                    mcol = {"k": 0, "v": 1, "r": 2, "g": 3}[pname]
                    mh, ml = (mxh, mxl) if i % 2 == 0 else (mxh2, mxl2)
                    mix_done = emit_mix(mcol, ln_done, mh, ml, mix_users[i % 2])
                    for oh in range(2):
                        sname = "wkvg" if oh == 0 else "wkvg2"
                        slot = wkvg if oh == 0 else wkvg2
                        w_rdy = load_kvg(slot, sname, pname, oh)
                        for cg in range(CPT):
                            ps = (e.psA if cg % 2 == 0 else e.psB)[:, 0:512]
                            pi = cg % 2

                            @blk.tensor
                            def _(t, slot=slot, w_rdy=w_rdy, mix_done=mix_done, ps=ps,
                                  pi=pi, cg=cg, oh=oh, mh=mh, ml=ml):
                                t.wait_ge(e.v.s, mix_done)
                                t.wait_ge(e.ld.s, w_rdy)
                                t.wait_ge(e.s.s, evac_guard[pi])
                                for dc in range(DC):
                                    xh = mh[:, dc * TT + cg * 128:dc * TT + (cg + 1) * 128]
                                    xl = ml[:, dc * TT + cg * 128:dc * TT + (cg + 1) * 128]
                                    wh_ = slot[:, dc * 512:(dc + 1) * 512]
                                    wl_ = slot[:, DC * 512 + dc * 512:DC * 512 + (dc + 1) * 512]
                                    t.matmul(ps, xh, wh_, start=(dc == 0), stop=False)
                                    t.matmul(ps, xh, wl_, start=False, stop=False)
                                    ins = t.matmul(ps, xl, wh_, start=False, stop=(dc == DC - 1))
                                e.t.inc(ins)
                            tn = e.t.n
                            slot_users[sname] = tn

                            @blk.scalar
                            def _(s, cg=cg, oh=oh, tn=tn, ps=ps, dst=dst, pname=pname):
                                s.wait_ge(e.t.s, tn)
                                fn = AF.Gelu if pname == "g" else AF.Copy
                                e.s.inc(s.activation(dst[:, cg * D + oh * 512:cg * D + oh * 512 + 512], ps, fn))
                            evac_guard[pi] = e.s.n
                    mix_users[i % 2] = e.t.n

                mh, ml = mxh2, mxl2
                mix_done = emit_mix(2, ln_done, mh, ml, mix_users[1])
                for oc in range(DC):
                    sname = "wro" if oc % 2 == 0 else "wro2"
                    slot = wro if oc % 2 == 0 else wro2
                    w_rdy = load_ro_oc(slot, sname, "r", oc)
                    ps = (e.psA if oc % 2 == 0 else e.psB)[:, 0:TT]
                    pi = oc % 2

                    @blk.tensor
                    def _(t, slot=slot, w_rdy=w_rdy, mix_done=mix_done, ps=ps, pi=pi, oc=oc,
                          mh=mh, ml=ml):
                        t.wait_ge(e.v.s, mix_done)
                        t.wait_ge(e.ld.s, w_rdy)
                        t.wait_ge(e.s.s, evac_guard[pi])
                        for dc in range(DC):
                            xh = mh[:, dc * TT:(dc + 1) * TT]
                            xl = ml[:, dc * TT:(dc + 1) * TT]
                            wh_ = slot[:, dc * 128:(dc + 1) * 128]
                            wl_ = slot[:, DC * 128 + dc * 128:DC * 128 + (dc + 1) * 128]
                            t.matmul(ps, wh_, xh, start=(dc == 0), stop=False)
                            t.matmul(ps, wl_, xh, start=False, stop=False)
                            ins = t.matmul(ps, wh_, xl, start=False, stop=(dc == DC - 1))
                        e.t.inc(ins)
                    tn = e.t.n
                    slot_users[sname] = tn

                    @blk.scalar
                    def _(s, oc=oc, tn=tn, ps=ps):
                        s.wait_ge(e.t.s, tn)
                        e.s.inc(s.activation(r_fm[:, oc * TT:(oc + 1) * TT], ps, AF.Copy))
                    evac_guard[pi] = e.s.n
                mix_users[1] = e.t.n

                proj_done = e.s.n

                for cg in range(CPT):
                    ci = ti * CPT + cg
                    ks = k_tok[:, cg * D:(cg + 1) * D]
                    vs = v_tok[:, cg * D:(cg + 1) * D]
                    gs = g_tok[:, cg * D:(cg + 1) * D]

                    kf_half_done = []
                    for half in range(2):
                        @blk.tensor
                        def _(t, ks=ks, proj_done=proj_done, half=half):
                            if half == 0:
                                t.wait_ge(e.s.s, proj_done)
                                t.wait_ge(e.v.s, e.v.n)
                            else:
                                t.wait_ge(e.v.s, kf_half_done[0])
                            for c4 in range(4):
                                c = half * 4 + c4
                                ins = t.transpose(e.psA[:, c4 * 128:(c4 + 1) * 128],
                                                  ks[:, c * 128:(c + 1) * 128], e.eye[:, :])
                            e.t.inc(ins)
                        tn = e.t.n

                        @blk.vector
                        def _(v, half=half, tn=tn):
                            v.wait_ge(e.t.s, tn)
                            e.v.inc(v.tensor_copy(k_fm[:, half * 512:(half + 1) * 512], e.psA[:, 0:512]))
                        kf_half_done.append(e.v.n)
                    kfm_done = e.v.n

                    @blk.vector
                    def _(v, ks=ks, proj_done=proj_done):
                        v.wait_ge(e.s.s, proj_done)
                        e.v.inc(v.tensor_tensor(kw_tok[:, :], ks, WKB[:, :], ALU.mult))
                    kw_done = e.v.n

                    def att_bank(hh):
                        return (e.psD, e.psB, e.psA)[hh % 3][:, 0:128]

                    def att_buf(hh):
                        return (attw, attw2, attw3)[hh % 3]

                    att_t_arr = [0] * H
                    attw_v_arr = [0] * H
                    y_t_arr = [0] * H
                    for hh in range(H + 1):
                        if hh < H:
                            hc, hr = hh // 2, (hh % 2) * 64
                            kfm_h = k_fm[hr:hr + 64, hc * 128:(hc + 1) * 128]
                            rfm_h = r_fm[hr:hr + 64, hc * TT + cg * 128:hc * TT + (cg + 1) * 128]

                            @blk.tensor
                            def _(t, hh=hh, kfm_h=kfm_h, rfm_h=rfm_h, kfm_done=kfm_done,
                                  kw_done=kw_done, proj_done=proj_done):
                                if hh == 0:
                                    t.wait_ge(e.v.s, max(kfm_done, kw_done))
                                elif hh == 1:
                                    t.wait_ge(e.s.s, proj_done)
                                elif hh == 2:
                                    pass
                                else:
                                    t.wait_ge(e.v.s, attw_v_arr[hh - 3])
                                e.t.inc(t.matmul(att_bank(hh), kfm_h, rfm_h, start=True, stop=True))
                            att_t_arr[hh] = e.t.n

                        if hh >= 1:
                            h0 = hh - 1
                            hc0, hr0 = h0 // 2, (h0 % 2) * 64
                            rfm_h0 = r_fm[hr0:hr0 + 64, hc0 * TT + cg * 128:hc0 * TT + (cg + 1) * 128]

                            @blk.vector
                            def _(v, h0=h0):
                                v.wait_ge(e.t.s, att_t_arr[h0])
                                if h0 >= 3:
                                    v.wait_ge(e.t.s, y_t_arr[h0 - 3])
                                e.v.inc(v.tensor_tensor(att_buf(h0)[:, :], att_bank(h0),
                                                        e.wmatT[:, h0 * 128:(h0 + 1) * 128], ALU.mult))
                            attw_v_arr[h0] = e.v.n

                            @blk.tensor
                            def _(t, h0=h0, vs=vs, rfm_h0=rfm_h0, hr0=hr0):
                                t.wait_ge(e.v.s, attw_v_arr[h0])
                                hcol = (h0 // 2) * HN
                                t.matmul(e.psY[:, h0 * HN:(h0 + 1) * HN], att_buf(h0)[:, :],
                                         vs[:, h0 * HN:(h0 + 1) * HN], start=True, stop=True)
                                t.matmul(e.psY2[:, h0 * HN:(h0 + 1) * HN], rfm_h0,
                                         state[hr0:hr0 + 64, hcol:hcol + HN], start=True, stop=True)
                                e.t.inc(t.matmul(e.psC[hr0:hr0 + 64, hcol:hcol + HN],
                                                 kw_tok[:, h0 * HN:(h0 + 1) * HN],
                                                 vs[:, h0 * HN:(h0 + 1) * HN], start=True, stop=True,
                                                 tile_position=(0, hr0)))
                            y_t_arr[h0] = e.t.n

                    all_y_t = e.t.n

                    @blk.scalar
                    def _(s):
                        s.wait_ge(e.t.s, all_y_t)
                        e.s.inc(s.activation(ysb[:, :], e.psY[:, 0:1024], AF.Copy))
                    y1_s = e.s.n

                    @blk.vector
                    def _(v, y1_s=y1_s):
                        v.wait_ge(e.s.s, y1_s)
                        v.tensor_tensor(zf[:, :], e.psY2[:, 0:1024], WBB[:, :], ALU.mult)
                        v.tensor_tensor(ysb[:, :], ysb[:, :], zf[:, :], ALU.add)
                        v.tensor_tensor(state[:, :], state[:, :], WSB[:, :], ALU.mult)
                        e.v.inc(v.tensor_tensor(state[:, :], state[:, :], e.psC[:, 0:(H // 2) * HN], ALU.add))
                    heads_done = e.v.n

                    @blk.vector
                    def _(v, heads_done=heads_done):
                        v.wait_ge(e.v.s, heads_done)
                        for hh in range(H):
                            ins = v.tensor_reduce(gn_m[:, hh:hh + 1], ysb[:, hh * HN:(hh + 1) * HN], AX.X, ALU.add)
                        e.v.inc(ins)
                        v.wait_ge(e.v.s, e.v.n)
                        e.v.inc(v.tensor_scalar(gn_m[:, :], gn_m[:, :], 1.0 / HN, None, ALU.mult))
                        v.wait_ge(e.v.s, e.v.n)
                        for hh in range(H):
                            e.v.inc(v.tensor_scalar(zf[:, hh * HN:(hh + 1) * HN],
                                                    ysb[:, hh * HN:(hh + 1) * HN],
                                                    gn_m[:, hh:hh + 1], None, ALU.subtract))
                    ycen_v = e.v.n

                    @blk.vector
                    def _(v, ycen_v=ycen_v):
                        v.wait_ge(e.v.s, ycen_v)
                        e.v.inc(v.tensor_tensor(k_fm[:, 0:D], zf[:, :], zf[:, :], ALU.mult))
                    ysq_s = e.v.n

                    @blk.vector
                    def _(v, ysq_s=ysq_s):
                        v.wait_ge(e.v.s, ysq_s)
                        for hh in range(H):
                            ins = v.tensor_reduce(gn_v[:, hh:hh + 1], k_fm[:, hh * HN:(hh + 1) * HN], AX.X, ALU.add)
                        e.v.inc(ins)
                        v.wait_ge(e.v.s, e.v.n)
                        e.v.inc(v.tensor_scalar(gn_v[:, :], gn_v[:, :], 1.0 / HN, None, ALU.mult))
                    gnv = e.v.n

                    @blk.scalar
                    def _(s, gnv=gnv):
                        s.wait_ge(e.v.s, gnv)
                        e.s.inc(s.activation(gn_t[:, :], gn_v[:, :], AF.Sqrt, bias=e.epsgn[:, :]))
                    gns = e.s.n

                    @blk.vector
                    def _(v, gns=gns, gs=gs):
                        v.wait_ge(e.s.s, gns)
                        e.v.inc(v.reciprocal(gn_rs[:, :], gn_t[:, :]))
                        v.wait_ge(e.v.s, e.v.n)
                        for hh in range(H):
                            v.tensor_scalar(zf[:, hh * HN:(hh + 1) * HN],
                                            zf[:, hh * HN:(hh + 1) * HN],
                                            gn_rs[:, hh:hh + 1], None, ALU.mult)
                        v.tensor_tensor(zf[:, :], zf[:, :], e.lnwb[:, :], ALU.mult)
                        v.tensor_tensor(zf[:, :], zf[:, :], e.lnbb[:, :], ALU.add)
                        e.v.inc(v.tensor_tensor(zf[:, :], zf[:, :], gs, ALU.mult))
                    zv = e.v.n

                    @blk.scalar
                    def _(s, zv=zv):
                        s.wait_ge(e.v.s, zv)
                        s.wait_ge(e.ld.s, e.ld.n)
                        e.s.inc(s.activation(zh[:, :], zf[:, :], AF.Copy))
                    zhs = e.s.n

                    @blk.vector
                    def _(v, zhs=zhs):
                        v.wait_ge(e.s.s, zhs)
                        e.v.inc(v.tensor_tensor(zl[:, :], zf[:, :], zh[:, :], ALU.subtract))
                    zlv = e.v.n

                    @blk.sync
                    def _(sy, zlv=zlv, cg=cg):
                        sy.wait_ge(e.v.s, zlv)
                        for c in range(DC):
                            e.ld.inc(sy.dma_start(out=zfmh[:, c * TT + cg * 128:c * TT + (cg + 1) * 128],
                                                  in_=zh[:, c * 128:(c + 1) * 128], transpose=True), 16)
                            e.ld.inc(sy.dma_start(out=zfml[:, c * TT + cg * 128:c * TT + (cg + 1) * 128],
                                                  in_=zl[:, c * 128:(c + 1) * 128], transpose=True), 16)
                zt_done = e.ld.n

                for oc in range(DC):
                    sname = "wro" if oc % 2 == 0 else "wro2"
                    slot = wro if oc % 2 == 0 else wro2
                    w_rdy = load_ro_oc(slot, sname, "o", oc)
                    ps = (e.psA if oc % 2 == 0 else e.psB)[:, 0:TT]
                    pi = oc % 2

                    @blk.tensor
                    def _(t, slot=slot, w_rdy=w_rdy, zt_done=zt_done, ps=ps, pi=pi, oc=oc):
                        t.wait_ge(e.ld.s, max(w_rdy, zt_done))
                        t.wait_ge(e.v.s, e.v.n)
                        for dc in range(DC):
                            xh = zfmh[:, dc * TT:(dc + 1) * TT]
                            xl = zfml[:, dc * TT:(dc + 1) * TT]
                            wh_ = slot[:, dc * 128:(dc + 1) * 128]
                            wl_ = slot[:, DC * 128 + dc * 128:DC * 128 + (dc + 1) * 128]
                            t.matmul(ps, wh_, xh, start=(dc == 0), stop=False)
                            t.matmul(ps, wl_, xh, start=False, stop=False)
                            ins = t.matmul(ps, wh_, xl, start=False, stop=(dc == DC - 1))
                        e.t.inc(ins)
                    tn = e.t.n
                    slot_users[sname] = tn

                    @blk.vector
                    def _(v, oc=oc, tn=tn, ps=ps):
                        v.wait_ge(e.t.s, tn)
                        e.v.inc(v.tensor_tensor(xt[:, oc * TT:(oc + 1) * TT],
                                                xt[:, oc * TT:(oc + 1) * TT], ps, ALU.add))
                x1_v = e.v.n
                evac_guard = [0, 0]

                @blk.gpsimd
                def _(g, t0=t0, x1_v=x1_v):
                    g.wait_ge(e.v.s, x1_v)
                    for c in range(DC):
                        e.gq.inc(g.dma_start(out=xs1[c * 128:(c + 1) * 128, t0:t0 + TT],
                                             in_=xt[:, c * TT:(c + 1) * TT]), 16)
                x1_store = e.gq.n

                if dev_dump == "probe" and ti == 0:
                    @blk.gpsimd
                    def _(g):
                        g.wait_ge(e.gq.s, e.gq.n)
                        g.wait_ge(e.v.s, e.v.n)
                        g.wait_ge(e.s.s, e.s.n)
                        
                        e.fin.inc(g.dma_start(out=dbg[0:128, 2048:4096], in_=r_fm[:, :]), 16)
                        e.fin.inc(g.dma_start(out=dbg[640:768, 0:2056], in_=ln[:, :]), 16)
                        e.fin.inc(g.dma_start(out=dbg[128:256, 0:2048], in_=k_tok[:, :]), 16)
                        e.fin.inc(g.dma_start(out=dbg[128:256, 2048:4096], in_=v_tok[:, :]), 16)
                        e.fin.inc(g.dma_start(out=dbg[256:384, 0:2048], in_=g_tok[:, :]), 16)
                        e.fin.inc(g.dma_start(out=dbg[256:384, 2048:2048 + 1024], in_=ysb[:, :]), 16)
                        e.fin.inc(g.dma_start(out=dbg[384:512, 0:1024], in_=zf[:, :]), 16)
                        e.fin.inc(g.dma_start(out=dbg[384:512, 1024:1024 + 512], in_=state[:, :]), 16)
                        e.fin.inc(g.dma_start(out=dbg[512:640, 0:2048], in_=zfmh[:, :], ), 16)
                        e.fin.inc(g.dma_start(out=dbg[512:640, 2048:2048 + 96], in_=st6[:, :]), 16)
                        e.fin.inc(g.dma_start(out=dbg[512:640, 2144:2144 + 16], in_=gn_m[:, :]), 16)
                        e.fin.inc(g.dma_start(out=dbg[512:640, 2160:2160 + 16], in_=gn_v[:, :]), 16)
                        e.fin.inc(g.dma_start(out=dbg[512:640, 2176:2176 + 16], in_=gn_rs[:, :]), 16)
                        g.wait_ge(e.fin.s, e.fin.n)

            if dev_dump == "x1":
                @blk.gpsimd
                def _(g):
                    g.wait_ge(e.gq.s, e.gq.n)
                    e.fin.inc(g.dma_start(out=dbg, in_=xs1), 16)
                    g.wait_ge(e.fin.s, e.fin.n)
            p1.close()

        if n_phases >= 2:
            p2 = ExitStack()
            Ah = sb("Ah", [128, DC * D], BF16, st=p2)
            Al = sb("Al", [128, DC * D], BF16, st=p2)
            Bh = sb("Bh", [128, DC * D], BF16, st=p2)
            Bl = sb("Bl", [128, DC * D], BF16, st=p2)
            tstg = sb("tstg", [128, D], st=p2)
            tstg2 = sb("tstg2", [128, D], st=p2)
            fsum = sb("fsum", [128, DC * TT], st=p2)
            halfpi = sb("halfpi", [128, 1], st=p2)
            bv1 = barrier_vals()

            @blk.vector
            def _(v):
                wait_barrier(v, bv1)
                e.v.inc(v.memset(halfpi[:, :], HALFPI))
            p2v0 = e.v.n

            tstg3 = sb("tstg3", [128, D], st=p2)
            tstg4 = sb("tstg4", [128, D], st=p2)
            ab_stg = [(tstg, tstg2), (tstg3, tstg4)]
            ab_busy = [p2v0, p2v0]
            abi = 0
            for (srcA, srcN, dsth, dstl, use_cos) in [
                (theta_in, theta_n_in, Ah, Al, True),
                (phi_in, phi_n_in, Bh, Bl, False),
            ]:
                for c in range(DC):
                    par = abi % 2
                    abi += 1
                    sA, sB = ab_stg[par]

                    @blk.sync
                    def _(sy, srcA=srcA, srcN=srcN, c=c, bt=bv1["t"], uw=ab_busy[par], sA=sA, sB=sB):
                        sy.wait_ge(e.t.s, bt)
                        sy.wait_ge(e.v.s, uw)
                        e.ld.inc(sy.dma_start(out=sA[:, :], in_=srcA[c * 128:(c + 1) * 128, :]), 16)
                        e.ld.inc(sy.dma_start(out=sB[:, :], in_=srcN[c * 128:(c + 1) * 128, :]), 16)
                    tl = e.ld.n

                    @blk.vector
                    def _(v, tl=tl, sA=sA, sB=sB):
                        v.wait_ge(e.ld.s, tl)
                        e.v.inc(v.tensor_tensor(sA[:, :], sA[:, :], sB[:, :], ALU.add))
                    tv = e.v.n

                    @blk.scalar
                    def _(s, tv=tv, use_cos=use_cos, sA=sA, sB=sB):
                        s.wait_ge(e.v.s, tv)
                        if use_cos:
                            e.s.inc(s.activation(sB[:, :], sA[:, :], AF.Sin, bias=halfpi[:, :]))
                        else:
                            e.s.inc(s.activation(sB[:, :], sA[:, :], AF.Sin))
                    ts_ = e.s.n

                    @blk.vector
                    def _(v, ts_=ts_, c=c, dsth=dsth, dstl=dstl, sB=sB):
                        v.wait_ge(e.s.s, ts_)
                        v.tensor_copy(dsth[:, c * D:(c + 1) * D], sB[:, :])
                        e.v.inc(v.tensor_tensor(dstl[:, c * D:(c + 1) * D], sB[:, :],
                                                dsth[:, c * D:(c + 1) * D], ALU.subtract))
                    ab_busy[par] = e.v.n
            ab_done = e.v.n

            x2_store = 0
            for ti in range(NT):
                t0 = ti * TT

                @blk.gpsimd
                def _(g, t0=t0, x2_store=max(x2_store, 0) if ti > 0 else None):
                    g.wait_ge(e.gq.s, x2_store if x2_store is not None else e.gq.n)
                    g.wait_ge(e.v.s, e.v.n)
                    for c in range(DC):
                        e.gq.inc(g.dma_start(out=xt[:, c * TT:(c + 1) * TT],
                                             in_=xs1[c * 128:(c + 1) * 128, t0:t0 + TT]), 16)
                x_ld = e.gq.n

                @blk.vector
                def _(v):
                    v.wait_ge(e.gq.s, x_ld)
                    e.v.inc(v.memset(ln[:, 0:1], 0.0))
                gv_ = e.v.n

                ln_done = emit_ln(xt, ln, e.v, gv_, 0)

                @blk.scalar
                def _(s, ln_done=ln_done):
                    s.wait_ge(e.v.s, ln_done)
                    s.wait_ge(e.t.s, e.t.n)
                    W1 = TT + 1
                    for c in range(DC):
                        e.s.inc(s.activation(mxh[:, c * TT:(c + 1) * TT],
                                             ln[:, c * W1 + 1:(c + 1) * W1], AF.Copy))
                sp_ = e.s.n

                @blk.vector
                def _(v, sp_=sp_):
                    v.wait_ge(e.s.s, sp_)
                    W1 = TT + 1
                    for c in range(DC):
                        e.v.inc(v.tensor_tensor(mxl[:, c * TT:(c + 1) * TT],
                                                ln[:, c * W1 + 1:(c + 1) * W1],
                                                mxh[:, c * TT:(c + 1) * TT], ALU.subtract))
                mix_done = e.v.n

                evac_v2 = [e.v.n]
                for oc in range(DC):
                    @blk.tensor
                    def _(t, oc=oc, mix_done=mix_done):
                        t.wait_ge(e.v.s, max(mix_done, ab_done))
                        t.wait_ge(e.s.s, e.s.n)
                        for (ps, wh_, wl_) in ((e.psA[:, 0:TT], Ah, Al), (e.psB[:, 0:TT], Bh, Bl)):
                            for dc in range(DC):
                                xh = mxh[:, dc * TT:(dc + 1) * TT]
                                xl = mxl[:, dc * TT:(dc + 1) * TT]
                                ww_h = wh_[:, dc * D + oc * 128: dc * D + oc * 128 + 128]
                                ww_l = wl_[:, dc * D + oc * 128: dc * D + oc * 128 + 128]
                                t.matmul(ps, ww_h, xh, start=(dc == 0), stop=False)
                                t.matmul(ps, ww_l, xh, start=False, stop=False)
                                ins = t.matmul(ps, ww_h, xl, start=False, stop=(dc == DC - 1))
                            e.t.inc(ins)
                    tn = e.t.n

                    @blk.scalar
                    def _(s, tn=tn, oc=oc):
                        s.wait_ge(e.t.s, tn)
                        s.wait_ge(e.v.s, e.v.n)
                        s.activation(sq[:, 0:TT], e.psA[:, 0:TT], AF.Square)
                        e.s.inc(s.activation(sq[:, TT:2 * TT], e.psB[:, 0:TT], AF.Square))
                    sq_s2 = e.s.n

                    @blk.vector
                    def _(v, sq_s2=sq_s2, oc=oc):
                        v.wait_ge(e.s.s, sq_s2)
                        e.v.inc(v.tensor_tensor(fsum[:, oc * TT:(oc + 1) * TT],
                                                sq[:, 0:TT], sq[:, TT:2 * TT], ALU.add))
                    evac_v2[0] = e.v.n

                @blk.scalar
                def _(s):
                    s.wait_ge(e.v.s, evac_v2[0])
                    e.s.inc(s.activation(fsum[:, :], fsum[:, :], AF.Sqrt))
                fs = e.s.n

                @blk.vector
                def _(v, fs=fs):
                    v.wait_ge(e.s.s, fs)
                    for oc in range(DC):
                        e.v.inc(v.tensor_tensor(xt[:, oc * TT:(oc + 1) * TT],
                                                xt[:, oc * TT:(oc + 1) * TT],
                                                fsum[:, oc * TT:(oc + 1) * TT], ALU.add))
                x2_v = e.v.n

                @blk.gpsimd
                def _(g, t0=t0, x2_v=x2_v):
                    g.wait_ge(e.v.s, x2_v)
                    for c in range(DC):
                        e.gq.inc(g.dma_start(out=xs2[c * 128:(c + 1) * 128, t0:t0 + TT],
                                             in_=xt[:, c * TT:(c + 1) * TT]), 16)
                x2_store = e.gq.n

            if dev_dump == "x2":
                @blk.gpsimd
                def _(g):
                    g.wait_ge(e.gq.s, e.gq.n)
                    e.fin.inc(g.dma_start(out=dbg, in_=xs2), 16)
                    g.wait_ge(e.fin.s, e.fin.n)
            p2.close()

        if n_phases >= 3:
            p3 = ExitStack()
            WwPh = sb("WwPh", [128, DC * D], BF16, st=p3)
            WwPl = sb("WwPl", [128, DC * D], BF16, st=p3)
            WrdPh = sb("WrdPh", [128, DC * D], BF16, st=p3)
            WrdPl = sb("WrdPl", [128, DC * D], BF16, st=p3)
            ST = sb("ST", [128, DC * D], st=p3)
            v_tok3 = sb("v_tok3", [128, CPT * D], st=p3)
            w_tok3 = sb("w_tok3", [128, D], st=p3)
            t_tok3 = sb("t_tok3", [128, D], st=p3)
            vgw = sb("vgw", [128, D], st=p3)
            stm = sb("stm", [128, 128], st=p3)
            intr = sb("intr", [128, D], st=p3)
            reads_tok = sb("reads_tok", [128, D], st=p3)
            reads_fm = sb("reads_fm", [128, DC * 128], st=p3)
            rdh = sb("rdh", [128, DC * 128], BF16, st=p3)
            rdl = sb("rdl", [128, DC * 128], BF16, st=p3)
            x3f = sb("x3f", [128, D], st=p3)
            x3bh = sb("x3bh", [128, D], BF16, st=p3)
            x3bl = sb("x3bl", [128, D], BF16, st=p3)
            MT = sb("MT", [128, 128], st=p3)
            l1row = sb("l1row", [1, 8], st=p3)
            lg_col = sb("lg_col", [128, 1], st=p3)
            al_col = sb("al_col", [128, 1], st=p3)
            dec_col = sb("dec_col", [128, 1], st=p3)
            gw_col = sb("gw_col", [128, 1], st=p3)
            gC_col = sb("gC_col", [128, 1], st=p3)
            bv2 = barrier_vals()

            for (srcW, dsth, dstl) in [(WwT_in, WwPh, WwPl), (WrdT_in, WrdPh, WrdPl)]:
                for c in range(DC):
                    @blk.sync
                    def _(sy, srcW=srcW, c=c, bt=bv2["t"]):
                        sy.wait_ge(e.t.s, bt)
                        sy.wait_ge(e.v.s, e.v.n)
                        sy.wait_ge(e.s.s, e.s.n)
                        e.ld.inc(sy.dma_start(out=wstg[:, :], in_=srcW[c * 128:(c + 1) * 128, :]), 16)
                    tl = e.ld.n

                    @blk.scalar
                    def _(s, tl=tl, c=c, dsth=dsth):
                        s.wait_ge(e.ld.s, tl)
                        e.s.inc(s.activation(dsth[:, c * D:(c + 1) * D], wstg[:, :], AF.Copy))
                    ts_ = e.s.n

                    @blk.vector
                    def _(v, ts_=ts_, c=c, dsth=dsth, dstl=dstl):
                        v.wait_ge(e.s.s, ts_)
                        e.v.inc(v.tensor_tensor(dstl[:, c * D:(c + 1) * D], wstg[:, :],
                                                dsth[:, c * D:(c + 1) * D], ALU.subtract))
            wsplit3 = e.v.n

            @blk.sync
            def _(sy, bt=bv2["t"]):
                sy.wait_ge(e.t.s, bt)
                e.ld.inc(sy.dma_start(out=l1row[0:1, 0:4], in_=l1s_in), 16)
                e.ld.inc(sy.dma_start(out=MT[:, :], in_=c2t_in), 16)
            sc_ld = e.ld.n

            @blk.scalar
            def _(s):
                s.wait_ge(e.ld.s, sc_ld)
                s.activation(l1row[0:1, 4:5], l1row[0:1, 0:1], AF.Sigmoid)
                e.s.inc(s.activation(l1row[0:1, 5:6], l1row[0:1, 4:5], AF.Ln))
            sc_s = e.s.n

            @blk.gpsimd
            def _(g):
                g.wait_ge(e.s.s, sc_s)
                e.gq.inc(g.dma_start(out=scal_d[0:1, 0:8], in_=l1row[0:1, 0:8]), 16)
            sc_g = e.gq.n

            @blk.sync
            def _(sy):
                sy.wait_ge(e.gq.s, sc_g)
                e.ld.inc(sy.dma_start(out=lg_col[:, :], in_=scal_d[0:1, 5:6].broadcast_to([128, 1])), 16)
                e.ld.inc(sy.dma_start(out=al_col[:, :], in_=scal_d[0:1, 1:2].broadcast_to([128, 1])), 16)
            sc_b = e.ld.n

            @blk.vector
            def _(v):
                v.wait_ge(e.ld.s, sc_b)
                e.v.inc(v.tensor_scalar(dec_col[:, :], e.ioi[:, :], lg_col[:, :], None, ALU.mult))
                e.v.inc(v.tensor_scalar(gw_col[:, :], e.ior[:, :], lg_col[:, :], None, ALU.mult))
                e.v.inc(v.tensor_scalar(gC_col[:, :], lg_col[:, :], float(Q), None, ALU.mult))
                e.v.inc(v.tensor_scalar(MT[:, :], MT[:, :], lg_col[:, :], None, ALU.mult))
                v.wait_ge(e.v.s, e.v.n)
            sc_v = e.v.n

            @blk.scalar
            def _(s):
                s.wait_ge(e.v.s, sc_v)
                s.activation(dec_col[:, :], dec_col[:, :], AF.Exp)
                s.activation(gw_col[:, :], gw_col[:, :], AF.Exp)
                s.activation(gC_col[:, :], gC_col[:, :], AF.Exp)
                s.activation(al_col[:, :], al_col[:, :], AF.Exp)
                e.s.inc(s.activation(MT[:, :], MT[:, :], AF.Exp))
            sc_s2 = e.s.n

            @blk.vector
            def _(v):
                v.wait_ge(e.s.s, sc_s2)
                v.memset(ST[:, :], 0.0)
                for c in range(DC):
                    e.v.inc(v.memset(ln[:, c * (TT + 1):c * (TT + 1) + 1], 0.0))
            p3_init = e.v.n

            x3_store = 0
            st_upd_v = p3_init
            for ti in range(NT):
                t0 = ti * TT

                @blk.gpsimd
                def _(g, t0=t0, w=x3_store if ti > 0 else None):
                    g.wait_ge(e.gq.s, w if w is not None else e.gq.n)
                    g.wait_ge(e.v.s, e.v.n)
                    for c in range(DC):
                        e.gq.inc(g.dma_start(out=xt[:, c * TT:(c + 1) * TT],
                                             in_=xs2[c * 128:(c + 1) * 128, t0:t0 + TT]), 16)
                x_ld = e.gq.n

                @blk.vector
                def _(v, ti=ti):
                    v.wait_ge(e.gq.s, x_ld)
                    W1 = TT + 1
                    if ti > 0:
                        for c in range(DC):
                            e.v.inc(v.tensor_copy(ln[:, c * W1:c * W1 + 1], ln[:, c * W1 + TT:c * W1 + TT + 1]))
                    else:
                        for c in range(DC):
                            e.v.inc(v.memset(ln[:, c * W1:c * W1 + 1], 0.0))
                guard_v = e.v.n

                ln_done = emit_ln(xt, ln, e.v, guard_v, 2)

                @blk.scalar
                def _(s, ln_done=ln_done):
                    s.wait_ge(e.v.s, ln_done)
                    s.wait_ge(e.t.s, e.t.n)
                    W1 = TT + 1
                    for c in range(DC):
                        e.s.inc(s.activation(mxh[:, c * TT:(c + 1) * TT],
                                             ln[:, c * W1 + 1:(c + 1) * W1], AF.Copy))
                sp_ = e.s.n

                @blk.vector
                def _(v, sp_=sp_):
                    v.wait_ge(e.s.s, sp_)
                    W1 = TT + 1
                    for c in range(DC):
                        e.v.inc(v.tensor_tensor(mxl[:, c * TT:(c + 1) * TT],
                                                ln[:, c * W1 + 1:(c + 1) * W1],
                                                mxh[:, c * TT:(c + 1) * TT], ALU.subtract))
                mix_done = e.v.n

                for cg in range(CPT):
                    ps = (e.psA if cg % 2 == 0 else e.psB)[:, 0:512]
                    for oh in range(2):
                        @blk.tensor
                        def _(t, cg=cg, oh=oh, mix_done=mix_done, ps=ps):
                            t.wait_ge(e.v.s, max(mix_done, wsplit3))
                            t.wait_ge(e.s.s, e.s.n)
                            for dc in range(DC):
                                xh = mxh[:, dc * TT + cg * 128:dc * TT + (cg + 1) * 128]
                                xl = mxl[:, dc * TT + cg * 128:dc * TT + (cg + 1) * 128]
                                wh_ = WwPh[:, dc * D + oh * 512:dc * D + oh * 512 + 512]
                                wl_ = WwPl[:, dc * D + oh * 512:dc * D + oh * 512 + 512]
                                t.matmul(ps[:, 0:512] if False else ps, xh, wh_, start=(dc == 0), stop=False)
                                t.matmul(ps, xh, wl_, start=False, stop=False)
                                ins = t.matmul(ps, xl, wh_, start=False, stop=(dc == DC - 1))
                            e.t.inc(ins)
                        tn = e.t.n

                        @blk.scalar
                        def _(s, cg=cg, oh=oh, tn=tn, ps=ps):
                            s.wait_ge(e.t.s, tn)
                            e.s.inc(s.activation(v_tok3[:, cg * D + oh * 512:cg * D + oh * 512 + 512],
                                                 ps, AF.Copy))
                vproj_done = e.s.n

                for cg in range(CPT):
                    ci = ti * CPT + cg
                    W1 = TT + 1
                    vs3 = v_tok3[:, cg * D:(cg + 1) * D]

                    trdone = []
                    for half in range(2):
                        @blk.tensor
                        def _(t, half=half, cg=cg, vproj_done=vproj_done):
                            if half == 0:
                                t.wait_ge(e.s.s, vproj_done)
                                t.wait_ge(e.v.s, e.v.n)
                            else:
                                t.wait_ge(e.v.s, trdone[0])
                            for c4 in range(4):
                                c = half * 4 + c4
                                ins = t.transpose(e.psC[:, c4 * 128:(c4 + 1) * 128],
                                                  ln[:, c * W1 + cg * 128:c * W1 + cg * 128 + 128],
                                                  e.eye[:, :])
                            e.t.inc(ins)
                        tn = e.t.n

                        @blk.vector
                        def _(v, half=half, tn=tn):
                            v.wait_ge(e.t.s, tn)
                            e.v.inc(v.tensor_copy(w_tok3[:, half * 512:(half + 1) * 512], e.psC[:, 0:512]))
                        trdone.append(e.v.n)

                    @blk.vector
                    def _(v, cg=cg):
                        v.wait_ge(e.v.s, trdone[1])
                        for c in range(DC):
                            e.v.inc(v.tensor_tensor(sq[:, c * TT + cg * 128:c * TT + (cg + 1) * 128],
                                                    xt[:, c * TT + cg * 128:c * TT + (cg + 1) * 128],
                                                    ln[:, c * W1 + cg * 128 + 1:c * W1 + (cg + 1) * 128 + 1],
                                                    ALU.add))
                    tfm_v = e.v.n

                    trdone2 = []
                    for half in range(2):
                        @blk.tensor
                        def _(t, half=half, cg=cg, tfm_v=tfm_v):
                            if half == 0:
                                t.wait_ge(e.v.s, tfm_v)
                            else:
                                t.wait_ge(e.v.s, trdone2[0])
                            for c4 in range(4):
                                c = half * 4 + c4
                                ins = t.transpose(e.psC[:, c4 * 128:(c4 + 1) * 128],
                                                  sq[:, c * TT + cg * 128:c * TT + (cg + 1) * 128],
                                                  e.eye[:, :])
                            e.t.inc(ins)
                        tn = e.t.n

                        @blk.vector
                        def _(v, half=half, tn=tn):
                            v.wait_ge(e.t.s, tn)
                            e.v.inc(v.tensor_copy(t_tok3[:, half * 512:(half + 1) * 512], e.psC[:, 0:512]))
                        trdone2.append(e.v.n)

                    @blk.vector
                    def _(v, vs3=vs3):
                        v.wait_ge(e.s.s, vproj_done)
                        e.v.inc(v.tensor_scalar(vgw[:, :], vs3, gw_col[:, :], None, ALU.mult))
                    vgw_v = e.v.n

                    @blk.tensor
                    def _(t, cg=cg, st_upd_v=st_upd_v):
                        t.wait_ge(e.v.s, max(trdone[1], st_upd_v))
                        for c in range(DC):
                            ins = t.matmul(e.psD[:, 0:128],
                                           ln[:, c * W1 + cg * 128:c * W1 + cg * 128 + 128],
                                           ln[:, c * W1 + cg * 128 + 1:c * W1 + (cg + 1) * 128 + 1],
                                           start=(c == 0), stop=(c == DC - 1))
                        e.t.inc(ins)
                    stp_t = e.t.n

                    @blk.vector
                    def _(v, stp_t=stp_t):
                        v.wait_ge(e.t.s, stp_t)
                        e.v.inc(v.tensor_tensor(stm[:, :], e.psD[:, 0:128], MT[:, :], ALU.mult))
                    stm_v = e.v.n

                    @blk.tensor
                    def _(t, cg=cg, stm_v=stm_v, vs3=vs3):
                        t.wait_ge(e.v.s, stm_v)
                        for nh in range(2):
                            ins = t.matmul(e.psY2[:, nh * 512:(nh + 1) * 512], stm[:, :],
                                           vs3[:, nh * 512:(nh + 1) * 512], start=True, stop=True)
                        e.t.inc(ins)
                        for nh in range(2):
                            for c in range(DC):
                                ins = t.matmul(e.psY[:, nh * 512:(nh + 1) * 512],
                                               ln[:, c * W1 + cg * 128 + 1:c * W1 + (cg + 1) * 128 + 1],
                                               ST[:, c * D + nh * 512:c * D + nh * 512 + 512],
                                               start=(c == 0), stop=(c == DC - 1))
                            e.t.inc(ins)
                    ii_t = e.t.n

                    @blk.scalar
                    def _(s, ii_t=ii_t):
                        s.wait_ge(e.t.s, ii_t)
                        e.s.inc(s.activation(intr[:, :], e.psY2[:, 0:1024], AF.Copy))
                    intr_s = e.s.n

                    upd_t = []
                    for ec in range(DC):
                        @blk.tensor
                        def _(t, ec=ec, vgw_v=vgw_v, intr_s=intr_s):
                            if ec == 0:
                                t.wait_ge(e.v.s, vgw_v)
                                t.wait_ge(e.s.s, max(e.s.n, intr_s))
                            if ec >= 2:
                                t.wait_ge(e.v.s, upd_t[ec - 2])
                            ps = (e.psA if ec % 2 == 0 else e.psB)[:, 0:512]
                            for nh in range(2):
                                ins = t.matmul(ps[:, 0:512], w_tok3[:, ec * 128:(ec + 1) * 128],
                                               vgw[:, nh * 512:(nh + 1) * 512],
                                               start=(nh == 0), stop=False) if False else None
                            ins = t.matmul(ps[:, 0:512], w_tok3[:, ec * 128:(ec + 1) * 128],
                                           vgw[:, 0:512], start=True, stop=True)
                            e.t.inc(ins)
                            ins = t.matmul((e.psD if ec % 2 == 0 else e.psY2)[:, 0:512],
                                           w_tok3[:, ec * 128:(ec + 1) * 128],
                                           vgw[:, 512:1024], start=True, stop=True)
                            e.t.inc(ins)
                        tn2 = e.t.n

                        @blk.vector
                        def _(v, ec=ec, tn2=tn2):
                            v.wait_ge(e.t.s, tn2)
                            ps1 = (e.psA if ec % 2 == 0 else e.psB)[:, 0:512]
                            ps2 = (e.psD if ec % 2 == 0 else e.psY2)[:, 0:512]
                            v.scalar_tensor_tensor(ST[:, ec * D:ec * D + 512],
                                                   ST[:, ec * D:ec * D + 512],
                                                   gC_col[:, :], ps1, ALU.mult, ALU.add)
                            e.v.inc(v.scalar_tensor_tensor(ST[:, ec * D + 512:ec * D + 1024],
                                                           ST[:, ec * D + 512:ec * D + 1024],
                                                           gC_col[:, :], ps2, ALU.mult, ALU.add))
                        upd_t.append(e.v.n)
                    st_upd_v = e.v.n

                    @blk.vector
                    def _(v, intr_s=intr_s):
                        v.wait_ge(e.s.s, intr_s)
                        e.v.inc(v.scalar_tensor_tensor(reads_tok[:, :], e.psY[:, 0:1024],
                                                       dec_col[:, :], intr[:, :],
                                                       ALU.mult, ALU.add))
                    reads_v = e.v.n

                    trdone3 = []
                    for half in range(2):
                        @blk.tensor
                        def _(t, half=half, reads_v=reads_v):
                            if half == 0:
                                t.wait_ge(e.v.s, reads_v)
                            else:
                                t.wait_ge(e.v.s, trdone3[0])
                            for c4 in range(4):
                                c = half * 4 + c4
                                ins = t.transpose(e.psC[:, c4 * 128:(c4 + 1) * 128],
                                                  reads_tok[:, c * 128:(c + 1) * 128], e.eye[:, :])
                            e.t.inc(ins)
                        tn = e.t.n

                        @blk.vector
                        def _(v, half=half, tn=tn):
                            v.wait_ge(e.t.s, tn)
                            e.v.inc(v.tensor_copy(reads_fm[:, half * 512:(half + 1) * 512], e.psC[:, 0:512]))
                        trdone3.append(e.v.n)

                    @blk.scalar
                    def _(s):
                        s.wait_ge(e.v.s, trdone3[1])
                        e.s.inc(s.activation(rdh[:, :], reads_fm[:, :], AF.Copy))
                    rdh_s = e.s.n

                    @blk.vector
                    def _(v, rdh_s=rdh_s):
                        v.wait_ge(e.s.s, rdh_s)
                        e.v.inc(v.tensor_tensor(rdl[:, :], reads_fm[:, :], rdh[:, :], ALU.subtract))
                    rdl_v = e.v.n

                    @blk.tensor
                    def _(t, rdl_v=rdl_v):
                        t.wait_ge(e.v.s, rdl_v)
                        for nh in range(2):
                            for dc in range(DC):
                                wh_ = WrdPh[:, dc * D + nh * 512:dc * D + nh * 512 + 512]
                                wl_ = WrdPl[:, dc * D + nh * 512:dc * D + nh * 512 + 512]
                                xh = rdh[:, dc * 128:(dc + 1) * 128]
                                xl = rdl[:, dc * 128:(dc + 1) * 128]
                                t.matmul(e.psY[:, nh * 512:(nh + 1) * 512], xh, wh_, start=(dc == 0), stop=False)
                                t.matmul(e.psY[:, nh * 512:(nh + 1) * 512], xh, wl_, start=False, stop=False)
                                ins = t.matmul(e.psY[:, nh * 512:(nh + 1) * 512], xl, wh_, start=False,
                                               stop=(dc == DC - 1))
                            e.t.inc(ins)
                    rp_t = e.t.n

                    @blk.vector
                    def _(v, rp_t=rp_t):
                        v.wait_ge(e.t.s, rp_t)
                        e.v.inc(v.scalar_tensor_tensor(x3f[:, :], e.psY[:, 0:1024], al_col[:, :],
                                                       t_tok3[:, :], ALU.mult, ALU.add))
                    x3_v = e.v.n

                    @blk.vector
                    def _(v, ci=ci, x3_v=x3_v):
                        v.wait_ge(e.v.s, x3_v)
                        e.v.inc(v.tensor_reduce(e.m4[:, ci:ci + 1], x3f[:, :], AX.X, ALU.add))
                        v.wait_ge(e.v.s, e.v.n)
                        e.v.inc(v.tensor_scalar(e.m4[:, ci:ci + 1], e.m4[:, ci:ci + 1], 1.0 / D, None, ALU.mult))
                        v.wait_ge(e.v.s, e.v.n)
                        e.v.inc(v.tensor_scalar(vgw[:, :], x3f[:, :], e.m4[:, ci:ci + 1], None, ALU.subtract))
                        e.v.inc(v.tensor_reduce(e.cmx4[:, ci:ci + 1], vgw[:, :], AX.X, ALU.max,
                                                apply_absolute_value=True))
                        e.v.inc(v.tensor_tensor(vgw[:, :], vgw[:, :], vgw[:, :], ALU.mult))
                        e.v.inc(v.tensor_reduce(e.var4[:, ci:ci + 1], vgw[:, :], AX.X, ALU.add))
                    s4b = e.v.n

                    @blk.scalar
                    def _(s, s4b=s4b):
                        s.wait_ge(e.v.s, s4b)
                        s.wait_ge(e.gq.s, e.gq.n)
                        e.s.inc(s.activation(x3bh[:, :], x3f[:, :], AF.Copy))
                    x3h_s = e.s.n

                    @blk.vector
                    def _(v, x3h_s=x3h_s):
                        v.wait_ge(e.s.s, x3h_s)
                        e.v.inc(v.tensor_tensor(x3bl[:, :], x3f[:, :], x3bh[:, :], ALU.subtract))
                    x3l_v = e.v.n

                    @blk.gpsimd
                    def _(g, ci=ci, x3l_v=x3l_v):
                        g.wait_ge(e.v.s, x3l_v)
                        e.gq.inc(g.dma_start(out=x3h_d[ci * 128:(ci + 1) * 128, :], in_=x3bh[:, :]), 16)
                        e.gq.inc(g.dma_start(out=x3l_d[ci * 128:(ci + 1) * 128, :], in_=x3bl[:, :]), 16)
                        if dev_dump == "x3" or n_phases == 3:
                            e.gq.inc(g.dma_start(out=out[ci * 128:(ci + 1) * 128, :], in_=x3f[:, :]), 16)
                    x3_store = e.gq.n

            if n_phases == 3:
                @blk.gpsimd
                def _(g):
                    g.wait_ge(e.gq.s, e.gq.n)
                    e.fin.inc(g.dma_start(out=scal_d[0:1, 8:9], in_=l1row[0:1, 0:1]), 16)
                    g.wait_ge(e.fin.s, e.fin.n)
            p3.close()

        psh.close()

        if n_phases >= 4:
            p4 = ExitStack()
            kwgT = sb("kwgT", [128, DC * DFF], BF16, st=p4)
            kwuT = sb("kwuT", [128, DC * DFF], BF16, st=p4)
            kwdT = sb("kwdT", [128, IC * D], BF16, st=p4)
            qstg = sb("qstg", [128, DFF], st=p4)
            qu = sb("qu", [128, DFF], st=p4)
            qk = sb("qk", [128, DFF], I32, st=p4)
            qb = sb("qb", [128, DFF], BF16, st=p4)
            wsg = sb("wsg", [128, IC], st=p4)
            wsu = sb("wsu", [128, IC], st=p4)
            wsd = sb("wsd", [128, DC], st=p4)
            wsgx = sb("wsgx", [128, IC], st=p4)
            wsux = sb("wsux", [128, IC], st=p4)
            m4 = e.m4
            var4 = e.var4
            cmx4 = e.cmx4
            rs4 = sb("rs4", [128, NCH], st=p4)
            rsi4 = sb("rsi4", [128, NCH], st=p4)
            sb4 = sb("sb4", [128, NCH], st=p4)
            x3th = sb("x3th", [128, D], BF16, st=p4)
            x3tl = sb("x3tl", [128, D], BF16, st=p4)
            x3f4 = sb("x3f4", [128, D], st=p4)
            scr4 = sb("scr4", [128, D], st=p4)
            kxf = sb("kxf", [128, DC * 512], BF16, st=p4)
            sgt = sb("sgt", [128, 512], st=p4)
            sut = sb("sut", [128, 512], st=p4)
            htile = sb("htile", [128, 512], st=p4)
            hmax = sb("hmax", [128, 1], st=p4)
            hab = sb("hab", [128, IC], st=p4)
            wsdq = sb("wsdq", [128, D], st=p4)
            p4row = sb("p4row", [1, 16], st=p4)
            mcol1 = sb("mcol1", [128, 1], st=p4)
            mcol2 = sb("mcol2", [128, 1], st=p4)
            colbuf = nc.dram_tensor("colbuf", [1, 256], F32).ap()
            bv3 = barrier_vals()

            winv_g = sb("winv_g", [128, IC], st=p4)
            winv_u = sb("winv_u", [128, IC], st=p4)
            winv_d = sb("winv_d", [128, DC], st=p4)
            wsd2 = sb("wsd2", [128, DC], st=p4)
            wsd3 = sb("wsd3", [128, DC], st=p4)
            units = []
            for rc in range(IC):
                units.append((wg_in, rc, 0, D, kwgT, DFF, (wsg, rc)))
            for rc in range(IC):
                units.append((wu_in, rc, 0, D, kwuT, DFF, (wsu, rc)))
            for rc in range(DC):
                units.append((wd_in, rc, 0, 1024, kwdT, D, (wsd, rc)))
                units.append((wd_in, rc, 1024, 1024, kwdT, D, (wsd2, rc)))
                units.append((wd_in, rc, 2048, DFF - 2048, kwdT, D, (wsd3, rc)))

            wbusy = [bv3["v"], bv3["v"]]
            for ui, (sap, rc, co, wd_, dstT, dstr, (wscol, wsi)) in enumerate(units):
                par = ui % 2
                wbuf = (qstg, qu)[par]

                @blk.sync
                def _(sy, sap=sap, rc=rc, co=co, wd_=wd_, wbuf=wbuf, bt=bv3["t"], uw=wbusy[par]):
                    sy.wait_ge(e.t.s, bt)
                    sy.wait_ge(e.v.s, uw)
                    e.ld.inc(sy.dma_start(out=wbuf[:, 0:wd_],
                                          in_=sap[rc * 128:(rc + 1) * 128, co:co + wd_]), 16)
                wl = e.ld.n

                @blk.vector
                def _(v, wl=wl, wbuf=wbuf, wd_=wd_, wscol=wscol, wsi=wsi):
                    v.wait_ge(e.ld.s, wl)
                    e.v.inc(v.tensor_reduce(wscol[:, wsi:wsi + 1], wbuf[:, 0:wd_],
                                            AX.X, ALU.max, apply_absolute_value=True))
                wbusy[par] = e.v.n

            @blk.vector
            def _(v):
                v.wait_ge(e.v.s, e.v.n)
                v.tensor_tensor(wsd[:, :], wsd[:, :], wsd2[:, :], ALU.max)
                e.v.inc(v.tensor_tensor(wsd[:, :], wsd[:, :], wsd3[:, :], ALU.max))
                v.wait_ge(e.v.s, e.v.n)
                v.reciprocal(winv_g[:, :], wsg[:, :])
                v.reciprocal(winv_u[:, :], wsu[:, :])
                e.v.inc(v.reciprocal(winv_d[:, :], wsd[:, :]))
                v.wait_ge(e.v.s, e.v.n)
            winv_v = e.v.n

            WINV = {id(wg_in): winv_g, id(wu_in): winv_u, id(wd_in): winv_d}
            wbusy = [winv_v, winv_v]
            ubusy = [winv_v, winv_v]
            tr_prev = e.ld.n
            for ui, (sap, rc, co, wd_, dstT, dstr, _ws) in enumerate(units):
                par = ui % 2
                wbuf = (qstg, qu)[par]
                ups = (e.psY, e.psY2)[par]

                @blk.sync
                def _(sy, sap=sap, rc=rc, co=co, wd_=wd_, wbuf=wbuf, uw=wbusy[par]):
                    sy.wait_ge(e.v.s, uw)
                    e.ld.inc(sy.dma_start(out=wbuf[:, 0:wd_],
                                          in_=sap[rc * 128:(rc + 1) * 128, co:co + wd_]), 16)
                wl = e.ld.n

                @blk.vector
                def _(v, wl=wl, wbuf=wbuf, ups=ups, wd_=wd_, rc=rc, winv=WINV[id(sap)],
                      trp=tr_prev):
                    v.wait_ge(e.ld.s, max(wl, trp))
                    v.tensor_scalar(ups[:, 0:wd_], wbuf[:, 0:wd_], winv[:, rc:rc + 1], None, ALU.mult)
                    v.tensor_scalar(ups[:, 0:wd_], ups[:, 0:wd_], 127.0, 0.5, ALU.mult, ALU.add)
                    v.tensor_copy(qk[:, 0:wd_], ups[:, 0:wd_])
                    v.tensor_copy(wbuf[:, 0:wd_], qk[:, 0:wd_])
                    v.tensor_tensor(ups[:, 0:wd_], ups[:, 0:wd_], wbuf[:, 0:wd_], ALU.subtract)
                    v.tensor_scalar(ups[:, 0:wd_], ups[:, 0:wd_], 0.0, None, ALU.is_lt)
                    v.tensor_tensor(qb[:, 0:wd_], wbuf[:, 0:wd_], ups[:, 0:wd_], ALU.subtract)
                    v.tensor_scalar(qb[:, 0:wd_], qb[:, 0:wd_], -128.0, None, ALU.max)
                    e.v.inc(v.tensor_scalar(qb[:, 0:wd_], qb[:, 0:wd_], 127.0, None, ALU.min))
                qv = e.v.n
                wbusy[par] = qv
                ubusy[par] = qv

                @blk.sync
                def _(sy, rc=rc, co=co, qv=qv, dstT=dstT, wd_=wd_, dstr=dstr):
                    sy.wait_ge(e.v.s, qv)
                    for tc in range(wd_ // 128):
                        e.ld.inc(sy.dma_start(
                            out=dstT[:, (co // 128 + tc) * dstr + rc * 128:(co // 128 + tc) * dstr + (rc + 1) * 128],
                            in_=qb[:, tc * 128:(tc + 1) * 128], transpose=True), 16)
                tr_prev = e.ld.n
            wq_done_ld = e.ld.n
            wq_done_v = e.v.n

            @blk.sync
            def _(sy, bt=bv3["t"]):
                sy.wait_ge(e.t.s, bt)
                e.ld.inc(sy.dma_start(out=p4row[0:1, 0:4], in_=l1s_in), 16)
            p4l = e.ld.n

            @blk.scalar
            def _(s):
                s.wait_ge(e.ld.s, p4l)
                e.s.inc(s.activation(p4row[0:1, 4:5], p4row[0:1, 2:3], AF.Sigmoid))
                s.wait_ge(e.s.s, e.s.n)
                e.s.inc(s.activation(p4row[0:1, 5:6], p4row[0:1, 3:4], AF.Sigmoid))
                s.wait_ge(e.s.s, e.s.n)
            p4s = e.s.n

            @blk.vector
            def _(v, bv=bv3["v"]):
                v.wait_ge(e.v.s, bv)
                v.wait_ge(e.s.s, p4s)
                e.v.inc(v.tensor_scalar(var4[:, :], var4[:, :], 1.0 / D, None, ALU.mult))
            v4a = e.v.n

            @blk.scalar
            def _(s):
                s.wait_ge(e.v.s, v4a)
                e.s.inc(s.activation(var4[:, :], var4[:, :], AF.Sqrt, bias=e.epsln[:, :]))
            v4b = e.s.n

            @blk.vector
            def _(v):
                v.wait_ge(e.s.s, v4b)
                e.v.inc(v.reciprocal(rs4[:, :], var4[:, :]))
                v.wait_ge(e.v.s, e.v.n)
                e.v.inc(v.tensor_tensor(cmx4[:, :], cmx4[:, :], rs4[:, :], ALU.mult))
                v.wait_ge(e.v.s, e.v.n)
                e.v.inc(v.tensor_reduce(hmax[:, :], cmx4[:, :], AX.X, ALU.max))
            v4c = e.v.n

            @blk.gpsimd
            def _(g):
                g.wait_ge(e.v.s, v4c)
                e.gq.inc(g.dma_start(out=colbuf[0:1, 0:128], in_=hmax[:, 0:1]), 16)
            c1g = e.gq.n

            @blk.sync
            def _(sy):
                sy.wait_ge(e.gq.s, c1g)
                e.ld.inc(sy.dma_start(out=qstg[0:1, 0:128], in_=colbuf[0:1, 0:128]), 16)
            c1l = e.ld.n

            @blk.vector
            def _(v):
                v.wait_ge(e.ld.s, c1l)
                v.wait_ge(e.v.s, wq_done_v)
                e.v.inc(v.tensor_reduce(qstg[0:1, 128:129], qstg[0:1, 0:128], AX.X, ALU.max))
            c1v = e.v.n

            @blk.gpsimd
            def _(g):
                g.wait_ge(e.v.s, c1v)
                e.gq.inc(g.dma_start(out=cc_in1.ap(), in_=qstg[0:1, 128:129]), 16)
                g.wait_ge(e.gq.s, e.gq.n)
                e.cc.inc(g.collective_compute("AllReduce", ALU.max,
                                              replica_groups=[list(range(B))],
                                              ins=[cc_in1.ap().opt()],
                                              outs=[cc_out1.ap().opt()]))
                g.wait_ge(e.cc.s, e.cc.n)
                e.gq.inc(g.dma_start(out=qstg[0:1, 130:131], in_=cc_out1.ap()), 16)
            cc1_g = e.gq.n

            @blk.vector
            def _(v):
                v.wait_ge(e.gq.s, cc1_g)
                e.v.inc(v.tensor_tensor(qstg[0:1, 131:132], qstg[0:1, 130:131], p4row[0:1, 4:5], ALU.mult))
                v.wait_ge(e.v.s, e.v.n)
                e.v.inc(v.tensor_scalar(qstg[0:1, 131:132], qstg[0:1, 131:132], 1.0 / 127.0, None, ALU.mult))
                v.wait_ge(e.v.s, e.v.n)
                e.v.inc(v.reciprocal(qstg[0:1, 132:133], qstg[0:1, 131:132]))
                v.wait_ge(e.v.s, e.v.n)
            sc1_v = e.v.n

            @blk.gpsimd
            def _(g):
                g.wait_ge(e.v.s, sc1_v)
                e.gq.inc(g.dma_start(out=scal_d[0:1, 0:4], in_=qstg[0:1, 130:134]), 16)
            sc1_g = e.gq.n

            @blk.sync
            def _(sy):
                sy.wait_ge(e.gq.s, sc1_g)
                e.ld.inc(sy.dma_start(out=mcol1[:, :], in_=scal_d[0:1, 2:3].broadcast_to([128, 1])), 16)
            sc1_b = e.ld.n

            @blk.vector
            def _(v):
                v.wait_ge(e.ld.s, sc1_b)
                e.v.inc(v.tensor_scalar(rsi4[:, :], rs4[:, :], mcol1[:, :], None, ALU.mult))
                v.wait_ge(e.v.s, e.v.n)
                e.v.inc(v.tensor_tensor(sb4[:, :], m4[:, :], rsi4[:, :], ALU.mult))
                v.wait_ge(e.v.s, e.v.n)
                e.v.inc(v.tensor_scalar(sb4[:, :], sb4[:, :], -1.0, 0.5, ALU.mult, ALU.add))
                v.wait_ge(e.v.s, e.v.n)
            sc2_v = e.v.n

            @blk.sync
            def _(sy):
                sy.wait_ge(e.gq.s, sc1_g)
                e.ld.inc(sy.dma_start(out=mcol2[:, :], in_=scal_d[0:1, 1:2].broadcast_to([128, 1])), 16)
            scx_b = e.ld.n

            @blk.vector
            def _(v):
                v.wait_ge(e.ld.s, scx_b)
                v.wait_ge(e.v.s, sc2_v)
                e.v.inc(v.tensor_scalar(wsgx[:, :], wsg[:, :], mcol2[:, :], None, ALU.mult))
                e.v.inc(v.tensor_scalar(wsux[:, :], wsu[:, :], mcol2[:, :], None, ALU.mult))
                v.wait_ge(e.v.s, e.v.n)
                e.v.inc(v.tensor_scalar(wsgx[:, :], wsgx[:, :], 1.0 / 127.0, None, ALU.mult))
                e.v.inc(v.tensor_scalar(wsux[:, :], wsux[:, :], 1.0 / 127.0, None, ALU.mult))
                v.wait_ge(e.v.s, e.v.n)
            wsx_v = e.v.n

            @blk.vector
            def _(v):
                v.wait_ge(e.v.s, wsx_v)
                e.v.inc(v.memset(hmax[:, :], 0.0))
            hm0 = e.v.n

            h_store = 0
            for gi in range(T // 512):
                for cg in range(4):
                    ci = gi * 4 + cg

                    @blk.gpsimd
                    def _(g, ci=ci, hm0=hm0):
                        g.wait_ge(e.v.s, e.v.n)
                        g.wait_ge(e.ld.s, e.ld.n)
                        e.gq.inc(g.dma_start(out=x3th[:, :], in_=x3h_d[ci * 128:(ci + 1) * 128, :]), 16)
                        e.gq.inc(g.dma_start(out=x3tl[:, :], in_=x3l_d[ci * 128:(ci + 1) * 128, :]), 16)
                    xl_ = e.gq.n

                    @blk.vector
                    def _(v, ci=ci, xl_=xl_):
                        v.wait_ge(e.gq.s, xl_)
                        e.v.inc(v.tensor_tensor(x3f4[:, :], x3th[:, :], x3tl[:, :], ALU.add))
                        v.wait_ge(e.v.s, e.v.n)
                        e.v.inc(v.tensor_scalar(scr4[:, :], x3f4[:, :], rsi4[:, ci:ci + 1],
                                                sb4[:, ci:ci + 1], ALU.mult, ALU.add))
                        v.wait_ge(e.v.s, e.v.n)
                        v.tensor_copy(qk[:, 0:D], scr4[:, :])
                        e.v.inc(v.tensor_copy(x3f4[:, :], qk[:, 0:D]))
                        v.wait_ge(e.v.s, e.v.n)
                        v.tensor_tensor(scr4[:, :], scr4[:, :], x3f4[:, :], ALU.subtract)
                        e.v.inc(v.tensor_scalar(scr4[:, :], scr4[:, :], 0.0, None, ALU.is_lt))
                        v.wait_ge(e.v.s, e.v.n)
                        v.tensor_tensor(qb[:, 0:D], x3f4[:, :], scr4[:, :], ALU.subtract)
                        v.tensor_scalar(qb[:, 0:D], qb[:, 0:D], -128.0, None, ALU.max)
                        e.v.inc(v.tensor_scalar(qb[:, 0:D], qb[:, 0:D], 127.0, None, ALU.min))
                    kx_v = e.v.n

                    @blk.sync
                    def _(sy, cg=cg, kx_v=kx_v):
                        sy.wait_ge(e.v.s, kx_v)
                        sy.wait_ge(e.t.s, e.t.n)
                        for dc in range(DC):
                            e.ld.inc(sy.dma_start(out=kxf[:, dc * 512 + cg * 128:dc * 512 + (cg + 1) * 128],
                                                  in_=qb[:, dc * 128:(dc + 1) * 128], transpose=True), 16)
                kx_t = e.ld.n

                if dev_dump == "p4" and gi == 0:
                    @blk.gpsimd
                    def _(g, kx_t=kx_t):
                        g.wait_ge(e.ld.s, kx_t)
                        e.fin.inc(g.dma_start(out=dbg[0:128, 0:4096], in_=kxf[:, :]), 16)

                for ic in range(IC):
                    @blk.tensor
                    def _(t, ic=ic, kx_t=kx_t, wq=wq_done_ld):
                        t.wait_ge(e.ld.s, max(kx_t, wq))
                        t.wait_ge(e.s.s, e.s.n)
                        t.wait_ge(e.v.s, e.v.n)
                        for dc in range(DC):
                            ins = t.matmul(e.psA[:, 0:512], kwgT[:, dc * DFF + ic * 128:dc * DFF + (ic + 1) * 128],
                                           kxf[:, dc * 512:(dc + 1) * 512], start=(dc == 0), stop=(dc == DC - 1))
                        e.t.inc(ins)
                        for dc in range(DC):
                            ins = t.matmul(e.psB[:, 0:512], kwuT[:, dc * DFF + ic * 128:dc * DFF + (ic + 1) * 128],
                                           kxf[:, dc * 512:(dc + 1) * 512], start=(dc == 0), stop=(dc == DC - 1))
                        e.t.inc(ins)
                    gu_t = e.t.n

                    @blk.scalar
                    def _(s, ic=ic, gu_t=gu_t):
                        s.wait_ge(e.t.s, gu_t)
                        e.s.inc(s.activation(sgt[:, :], e.psA[:, 0:512], AF.Silu, scale=wsgx[:, ic:ic + 1]))
                    gu_s = e.s.n

                    @blk.vector
                    def _(v, ic=ic, gu_s=gu_s, gu_t=gu_t):
                        v.wait_ge(e.t.s, gu_t)
                        v.wait_ge(e.gq.s, e.gq.n)
                        e.v.inc(v.tensor_scalar(sut[:, :], e.psB[:, 0:512], wsux[:, ic:ic + 1], None, ALU.mult))
                        v.wait_ge(e.s.s, gu_s)
                        e.v.inc(v.tensor_tensor(htile[:, :], sgt[:, :], sut[:, :], ALU.mult))
                        v.wait_ge(e.v.s, e.v.n)
                        e.v.inc(v.tensor_reduce(hab[:, ic:ic + 1], htile[:, :], AX.X, ALU.max,
                                                apply_absolute_value=True))
                    h_v = e.v.n

                    @blk.gpsimd
                    def _(g, ic=ic, gi=gi, h_v=h_v):
                        g.wait_ge(e.v.s, h_v)
                        e.gq.inc(g.dma_start(out=h_sp[ic * 128:(ic + 1) * 128, gi * 512:(gi + 1) * 512],
                                             in_=htile[:, :]), 16)
                        if dev_dump == "p4" and gi == 0 and ic == 0:
                            e.fin.inc(g.dma_start(out=dbg[128:256, 0:512], in_=sgt[:, :]), 16)
                            e.fin.inc(g.dma_start(out=dbg[128:256, 512:1024], in_=sut[:, :]), 16)
                            e.fin.inc(g.dma_start(out=dbg[128:256, 1024:1536], in_=htile[:, :]), 16)
                h_store = e.gq.n

                @blk.vector
                def _(v, h_store=h_store):
                    v.wait_ge(e.v.s, e.v.n)
                    e.v.inc(v.tensor_reduce(qu[:, 0:1], hab[:, :], AX.X, ALU.max))
                    v.wait_ge(e.v.s, e.v.n)
                    e.v.inc(v.tensor_tensor(hmax[:, :], hmax[:, :], qu[:, 0:1], ALU.max))
            hmax_v = e.v.n

            @blk.gpsimd
            def _(g):
                g.wait_ge(e.v.s, hmax_v)
                e.gq.inc(g.dma_start(out=colbuf[0:1, 128:256], in_=hmax[:, 0:1]), 16)
            c2g = e.gq.n

            @blk.sync
            def _(sy):
                sy.wait_ge(e.gq.s, c2g)
                e.ld.inc(sy.dma_start(out=qstg[0:1, 0:128], in_=colbuf[0:1, 128:256]), 16)
            c2l = e.ld.n

            @blk.vector
            def _(v):
                v.wait_ge(e.ld.s, c2l)
                e.v.inc(v.tensor_reduce(qstg[0:1, 128:129], qstg[0:1, 0:128], AX.X, ALU.max))
            c2v = e.v.n

            @blk.gpsimd
            def _(g):
                g.wait_ge(e.v.s, c2v)
                e.gq.inc(g.dma_start(out=cc_in2.ap(), in_=qstg[0:1, 128:129]), 16)
                g.wait_ge(e.gq.s, e.gq.n)
                e.cc.inc(g.collective_compute("AllReduce", ALU.max,
                                              replica_groups=[list(range(B))],
                                              ins=[cc_in2.ap().opt()],
                                              outs=[cc_out2.ap().opt()]))
                g.wait_ge(e.cc.s, e.cc.n)
                e.gq.inc(g.dma_start(out=qstg[0:1, 130:131], in_=cc_out2.ap()), 16)
            cc2_g = e.gq.n

            @blk.vector
            def _(v):
                v.wait_ge(e.gq.s, cc2_g)
                e.v.inc(v.tensor_tensor(qstg[0:1, 131:132], qstg[0:1, 130:131], p4row[0:1, 5:6], ALU.mult))
                v.wait_ge(e.v.s, e.v.n)
                e.v.inc(v.tensor_scalar(qstg[0:1, 131:132], qstg[0:1, 131:132], 1.0 / 127.0, None, ALU.mult))
                v.wait_ge(e.v.s, e.v.n)
                e.v.inc(v.reciprocal(qstg[0:1, 132:133], qstg[0:1, 131:132]))
                v.wait_ge(e.v.s, e.v.n)
            sc3_v = e.v.n

            @blk.gpsimd
            def _(g):
                g.wait_ge(e.v.s, sc3_v)
                e.gq.inc(g.dma_start(out=scal_d[0:1, 4:8], in_=qstg[0:1, 130:134]), 16)
            sc3_g = e.gq.n

            @blk.sync
            def _(sy):
                sy.wait_ge(e.gq.s, sc3_g)
                e.ld.inc(sy.dma_start(out=mcol2[:, :], in_=scal_d[0:1, 6:7].broadcast_to([128, 1])), 16)
                e.ld.inc(sy.dma_start(out=mcol1[:, :], in_=scal_d[0:1, 5:6].broadcast_to([128, 1])), 16)
            sc3_b = e.ld.n

            @blk.gpsimd
            def _(g, wq=wq_done_v):
                g.wait_ge(e.v.s, wq)
                for hc in range(DC):
                    e.gq.inc(g.dma_start(out=rowb[4:5, hc * 128:(hc + 1) * 128],
                                         in_=wsd[:, hc:hc + 1]), 16)
            wsd_g = e.gq.n

            @blk.sync
            def _(sy):
                sy.wait_ge(e.gq.s, wsd_g)
                e.ld.inc(sy.dma_start(out=wsdq[:, :], in_=rowb[4:5, 0:D].broadcast_to([128, D])), 16)
            wsd_b = e.ld.n

            @blk.vector
            def _(v):
                v.wait_ge(e.ld.s, wsd_b)
                v.wait_ge(e.ld.s, sc3_b)
                v.tensor_scalar(wsdq[:, :], wsdq[:, :], mcol1[:, :], None, ALU.mult)
                e.v.inc(v.tensor_scalar(wsdq[:, :], wsdq[:, :], 1.0 / 127.0, None, ALU.mult))
            wsdq_v = e.v.n

            out_st = 0
            for ci in range(NCH):
                @blk.gpsimd
                def _(g, ci=ci, wsdq_v=wsdq_v):
                    if ci == 0:
                        g.wait_ge(e.v.s, wsdq_v)
                    g.wait_ge(e.v.s, e.v.n)
                    g.wait_ge(e.t.s, e.t.n)
                    src3 = h_sp[:, ci * 128:(ci + 1) * 128].rearrange("(c p) w -> p c w", p=128)
                    dst3 = qstg[:, 0:IC * 128].rearrange("p (c w) -> p c w", w=128)
                    e.gq.inc(g.dma_start(out=dst3, in_=src3), 16)
                    e.gq.inc(g.dma_start(out=x3th[:, :], in_=x3h_d[ci * 128:(ci + 1) * 128, :]), 16)
                    e.gq.inc(g.dma_start(out=x3tl[:, :], in_=x3l_d[ci * 128:(ci + 1) * 128, :]), 16)
                hr_g = e.gq.n

                @blk.vector
                def _(v, hr_g=hr_g):
                    v.wait_ge(e.gq.s, hr_g)
                    e.v.inc(v.tensor_scalar(qu[:, 0:IC * 128], qstg[:, 0:IC * 128], mcol2[:, :], 0.5, ALU.mult, ALU.add))
                    v.wait_ge(e.v.s, e.v.n)
                    v.tensor_copy(qk[:, 0:IC * 128], qu[:, 0:IC * 128])
                    e.v.inc(v.tensor_copy(qstg[:, 0:IC * 128], qk[:, 0:IC * 128]))
                    v.wait_ge(e.v.s, e.v.n)
                    v.tensor_tensor(qu[:, 0:IC * 128], qu[:, 0:IC * 128], qstg[:, 0:IC * 128], ALU.subtract)
                    e.v.inc(v.tensor_scalar(qu[:, 0:IC * 128], qu[:, 0:IC * 128], 0.0, None, ALU.is_lt))
                    v.wait_ge(e.v.s, e.v.n)
                    v.tensor_tensor(qb[:, 0:IC * 128], qstg[:, 0:IC * 128], qu[:, 0:IC * 128], ALU.subtract)
                    v.tensor_scalar(qb[:, 0:IC * 128], qb[:, 0:IC * 128], -128.0, None, ALU.max)
                    e.v.inc(v.tensor_scalar(qb[:, 0:IC * 128], qb[:, 0:IC * 128], 127.0, None, ALU.min))
                kh_v = e.v.n

                @blk.tensor
                def _(t, kh_v=kh_v):
                    t.wait_ge(e.v.s, kh_v)
                    for nh in range(2):
                        for ic in range(IC):
                            ins = t.matmul(e.psY[:, nh * 512:(nh + 1) * 512],
                                           qb[:, ic * 128:(ic + 1) * 128],
                                           kwdT[:, ic * D + nh * 512:ic * D + nh * 512 + 512],
                                           start=(ic == 0), stop=(ic == IC - 1))
                        e.t.inc(ins)
                dn_t = e.t.n

                @blk.vector
                def _(v, dn_t=dn_t, ci=ci):
                    v.wait_ge(e.t.s, dn_t)
                    v.wait_ge(e.fin.s, e.fin.n)
                    v.tensor_tensor(scr4[:, :], e.psY[:, 0:1024], wsdq[:, :], ALU.mult)
                    v.tensor_tensor(x3f4[:, :], x3th[:, :], x3tl[:, :], ALU.add)
                    e.v.inc(v.tensor_tensor(scr4[:, :], scr4[:, :], x3f4[:, :], ALU.add))
                o_v = e.v.n

                @blk.gpsimd
                def _(g, ci=ci, o_v=o_v):
                    g.wait_ge(e.v.s, o_v)
                    e.fin.inc(g.dma_start(out=out[ci * 128:(ci + 1) * 128, :], in_=scr4[:, :]), 16)

            if dev_dump == "p4":
                @blk.gpsimd
                def _(g):
                    g.wait_ge(e.fin.s, e.fin.n)
                    g.wait_ge(e.gq.s, e.gq.n)
                    g.wait_ge(e.v.s, e.v.n)
                    e.fin.inc(g.dma_start(out=dbg[256:384, 0:32], in_=m4[:, :]), 16)
                    e.fin.inc(g.dma_start(out=dbg[256:384, 32:64], in_=rs4[:, :]), 16)
                    e.fin.inc(g.dma_start(out=dbg[256:384, 64:86], in_=wsg[:, :]), 16)
                    e.fin.inc(g.dma_start(out=dbg[256:384, 86:108], in_=wsu[:, :]), 16)
                    e.fin.inc(g.dma_start(out=dbg[256:384, 108:116], in_=wsd[:, :]), 16)
                    e.fin.inc(g.dma_start(out=dbg[256:384, 116:138], in_=wsgx[:, :]), 16)
                    e.fin.inc(g.dma_start(out=dbg[256:384, 138:160], in_=wsux[:, :]), 16)
                    e.fin.inc(g.dma_start(out=dbg[256:257, 160:168], in_=scal_d[0:1, 0:8]), 16)
                    e.fin.inc(g.dma_start(out=dbg[256:384, 168:169], in_=hmax[:, :], allow_slow_non_contiguous=True), 16)
                    e.fin.inc(g.dma_start(out=dbg[384:512, 0:2816], in_=kwgT[:, 0:2816]), 16)
                    e.fin.inc(g.dma_start(out=dbg[384:512, 2816:2816 + 1024], in_=wsdq[:, :]), 16)
                    g.wait_ge(e.fin.s, e.fin.n)

            @blk.gpsimd
            def _(g):
                g.wait_ge(e.fin.s, e.fin.n)
            p4.close()

    return nc



def _constants():
    q = np.arange(128, dtype=np.float64)
    diff = q[None, :] - q[:, None] - 1.0
    c1t = np.where(q[None, :] > q[:, None], diff, BIGNEG).astype(np.float32)
    c2t = np.where(q[None, :] > q[:, None], np.maximum(diff, 0.0), BIGNEG).astype(np.float32)
    return {
        "c1t": np.ascontiguousarray(c1t),
        "c2t": np.ascontiguousarray(c2t),
        "eye": np.eye(128, dtype=np.float32),
        "iota_i": np.arange(128, dtype=np.float32).reshape(128, 1),
        "iota_r": (127.0 - np.arange(128, dtype=np.float32)).reshape(128, 1),
        "ones_in": np.ones((128, 1), dtype=np.float32),
    }


def make_in_maps(ii):
    consts = _constants()
    maa_all = np.stack([ii["L0_attn_time_maa_k"].reshape(-1),
                        ii["L0_attn_time_maa_v"].reshape(-1),
                        ii["L0_attn_time_maa_r"].reshape(-1),
                        ii["L0_attn_time_maa_g"].reshape(-1)], axis=0)
    maa_dev = np.ascontiguousarray(
        maa_all.reshape(4, DC, 128).transpose(2, 0, 1).reshape(128, 4 * DC)).astype(np.float32)
    common = {
        "WrT": np.ascontiguousarray(ii["L0_attn_Wr"].T),
        "WkT": np.ascontiguousarray(ii["L0_attn_Wk"].T),
        "WvT": np.ascontiguousarray(ii["L0_attn_Wv"].T),
        "WgT": np.ascontiguousarray(ii["L0_attn_Wg"].T),
        "WoT": np.ascontiguousarray(ii["L0_attn_Wo"].T),
        "maa": maa_dev,
        "decay_row": np.ascontiguousarray(ii["L0_attn_time_decay"].reshape(1, H)),
        "faaaa_row": np.ascontiguousarray(ii["L0_attn_time_faaaa"].reshape(1, H)),
        "lnw_row": np.ascontiguousarray(ii["L0_attn_ln_w"].reshape(1, D)),
        "lnb_row": np.ascontiguousarray(ii["L0_attn_ln_b"].reshape(1, D)),
        "theta": np.ascontiguousarray(ii["L0_ffn_theta"]),
        "theta_n": np.ascontiguousarray(ii["L0_ffn_theta_noise"]),
        "phi": np.ascontiguousarray(ii["L0_ffn_phi"]),
        "phi_n": np.ascontiguousarray(ii["L0_ffn_phi_noise"]),
        "WwT": np.ascontiguousarray(ii["L1_attn_W_write"].T),
        "WrdT": np.ascontiguousarray(ii["L1_attn_W_read"].T),
        "l1_scal": np.array([[float(ii["L1_attn_decay"]), float(ii["L1_attn_log_alpha"]),
                              float(np.asarray(ii["L1_ffn_clip_x"]).reshape(-1)[0]),
                              float(np.asarray(ii["L1_ffn_clip_down"]).reshape(-1)[0])]],
                            dtype=np.float32),
        "w_gate": np.ascontiguousarray(ii["L1_ffn_w_gate"]),
        "w_up": np.ascontiguousarray(ii["L1_ffn_w_up"]),
        "w_down": np.ascontiguousarray(ii["L1_ffn_w_down"]),
        **consts,
    }
    x = ii["L0_attn_x"]
    in_maps = []
    for b in range(B):
        m = dict(common)
        m["xin"] = np.ascontiguousarray(x[b].T)
        in_maps.append(m)
    return in_maps


def kernel(**inputs):
    ii = {k: np.asarray(v) for k, v in inputs.items()}
    nc = build(N_PHASES)
    in_maps = make_in_maps(ii)
    res = run_bass_kernel_spmd(nc, in_maps, core_ids=list(range(B)), trace_cores=[0])
    return np.stack([np.asarray(res.results[b]["out"]) for b in range(B)], axis=0)


# version 07
# speedup vs baseline: 1.0109x; 1.0109x over previous
import numpy as np
import ml_dtypes
from contextlib import ExitStack
import concourse.bass as bass
import concourse.mybir as mybir
from concourse.bass_utils import run_bass_kernel_spmd

F32 = mybir.dt.float32
BF16 = mybir.dt.bfloat16
I32 = mybir.dt.int32
AF = mybir.ActivationFunctionType
ALU = mybir.AluOpType
AX = mybir.AxisListType

B, T, D, H, HN, DFF = 8, 4096, 1024, 16, 64, 2816
DC = D // 128
Q = 128
NCH = T // Q
TT = 256
NT = T // TT
CPT = TT // Q
IC = DFF // 128
EPS_LN = 1e-5
EPS_GN = 1e-5 * 64
BIGNEG = 1.0e9
HALFPI = 1.5707963267948966

N_PHASES = 4


class Sem:
    def __init__(self, nc, ctx, name):
        self.s = ctx.enter_context(nc.semaphore(name))
        self.n = 0

    def inc(self, instr, k=1):
        instr.then_inc(self.s, k)
        self.n += k
        return self.n


class E:
    pass


def build(n_phases=N_PHASES, dev_dump=None):
    nc = bass.Bass("TRN2", target_bir_lowering=False)
    e = E()
    e.nc = nc

    xin = nc.dram_tensor("xin", [D, T], F32, kind="ExternalInput").ap()
    WT = {}
    for nm in ["WrT", "WkT", "WvT", "WgT", "WoT"]:
        WT[nm] = nc.dram_tensor(nm, [D, D], F32, kind="ExternalInput").ap()
    maa_in = nc.dram_tensor("maa", [128, DC * 4], F32, kind="ExternalInput").ap()
    decay_row = nc.dram_tensor("decay_row", [1, H], F32, kind="ExternalInput").ap()
    faaaa_row = nc.dram_tensor("faaaa_row", [1, H], F32, kind="ExternalInput").ap()
    lnw_row = nc.dram_tensor("lnw_row", [1, D], F32, kind="ExternalInput").ap()
    lnb_row = nc.dram_tensor("lnb_row", [1, D], F32, kind="ExternalInput").ap()
    c1t_in = nc.dram_tensor("c1t", [128, 128], F32, kind="ExternalInput").ap()
    c2t_in = nc.dram_tensor("c2t", [128, 128], F32, kind="ExternalInput").ap()
    eye_in = nc.dram_tensor("eye", [128, 128], F32, kind="ExternalInput").ap()
    iota_i_in = nc.dram_tensor("iota_i", [128, 1], F32, kind="ExternalInput").ap()
    iota_r_in = nc.dram_tensor("iota_r", [128, 1], F32, kind="ExternalInput").ap()
    ones_in = nc.dram_tensor("ones_in", [128, 1], F32, kind="ExternalInput").ap()
    theta_in = nc.dram_tensor("theta", [D, D], F32, kind="ExternalInput").ap()
    theta_n_in = nc.dram_tensor("theta_n", [D, D], F32, kind="ExternalInput").ap()
    phi_in = nc.dram_tensor("phi", [D, D], F32, kind="ExternalInput").ap()
    phi_n_in = nc.dram_tensor("phi_n", [D, D], F32, kind="ExternalInput").ap()
    WwT_in = nc.dram_tensor("WwT", [D, D], F32, kind="ExternalInput").ap()
    WrdT_in = nc.dram_tensor("WrdT", [D, D], F32, kind="ExternalInput").ap()
    l1s_in = nc.dram_tensor("l1_scal", [1, 4], F32, kind="ExternalInput").ap()
    wg_in = nc.dram_tensor("w_gate", [DFF, D], F32, kind="ExternalInput").ap()
    wu_in = nc.dram_tensor("w_up", [DFF, D], F32, kind="ExternalInput").ap()
    wd_in = nc.dram_tensor("w_down", [D, DFF], F32, kind="ExternalInput").ap()

    out = nc.dram_tensor("out", [T, D], F32, kind="ExternalOutput").ap()
    if dev_dump:
        dbg = nc.dram_tensor("dbg", [D, T], F32, kind="ExternalOutput").ap()

    xs1 = nc.dram_tensor("xs1", [D, T], F32).ap()
    xs2 = nc.dram_tensor("xs2", [D, T], F32).ap()
    x3h_d = nc.dram_tensor("x3h_d", [T, D], BF16).ap()
    x3l_d = nc.dram_tensor("x3l_d", [T, D], BF16).ap()
    h_sp = nc.dram_tensor("h_spill", [DFF, T], F32).ap()
    rowb = nc.dram_tensor("rowbuf", [16, T], F32).ap()
    wsplit = {}
    for nm in ["k", "v", "g"]:
        wsplit[nm] = nc.dram_tensor("ws_" + nm, [2, 2, 128, DC * 512], BF16).ap()
    for nm in ["r", "o"]:
        wsplit[nm] = nc.dram_tensor("ws_" + nm, [2, DC, 128, DC * 128], BF16).ap()
    cc_in1 = nc.dram_tensor("cc_in1", [1, 1], F32)
    cc_out1 = nc.dram_tensor("cc_out1", [1, 1], F32)
    cc_in2 = nc.dram_tensor("cc_in2", [1, 1], F32)
    cc_out2 = nc.dram_tensor("cc_out2", [1, 1], F32)
    scal_d = nc.dram_tensor("scal_d", [1, 16], F32).ap()

    stk = ExitStack()
    with stk:
        def sb(nm, shape, dt=F32, st=stk):
            return st.enter_context(nc.sbuf_tensor(nm, shape, dt))

        e.ones = sb("ones_c", [128, 1])
        e.ioi = sb("ioi", [128, 1])
        e.ior = sb("ior", [128, 1])
        e.eye = sb("eye_t", [128, 128])
        e.maa = sb("maa_t", [128, DC * 4])
        e.Ecol = sb("Ecol", [128, H])
        e.Ucol = sb("Ucol", [128, H])
        e.wb = sb("wb_t", [128, H])
        e.wkd = sb("wkd_t", [128, H])
        e.wsd = sb("wsd_t", [128, H])
        e.scal = sb("scal", [128, 16])
        e.epsln = sb("epsln", [128, 1])
        e.epsgn = sb("epsgn", [128, 1])
        e.m4 = sb("m4_g", [128, NCH])
        e.var4 = sb("var4_g", [128, NCH])
        e.cmx4 = sb("cmx4_g", [128, NCH])
        e.row1 = sb("row1", [1, 512])
        e.ones_row = sb("ones_row", [1, 128])
        e.psA = stk.enter_context(nc.psum_tensor("psA", [128, 512], F32))
        e.psB = stk.enter_context(nc.psum_tensor("psB", [128, 512], F32))
        e.psY = stk.enter_context(nc.psum_tensor("psY", [128, 1024], F32))
        e.psY2 = stk.enter_context(nc.psum_tensor("psY2", [128, 1024], F32))
        e.psC = stk.enter_context(nc.psum_tensor("psC", [128, 512], F32))
        e.psD = stk.enter_context(nc.psum_tensor("psD", [128, 512], F32))

        e.wmatT = None
        e.lnwb = None
        e.lnbb = None
        e.ld = Sem(nc, stk, "ld")
        e.sc = Sem(nc, stk, "scq")
        e.gq = Sem(nc, stk, "gq")
        e.t = Sem(nc, stk, "tse")
        e.v = Sem(nc, stk, "vse")
        e.s = Sem(nc, stk, "sse")
        e.cc = Sem(nc, stk, "ccse")
        e.fin = Sem(nc, stk, "fin")

        blk = stk.enter_context(nc.Block())
        e.blk = blk

        def barrier_vals():
            return dict(ld=e.ld.n, sc=e.sc.n, gq=e.gq.n, t=e.t.n, v=e.v.n, s=e.s.n)

        def wait_barrier(engine, bv):
            engine.wait_ge(e.ld.s, bv["ld"])
            engine.wait_ge(e.sc.s, bv["sc"])
            engine.wait_ge(e.gq.s, bv["gq"])
            engine.wait_ge(e.t.s, bv["t"])
            engine.wait_ge(e.v.s, bv["v"])
            engine.wait_ge(e.s.s, bv["s"])

        psh = ExitStack()
        e.wmatT = sb("wmatT", [128, H * 128], st=psh)
        e.lnwb = sb("lnwb", [128, D], st=psh)
        e.lnbb = sb("lnbb", [128, D], st=psh)
        xt = sb("xt", [128, DC * TT], st=psh)
        ln = sb("ln", [128, DC * (TT + 1)], st=psh)
        sq = sb("sq", [128, DC * TT], st=psh)
        mxh = sb("mxh", [128, DC * TT], BF16, st=psh)
        mxl = sb("mxl", [128, DC * TT], BF16, st=psh)
        wstg = sb("wstg", [128, D], st=psh)
        wsth = sb("wsth", [128, D], BF16, st=psh)
        wstl = sb("wstl", [128, D], BF16, st=psh)
        rw_m = sb("rw_m", [1, TT], st=psh)
        rw_v = sb("rw_v", [1, TT], st=psh)
        rw_t = sb("rw_t", [1, TT], st=psh)
        rw_rs = sb("rw_rs", [1, TT], st=psh)
        rw_mrs = sb("rw_mrs", [1, TT], st=psh)

        def emit_ln(xtile, dst_ln, wait_sem, wait_val, roff):
            @blk.scalar
            def _(s):
                s.wait_ge(wait_sem.s, wait_val)
                for c in range(DC):
                    e.s.inc(s.activation(sq[:, c * TT:(c + 1) * TT], xtile[:, c * TT:(c + 1) * TT], AF.Square))
            sqd = e.s.n

            @blk.tensor
            def _(t):
                t.wait_ge(e.s.s, sqd)
                t.wait_ge(e.v.s, e.v.n)
                for c in range(DC):
                    ins = t.matmul(e.psC[0:1, 0:TT], e.ones[:, :], xtile[:, c * TT:(c + 1) * TT],
                                   start=(c == 0), stop=(c == DC - 1))
                e.t.inc(ins)
                for c in range(DC):
                    ins = t.matmul(e.psD[0:1, 0:TT], e.ones[:, :], sq[:, c * TT:(c + 1) * TT],
                                   start=(c == 0), stop=(c == DC - 1))
                e.t.inc(ins)
            mmd = e.t.n

            @blk.vector
            def _(v):
                v.wait_ge(e.t.s, mmd)
                v.tensor_scalar(rw_m[0:1, 0:TT], e.psC[0:1, 0:TT], 1.0 / D, None, ALU.mult)
                e.v.inc(v.tensor_scalar(rw_v[0:1, 0:TT], e.psD[0:1, 0:TT], 1.0 / D, None, ALU.mult))
                v.wait_ge(e.v.s, e.v.n)
                v.tensor_tensor(rw_t[0:1, 0:TT], rw_m[0:1, 0:TT], rw_m[0:1, 0:TT], ALU.mult)
                e.v.inc(v.tensor_tensor(rw_v[0:1, 0:TT], rw_v[0:1, 0:TT], rw_t[0:1, 0:TT], ALU.subtract))
            vd = e.v.n

            @blk.scalar
            def _(s):
                s.wait_ge(e.v.s, vd)
                e.s.inc(s.activation(rw_t[0:1, 0:TT], rw_v[0:1, 0:TT], AF.Sqrt, bias=e.epsln[0:1, :]))
            sd = e.s.n

            @blk.vector
            def _(v):
                v.wait_ge(e.s.s, sd)
                e.v.inc(v.reciprocal(rw_rs[0:1, 0:TT], rw_t[0:1, 0:TT]))
                v.wait_ge(e.v.s, e.v.n)
                e.v.inc(v.tensor_tensor(rw_mrs[0:1, 0:TT], rw_m[0:1, 0:TT], rw_rs[0:1, 0:TT], ALU.mult))
            vd2 = e.v.n

            @blk.tensor
            def _(t):
                t.wait_ge(e.v.s, vd2)
                e.t.inc(t.matmul(e.psC[:, 0:TT], e.ones_row[0:1, :], rw_rs[0:1, 0:TT],
                                 start=True, stop=True))
                e.t.inc(t.matmul(e.psD[:, 0:TT], e.ones_row[0:1, :], rw_mrs[0:1, 0:TT],
                                 start=True, stop=True))
            bd = e.t.n

            @blk.vector
            def _(v):
                v.wait_ge(e.t.s, bd)
                for c in range(DC):
                    v.tensor_tensor(dst_ln[:, c * (TT + 1) + 1:(c + 1) * (TT + 1)],
                                    xtile[:, c * TT:(c + 1) * TT], e.psC[:, 0:TT], ALU.mult)
                    e.v.inc(v.tensor_tensor(dst_ln[:, c * (TT + 1) + 1:(c + 1) * (TT + 1)],
                                            dst_ln[:, c * (TT + 1) + 1:(c + 1) * (TT + 1)],
                                            e.psD[:, 0:TT], ALU.subtract))
            return e.v.n



        @blk.sync
        def _(sy):
            e.ld.inc(sy.dma_start(out=e.ones[:, :], in_=ones_in), 16)
            e.ld.inc(sy.dma_start(out=e.ones_row[0:1, :], in_=ones_in), 16)
            e.ld.inc(sy.dma_start(out=e.ioi[:, :], in_=iota_i_in), 16)
            e.ld.inc(sy.dma_start(out=e.ior[:, :], in_=iota_r_in), 16)
            e.ld.inc(sy.dma_start(out=e.eye[:, :], in_=eye_in), 16)
            e.ld.inc(sy.dma_start(out=e.maa[:, :], in_=maa_in), 16)
            e.ld.inc(sy.dma_start(out=e.lnwb[:, :], in_=lnw_row.broadcast_to([128, D])), 16)
            e.ld.inc(sy.dma_start(out=e.lnbb[:, :], in_=lnb_row.broadcast_to([128, D])), 16)
            e.ld.inc(sy.dma_start(out=e.Ecol[:, 0:H], in_=decay_row.broadcast_to([128, H])), 16)
            e.ld.inc(sy.dma_start(out=e.Ucol[:, 0:H], in_=faaaa_row.broadcast_to([128, H])), 16)
            for hh in range(H):
                e.ld.inc(sy.dma_start(out=e.wmatT[:, hh * 128:(hh + 1) * 128], in_=c1t_in), 16)
        n0 = e.ld.n

        @blk.scalar
        def _(s):
            s.wait_ge(e.ld.s, n0)
            e.s.inc(s.activation(e.Ecol[:, 0:H], e.Ecol[:, 0:H], AF.Exp))
        n1 = e.s.n

        @blk.vector
        def _(v):
            v.wait_ge(e.s.s, n1)
            v.memset(e.epsln[:, :], EPS_LN)
            v.memset(e.epsgn[:, :], EPS_GN)
            for hh in range(H):
                v.tensor_scalar(e.wb[:, hh:hh + 1], e.ioi[:, :], e.Ecol[:, hh:hh + 1], None, ALU.mult)
                v.tensor_scalar(e.wkd[:, hh:hh + 1], e.ior[:, :], e.Ecol[:, hh:hh + 1], None, ALU.mult)
            e.v.inc(v.tensor_scalar(e.wsd[:, 0:H], e.Ecol[:, 0:H], float(Q), None, ALU.mult))
        n2 = e.v.n

        @blk.scalar
        def _(s):
            s.wait_ge(e.v.s, n2)
            s.activation(e.wb[:, 0:H], e.wb[:, 0:H], AF.Exp, scale=-1.0)
            s.activation(e.wkd[:, 0:H], e.wkd[:, 0:H], AF.Exp, scale=-1.0)
            e.s.inc(s.activation(e.wsd[:, 0:H], e.wsd[:, 0:H], AF.Exp, scale=-1.0))
        n3 = e.s.n

        @blk.vector
        def _(v):
            v.wait_ge(e.s.s, n3)
            for hh in range(H):
                e.v.inc(v.tensor_scalar(e.wmatT[:, hh * 128:(hh + 1) * 128],
                                        e.wmatT[:, hh * 128:(hh + 1) * 128],
                                        e.Ecol[:, hh:hh + 1], None, ALU.mult))
        n4 = e.v.n

        @blk.scalar
        def _(s):
            s.wait_ge(e.v.s, n4)
            for hh in range(H):
                e.s.inc(s.activation(e.wmatT[:, hh * 128:(hh + 1) * 128],
                                     e.wmatT[:, hh * 128:(hh + 1) * 128], AF.Exp, scale=-1.0))
        n5 = e.s.n

        @blk.vector
        def _(v):
            v.wait_ge(e.s.s, n5)
            for hh in range(H):
                e.v.inc(v.scalar_tensor_tensor(e.wmatT[:, hh * 128:(hh + 1) * 128],
                                               e.eye[:, :], e.Ucol[:, hh:hh + 1],
                                               e.wmatT[:, hh * 128:(hh + 1) * 128],
                                               ALU.mult, ALU.add))
        prep_v = e.v.n

        if n_phases >= 1:
            p1 = ExitStack()
            mxh2 = sb("mxh2", [128, DC * TT], BF16, st=p1)
            mxl2 = sb("mxl2", [128, DC * TT], BF16, st=p1)
            wkvg = sb("wkvg", [128, 2 * DC * 512], BF16, st=p1)
            wkvg2 = sb("wkvg2", [128, 2 * DC * 512], BF16, st=p1)
            wro = sb("wro", [128, 2 * DC * 128], BF16, st=p1)
            wro2 = sb("wro2", [128, 2 * DC * 128], BF16, st=p1)
            r_fm = sb("r_fm", [128, DC * TT], st=p1)
            k_tok = sb("k_tok", [128, CPT * D], st=p1)
            v_tok = sb("v_tok", [128, CPT * D], st=p1)
            g_tok = sb("g_tok", [128, CPT * D], st=p1)
            k_fm = sb("k_fm", [128, DC * 128], st=p1)
            kw_tok = sb("kw_tok", [128, D], st=p1)
            attw = sb("attw", [128, 128], st=p1)
            WKB = sb("WKB", [128, D], st=p1)
            WBB = sb("WBB", [128, D], st=p1)
            WSB = sb("WSB", [128, (H // 2) * HN], st=p1)
            attw2 = sb("attw2", [128, 128], st=p1)
            attw3 = sb("attw3", [128, 128], st=p1)
            state = sb("state", [128, (H // 2) * HN], st=p1)
            ysb = sb("ysb", [128, D], st=p1)
            st6 = sb("st6", [128, 6 * H], st=p1)
            gn_m = sb("gn_m", [128, H], st=p1)
            gn_v = sb("gn_v", [128, H], st=p1)
            gn_t = sb("gn_t", [128, H], st=p1)
            gn_rs = sb("gn_rs", [128, H], st=p1)
            zf = sb("zf", [128, D], st=p1)
            zh = sb("zh", [128, D], BF16, st=p1)
            zl = sb("zl", [128, D], BF16, st=p1)
            zfmh = sb("zfmh", [128, DC * TT], BF16, st=p1)
            zfml = sb("zfml", [128, DC * TT], BF16, st=p1)

            wstg2 = sb("wstg2", [128, D], st=p1)
            wsth2 = sb("wsth2", [128, D], BF16, st=p1)
            wstl2 = sb("wstl2", [128, D], BF16, st=p1)
            wsp_stg = [(wstg, wsth, wstl), (wstg2, wsth2, wstl2)]
            wsp_busy = [prep_v, prep_v]
            wsp_st = [0, 0]
            wu_i = 0
            for wi, nm in enumerate(["k", "v", "g", "r", "o"]):
                src = WT["W" + nm + "T"]
                for c in range(DC):
                    par = wu_i % 2
                    wu_i += 1
                    stg_, sth_, stl_ = wsp_stg[par]

                    @blk.sync
                    def _(sy, src=src, c=c, uw=wsp_busy[par], uw2=wsp_st[par], stg_=stg_):
                        sy.wait_ge(e.v.s, uw)
                        sy.wait_ge(e.gq.s, uw2)
                        e.ld.inc(sy.dma_start(out=stg_[:, :], in_=src[c * 128:(c + 1) * 128, :]), 16)
                    w_ld = e.ld.n

                    @blk.scalar
                    def _(s, w_ld=w_ld, sth_=sth_, stg_=stg_, uw2=wsp_st[par]):
                        s.wait_ge(e.ld.s, w_ld)
                        s.wait_ge(e.gq.s, uw2)
                        e.s.inc(s.activation(sth_[:, :], stg_[:, :], AF.Copy))
                    w_s = e.s.n

                    @blk.vector
                    def _(v, w_s=w_s, stl_=stl_, stg_=stg_, sth_=sth_):
                        v.wait_ge(e.s.s, w_s)
                        e.v.inc(v.tensor_tensor(stl_[:, :], stg_[:, :], sth_[:, :], ALU.subtract))
                    w_v = e.v.n
                    wsp_busy[par] = w_v

                    @blk.gpsimd
                    def _(g, nm=nm, c=c, w_v=w_v, wsth=sth_, wstl=stl_):
                        g.wait_ge(e.v.s, w_v)
                        if nm in ("k", "v", "g"):
                            for oh in range(2):
                                e.gq.inc(g.dma_start(out=wsplit[nm][0, oh, :, c * 512:(c + 1) * 512],
                                                     in_=wsth[:, oh * 512:(oh + 1) * 512]), 16)
                                e.gq.inc(g.dma_start(out=wsplit[nm][1, oh, :, c * 512:(c + 1) * 512],
                                                     in_=wstl[:, oh * 512:(oh + 1) * 512]), 16)
                        else:
                            for oc in range(DC):
                                e.gq.inc(g.dma_start(out=wsplit[nm][0, oc, :, c * 128:(c + 1) * 128],
                                                     in_=wsth[:, oc * 128:(oc + 1) * 128]), 16)
                                e.gq.inc(g.dma_start(out=wsplit[nm][1, oc, :, c * 128:(c + 1) * 128],
                                                     in_=wstl[:, oc * 128:(oc + 1) * 128]), 16)
                    wsp_st[par] = e.gq.n
            wsplit_done = e.gq.n

            @blk.vector
            def _(v):
                v.memset(state[:, :], 0.0)
                v.memset(ln[:, :], 0.0)
                v.memset(k_tok[:, :], 0.0)
                v.wait_ge(e.v.s, prep_v)
                e.v.inc(v.memset(zf[:, 0:HN], 1.0))
                v.wait_ge(e.v.s, e.v.n)
                for hh in range(H):
                    hr3, hc3 = (hh % 2) * 64, (hh // 2) * HN
                    v.tensor_scalar(WKB[:, hh * HN:(hh + 1) * HN], zf[:, 0:HN],
                                    e.wkd[:, hh:hh + 1], None, ALU.mult)
                    v.tensor_scalar(WBB[:, hh * HN:(hh + 1) * HN], zf[:, 0:HN],
                                    e.wb[:, hh:hh + 1], None, ALU.mult)
                    e.v.inc(v.tensor_scalar(WSB[hr3:hr3 + 64, hc3:hc3 + HN], zf[hr3:hr3 + 64, 0:HN],
                                            e.wsd[hr3:hr3 + 64, hh:hh + 1], None, ALU.mult))
            p1_init_v = e.v.n

            slot_users = {"wkvg": 0, "wkvg2": 0, "wro": 0, "wro2": 0}
            mix_users = [0, 0]

            def load_kvg(slot, sname, nm, oh):
                @blk.sync
                def _(sy, slot=slot, nm=nm, oh=oh, uw=slot_users[sname]):
                    sy.wait_ge(e.gq.s, wsplit_done)
                    sy.wait_ge(e.t.s, uw)
                    for half in range(2):
                        e.ld.inc(sy.dma_start(out=slot[:, half * DC * 512:(half + 1) * DC * 512],
                                              in_=wsplit[nm][half, oh, :, :]), 16)
                return e.ld.n

            def load_ro_oc(slot, sname, nm, oc):
                @blk.sync
                def _(sy, slot=slot, nm=nm, oc=oc, uw=slot_users[sname]):
                    sy.wait_ge(e.gq.s, wsplit_done)
                    sy.wait_ge(e.t.s, uw)
                    for half in range(2):
                        e.ld.inc(sy.dma_start(out=slot[:, half * DC * 128:(half + 1) * DC * 128],
                                              in_=wsplit[nm][half, oc, :, :]), 16)
                return e.ld.n

            def emit_mix(mcol, ln_ready, mh, ml, uw):
                @blk.vector
                def _(v):
                    v.wait_ge(e.v.s, ln_ready)
                    v.wait_ge(e.t.s, uw)
                    W1 = TT + 1
                    for c in range(DC):
                        lncur = ln[:, c * W1 + 1:(c + 1) * W1]
                        lnsh = ln[:, c * W1:(c + 1) * W1 - 1]
                        xx = sq[:, c * TT:(c + 1) * TT]
                        v.tensor_tensor(xx, lnsh, lncur, ALU.subtract)
                        v.scalar_tensor_tensor(mh[:, c * TT:(c + 1) * TT], xx,
                                               e.maa[:, mcol * DC + c:mcol * DC + c + 1],
                                               lncur, ALU.mult, ALU.add)
                        v.scalar_tensor_tensor(xx, xx,
                                               e.maa[:, mcol * DC + c:mcol * DC + c + 1],
                                               lncur, ALU.mult, ALU.add)
                        e.v.inc(v.tensor_tensor(ml[:, c * TT:(c + 1) * TT], xx,
                                                mh[:, c * TT:(c + 1) * TT], ALU.subtract))
                return e.v.n

            evac_guard = [0, 0]
            x1_store = 0
            for ti in range(1 if dev_dump == 'probe' else NT):
                t0 = ti * TT

                @blk.gpsimd
                def _(g, t0=t0, x1_store=x1_store):
                    g.wait_ge(e.gq.s, max(x1_store, wsplit_done))
                    g.wait_ge(e.v.s, e.v.n)
                    for c in range(DC):
                        e.gq.inc(g.dma_start(out=xt[:, c * TT:(c + 1) * TT],
                                             in_=xin[c * 128:(c + 1) * 128, t0:t0 + TT]), 16)
                x_ld = e.gq.n

                @blk.vector
                def _(v, ti=ti):
                    v.wait_ge(e.gq.s, x_ld)
                    if ti > 0:
                        W1 = TT + 1
                        for c in range(DC):
                            e.v.inc(v.tensor_copy(ln[:, c * W1:c * W1 + 1], ln[:, c * W1 + TT:c * W1 + TT + 1]))
                    else:
                        W1 = TT + 1
                        for c in range(DC):
                            e.v.inc(v.memset(ln[:, c * W1:c * W1 + 1], 0.0))
                guard_v = e.v.n

                ln_done = emit_ln(xt, ln, e.v, guard_v, 0)

                for i, pname in enumerate(["k", "v", "g"]):
                    dst = {"k": k_tok, "v": v_tok, "g": g_tok}[pname]
                    mcol = {"k": 0, "v": 1, "r": 2, "g": 3}[pname]
                    mh, ml = (mxh, mxl) if i % 2 == 0 else (mxh2, mxl2)
                    mix_done = emit_mix(mcol, ln_done, mh, ml, mix_users[i % 2])
                    for oh in range(2):
                        sname = "wkvg" if oh == 0 else "wkvg2"
                        slot = wkvg if oh == 0 else wkvg2
                        w_rdy = load_kvg(slot, sname, pname, oh)
                        for cg in range(CPT):
                            ps = (e.psA if cg % 2 == 0 else e.psB)[:, 0:512]
                            pi = cg % 2

                            @blk.tensor
                            def _(t, slot=slot, w_rdy=w_rdy, mix_done=mix_done, ps=ps,
                                  pi=pi, cg=cg, oh=oh, mh=mh, ml=ml):
                                t.wait_ge(e.v.s, mix_done)
                                t.wait_ge(e.ld.s, w_rdy)
                                t.wait_ge(e.s.s, evac_guard[pi])
                                for dc in range(DC):
                                    xh = mh[:, dc * TT + cg * 128:dc * TT + (cg + 1) * 128]
                                    xl = ml[:, dc * TT + cg * 128:dc * TT + (cg + 1) * 128]
                                    wh_ = slot[:, dc * 512:(dc + 1) * 512]
                                    wl_ = slot[:, DC * 512 + dc * 512:DC * 512 + (dc + 1) * 512]
                                    t.matmul(ps, xh, wh_, start=(dc == 0), stop=False)
                                    t.matmul(ps, xh, wl_, start=False, stop=False)
                                    ins = t.matmul(ps, xl, wh_, start=False, stop=(dc == DC - 1))
                                e.t.inc(ins)
                            tn = e.t.n
                            slot_users[sname] = tn

                            @blk.scalar
                            def _(s, cg=cg, oh=oh, tn=tn, ps=ps, dst=dst, pname=pname):
                                s.wait_ge(e.t.s, tn)
                                fn = AF.Gelu if pname == "g" else AF.Copy
                                e.s.inc(s.activation(dst[:, cg * D + oh * 512:cg * D + oh * 512 + 512], ps, fn))
                            evac_guard[pi] = e.s.n
                    mix_users[i % 2] = e.t.n

                mh, ml = mxh2, mxl2
                mix_done = emit_mix(2, ln_done, mh, ml, mix_users[1])
                for oc in range(DC):
                    sname = "wro" if oc % 2 == 0 else "wro2"
                    slot = wro if oc % 2 == 0 else wro2
                    w_rdy = load_ro_oc(slot, sname, "r", oc)
                    ps = (e.psA if oc % 2 == 0 else e.psB)[:, 0:TT]
                    pi = oc % 2

                    @blk.tensor
                    def _(t, slot=slot, w_rdy=w_rdy, mix_done=mix_done, ps=ps, pi=pi, oc=oc,
                          mh=mh, ml=ml):
                        t.wait_ge(e.v.s, mix_done)
                        t.wait_ge(e.ld.s, w_rdy)
                        t.wait_ge(e.s.s, evac_guard[pi])
                        for dc in range(DC):
                            xh = mh[:, dc * TT:(dc + 1) * TT]
                            xl = ml[:, dc * TT:(dc + 1) * TT]
                            wh_ = slot[:, dc * 128:(dc + 1) * 128]
                            wl_ = slot[:, DC * 128 + dc * 128:DC * 128 + (dc + 1) * 128]
                            t.matmul(ps, wh_, xh, start=(dc == 0), stop=False)
                            t.matmul(ps, wl_, xh, start=False, stop=False)
                            ins = t.matmul(ps, wh_, xl, start=False, stop=(dc == DC - 1))
                        e.t.inc(ins)
                    tn = e.t.n
                    slot_users[sname] = tn

                    @blk.scalar
                    def _(s, oc=oc, tn=tn, ps=ps):
                        s.wait_ge(e.t.s, tn)
                        e.s.inc(s.activation(r_fm[:, oc * TT:(oc + 1) * TT], ps, AF.Copy))
                    evac_guard[pi] = e.s.n
                mix_users[1] = e.t.n

                proj_done = e.s.n

                for cg in range(CPT):
                    ci = ti * CPT + cg
                    ks = k_tok[:, cg * D:(cg + 1) * D]
                    vs = v_tok[:, cg * D:(cg + 1) * D]
                    gs = g_tok[:, cg * D:(cg + 1) * D]

                    kf_half_done = []
                    for half in range(2):
                        @blk.tensor
                        def _(t, ks=ks, proj_done=proj_done, half=half):
                            if half == 0:
                                t.wait_ge(e.s.s, proj_done)
                                t.wait_ge(e.v.s, e.v.n)
                            else:
                                t.wait_ge(e.v.s, kf_half_done[0])
                            for c4 in range(4):
                                c = half * 4 + c4
                                ins = t.transpose(e.psA[:, c4 * 128:(c4 + 1) * 128],
                                                  ks[:, c * 128:(c + 1) * 128], e.eye[:, :])
                            e.t.inc(ins)
                        tn = e.t.n

                        @blk.vector
                        def _(v, half=half, tn=tn):
                            v.wait_ge(e.t.s, tn)
                            e.v.inc(v.tensor_copy(k_fm[:, half * 512:(half + 1) * 512], e.psA[:, 0:512]))
                        kf_half_done.append(e.v.n)
                    kfm_done = e.v.n

                    @blk.vector
                    def _(v, ks=ks, proj_done=proj_done):
                        v.wait_ge(e.s.s, proj_done)
                        e.v.inc(v.tensor_tensor(kw_tok[:, :], ks, WKB[:, :], ALU.mult))
                    kw_done = e.v.n

                    def att_bank(hh):
                        return (e.psD, e.psB, e.psA)[hh % 3][:, 0:128]

                    def att_buf(hh):
                        return (attw, attw2, attw3)[hh % 3]

                    att_t_arr = [0] * H
                    attw_v_arr = [0] * H
                    y_t_arr = [0] * H
                    for hh in range(H + 1):
                        if hh < H:
                            hc, hr = hh // 2, (hh % 2) * 64
                            kfm_h = k_fm[hr:hr + 64, hc * 128:(hc + 1) * 128]
                            rfm_h = r_fm[hr:hr + 64, hc * TT + cg * 128:hc * TT + (cg + 1) * 128]

                            @blk.tensor
                            def _(t, hh=hh, kfm_h=kfm_h, rfm_h=rfm_h, kfm_done=kfm_done,
                                  kw_done=kw_done, proj_done=proj_done):
                                if hh == 0:
                                    t.wait_ge(e.v.s, max(kfm_done, kw_done))
                                elif hh == 1:
                                    t.wait_ge(e.s.s, proj_done)
                                elif hh == 2:
                                    pass
                                else:
                                    t.wait_ge(e.v.s, attw_v_arr[hh - 3])
                                e.t.inc(t.matmul(att_bank(hh), kfm_h, rfm_h, start=True, stop=True))
                            att_t_arr[hh] = e.t.n

                        if hh >= 1:
                            h0 = hh - 1
                            hc0, hr0 = h0 // 2, (h0 % 2) * 64
                            rfm_h0 = r_fm[hr0:hr0 + 64, hc0 * TT + cg * 128:hc0 * TT + (cg + 1) * 128]

                            @blk.vector
                            def _(v, h0=h0):
                                v.wait_ge(e.t.s, att_t_arr[h0])
                                if h0 >= 3:
                                    v.wait_ge(e.t.s, y_t_arr[h0 - 3])
                                e.v.inc(v.tensor_tensor(att_buf(h0)[:, :], att_bank(h0),
                                                        e.wmatT[:, h0 * 128:(h0 + 1) * 128], ALU.mult))
                            attw_v_arr[h0] = e.v.n

                            @blk.tensor
                            def _(t, h0=h0, vs=vs, rfm_h0=rfm_h0, hr0=hr0):
                                t.wait_ge(e.v.s, attw_v_arr[h0])
                                hcol = (h0 // 2) * HN
                                t.matmul(e.psY[:, h0 * HN:(h0 + 1) * HN], att_buf(h0)[:, :],
                                         vs[:, h0 * HN:(h0 + 1) * HN], start=True, stop=True)
                                t.matmul(e.psY2[:, h0 * HN:(h0 + 1) * HN], rfm_h0,
                                         state[hr0:hr0 + 64, hcol:hcol + HN], start=True, stop=True)
                                e.t.inc(t.matmul(e.psC[hr0:hr0 + 64, hcol:hcol + HN],
                                                 kw_tok[:, h0 * HN:(h0 + 1) * HN],
                                                 vs[:, h0 * HN:(h0 + 1) * HN], start=True, stop=True,
                                                 tile_position=(0, hr0)))
                            y_t_arr[h0] = e.t.n

                    all_y_t = e.t.n

                    @blk.scalar
                    def _(s):
                        s.wait_ge(e.t.s, all_y_t)
                        e.s.inc(s.activation(ysb[:, :], e.psY[:, 0:1024], AF.Copy))
                    y1_s = e.s.n

                    @blk.vector
                    def _(v, y1_s=y1_s):
                        v.wait_ge(e.s.s, y1_s)
                        v.tensor_tensor(zf[:, :], e.psY2[:, 0:1024], WBB[:, :], ALU.mult)
                        v.tensor_tensor(ysb[:, :], ysb[:, :], zf[:, :], ALU.add)
                        v.tensor_tensor(state[:, :], state[:, :], WSB[:, :], ALU.mult)
                        e.v.inc(v.tensor_tensor(state[:, :], state[:, :], e.psC[:, 0:(H // 2) * HN], ALU.add))
                    heads_done = e.v.n

                    @blk.vector
                    def _(v, heads_done=heads_done):
                        v.wait_ge(e.v.s, heads_done)
                        for hh in range(H):
                            ins = v.tensor_reduce(gn_m[:, hh:hh + 1], ysb[:, hh * HN:(hh + 1) * HN], AX.X, ALU.add)
                        e.v.inc(ins)
                        v.wait_ge(e.v.s, e.v.n)
                        e.v.inc(v.tensor_scalar(gn_m[:, :], gn_m[:, :], 1.0 / HN, None, ALU.mult))
                        v.wait_ge(e.v.s, e.v.n)
                        for hh in range(H):
                            e.v.inc(v.tensor_scalar(zf[:, hh * HN:(hh + 1) * HN],
                                                    ysb[:, hh * HN:(hh + 1) * HN],
                                                    gn_m[:, hh:hh + 1], None, ALU.subtract))
                    ycen_v = e.v.n

                    @blk.vector
                    def _(v, ycen_v=ycen_v):
                        v.wait_ge(e.v.s, ycen_v)
                        e.v.inc(v.tensor_tensor(k_fm[:, 0:D], zf[:, :], zf[:, :], ALU.mult))
                    ysq_s = e.v.n

                    @blk.vector
                    def _(v, ysq_s=ysq_s):
                        v.wait_ge(e.v.s, ysq_s)
                        for hh in range(H):
                            ins = v.tensor_reduce(gn_v[:, hh:hh + 1], k_fm[:, hh * HN:(hh + 1) * HN], AX.X, ALU.add)
                        e.v.inc(ins)
                        v.wait_ge(e.v.s, e.v.n)
                        e.v.inc(v.tensor_scalar(gn_v[:, :], gn_v[:, :], 1.0 / HN, None, ALU.mult))
                    gnv = e.v.n

                    @blk.scalar
                    def _(s, gnv=gnv):
                        s.wait_ge(e.v.s, gnv)
                        e.s.inc(s.activation(gn_t[:, :], gn_v[:, :], AF.Sqrt, bias=e.epsgn[:, :]))
                    gns = e.s.n

                    @blk.vector
                    def _(v, gns=gns, gs=gs):
                        v.wait_ge(e.s.s, gns)
                        e.v.inc(v.reciprocal(gn_rs[:, :], gn_t[:, :]))
                        v.wait_ge(e.v.s, e.v.n)
                        for hh in range(H):
                            v.tensor_scalar(zf[:, hh * HN:(hh + 1) * HN],
                                            zf[:, hh * HN:(hh + 1) * HN],
                                            gn_rs[:, hh:hh + 1], None, ALU.mult)
                        v.tensor_tensor(zf[:, :], zf[:, :], e.lnwb[:, :], ALU.mult)
                        v.tensor_tensor(zf[:, :], zf[:, :], e.lnbb[:, :], ALU.add)
                        e.v.inc(v.tensor_tensor(zf[:, :], zf[:, :], gs, ALU.mult))
                    zv = e.v.n

                    @blk.scalar
                    def _(s, zv=zv):
                        s.wait_ge(e.v.s, zv)
                        s.wait_ge(e.ld.s, e.ld.n)
                        e.s.inc(s.activation(zh[:, :], zf[:, :], AF.Copy))
                    zhs = e.s.n

                    @blk.vector
                    def _(v, zhs=zhs):
                        v.wait_ge(e.s.s, zhs)
                        e.v.inc(v.tensor_tensor(zl[:, :], zf[:, :], zh[:, :], ALU.subtract))
                    zlv = e.v.n

                    @blk.sync
                    def _(sy, zlv=zlv, cg=cg):
                        sy.wait_ge(e.v.s, zlv)
                        for c in range(DC):
                            e.ld.inc(sy.dma_start(out=zfmh[:, c * TT + cg * 128:c * TT + (cg + 1) * 128],
                                                  in_=zh[:, c * 128:(c + 1) * 128], transpose=True), 16)
                            e.ld.inc(sy.dma_start(out=zfml[:, c * TT + cg * 128:c * TT + (cg + 1) * 128],
                                                  in_=zl[:, c * 128:(c + 1) * 128], transpose=True), 16)
                zt_done = e.ld.n

                for oc in range(DC):
                    sname = "wro" if oc % 2 == 0 else "wro2"
                    slot = wro if oc % 2 == 0 else wro2
                    w_rdy = load_ro_oc(slot, sname, "o", oc)
                    ps = (e.psA if oc % 2 == 0 else e.psB)[:, 0:TT]
                    pi = oc % 2

                    @blk.tensor
                    def _(t, slot=slot, w_rdy=w_rdy, zt_done=zt_done, ps=ps, pi=pi, oc=oc):
                        t.wait_ge(e.ld.s, max(w_rdy, zt_done))
                        t.wait_ge(e.v.s, e.v.n)
                        for dc in range(DC):
                            xh = zfmh[:, dc * TT:(dc + 1) * TT]
                            xl = zfml[:, dc * TT:(dc + 1) * TT]
                            wh_ = slot[:, dc * 128:(dc + 1) * 128]
                            wl_ = slot[:, DC * 128 + dc * 128:DC * 128 + (dc + 1) * 128]
                            t.matmul(ps, wh_, xh, start=(dc == 0), stop=False)
                            t.matmul(ps, wl_, xh, start=False, stop=False)
                            ins = t.matmul(ps, wh_, xl, start=False, stop=(dc == DC - 1))
                        e.t.inc(ins)
                    tn = e.t.n
                    slot_users[sname] = tn

                    @blk.vector
                    def _(v, oc=oc, tn=tn, ps=ps):
                        v.wait_ge(e.t.s, tn)
                        e.v.inc(v.tensor_tensor(xt[:, oc * TT:(oc + 1) * TT],
                                                xt[:, oc * TT:(oc + 1) * TT], ps, ALU.add))
                x1_v = e.v.n
                evac_guard = [0, 0]

                @blk.gpsimd
                def _(g, t0=t0, x1_v=x1_v):
                    g.wait_ge(e.v.s, x1_v)
                    for c in range(DC):
                        e.gq.inc(g.dma_start(out=xs1[c * 128:(c + 1) * 128, t0:t0 + TT],
                                             in_=xt[:, c * TT:(c + 1) * TT]), 16)
                x1_store = e.gq.n

                if dev_dump == "probe" and ti == 0:
                    @blk.gpsimd
                    def _(g):
                        g.wait_ge(e.gq.s, e.gq.n)
                        g.wait_ge(e.v.s, e.v.n)
                        g.wait_ge(e.s.s, e.s.n)
                        
                        e.fin.inc(g.dma_start(out=dbg[0:128, 2048:4096], in_=r_fm[:, :]), 16)
                        e.fin.inc(g.dma_start(out=dbg[640:768, 0:2056], in_=ln[:, :]), 16)
                        e.fin.inc(g.dma_start(out=dbg[128:256, 0:2048], in_=k_tok[:, :]), 16)
                        e.fin.inc(g.dma_start(out=dbg[128:256, 2048:4096], in_=v_tok[:, :]), 16)
                        e.fin.inc(g.dma_start(out=dbg[256:384, 0:2048], in_=g_tok[:, :]), 16)
                        e.fin.inc(g.dma_start(out=dbg[256:384, 2048:2048 + 1024], in_=ysb[:, :]), 16)
                        e.fin.inc(g.dma_start(out=dbg[384:512, 0:1024], in_=zf[:, :]), 16)
                        e.fin.inc(g.dma_start(out=dbg[384:512, 1024:1024 + 512], in_=state[:, :]), 16)
                        e.fin.inc(g.dma_start(out=dbg[512:640, 0:2048], in_=zfmh[:, :], ), 16)
                        e.fin.inc(g.dma_start(out=dbg[512:640, 2048:2048 + 96], in_=st6[:, :]), 16)
                        e.fin.inc(g.dma_start(out=dbg[512:640, 2144:2144 + 16], in_=gn_m[:, :]), 16)
                        e.fin.inc(g.dma_start(out=dbg[512:640, 2160:2160 + 16], in_=gn_v[:, :]), 16)
                        e.fin.inc(g.dma_start(out=dbg[512:640, 2176:2176 + 16], in_=gn_rs[:, :]), 16)
                        g.wait_ge(e.fin.s, e.fin.n)

            if dev_dump == "x1":
                @blk.gpsimd
                def _(g):
                    g.wait_ge(e.gq.s, e.gq.n)
                    e.fin.inc(g.dma_start(out=dbg, in_=xs1), 16)
                    g.wait_ge(e.fin.s, e.fin.n)
            p1.close()

        if n_phases >= 2:
            p2 = ExitStack()
            Ah = sb("Ah", [128, DC * D], BF16, st=p2)
            Al = sb("Al", [128, DC * D], BF16, st=p2)
            Bh = sb("Bh", [128, DC * D], BF16, st=p2)
            Bl = sb("Bl", [128, DC * D], BF16, st=p2)
            tstg = sb("tstg", [128, D], st=p2)
            tstg2 = sb("tstg2", [128, D], st=p2)
            fsum = sb("fsum", [128, DC * TT], st=p2)
            halfpi = sb("halfpi", [128, 1], st=p2)
            bv1 = barrier_vals()

            @blk.vector
            def _(v):
                wait_barrier(v, bv1)
                e.v.inc(v.memset(halfpi[:, :], HALFPI))
            p2v0 = e.v.n

            tstg3 = sb("tstg3", [128, D], st=p2)
            tstg4 = sb("tstg4", [128, D], st=p2)
            ab_stg = [(tstg, tstg2), (tstg3, tstg4)]
            ab_busy = [p2v0, p2v0]
            abi = 0
            for (srcA, srcN, dsth, dstl, use_cos) in [
                (theta_in, theta_n_in, Ah, Al, True),
                (phi_in, phi_n_in, Bh, Bl, False),
            ]:
                for c in range(DC):
                    par = abi % 2
                    abi += 1
                    sA, sB = ab_stg[par]

                    @blk.sync
                    def _(sy, srcA=srcA, srcN=srcN, c=c, bt=bv1["t"], uw=ab_busy[par], sA=sA, sB=sB):
                        sy.wait_ge(e.t.s, bt)
                        sy.wait_ge(e.v.s, uw)
                        e.ld.inc(sy.dma_start(out=sA[:, :], in_=srcA[c * 128:(c + 1) * 128, :]), 16)
                        e.ld.inc(sy.dma_start(out=sB[:, :], in_=srcN[c * 128:(c + 1) * 128, :]), 16)
                    tl = e.ld.n

                    @blk.vector
                    def _(v, tl=tl, sA=sA, sB=sB):
                        v.wait_ge(e.ld.s, tl)
                        e.v.inc(v.tensor_tensor(sA[:, :], sA[:, :], sB[:, :], ALU.add))
                    tv = e.v.n

                    @blk.scalar
                    def _(s, tv=tv, use_cos=use_cos, sA=sA, sB=sB):
                        s.wait_ge(e.v.s, tv)
                        if use_cos:
                            e.s.inc(s.activation(sB[:, :], sA[:, :], AF.Sin, bias=halfpi[:, :]))
                        else:
                            e.s.inc(s.activation(sB[:, :], sA[:, :], AF.Sin))
                    ts_ = e.s.n

                    @blk.vector
                    def _(v, ts_=ts_, c=c, dsth=dsth, dstl=dstl, sB=sB):
                        v.wait_ge(e.s.s, ts_)
                        v.tensor_copy(dsth[:, c * D:(c + 1) * D], sB[:, :])
                        e.v.inc(v.tensor_tensor(dstl[:, c * D:(c + 1) * D], sB[:, :],
                                                dsth[:, c * D:(c + 1) * D], ALU.subtract))
                    ab_busy[par] = e.v.n
            ab_done = e.v.n

            x2_store = 0
            for ti in range(NT):
                t0 = ti * TT

                @blk.gpsimd
                def _(g, t0=t0, x2_store=max(x2_store, 0) if ti > 0 else None):
                    g.wait_ge(e.gq.s, x2_store if x2_store is not None else e.gq.n)
                    g.wait_ge(e.v.s, e.v.n)
                    for c in range(DC):
                        e.gq.inc(g.dma_start(out=xt[:, c * TT:(c + 1) * TT],
                                             in_=xs1[c * 128:(c + 1) * 128, t0:t0 + TT]), 16)
                x_ld = e.gq.n

                @blk.vector
                def _(v):
                    v.wait_ge(e.gq.s, x_ld)
                    e.v.inc(v.memset(ln[:, 0:1], 0.0))
                gv_ = e.v.n

                ln_done = emit_ln(xt, ln, e.v, gv_, 0)

                @blk.scalar
                def _(s, ln_done=ln_done):
                    s.wait_ge(e.v.s, ln_done)
                    s.wait_ge(e.t.s, e.t.n)
                    W1 = TT + 1
                    for c in range(DC):
                        e.s.inc(s.activation(mxh[:, c * TT:(c + 1) * TT],
                                             ln[:, c * W1 + 1:(c + 1) * W1], AF.Copy))
                sp_ = e.s.n

                @blk.vector
                def _(v, sp_=sp_):
                    v.wait_ge(e.s.s, sp_)
                    W1 = TT + 1
                    for c in range(DC):
                        e.v.inc(v.tensor_tensor(mxl[:, c * TT:(c + 1) * TT],
                                                ln[:, c * W1 + 1:(c + 1) * W1],
                                                mxh[:, c * TT:(c + 1) * TT], ALU.subtract))
                mix_done = e.v.n

                evac_v2 = [e.v.n]
                sq_guard = [0, 0]
                for oc in range(DC):
                    bp = ((e.psA, e.psB), (e.psC, e.psD))[oc % 2]

                    @blk.tensor
                    def _(t, oc=oc, mix_done=mix_done, bp=bp, gw=None):
                        t.wait_ge(e.v.s, max(mix_done, ab_done))
                        t.wait_ge(e.s.s, sq_guard[oc % 2])
                        for (ps, wh_, wl_) in ((bp[0][:, 0:TT], Ah, Al), (bp[1][:, 0:TT], Bh, Bl)):
                            for dc in range(DC):
                                xh = mxh[:, dc * TT:(dc + 1) * TT]
                                xl = mxl[:, dc * TT:(dc + 1) * TT]
                                ww_h = wh_[:, dc * D + oc * 128: dc * D + oc * 128 + 128]
                                ww_l = wl_[:, dc * D + oc * 128: dc * D + oc * 128 + 128]
                                t.matmul(ps, ww_h, xh, start=(dc == 0), stop=False)
                                t.matmul(ps, ww_l, xh, start=False, stop=False)
                                ins = t.matmul(ps, ww_h, xl, start=False, stop=(dc == DC - 1))
                            e.t.inc(ins)
                    tn = e.t.n

                    @blk.scalar
                    def _(s, tn=tn, oc=oc, bp=bp):
                        s.wait_ge(e.t.s, tn)
                        s.wait_ge(e.v.s, e.v.n)
                        s.activation(sq[:, 0:TT], bp[0][:, 0:TT], AF.Square)
                        e.s.inc(s.activation(sq[:, TT:2 * TT], bp[1][:, 0:TT], AF.Square))
                    sq_s2 = e.s.n
                    sq_guard[oc % 2] = sq_s2

                    @blk.vector
                    def _(v, sq_s2=sq_s2, oc=oc):
                        v.wait_ge(e.s.s, sq_s2)
                        e.v.inc(v.tensor_tensor(fsum[:, oc * TT:(oc + 1) * TT],
                                                sq[:, 0:TT], sq[:, TT:2 * TT], ALU.add))
                    evac_v2[0] = e.v.n

                @blk.scalar
                def _(s):
                    s.wait_ge(e.v.s, evac_v2[0])
                    e.s.inc(s.activation(fsum[:, :], fsum[:, :], AF.Sqrt))
                fs = e.s.n

                @blk.vector
                def _(v, fs=fs):
                    v.wait_ge(e.s.s, fs)
                    for oc in range(DC):
                        e.v.inc(v.tensor_tensor(xt[:, oc * TT:(oc + 1) * TT],
                                                xt[:, oc * TT:(oc + 1) * TT],
                                                fsum[:, oc * TT:(oc + 1) * TT], ALU.add))
                x2_v = e.v.n

                @blk.gpsimd
                def _(g, t0=t0, x2_v=x2_v):
                    g.wait_ge(e.v.s, x2_v)
                    for c in range(DC):
                        e.gq.inc(g.dma_start(out=xs2[c * 128:(c + 1) * 128, t0:t0 + TT],
                                             in_=xt[:, c * TT:(c + 1) * TT]), 16)
                x2_store = e.gq.n

            if dev_dump == "x2":
                @blk.gpsimd
                def _(g):
                    g.wait_ge(e.gq.s, e.gq.n)
                    e.fin.inc(g.dma_start(out=dbg, in_=xs2), 16)
                    g.wait_ge(e.fin.s, e.fin.n)
            p2.close()

        if n_phases >= 3:
            p3 = ExitStack()
            WwPh = sb("WwPh", [128, DC * D], BF16, st=p3)
            WwPl = sb("WwPl", [128, DC * D], BF16, st=p3)
            WrdPh = sb("WrdPh", [128, DC * D], BF16, st=p3)
            WrdPl = sb("WrdPl", [128, DC * D], BF16, st=p3)
            ST = sb("ST", [128, DC * D], st=p3)
            v_tok3 = sb("v_tok3", [128, CPT * D], st=p3)
            w_tok3 = sb("w_tok3", [128, D], st=p3)
            t_tok3 = sb("t_tok3", [128, D], st=p3)
            vgw = sb("vgw", [128, D], st=p3)
            stm = sb("stm", [128, 128], st=p3)
            intr = sb("intr", [128, D], st=p3)
            reads_tok = sb("reads_tok", [128, D], st=p3)
            reads_fm = sb("reads_fm", [128, DC * 128], st=p3)
            rdh = sb("rdh", [128, DC * 128], BF16, st=p3)
            rdl = sb("rdl", [128, DC * 128], BF16, st=p3)
            x3f = sb("x3f", [128, D], st=p3)
            x3bh = sb("x3bh", [128, D], BF16, st=p3)
            x3bl = sb("x3bl", [128, D], BF16, st=p3)
            MT = sb("MT", [128, 128], st=p3)
            l1row = sb("l1row", [1, 8], st=p3)
            lg_col = sb("lg_col", [128, 1], st=p3)
            al_col = sb("al_col", [128, 1], st=p3)
            dec_col = sb("dec_col", [128, 1], st=p3)
            gw_col = sb("gw_col", [128, 1], st=p3)
            gC_col = sb("gC_col", [128, 1], st=p3)
            bv2 = barrier_vals()

            for (srcW, dsth, dstl) in [(WwT_in, WwPh, WwPl), (WrdT_in, WrdPh, WrdPl)]:
                for c in range(DC):
                    @blk.sync
                    def _(sy, srcW=srcW, c=c, bt=bv2["t"]):
                        sy.wait_ge(e.t.s, bt)
                        sy.wait_ge(e.v.s, e.v.n)
                        sy.wait_ge(e.s.s, e.s.n)
                        e.ld.inc(sy.dma_start(out=wstg[:, :], in_=srcW[c * 128:(c + 1) * 128, :]), 16)
                    tl = e.ld.n

                    @blk.scalar
                    def _(s, tl=tl, c=c, dsth=dsth):
                        s.wait_ge(e.ld.s, tl)
                        e.s.inc(s.activation(dsth[:, c * D:(c + 1) * D], wstg[:, :], AF.Copy))
                    ts_ = e.s.n

                    @blk.vector
                    def _(v, ts_=ts_, c=c, dsth=dsth, dstl=dstl):
                        v.wait_ge(e.s.s, ts_)
                        e.v.inc(v.tensor_tensor(dstl[:, c * D:(c + 1) * D], wstg[:, :],
                                                dsth[:, c * D:(c + 1) * D], ALU.subtract))
            wsplit3 = e.v.n

            @blk.sync
            def _(sy, bt=bv2["t"]):
                sy.wait_ge(e.t.s, bt)
                e.ld.inc(sy.dma_start(out=l1row[0:1, 0:4], in_=l1s_in), 16)
                e.ld.inc(sy.dma_start(out=MT[:, :], in_=c2t_in), 16)
            sc_ld = e.ld.n

            @blk.scalar
            def _(s):
                s.wait_ge(e.ld.s, sc_ld)
                s.activation(l1row[0:1, 4:5], l1row[0:1, 0:1], AF.Sigmoid)
                e.s.inc(s.activation(l1row[0:1, 5:6], l1row[0:1, 4:5], AF.Ln))
            sc_s = e.s.n

            @blk.gpsimd
            def _(g):
                g.wait_ge(e.s.s, sc_s)
                e.gq.inc(g.dma_start(out=scal_d[0:1, 0:8], in_=l1row[0:1, 0:8]), 16)
            sc_g = e.gq.n

            @blk.sync
            def _(sy):
                sy.wait_ge(e.gq.s, sc_g)
                e.ld.inc(sy.dma_start(out=lg_col[:, :], in_=scal_d[0:1, 5:6].broadcast_to([128, 1])), 16)
                e.ld.inc(sy.dma_start(out=al_col[:, :], in_=scal_d[0:1, 1:2].broadcast_to([128, 1])), 16)
            sc_b = e.ld.n

            @blk.vector
            def _(v):
                v.wait_ge(e.ld.s, sc_b)
                e.v.inc(v.tensor_scalar(dec_col[:, :], e.ioi[:, :], lg_col[:, :], None, ALU.mult))
                e.v.inc(v.tensor_scalar(gw_col[:, :], e.ior[:, :], lg_col[:, :], None, ALU.mult))
                e.v.inc(v.tensor_scalar(gC_col[:, :], lg_col[:, :], float(Q), None, ALU.mult))
                e.v.inc(v.tensor_scalar(MT[:, :], MT[:, :], lg_col[:, :], None, ALU.mult))
                v.wait_ge(e.v.s, e.v.n)
            sc_v = e.v.n

            @blk.scalar
            def _(s):
                s.wait_ge(e.v.s, sc_v)
                s.activation(dec_col[:, :], dec_col[:, :], AF.Exp)
                s.activation(gw_col[:, :], gw_col[:, :], AF.Exp)
                s.activation(gC_col[:, :], gC_col[:, :], AF.Exp)
                s.activation(al_col[:, :], al_col[:, :], AF.Exp)
                e.s.inc(s.activation(MT[:, :], MT[:, :], AF.Exp))
            sc_s2 = e.s.n

            @blk.vector
            def _(v):
                v.wait_ge(e.s.s, sc_s2)
                v.memset(ST[:, :], 0.0)
                for c in range(DC):
                    e.v.inc(v.memset(ln[:, c * (TT + 1):c * (TT + 1) + 1], 0.0))
            p3_init = e.v.n

            x3_store = 0
            st_upd_v = p3_init
            for ti in range(NT):
                t0 = ti * TT

                @blk.gpsimd
                def _(g, t0=t0, w=x3_store if ti > 0 else None):
                    g.wait_ge(e.gq.s, w if w is not None else e.gq.n)
                    g.wait_ge(e.v.s, e.v.n)
                    for c in range(DC):
                        e.gq.inc(g.dma_start(out=xt[:, c * TT:(c + 1) * TT],
                                             in_=xs2[c * 128:(c + 1) * 128, t0:t0 + TT]), 16)
                x_ld = e.gq.n

                @blk.vector
                def _(v, ti=ti):
                    v.wait_ge(e.gq.s, x_ld)
                    W1 = TT + 1
                    if ti > 0:
                        for c in range(DC):
                            e.v.inc(v.tensor_copy(ln[:, c * W1:c * W1 + 1], ln[:, c * W1 + TT:c * W1 + TT + 1]))
                    else:
                        for c in range(DC):
                            e.v.inc(v.memset(ln[:, c * W1:c * W1 + 1], 0.0))
                guard_v = e.v.n

                ln_done = emit_ln(xt, ln, e.v, guard_v, 2)

                @blk.scalar
                def _(s, ln_done=ln_done):
                    s.wait_ge(e.v.s, ln_done)
                    s.wait_ge(e.t.s, e.t.n)
                    W1 = TT + 1
                    for c in range(DC):
                        e.s.inc(s.activation(mxh[:, c * TT:(c + 1) * TT],
                                             ln[:, c * W1 + 1:(c + 1) * W1], AF.Copy))
                sp_ = e.s.n

                @blk.vector
                def _(v, sp_=sp_):
                    v.wait_ge(e.s.s, sp_)
                    W1 = TT + 1
                    for c in range(DC):
                        e.v.inc(v.tensor_tensor(mxl[:, c * TT:(c + 1) * TT],
                                                ln[:, c * W1 + 1:(c + 1) * W1],
                                                mxh[:, c * TT:(c + 1) * TT], ALU.subtract))
                mix_done = e.v.n

                for cg in range(CPT):
                    ps = (e.psA if cg % 2 == 0 else e.psB)[:, 0:512]
                    for oh in range(2):
                        @blk.tensor
                        def _(t, cg=cg, oh=oh, mix_done=mix_done, ps=ps):
                            t.wait_ge(e.v.s, max(mix_done, wsplit3))
                            t.wait_ge(e.s.s, e.s.n)
                            for dc in range(DC):
                                xh = mxh[:, dc * TT + cg * 128:dc * TT + (cg + 1) * 128]
                                xl = mxl[:, dc * TT + cg * 128:dc * TT + (cg + 1) * 128]
                                wh_ = WwPh[:, dc * D + oh * 512:dc * D + oh * 512 + 512]
                                wl_ = WwPl[:, dc * D + oh * 512:dc * D + oh * 512 + 512]
                                t.matmul(ps[:, 0:512] if False else ps, xh, wh_, start=(dc == 0), stop=False)
                                t.matmul(ps, xh, wl_, start=False, stop=False)
                                ins = t.matmul(ps, xl, wh_, start=False, stop=(dc == DC - 1))
                            e.t.inc(ins)
                        tn = e.t.n

                        @blk.scalar
                        def _(s, cg=cg, oh=oh, tn=tn, ps=ps):
                            s.wait_ge(e.t.s, tn)
                            e.s.inc(s.activation(v_tok3[:, cg * D + oh * 512:cg * D + oh * 512 + 512],
                                                 ps, AF.Copy))
                vproj_done = e.s.n

                for cg in range(CPT):
                    ci = ti * CPT + cg
                    W1 = TT + 1
                    vs3 = v_tok3[:, cg * D:(cg + 1) * D]

                    trdone = []
                    for half in range(2):
                        @blk.tensor
                        def _(t, half=half, cg=cg, vproj_done=vproj_done):
                            if half == 0:
                                t.wait_ge(e.s.s, vproj_done)
                                t.wait_ge(e.v.s, e.v.n)
                            else:
                                t.wait_ge(e.v.s, trdone[0])
                            for c4 in range(4):
                                c = half * 4 + c4
                                ins = t.transpose(e.psC[:, c4 * 128:(c4 + 1) * 128],
                                                  ln[:, c * W1 + cg * 128:c * W1 + cg * 128 + 128],
                                                  e.eye[:, :])
                            e.t.inc(ins)
                        tn = e.t.n

                        @blk.vector
                        def _(v, half=half, tn=tn):
                            v.wait_ge(e.t.s, tn)
                            e.v.inc(v.tensor_copy(w_tok3[:, half * 512:(half + 1) * 512], e.psC[:, 0:512]))
                        trdone.append(e.v.n)

                    @blk.vector
                    def _(v, cg=cg):
                        v.wait_ge(e.v.s, trdone[1])
                        for c in range(DC):
                            e.v.inc(v.tensor_tensor(sq[:, c * TT + cg * 128:c * TT + (cg + 1) * 128],
                                                    xt[:, c * TT + cg * 128:c * TT + (cg + 1) * 128],
                                                    ln[:, c * W1 + cg * 128 + 1:c * W1 + (cg + 1) * 128 + 1],
                                                    ALU.add))
                    tfm_v = e.v.n

                    trdone2 = []
                    for half in range(2):
                        @blk.tensor
                        def _(t, half=half, cg=cg, tfm_v=tfm_v):
                            if half == 0:
                                t.wait_ge(e.v.s, tfm_v)
                            else:
                                t.wait_ge(e.v.s, trdone2[0])
                            for c4 in range(4):
                                c = half * 4 + c4
                                ins = t.transpose(e.psC[:, c4 * 128:(c4 + 1) * 128],
                                                  sq[:, c * TT + cg * 128:c * TT + (cg + 1) * 128],
                                                  e.eye[:, :])
                            e.t.inc(ins)
                        tn = e.t.n

                        @blk.vector
                        def _(v, half=half, tn=tn):
                            v.wait_ge(e.t.s, tn)
                            e.v.inc(v.tensor_copy(t_tok3[:, half * 512:(half + 1) * 512], e.psC[:, 0:512]))
                        trdone2.append(e.v.n)

                    @blk.vector
                    def _(v, vs3=vs3):
                        v.wait_ge(e.s.s, vproj_done)
                        e.v.inc(v.tensor_scalar(vgw[:, :], vs3, gw_col[:, :], None, ALU.mult))
                    vgw_v = e.v.n

                    @blk.tensor
                    def _(t, cg=cg, st_upd_v=st_upd_v):
                        t.wait_ge(e.v.s, max(trdone[1], st_upd_v))
                        for c in range(DC):
                            ins = t.matmul(e.psD[:, 0:128],
                                           ln[:, c * W1 + cg * 128:c * W1 + cg * 128 + 128],
                                           ln[:, c * W1 + cg * 128 + 1:c * W1 + (cg + 1) * 128 + 1],
                                           start=(c == 0), stop=(c == DC - 1))
                        e.t.inc(ins)
                    stp_t = e.t.n

                    @blk.vector
                    def _(v, stp_t=stp_t):
                        v.wait_ge(e.t.s, stp_t)
                        e.v.inc(v.tensor_tensor(stm[:, :], e.psD[:, 0:128], MT[:, :], ALU.mult))
                    stm_v = e.v.n

                    @blk.tensor
                    def _(t, cg=cg, stm_v=stm_v, vs3=vs3):
                        t.wait_ge(e.v.s, stm_v)
                        for nh in range(2):
                            ins = t.matmul(e.psY2[:, nh * 512:(nh + 1) * 512], stm[:, :],
                                           vs3[:, nh * 512:(nh + 1) * 512], start=True, stop=True)
                        e.t.inc(ins)
                        for nh in range(2):
                            for c in range(DC):
                                ins = t.matmul(e.psY[:, nh * 512:(nh + 1) * 512],
                                               ln[:, c * W1 + cg * 128 + 1:c * W1 + (cg + 1) * 128 + 1],
                                               ST[:, c * D + nh * 512:c * D + nh * 512 + 512],
                                               start=(c == 0), stop=(c == DC - 1))
                            e.t.inc(ins)
                    ii_t = e.t.n

                    @blk.scalar
                    def _(s, ii_t=ii_t):
                        s.wait_ge(e.t.s, ii_t)
                        e.s.inc(s.activation(intr[:, :], e.psY2[:, 0:1024], AF.Copy))
                    intr_s = e.s.n

                    upd_t = []
                    for ec in range(DC):
                        @blk.tensor
                        def _(t, ec=ec, vgw_v=vgw_v, intr_s=intr_s):
                            if ec == 0:
                                t.wait_ge(e.v.s, vgw_v)
                                t.wait_ge(e.s.s, max(e.s.n, intr_s))
                            if ec >= 2:
                                t.wait_ge(e.v.s, upd_t[ec - 2])
                            ps = (e.psA if ec % 2 == 0 else e.psB)[:, 0:512]
                            for nh in range(2):
                                ins = t.matmul(ps[:, 0:512], w_tok3[:, ec * 128:(ec + 1) * 128],
                                               vgw[:, nh * 512:(nh + 1) * 512],
                                               start=(nh == 0), stop=False) if False else None
                            ins = t.matmul(ps[:, 0:512], w_tok3[:, ec * 128:(ec + 1) * 128],
                                           vgw[:, 0:512], start=True, stop=True)
                            e.t.inc(ins)
                            ins = t.matmul((e.psD if ec % 2 == 0 else e.psY2)[:, 0:512],
                                           w_tok3[:, ec * 128:(ec + 1) * 128],
                                           vgw[:, 512:1024], start=True, stop=True)
                            e.t.inc(ins)
                        tn2 = e.t.n

                        @blk.vector
                        def _(v, ec=ec, tn2=tn2):
                            v.wait_ge(e.t.s, tn2)
                            ps1 = (e.psA if ec % 2 == 0 else e.psB)[:, 0:512]
                            ps2 = (e.psD if ec % 2 == 0 else e.psY2)[:, 0:512]
                            v.scalar_tensor_tensor(ST[:, ec * D:ec * D + 512],
                                                   ST[:, ec * D:ec * D + 512],
                                                   gC_col[:, :], ps1, ALU.mult, ALU.add)
                            e.v.inc(v.scalar_tensor_tensor(ST[:, ec * D + 512:ec * D + 1024],
                                                           ST[:, ec * D + 512:ec * D + 1024],
                                                           gC_col[:, :], ps2, ALU.mult, ALU.add))
                        upd_t.append(e.v.n)
                    st_upd_v = e.v.n

                    @blk.vector
                    def _(v, intr_s=intr_s):
                        v.wait_ge(e.s.s, intr_s)
                        e.v.inc(v.scalar_tensor_tensor(reads_tok[:, :], e.psY[:, 0:1024],
                                                       dec_col[:, :], intr[:, :],
                                                       ALU.mult, ALU.add))
                    reads_v = e.v.n

                    trdone3 = []
                    for half in range(2):
                        @blk.tensor
                        def _(t, half=half, reads_v=reads_v):
                            if half == 0:
                                t.wait_ge(e.v.s, reads_v)
                            else:
                                t.wait_ge(e.v.s, trdone3[0])
                            for c4 in range(4):
                                c = half * 4 + c4
                                ins = t.transpose(e.psC[:, c4 * 128:(c4 + 1) * 128],
                                                  reads_tok[:, c * 128:(c + 1) * 128], e.eye[:, :])
                            e.t.inc(ins)
                        tn = e.t.n

                        @blk.vector
                        def _(v, half=half, tn=tn):
                            v.wait_ge(e.t.s, tn)
                            e.v.inc(v.tensor_copy(reads_fm[:, half * 512:(half + 1) * 512], e.psC[:, 0:512]))
                        trdone3.append(e.v.n)

                    @blk.scalar
                    def _(s):
                        s.wait_ge(e.v.s, trdone3[1])
                        e.s.inc(s.activation(rdh[:, :], reads_fm[:, :], AF.Copy))
                    rdh_s = e.s.n

                    @blk.vector
                    def _(v, rdh_s=rdh_s):
                        v.wait_ge(e.s.s, rdh_s)
                        e.v.inc(v.tensor_tensor(rdl[:, :], reads_fm[:, :], rdh[:, :], ALU.subtract))
                    rdl_v = e.v.n

                    @blk.tensor
                    def _(t, rdl_v=rdl_v):
                        t.wait_ge(e.v.s, rdl_v)
                        for nh in range(2):
                            for dc in range(DC):
                                wh_ = WrdPh[:, dc * D + nh * 512:dc * D + nh * 512 + 512]
                                wl_ = WrdPl[:, dc * D + nh * 512:dc * D + nh * 512 + 512]
                                xh = rdh[:, dc * 128:(dc + 1) * 128]
                                xl = rdl[:, dc * 128:(dc + 1) * 128]
                                t.matmul(e.psY[:, nh * 512:(nh + 1) * 512], xh, wh_, start=(dc == 0), stop=False)
                                t.matmul(e.psY[:, nh * 512:(nh + 1) * 512], xh, wl_, start=False, stop=False)
                                ins = t.matmul(e.psY[:, nh * 512:(nh + 1) * 512], xl, wh_, start=False,
                                               stop=(dc == DC - 1))
                            e.t.inc(ins)
                    rp_t = e.t.n

                    @blk.vector
                    def _(v, rp_t=rp_t):
                        v.wait_ge(e.t.s, rp_t)
                        e.v.inc(v.scalar_tensor_tensor(x3f[:, :], e.psY[:, 0:1024], al_col[:, :],
                                                       t_tok3[:, :], ALU.mult, ALU.add))
                    x3_v = e.v.n

                    @blk.vector
                    def _(v, ci=ci, x3_v=x3_v):
                        v.wait_ge(e.v.s, x3_v)
                        e.v.inc(v.tensor_reduce(e.m4[:, ci:ci + 1], x3f[:, :], AX.X, ALU.add))
                        v.wait_ge(e.v.s, e.v.n)
                        e.v.inc(v.tensor_scalar(e.m4[:, ci:ci + 1], e.m4[:, ci:ci + 1], 1.0 / D, None, ALU.mult))
                        v.wait_ge(e.v.s, e.v.n)
                        e.v.inc(v.tensor_scalar(vgw[:, :], x3f[:, :], e.m4[:, ci:ci + 1], None, ALU.subtract))
                        e.v.inc(v.tensor_reduce(e.cmx4[:, ci:ci + 1], vgw[:, :], AX.X, ALU.max,
                                                apply_absolute_value=True))
                        e.v.inc(v.tensor_tensor(vgw[:, :], vgw[:, :], vgw[:, :], ALU.mult))
                        e.v.inc(v.tensor_reduce(e.var4[:, ci:ci + 1], vgw[:, :], AX.X, ALU.add))
                    s4b = e.v.n

                    @blk.scalar
                    def _(s, s4b=s4b):
                        s.wait_ge(e.v.s, s4b)
                        s.wait_ge(e.gq.s, e.gq.n)
                        e.s.inc(s.activation(x3bh[:, :], x3f[:, :], AF.Copy))
                    x3h_s = e.s.n

                    @blk.vector
                    def _(v, x3h_s=x3h_s):
                        v.wait_ge(e.s.s, x3h_s)
                        e.v.inc(v.tensor_tensor(x3bl[:, :], x3f[:, :], x3bh[:, :], ALU.subtract))
                    x3l_v = e.v.n

                    @blk.gpsimd
                    def _(g, ci=ci, x3l_v=x3l_v):
                        g.wait_ge(e.v.s, x3l_v)
                        e.gq.inc(g.dma_start(out=x3h_d[ci * 128:(ci + 1) * 128, :], in_=x3bh[:, :]), 16)
                        e.gq.inc(g.dma_start(out=x3l_d[ci * 128:(ci + 1) * 128, :], in_=x3bl[:, :]), 16)
                        if dev_dump == "x3" or n_phases == 3:
                            e.gq.inc(g.dma_start(out=out[ci * 128:(ci + 1) * 128, :], in_=x3f[:, :]), 16)
                    x3_store = e.gq.n

            if n_phases == 3:
                @blk.gpsimd
                def _(g):
                    g.wait_ge(e.gq.s, e.gq.n)
                    e.fin.inc(g.dma_start(out=scal_d[0:1, 8:9], in_=l1row[0:1, 0:1]), 16)
                    g.wait_ge(e.fin.s, e.fin.n)
            p3.close()

        psh.close()

        if n_phases >= 4:
            p4 = ExitStack()
            kwgT = sb("kwgT", [128, DC * DFF], BF16, st=p4)
            kwuT = sb("kwuT", [128, DC * DFF], BF16, st=p4)
            kwdT = sb("kwdT", [128, IC * D], BF16, st=p4)
            qstg = sb("qstg", [128, DFF], st=p4)
            qu = sb("qu", [128, DFF], st=p4)
            qk = sb("qk", [128, DFF], I32, st=p4)
            qb = sb("qb", [128, DFF], BF16, st=p4)
            wsg = sb("wsg", [128, IC], st=p4)
            wsu = sb("wsu", [128, IC], st=p4)
            wsd = sb("wsd", [128, DC], st=p4)
            wsgx = sb("wsgx", [128, IC], st=p4)
            wsux = sb("wsux", [128, IC], st=p4)
            m4 = e.m4
            var4 = e.var4
            cmx4 = e.cmx4
            rs4 = sb("rs4", [128, NCH], st=p4)
            rsi4 = sb("rsi4", [128, NCH], st=p4)
            sb4 = sb("sb4", [128, NCH], st=p4)
            x3th = sb("x3th", [128, D], BF16, st=p4)
            x3tl = sb("x3tl", [128, D], BF16, st=p4)
            x3f4 = sb("x3f4", [128, D], st=p4)
            scr4 = sb("scr4", [128, D], st=p4)
            kxf = sb("kxf", [128, DC * 512], BF16, st=p4)
            sgt = sb("sgt", [128, 512], st=p4)
            sut = sb("sut", [128, 512], st=p4)
            htile = sb("htile", [128, 512], st=p4)
            hmax = sb("hmax", [128, 1], st=p4)
            hab = sb("hab", [128, IC], st=p4)
            wsdq = sb("wsdq", [128, D], st=p4)
            p4row = sb("p4row", [1, 16], st=p4)
            mcol1 = sb("mcol1", [128, 1], st=p4)
            mcol2 = sb("mcol2", [128, 1], st=p4)
            colbuf = nc.dram_tensor("colbuf", [1, 256], F32).ap()
            bv3 = barrier_vals()

            winv_g = sb("winv_g", [128, IC], st=p4)
            winv_u = sb("winv_u", [128, IC], st=p4)
            winv_d = sb("winv_d", [128, DC], st=p4)
            wsd2 = sb("wsd2", [128, DC], st=p4)
            wsd3 = sb("wsd3", [128, DC], st=p4)
            units = []
            for rc in range(IC):
                units.append((wg_in, rc, 0, D, kwgT, DFF, (wsg, rc)))
            for rc in range(IC):
                units.append((wu_in, rc, 0, D, kwuT, DFF, (wsu, rc)))
            for rc in range(DC):
                units.append((wd_in, rc, 0, 1024, kwdT, D, (wsd, rc)))
                units.append((wd_in, rc, 1024, 1024, kwdT, D, (wsd2, rc)))
                units.append((wd_in, rc, 2048, DFF - 2048, kwdT, D, (wsd3, rc)))

            wbusy = [bv3["v"], bv3["v"]]
            for ui, (sap, rc, co, wd_, dstT, dstr, (wscol, wsi)) in enumerate(units):
                par = ui % 2
                wbuf = (qstg, qu)[par]

                @blk.sync
                def _(sy, sap=sap, rc=rc, co=co, wd_=wd_, wbuf=wbuf, bt=bv3["t"], uw=wbusy[par]):
                    sy.wait_ge(e.t.s, bt)
                    sy.wait_ge(e.v.s, uw)
                    e.ld.inc(sy.dma_start(out=wbuf[:, 0:wd_],
                                          in_=sap[rc * 128:(rc + 1) * 128, co:co + wd_]), 16)
                wl = e.ld.n

                @blk.vector
                def _(v, wl=wl, wbuf=wbuf, wd_=wd_, wscol=wscol, wsi=wsi):
                    v.wait_ge(e.ld.s, wl)
                    e.v.inc(v.tensor_reduce(wscol[:, wsi:wsi + 1], wbuf[:, 0:wd_],
                                            AX.X, ALU.max, apply_absolute_value=True))
                wbusy[par] = e.v.n

            @blk.vector
            def _(v):
                v.wait_ge(e.v.s, e.v.n)
                v.tensor_tensor(wsd[:, :], wsd[:, :], wsd2[:, :], ALU.max)
                e.v.inc(v.tensor_tensor(wsd[:, :], wsd[:, :], wsd3[:, :], ALU.max))
                v.wait_ge(e.v.s, e.v.n)
                v.reciprocal(winv_g[:, :], wsg[:, :])
                v.reciprocal(winv_u[:, :], wsu[:, :])
                e.v.inc(v.reciprocal(winv_d[:, :], wsd[:, :]))
                v.wait_ge(e.v.s, e.v.n)
            winv_v = e.v.n

            WINV = {id(wg_in): winv_g, id(wu_in): winv_u, id(wd_in): winv_d}
            wbusy = [winv_v, winv_v]
            ubusy = [winv_v, winv_v]
            tr_prev = e.ld.n
            for ui, (sap, rc, co, wd_, dstT, dstr, _ws) in enumerate(units):
                par = ui % 2
                wbuf = (qstg, qu)[par]
                ups = (e.psY, e.psY2)[par]

                @blk.sync
                def _(sy, sap=sap, rc=rc, co=co, wd_=wd_, wbuf=wbuf, uw=wbusy[par]):
                    sy.wait_ge(e.v.s, uw)
                    e.ld.inc(sy.dma_start(out=wbuf[:, 0:wd_],
                                          in_=sap[rc * 128:(rc + 1) * 128, co:co + wd_]), 16)
                wl = e.ld.n

                @blk.vector
                def _(v, wl=wl, wbuf=wbuf, ups=ups, wd_=wd_, rc=rc, winv=WINV[id(sap)],
                      trp=tr_prev):
                    v.wait_ge(e.ld.s, max(wl, trp))
                    v.tensor_scalar(ups[:, 0:wd_], wbuf[:, 0:wd_], winv[:, rc:rc + 1], None, ALU.mult)
                    v.tensor_scalar(ups[:, 0:wd_], ups[:, 0:wd_], 127.0, 0.5, ALU.mult, ALU.add)
                    v.tensor_copy(qk[:, 0:wd_], ups[:, 0:wd_])
                    v.tensor_copy(wbuf[:, 0:wd_], qk[:, 0:wd_])
                    v.tensor_tensor(ups[:, 0:wd_], ups[:, 0:wd_], wbuf[:, 0:wd_], ALU.subtract)
                    v.tensor_scalar(ups[:, 0:wd_], ups[:, 0:wd_], 0.0, None, ALU.is_lt)
                    v.tensor_tensor(qb[:, 0:wd_], wbuf[:, 0:wd_], ups[:, 0:wd_], ALU.subtract)
                    v.tensor_scalar(qb[:, 0:wd_], qb[:, 0:wd_], -128.0, None, ALU.max)
                    e.v.inc(v.tensor_scalar(qb[:, 0:wd_], qb[:, 0:wd_], 127.0, None, ALU.min))
                qv = e.v.n
                wbusy[par] = qv
                ubusy[par] = qv

                @blk.sync
                def _(sy, rc=rc, co=co, qv=qv, dstT=dstT, wd_=wd_, dstr=dstr):
                    sy.wait_ge(e.v.s, qv)
                    for tc in range(wd_ // 128):
                        e.ld.inc(sy.dma_start(
                            out=dstT[:, (co // 128 + tc) * dstr + rc * 128:(co // 128 + tc) * dstr + (rc + 1) * 128],
                            in_=qb[:, tc * 128:(tc + 1) * 128], transpose=True), 16)
                tr_prev = e.ld.n
            wq_done_ld = e.ld.n
            wq_done_v = e.v.n

            @blk.sync
            def _(sy, bt=bv3["t"]):
                sy.wait_ge(e.t.s, bt)
                e.ld.inc(sy.dma_start(out=p4row[0:1, 0:4], in_=l1s_in), 16)
            p4l = e.ld.n

            @blk.scalar
            def _(s):
                s.wait_ge(e.ld.s, p4l)
                e.s.inc(s.activation(p4row[0:1, 4:5], p4row[0:1, 2:3], AF.Sigmoid))
                s.wait_ge(e.s.s, e.s.n)
                e.s.inc(s.activation(p4row[0:1, 5:6], p4row[0:1, 3:4], AF.Sigmoid))
                s.wait_ge(e.s.s, e.s.n)
            p4s = e.s.n

            @blk.vector
            def _(v, bv=bv3["v"]):
                v.wait_ge(e.v.s, bv)
                v.wait_ge(e.s.s, p4s)
                e.v.inc(v.tensor_scalar(var4[:, :], var4[:, :], 1.0 / D, None, ALU.mult))
            v4a = e.v.n

            @blk.scalar
            def _(s):
                s.wait_ge(e.v.s, v4a)
                e.s.inc(s.activation(var4[:, :], var4[:, :], AF.Sqrt, bias=e.epsln[:, :]))
            v4b = e.s.n

            @blk.vector
            def _(v):
                v.wait_ge(e.s.s, v4b)
                e.v.inc(v.reciprocal(rs4[:, :], var4[:, :]))
                v.wait_ge(e.v.s, e.v.n)
                e.v.inc(v.tensor_tensor(cmx4[:, :], cmx4[:, :], rs4[:, :], ALU.mult))
                v.wait_ge(e.v.s, e.v.n)
                e.v.inc(v.tensor_reduce(hmax[:, :], cmx4[:, :], AX.X, ALU.max))
            v4c = e.v.n

            @blk.gpsimd
            def _(g):
                g.wait_ge(e.v.s, v4c)
                e.gq.inc(g.dma_start(out=colbuf[0:1, 0:128], in_=hmax[:, 0:1]), 16)
            c1g = e.gq.n

            @blk.sync
            def _(sy):
                sy.wait_ge(e.gq.s, c1g)
                e.ld.inc(sy.dma_start(out=qstg[0:1, 0:128], in_=colbuf[0:1, 0:128]), 16)
            c1l = e.ld.n

            @blk.vector
            def _(v):
                v.wait_ge(e.ld.s, c1l)
                v.wait_ge(e.v.s, wq_done_v)
                e.v.inc(v.tensor_reduce(qstg[0:1, 128:129], qstg[0:1, 0:128], AX.X, ALU.max))
            c1v = e.v.n

            @blk.gpsimd
            def _(g):
                g.wait_ge(e.v.s, c1v)
                e.gq.inc(g.dma_start(out=cc_in1.ap(), in_=qstg[0:1, 128:129]), 16)
                g.wait_ge(e.gq.s, e.gq.n)
                e.cc.inc(g.collective_compute("AllReduce", ALU.max,
                                              replica_groups=[list(range(B))],
                                              ins=[cc_in1.ap().opt()],
                                              outs=[cc_out1.ap().opt()]))
                g.wait_ge(e.cc.s, e.cc.n)
                e.gq.inc(g.dma_start(out=qstg[0:1, 130:131], in_=cc_out1.ap()), 16)
            cc1_g = e.gq.n

            @blk.vector
            def _(v):
                v.wait_ge(e.gq.s, cc1_g)
                e.v.inc(v.tensor_tensor(qstg[0:1, 131:132], qstg[0:1, 130:131], p4row[0:1, 4:5], ALU.mult))
                v.wait_ge(e.v.s, e.v.n)
                e.v.inc(v.tensor_scalar(qstg[0:1, 131:132], qstg[0:1, 131:132], 1.0 / 127.0, None, ALU.mult))
                v.wait_ge(e.v.s, e.v.n)
                e.v.inc(v.reciprocal(qstg[0:1, 132:133], qstg[0:1, 131:132]))
                v.wait_ge(e.v.s, e.v.n)
            sc1_v = e.v.n

            @blk.gpsimd
            def _(g):
                g.wait_ge(e.v.s, sc1_v)
                e.gq.inc(g.dma_start(out=scal_d[0:1, 0:4], in_=qstg[0:1, 130:134]), 16)
            sc1_g = e.gq.n

            @blk.sync
            def _(sy):
                sy.wait_ge(e.gq.s, sc1_g)
                e.ld.inc(sy.dma_start(out=mcol1[:, :], in_=scal_d[0:1, 2:3].broadcast_to([128, 1])), 16)
            sc1_b = e.ld.n

            @blk.vector
            def _(v):
                v.wait_ge(e.ld.s, sc1_b)
                e.v.inc(v.tensor_scalar(rsi4[:, :], rs4[:, :], mcol1[:, :], None, ALU.mult))
                v.wait_ge(e.v.s, e.v.n)
                e.v.inc(v.tensor_tensor(sb4[:, :], m4[:, :], rsi4[:, :], ALU.mult))
                v.wait_ge(e.v.s, e.v.n)
                e.v.inc(v.tensor_scalar(sb4[:, :], sb4[:, :], -1.0, 0.5, ALU.mult, ALU.add))
                v.wait_ge(e.v.s, e.v.n)
            sc2_v = e.v.n

            @blk.sync
            def _(sy):
                sy.wait_ge(e.gq.s, sc1_g)
                e.ld.inc(sy.dma_start(out=mcol2[:, :], in_=scal_d[0:1, 1:2].broadcast_to([128, 1])), 16)
            scx_b = e.ld.n

            @blk.vector
            def _(v):
                v.wait_ge(e.ld.s, scx_b)
                v.wait_ge(e.v.s, sc2_v)
                e.v.inc(v.tensor_scalar(wsgx[:, :], wsg[:, :], mcol2[:, :], None, ALU.mult))
                e.v.inc(v.tensor_scalar(wsux[:, :], wsu[:, :], mcol2[:, :], None, ALU.mult))
                v.wait_ge(e.v.s, e.v.n)
                e.v.inc(v.tensor_scalar(wsgx[:, :], wsgx[:, :], 1.0 / 127.0, None, ALU.mult))
                e.v.inc(v.tensor_scalar(wsux[:, :], wsux[:, :], 1.0 / 127.0, None, ALU.mult))
                v.wait_ge(e.v.s, e.v.n)
            wsx_v = e.v.n

            @blk.vector
            def _(v):
                v.wait_ge(e.v.s, wsx_v)
                e.v.inc(v.memset(hmax[:, :], 0.0))
            hm0 = e.v.n

            h_store = 0
            for gi in range(T // 512):
                for cg in range(4):
                    ci = gi * 4 + cg

                    @blk.gpsimd
                    def _(g, ci=ci, hm0=hm0):
                        g.wait_ge(e.v.s, e.v.n)
                        g.wait_ge(e.ld.s, e.ld.n)
                        e.gq.inc(g.dma_start(out=x3th[:, :], in_=x3h_d[ci * 128:(ci + 1) * 128, :]), 16)
                        e.gq.inc(g.dma_start(out=x3tl[:, :], in_=x3l_d[ci * 128:(ci + 1) * 128, :]), 16)
                    xl_ = e.gq.n

                    @blk.vector
                    def _(v, ci=ci, xl_=xl_):
                        v.wait_ge(e.gq.s, xl_)
                        e.v.inc(v.tensor_tensor(x3f4[:, :], x3th[:, :], x3tl[:, :], ALU.add))
                        v.wait_ge(e.v.s, e.v.n)
                        e.v.inc(v.tensor_scalar(scr4[:, :], x3f4[:, :], rsi4[:, ci:ci + 1],
                                                sb4[:, ci:ci + 1], ALU.mult, ALU.add))
                        v.wait_ge(e.v.s, e.v.n)
                        v.tensor_copy(qk[:, 0:D], scr4[:, :])
                        e.v.inc(v.tensor_copy(x3f4[:, :], qk[:, 0:D]))
                        v.wait_ge(e.v.s, e.v.n)
                        v.tensor_tensor(scr4[:, :], scr4[:, :], x3f4[:, :], ALU.subtract)
                        e.v.inc(v.tensor_scalar(scr4[:, :], scr4[:, :], 0.0, None, ALU.is_lt))
                        v.wait_ge(e.v.s, e.v.n)
                        v.tensor_tensor(qb[:, 0:D], x3f4[:, :], scr4[:, :], ALU.subtract)
                        v.tensor_scalar(qb[:, 0:D], qb[:, 0:D], -128.0, None, ALU.max)
                        e.v.inc(v.tensor_scalar(qb[:, 0:D], qb[:, 0:D], 127.0, None, ALU.min))
                    kx_v = e.v.n

                    @blk.sync
                    def _(sy, cg=cg, kx_v=kx_v):
                        sy.wait_ge(e.v.s, kx_v)
                        sy.wait_ge(e.t.s, e.t.n)
                        for dc in range(DC):
                            e.ld.inc(sy.dma_start(out=kxf[:, dc * 512 + cg * 128:dc * 512 + (cg + 1) * 128],
                                                  in_=qb[:, dc * 128:(dc + 1) * 128], transpose=True), 16)
                kx_t = e.ld.n

                if dev_dump == "p4" and gi == 0:
                    @blk.gpsimd
                    def _(g, kx_t=kx_t):
                        g.wait_ge(e.ld.s, kx_t)
                        e.fin.inc(g.dma_start(out=dbg[0:128, 0:4096], in_=kxf[:, :]), 16)

                for ic in range(IC):
                    @blk.tensor
                    def _(t, ic=ic, kx_t=kx_t, wq=wq_done_ld):
                        t.wait_ge(e.ld.s, max(kx_t, wq))
                        t.wait_ge(e.s.s, e.s.n)
                        t.wait_ge(e.v.s, e.v.n)
                        for dc in range(DC):
                            ins = t.matmul(e.psA[:, 0:512], kwgT[:, dc * DFF + ic * 128:dc * DFF + (ic + 1) * 128],
                                           kxf[:, dc * 512:(dc + 1) * 512], start=(dc == 0), stop=(dc == DC - 1))
                        e.t.inc(ins)
                        for dc in range(DC):
                            ins = t.matmul(e.psB[:, 0:512], kwuT[:, dc * DFF + ic * 128:dc * DFF + (ic + 1) * 128],
                                           kxf[:, dc * 512:(dc + 1) * 512], start=(dc == 0), stop=(dc == DC - 1))
                        e.t.inc(ins)
                    gu_t = e.t.n

                    @blk.scalar
                    def _(s, ic=ic, gu_t=gu_t):
                        s.wait_ge(e.t.s, gu_t)
                        e.s.inc(s.activation(sgt[:, :], e.psA[:, 0:512], AF.Silu, scale=wsgx[:, ic:ic + 1]))
                    gu_s = e.s.n

                    @blk.vector
                    def _(v, ic=ic, gu_s=gu_s, gu_t=gu_t):
                        v.wait_ge(e.t.s, gu_t)
                        v.wait_ge(e.gq.s, e.gq.n)
                        e.v.inc(v.tensor_scalar(sut[:, :], e.psB[:, 0:512], wsux[:, ic:ic + 1], None, ALU.mult))
                        v.wait_ge(e.s.s, gu_s)
                        e.v.inc(v.tensor_tensor(htile[:, :], sgt[:, :], sut[:, :], ALU.mult))
                        v.wait_ge(e.v.s, e.v.n)
                        e.v.inc(v.tensor_reduce(hab[:, ic:ic + 1], htile[:, :], AX.X, ALU.max,
                                                apply_absolute_value=True))
                    h_v = e.v.n

                    @blk.gpsimd
                    def _(g, ic=ic, gi=gi, h_v=h_v):
                        g.wait_ge(e.v.s, h_v)
                        e.gq.inc(g.dma_start(out=h_sp[ic * 128:(ic + 1) * 128, gi * 512:(gi + 1) * 512],
                                             in_=htile[:, :]), 16)
                        if dev_dump == "p4" and gi == 0 and ic == 0:
                            e.fin.inc(g.dma_start(out=dbg[128:256, 0:512], in_=sgt[:, :]), 16)
                            e.fin.inc(g.dma_start(out=dbg[128:256, 512:1024], in_=sut[:, :]), 16)
                            e.fin.inc(g.dma_start(out=dbg[128:256, 1024:1536], in_=htile[:, :]), 16)
                h_store = e.gq.n

                @blk.vector
                def _(v, h_store=h_store):
                    v.wait_ge(e.v.s, e.v.n)
                    e.v.inc(v.tensor_reduce(qu[:, 0:1], hab[:, :], AX.X, ALU.max))
                    v.wait_ge(e.v.s, e.v.n)
                    e.v.inc(v.tensor_tensor(hmax[:, :], hmax[:, :], qu[:, 0:1], ALU.max))
            hmax_v = e.v.n

            @blk.gpsimd
            def _(g):
                g.wait_ge(e.v.s, hmax_v)
                e.gq.inc(g.dma_start(out=colbuf[0:1, 128:256], in_=hmax[:, 0:1]), 16)
            c2g = e.gq.n

            @blk.sync
            def _(sy):
                sy.wait_ge(e.gq.s, c2g)
                e.ld.inc(sy.dma_start(out=qstg[0:1, 0:128], in_=colbuf[0:1, 128:256]), 16)
            c2l = e.ld.n

            @blk.vector
            def _(v):
                v.wait_ge(e.ld.s, c2l)
                e.v.inc(v.tensor_reduce(qstg[0:1, 128:129], qstg[0:1, 0:128], AX.X, ALU.max))
            c2v = e.v.n

            @blk.gpsimd
            def _(g):
                g.wait_ge(e.v.s, c2v)
                e.gq.inc(g.dma_start(out=cc_in2.ap(), in_=qstg[0:1, 128:129]), 16)
                g.wait_ge(e.gq.s, e.gq.n)
                e.cc.inc(g.collective_compute("AllReduce", ALU.max,
                                              replica_groups=[list(range(B))],
                                              ins=[cc_in2.ap().opt()],
                                              outs=[cc_out2.ap().opt()]))
                g.wait_ge(e.cc.s, e.cc.n)
                e.gq.inc(g.dma_start(out=qstg[0:1, 130:131], in_=cc_out2.ap()), 16)
            cc2_g = e.gq.n

            @blk.vector
            def _(v):
                v.wait_ge(e.gq.s, cc2_g)
                e.v.inc(v.tensor_tensor(qstg[0:1, 131:132], qstg[0:1, 130:131], p4row[0:1, 5:6], ALU.mult))
                v.wait_ge(e.v.s, e.v.n)
                e.v.inc(v.tensor_scalar(qstg[0:1, 131:132], qstg[0:1, 131:132], 1.0 / 127.0, None, ALU.mult))
                v.wait_ge(e.v.s, e.v.n)
                e.v.inc(v.reciprocal(qstg[0:1, 132:133], qstg[0:1, 131:132]))
                v.wait_ge(e.v.s, e.v.n)
            sc3_v = e.v.n

            @blk.gpsimd
            def _(g):
                g.wait_ge(e.v.s, sc3_v)
                e.gq.inc(g.dma_start(out=scal_d[0:1, 4:8], in_=qstg[0:1, 130:134]), 16)
            sc3_g = e.gq.n

            @blk.sync
            def _(sy):
                sy.wait_ge(e.gq.s, sc3_g)
                e.ld.inc(sy.dma_start(out=mcol2[:, :], in_=scal_d[0:1, 6:7].broadcast_to([128, 1])), 16)
                e.ld.inc(sy.dma_start(out=mcol1[:, :], in_=scal_d[0:1, 5:6].broadcast_to([128, 1])), 16)
            sc3_b = e.ld.n

            @blk.gpsimd
            def _(g, wq=wq_done_v):
                g.wait_ge(e.v.s, wq)
                for hc in range(DC):
                    e.gq.inc(g.dma_start(out=rowb[4:5, hc * 128:(hc + 1) * 128],
                                         in_=wsd[:, hc:hc + 1]), 16)
            wsd_g = e.gq.n

            @blk.sync
            def _(sy):
                sy.wait_ge(e.gq.s, wsd_g)
                e.ld.inc(sy.dma_start(out=wsdq[:, :], in_=rowb[4:5, 0:D].broadcast_to([128, D])), 16)
            wsd_b = e.ld.n

            @blk.vector
            def _(v):
                v.wait_ge(e.ld.s, wsd_b)
                v.wait_ge(e.ld.s, sc3_b)
                v.tensor_scalar(wsdq[:, :], wsdq[:, :], mcol1[:, :], None, ALU.mult)
                e.v.inc(v.tensor_scalar(wsdq[:, :], wsdq[:, :], 1.0 / 127.0, None, ALU.mult))
            wsdq_v = e.v.n

            out_st = 0
            for ci in range(NCH):
                @blk.gpsimd
                def _(g, ci=ci, wsdq_v=wsdq_v):
                    if ci == 0:
                        g.wait_ge(e.v.s, wsdq_v)
                    g.wait_ge(e.v.s, e.v.n)
                    g.wait_ge(e.t.s, e.t.n)
                    src3 = h_sp[:, ci * 128:(ci + 1) * 128].rearrange("(c p) w -> p c w", p=128)
                    dst3 = qstg[:, 0:IC * 128].rearrange("p (c w) -> p c w", w=128)
                    e.gq.inc(g.dma_start(out=dst3, in_=src3), 16)
                    e.gq.inc(g.dma_start(out=x3th[:, :], in_=x3h_d[ci * 128:(ci + 1) * 128, :]), 16)
                    e.gq.inc(g.dma_start(out=x3tl[:, :], in_=x3l_d[ci * 128:(ci + 1) * 128, :]), 16)
                hr_g = e.gq.n

                @blk.vector
                def _(v, hr_g=hr_g):
                    v.wait_ge(e.gq.s, hr_g)
                    e.v.inc(v.tensor_scalar(qu[:, 0:IC * 128], qstg[:, 0:IC * 128], mcol2[:, :], 0.5, ALU.mult, ALU.add))
                    v.wait_ge(e.v.s, e.v.n)
                    v.tensor_copy(qk[:, 0:IC * 128], qu[:, 0:IC * 128])
                    e.v.inc(v.tensor_copy(qstg[:, 0:IC * 128], qk[:, 0:IC * 128]))
                    v.wait_ge(e.v.s, e.v.n)
                    v.tensor_tensor(qu[:, 0:IC * 128], qu[:, 0:IC * 128], qstg[:, 0:IC * 128], ALU.subtract)
                    e.v.inc(v.tensor_scalar(qu[:, 0:IC * 128], qu[:, 0:IC * 128], 0.0, None, ALU.is_lt))
                    v.wait_ge(e.v.s, e.v.n)
                    v.tensor_tensor(qb[:, 0:IC * 128], qstg[:, 0:IC * 128], qu[:, 0:IC * 128], ALU.subtract)
                    v.tensor_scalar(qb[:, 0:IC * 128], qb[:, 0:IC * 128], -128.0, None, ALU.max)
                    e.v.inc(v.tensor_scalar(qb[:, 0:IC * 128], qb[:, 0:IC * 128], 127.0, None, ALU.min))
                kh_v = e.v.n

                @blk.tensor
                def _(t, kh_v=kh_v):
                    t.wait_ge(e.v.s, kh_v)
                    for nh in range(2):
                        for ic in range(IC):
                            ins = t.matmul(e.psY[:, nh * 512:(nh + 1) * 512],
                                           qb[:, ic * 128:(ic + 1) * 128],
                                           kwdT[:, ic * D + nh * 512:ic * D + nh * 512 + 512],
                                           start=(ic == 0), stop=(ic == IC - 1))
                        e.t.inc(ins)
                dn_t = e.t.n

                @blk.vector
                def _(v, dn_t=dn_t, ci=ci):
                    v.wait_ge(e.t.s, dn_t)
                    v.wait_ge(e.fin.s, e.fin.n)
                    v.tensor_tensor(scr4[:, :], e.psY[:, 0:1024], wsdq[:, :], ALU.mult)
                    v.tensor_tensor(x3f4[:, :], x3th[:, :], x3tl[:, :], ALU.add)
                    e.v.inc(v.tensor_tensor(scr4[:, :], scr4[:, :], x3f4[:, :], ALU.add))
                o_v = e.v.n

                @blk.gpsimd
                def _(g, ci=ci, o_v=o_v):
                    g.wait_ge(e.v.s, o_v)
                    e.fin.inc(g.dma_start(out=out[ci * 128:(ci + 1) * 128, :], in_=scr4[:, :]), 16)

            if dev_dump == "p4":
                @blk.gpsimd
                def _(g):
                    g.wait_ge(e.fin.s, e.fin.n)
                    g.wait_ge(e.gq.s, e.gq.n)
                    g.wait_ge(e.v.s, e.v.n)
                    e.fin.inc(g.dma_start(out=dbg[256:384, 0:32], in_=m4[:, :]), 16)
                    e.fin.inc(g.dma_start(out=dbg[256:384, 32:64], in_=rs4[:, :]), 16)
                    e.fin.inc(g.dma_start(out=dbg[256:384, 64:86], in_=wsg[:, :]), 16)
                    e.fin.inc(g.dma_start(out=dbg[256:384, 86:108], in_=wsu[:, :]), 16)
                    e.fin.inc(g.dma_start(out=dbg[256:384, 108:116], in_=wsd[:, :]), 16)
                    e.fin.inc(g.dma_start(out=dbg[256:384, 116:138], in_=wsgx[:, :]), 16)
                    e.fin.inc(g.dma_start(out=dbg[256:384, 138:160], in_=wsux[:, :]), 16)
                    e.fin.inc(g.dma_start(out=dbg[256:257, 160:168], in_=scal_d[0:1, 0:8]), 16)
                    e.fin.inc(g.dma_start(out=dbg[256:384, 168:169], in_=hmax[:, :], allow_slow_non_contiguous=True), 16)
                    e.fin.inc(g.dma_start(out=dbg[384:512, 0:2816], in_=kwgT[:, 0:2816]), 16)
                    e.fin.inc(g.dma_start(out=dbg[384:512, 2816:2816 + 1024], in_=wsdq[:, :]), 16)
                    g.wait_ge(e.fin.s, e.fin.n)

            @blk.gpsimd
            def _(g):
                g.wait_ge(e.fin.s, e.fin.n)
            p4.close()

    return nc



def _constants():
    q = np.arange(128, dtype=np.float64)
    diff = q[None, :] - q[:, None] - 1.0
    c1t = np.where(q[None, :] > q[:, None], diff, BIGNEG).astype(np.float32)
    c2t = np.where(q[None, :] > q[:, None], np.maximum(diff, 0.0), BIGNEG).astype(np.float32)
    return {
        "c1t": np.ascontiguousarray(c1t),
        "c2t": np.ascontiguousarray(c2t),
        "eye": np.eye(128, dtype=np.float32),
        "iota_i": np.arange(128, dtype=np.float32).reshape(128, 1),
        "iota_r": (127.0 - np.arange(128, dtype=np.float32)).reshape(128, 1),
        "ones_in": np.ones((128, 1), dtype=np.float32),
    }


def make_in_maps(ii):
    consts = _constants()
    maa_all = np.stack([ii["L0_attn_time_maa_k"].reshape(-1),
                        ii["L0_attn_time_maa_v"].reshape(-1),
                        ii["L0_attn_time_maa_r"].reshape(-1),
                        ii["L0_attn_time_maa_g"].reshape(-1)], axis=0)
    maa_dev = np.ascontiguousarray(
        maa_all.reshape(4, DC, 128).transpose(2, 0, 1).reshape(128, 4 * DC)).astype(np.float32)
    common = {
        "WrT": np.ascontiguousarray(ii["L0_attn_Wr"].T),
        "WkT": np.ascontiguousarray(ii["L0_attn_Wk"].T),
        "WvT": np.ascontiguousarray(ii["L0_attn_Wv"].T),
        "WgT": np.ascontiguousarray(ii["L0_attn_Wg"].T),
        "WoT": np.ascontiguousarray(ii["L0_attn_Wo"].T),
        "maa": maa_dev,
        "decay_row": np.ascontiguousarray(ii["L0_attn_time_decay"].reshape(1, H)),
        "faaaa_row": np.ascontiguousarray(ii["L0_attn_time_faaaa"].reshape(1, H)),
        "lnw_row": np.ascontiguousarray(ii["L0_attn_ln_w"].reshape(1, D)),
        "lnb_row": np.ascontiguousarray(ii["L0_attn_ln_b"].reshape(1, D)),
        "theta": np.ascontiguousarray(ii["L0_ffn_theta"]),
        "theta_n": np.ascontiguousarray(ii["L0_ffn_theta_noise"]),
        "phi": np.ascontiguousarray(ii["L0_ffn_phi"]),
        "phi_n": np.ascontiguousarray(ii["L0_ffn_phi_noise"]),
        "WwT": np.ascontiguousarray(ii["L1_attn_W_write"].T),
        "WrdT": np.ascontiguousarray(ii["L1_attn_W_read"].T),
        "l1_scal": np.array([[float(ii["L1_attn_decay"]), float(ii["L1_attn_log_alpha"]),
                              float(np.asarray(ii["L1_ffn_clip_x"]).reshape(-1)[0]),
                              float(np.asarray(ii["L1_ffn_clip_down"]).reshape(-1)[0])]],
                            dtype=np.float32),
        "w_gate": np.ascontiguousarray(ii["L1_ffn_w_gate"]),
        "w_up": np.ascontiguousarray(ii["L1_ffn_w_up"]),
        "w_down": np.ascontiguousarray(ii["L1_ffn_w_down"]),
        **consts,
    }
    x = ii["L0_attn_x"]
    in_maps = []
    for b in range(B):
        m = dict(common)
        m["xin"] = np.ascontiguousarray(x[b].T)
        in_maps.append(m)
    return in_maps


def kernel(**inputs):
    ii = {k: np.asarray(v) for k, v in inputs.items()}
    nc = build(N_PHASES)
    in_maps = make_in_maps(ii)
    res = run_bass_kernel_spmd(nc, in_maps, core_ids=list(range(B)), trace_cores=[0])
    return np.stack([np.asarray(res.results[b]["out"]) for b in range(B)], axis=0)


# version 09
# speedup vs baseline: 1.0164x; 1.0054x over previous
import numpy as np
import ml_dtypes
from contextlib import ExitStack
import concourse.bass as bass
import concourse.mybir as mybir
from concourse.bass_utils import run_bass_kernel_spmd

F32 = mybir.dt.float32
BF16 = mybir.dt.bfloat16
I32 = mybir.dt.int32
AF = mybir.ActivationFunctionType
ALU = mybir.AluOpType
AX = mybir.AxisListType

B, T, D, H, HN, DFF = 8, 4096, 1024, 16, 64, 2816
DC = D // 128
Q = 128
NCH = T // Q
TT = 256
NT = T // TT
CPT = TT // Q
IC = DFF // 128
EPS_LN = 1e-5
EPS_GN = 1e-5 * 64
BIGNEG = 1.0e9
HALFPI = 1.5707963267948966

N_PHASES = 4


class Sem:
    def __init__(self, nc, ctx, name):
        self.s = ctx.enter_context(nc.semaphore(name))
        self.n = 0

    def inc(self, instr, k=1):
        instr.then_inc(self.s, k)
        self.n += k
        return self.n


class E:
    pass


def build(n_phases=N_PHASES, dev_dump=None):
    nc = bass.Bass("TRN2", target_bir_lowering=False)
    e = E()
    e.nc = nc

    xin = nc.dram_tensor("xin", [D, T], F32, kind="ExternalInput").ap()
    WT = {}
    for nm in ["WrT", "WkT", "WvT", "WgT", "WoT"]:
        WT[nm] = nc.dram_tensor(nm, [D, D], F32, kind="ExternalInput").ap()
    maa_in = nc.dram_tensor("maa", [128, DC * 4], F32, kind="ExternalInput").ap()
    decay_row = nc.dram_tensor("decay_row", [1, H], F32, kind="ExternalInput").ap()
    faaaa_row = nc.dram_tensor("faaaa_row", [1, H], F32, kind="ExternalInput").ap()
    lnw_row = nc.dram_tensor("lnw_row", [1, D], F32, kind="ExternalInput").ap()
    lnb_row = nc.dram_tensor("lnb_row", [1, D], F32, kind="ExternalInput").ap()
    c1t_in = nc.dram_tensor("c1t", [128, 128], F32, kind="ExternalInput").ap()
    c2t_in = nc.dram_tensor("c2t", [128, 128], F32, kind="ExternalInput").ap()
    eye_in = nc.dram_tensor("eye", [128, 128], F32, kind="ExternalInput").ap()
    iota_i_in = nc.dram_tensor("iota_i", [128, 1], F32, kind="ExternalInput").ap()
    iota_r_in = nc.dram_tensor("iota_r", [128, 1], F32, kind="ExternalInput").ap()
    ones_in = nc.dram_tensor("ones_in", [128, 1], F32, kind="ExternalInput").ap()
    theta_in = nc.dram_tensor("theta", [D, D], F32, kind="ExternalInput").ap()
    theta_n_in = nc.dram_tensor("theta_n", [D, D], F32, kind="ExternalInput").ap()
    phi_in = nc.dram_tensor("phi", [D, D], F32, kind="ExternalInput").ap()
    phi_n_in = nc.dram_tensor("phi_n", [D, D], F32, kind="ExternalInput").ap()
    WwT_in = nc.dram_tensor("WwT", [D, D], F32, kind="ExternalInput").ap()
    WrdT_in = nc.dram_tensor("WrdT", [D, D], F32, kind="ExternalInput").ap()
    l1s_in = nc.dram_tensor("l1_scal", [1, 4], F32, kind="ExternalInput").ap()
    wg_in = nc.dram_tensor("w_gate", [DFF, D], F32, kind="ExternalInput").ap()
    wu_in = nc.dram_tensor("w_up", [DFF, D], F32, kind="ExternalInput").ap()
    wd_in = nc.dram_tensor("w_down", [D, DFF], F32, kind="ExternalInput").ap()

    out = nc.dram_tensor("out", [T, D], F32, kind="ExternalOutput").ap()
    if dev_dump:
        dbg = nc.dram_tensor("dbg", [D, T], F32, kind="ExternalOutput").ap()

    xs1 = nc.dram_tensor("xs1", [D, T], F32).ap()
    xs2 = nc.dram_tensor("xs2", [D, T], F32).ap()
    x3h_d = nc.dram_tensor("x3h_d", [T, D], BF16).ap()
    x3l_d = nc.dram_tensor("x3l_d", [T, D], BF16).ap()
    h_sp = nc.dram_tensor("h_spill", [DFF, T], F32).ap()
    rowb = nc.dram_tensor("rowbuf", [16, T], F32).ap()
    wsplit = {}
    for nm in ["k", "v", "g"]:
        wsplit[nm] = nc.dram_tensor("ws_" + nm, [2, 2, 128, DC * 512], BF16).ap()
    for nm in ["r", "o"]:
        wsplit[nm] = nc.dram_tensor("ws_" + nm, [2, DC, 128, DC * 128], BF16).ap()
    cc_in1 = nc.dram_tensor("cc_in1", [1, 1], F32)
    cc_out1 = nc.dram_tensor("cc_out1", [1, 1], F32)
    cc_in2 = nc.dram_tensor("cc_in2", [1, 1], F32)
    cc_out2 = nc.dram_tensor("cc_out2", [1, 1], F32)
    scal_d = nc.dram_tensor("scal_d", [1, 16], F32).ap()

    stk = ExitStack()
    with stk:
        def sb(nm, shape, dt=F32, st=stk):
            return st.enter_context(nc.sbuf_tensor(nm, shape, dt))

        e.ones = sb("ones_c", [128, 1])
        e.ioi = sb("ioi", [128, 1])
        e.ior = sb("ior", [128, 1])
        e.eye = sb("eye_t", [128, 128])
        e.maa = sb("maa_t", [128, DC * 4])
        e.Ecol = sb("Ecol", [128, H])
        e.Ucol = sb("Ucol", [128, H])
        e.wb = sb("wb_t", [128, H])
        e.wkd = sb("wkd_t", [128, H])
        e.wsd = sb("wsd_t", [128, H])
        e.scal = sb("scal", [128, 16])
        e.epsln = sb("epsln", [128, 1])
        e.epsgn = sb("epsgn", [128, 1])
        e.m4 = sb("m4_g", [128, NCH])
        e.var4 = sb("var4_g", [128, NCH])
        e.cmx4 = sb("cmx4_g", [128, NCH])
        e.row1 = sb("row1", [1, 512])
        e.ones_row = sb("ones_row", [1, 128])
        e.psA = stk.enter_context(nc.psum_tensor("psA", [128, 512], F32))
        e.psB = stk.enter_context(nc.psum_tensor("psB", [128, 512], F32))
        e.psY = stk.enter_context(nc.psum_tensor("psY", [128, 1024], F32))
        e.psY2 = stk.enter_context(nc.psum_tensor("psY2", [128, 1024], F32))
        e.psC = stk.enter_context(nc.psum_tensor("psC", [128, 512], F32))
        e.psD = stk.enter_context(nc.psum_tensor("psD", [128, 512], F32))

        e.wmatT = None
        e.lnwb = None
        e.lnbb = None
        e.ld = Sem(nc, stk, "ld")
        e.sc = Sem(nc, stk, "scq")
        e.gq = Sem(nc, stk, "gq")
        e.t = Sem(nc, stk, "tse")
        e.v = Sem(nc, stk, "vse")
        e.s = Sem(nc, stk, "sse")
        e.cc = Sem(nc, stk, "ccse")
        e.fin = Sem(nc, stk, "fin")

        blk = stk.enter_context(nc.Block())
        e.blk = blk

        def barrier_vals():
            return dict(ld=e.ld.n, sc=e.sc.n, gq=e.gq.n, t=e.t.n, v=e.v.n, s=e.s.n)

        def wait_barrier(engine, bv):
            engine.wait_ge(e.ld.s, bv["ld"])
            engine.wait_ge(e.sc.s, bv["sc"])
            engine.wait_ge(e.gq.s, bv["gq"])
            engine.wait_ge(e.t.s, bv["t"])
            engine.wait_ge(e.v.s, bv["v"])
            engine.wait_ge(e.s.s, bv["s"])

        psh = ExitStack()
        e.wmatT = sb("wmatT", [128, H * 128], st=psh)
        e.lnwb = sb("lnwb", [128, D], st=psh)
        e.lnbb = sb("lnbb", [128, D], st=psh)
        xt = sb("xt", [128, DC * TT], st=psh)
        ln = sb("ln", [128, DC * (TT + 1)], st=psh)
        sq = sb("sq", [128, DC * TT], st=psh)
        mxh = sb("mxh", [128, DC * TT], BF16, st=psh)
        mxl = sb("mxl", [128, DC * TT], BF16, st=psh)
        wstg = sb("wstg", [128, D], st=psh)
        wsth = sb("wsth", [128, D], BF16, st=psh)
        wstl = sb("wstl", [128, D], BF16, st=psh)
        rw_m = sb("rw_m", [1, TT], st=psh)
        rw_v = sb("rw_v", [1, TT], st=psh)
        rw_t = sb("rw_t", [1, TT], st=psh)
        rw_rs = sb("rw_rs", [1, TT], st=psh)
        rw_mrs = sb("rw_mrs", [1, TT], st=psh)

        def emit_ln(xtile, dst_ln, wait_sem, wait_val, roff):
            @blk.scalar
            def _(s):
                s.wait_ge(wait_sem.s, wait_val)
                for c in range(DC):
                    e.s.inc(s.activation(sq[:, c * TT:(c + 1) * TT], xtile[:, c * TT:(c + 1) * TT], AF.Square))
            sqd = e.s.n

            @blk.tensor
            def _(t):
                t.wait_ge(e.s.s, sqd)
                t.wait_ge(e.v.s, e.v.n)
                for c in range(DC):
                    ins = t.matmul(e.psC[0:1, 0:TT], e.ones[:, :], xtile[:, c * TT:(c + 1) * TT],
                                   start=(c == 0), stop=(c == DC - 1))
                e.t.inc(ins)
                for c in range(DC):
                    ins = t.matmul(e.psD[0:1, 0:TT], e.ones[:, :], sq[:, c * TT:(c + 1) * TT],
                                   start=(c == 0), stop=(c == DC - 1))
                e.t.inc(ins)
            mmd = e.t.n

            @blk.vector
            def _(v):
                v.wait_ge(e.t.s, mmd)
                v.tensor_scalar(rw_m[0:1, 0:TT], e.psC[0:1, 0:TT], 1.0 / D, None, ALU.mult)
                e.v.inc(v.tensor_scalar(rw_v[0:1, 0:TT], e.psD[0:1, 0:TT], 1.0 / D, None, ALU.mult))
                v.wait_ge(e.v.s, e.v.n)
                v.tensor_tensor(rw_t[0:1, 0:TT], rw_m[0:1, 0:TT], rw_m[0:1, 0:TT], ALU.mult)
                e.v.inc(v.tensor_tensor(rw_v[0:1, 0:TT], rw_v[0:1, 0:TT], rw_t[0:1, 0:TT], ALU.subtract))
            vd = e.v.n

            @blk.scalar
            def _(s):
                s.wait_ge(e.v.s, vd)
                e.s.inc(s.activation(rw_t[0:1, 0:TT], rw_v[0:1, 0:TT], AF.Sqrt, bias=e.epsln[0:1, :]))
            sd = e.s.n

            @blk.vector
            def _(v):
                v.wait_ge(e.s.s, sd)
                e.v.inc(v.reciprocal(rw_rs[0:1, 0:TT], rw_t[0:1, 0:TT]))
                v.wait_ge(e.v.s, e.v.n)
                e.v.inc(v.tensor_tensor(rw_mrs[0:1, 0:TT], rw_m[0:1, 0:TT], rw_rs[0:1, 0:TT], ALU.mult))
            vd2 = e.v.n

            @blk.tensor
            def _(t):
                t.wait_ge(e.v.s, vd2)
                e.t.inc(t.matmul(e.psC[:, 0:TT], e.ones_row[0:1, :], rw_rs[0:1, 0:TT],
                                 start=True, stop=True))
                e.t.inc(t.matmul(e.psD[:, 0:TT], e.ones_row[0:1, :], rw_mrs[0:1, 0:TT],
                                 start=True, stop=True))
            bd = e.t.n

            @blk.vector
            def _(v):
                v.wait_ge(e.t.s, bd)
                for c in range(DC):
                    v.tensor_tensor(dst_ln[:, c * (TT + 1) + 1:(c + 1) * (TT + 1)],
                                    xtile[:, c * TT:(c + 1) * TT], e.psC[:, 0:TT], ALU.mult)
                    e.v.inc(v.tensor_tensor(dst_ln[:, c * (TT + 1) + 1:(c + 1) * (TT + 1)],
                                            dst_ln[:, c * (TT + 1) + 1:(c + 1) * (TT + 1)],
                                            e.psD[:, 0:TT], ALU.subtract))
            return e.v.n



        @blk.sync
        def _(sy):
            e.ld.inc(sy.dma_start(out=e.ones[:, :], in_=ones_in), 16)
            e.ld.inc(sy.dma_start(out=e.ones_row[0:1, :], in_=ones_in), 16)
            e.ld.inc(sy.dma_start(out=e.ioi[:, :], in_=iota_i_in), 16)
            e.ld.inc(sy.dma_start(out=e.ior[:, :], in_=iota_r_in), 16)
            e.ld.inc(sy.dma_start(out=e.eye[:, :], in_=eye_in), 16)
            e.ld.inc(sy.dma_start(out=e.maa[:, :], in_=maa_in), 16)
            e.ld.inc(sy.dma_start(out=e.lnwb[:, :], in_=lnw_row.broadcast_to([128, D])), 16)
            e.ld.inc(sy.dma_start(out=e.lnbb[:, :], in_=lnb_row.broadcast_to([128, D])), 16)
            e.ld.inc(sy.dma_start(out=e.Ecol[:, 0:H], in_=decay_row.broadcast_to([128, H])), 16)
            e.ld.inc(sy.dma_start(out=e.Ucol[:, 0:H], in_=faaaa_row.broadcast_to([128, H])), 16)
            for hh in range(H):
                e.ld.inc(sy.dma_start(out=e.wmatT[:, hh * 128:(hh + 1) * 128], in_=c1t_in), 16)
        n0 = e.ld.n

        @blk.scalar
        def _(s):
            s.wait_ge(e.ld.s, n0)
            e.s.inc(s.activation(e.Ecol[:, 0:H], e.Ecol[:, 0:H], AF.Exp))
        n1 = e.s.n

        @blk.vector
        def _(v):
            v.wait_ge(e.s.s, n1)
            v.memset(e.epsln[:, :], EPS_LN)
            v.memset(e.epsgn[:, :], EPS_GN)
            for hh in range(H):
                v.tensor_scalar(e.wb[:, hh:hh + 1], e.ioi[:, :], e.Ecol[:, hh:hh + 1], None, ALU.mult)
                v.tensor_scalar(e.wkd[:, hh:hh + 1], e.ior[:, :], e.Ecol[:, hh:hh + 1], None, ALU.mult)
            e.v.inc(v.tensor_scalar(e.wsd[:, 0:H], e.Ecol[:, 0:H], float(Q), None, ALU.mult))
        n2 = e.v.n

        @blk.scalar
        def _(s):
            s.wait_ge(e.v.s, n2)
            s.activation(e.wb[:, 0:H], e.wb[:, 0:H], AF.Exp, scale=-1.0)
            s.activation(e.wkd[:, 0:H], e.wkd[:, 0:H], AF.Exp, scale=-1.0)
            e.s.inc(s.activation(e.wsd[:, 0:H], e.wsd[:, 0:H], AF.Exp, scale=-1.0))
        n3 = e.s.n

        @blk.vector
        def _(v):
            v.wait_ge(e.s.s, n3)
            for hh in range(H):
                e.v.inc(v.tensor_scalar(e.wmatT[:, hh * 128:(hh + 1) * 128],
                                        e.wmatT[:, hh * 128:(hh + 1) * 128],
                                        e.Ecol[:, hh:hh + 1], None, ALU.mult))
        n4 = e.v.n

        @blk.scalar
        def _(s):
            s.wait_ge(e.v.s, n4)
            for hh in range(H):
                e.s.inc(s.activation(e.wmatT[:, hh * 128:(hh + 1) * 128],
                                     e.wmatT[:, hh * 128:(hh + 1) * 128], AF.Exp, scale=-1.0))
        n5 = e.s.n

        @blk.vector
        def _(v):
            v.wait_ge(e.s.s, n5)
            for hh in range(H):
                e.v.inc(v.scalar_tensor_tensor(e.wmatT[:, hh * 128:(hh + 1) * 128],
                                               e.eye[:, :], e.Ucol[:, hh:hh + 1],
                                               e.wmatT[:, hh * 128:(hh + 1) * 128],
                                               ALU.mult, ALU.add))
        prep_v = e.v.n

        if n_phases >= 1:
            p1 = ExitStack()
            mxh2 = sb("mxh2", [128, DC * TT], BF16, st=p1)
            mxl2 = sb("mxl2", [128, DC * TT], BF16, st=p1)
            wkvg = sb("wkvg", [128, 2 * DC * 512], BF16, st=p1)
            wkvg2 = sb("wkvg2", [128, 2 * DC * 512], BF16, st=p1)
            wro = sb("wro", [128, 2 * DC * 128], BF16, st=p1)
            wro2 = sb("wro2", [128, 2 * DC * 128], BF16, st=p1)
            r_fm = sb("r_fm", [128, DC * TT], st=p1)
            k_tok = sb("k_tok", [128, CPT * D], st=p1)
            v_tok = sb("v_tok", [128, CPT * D], st=p1)
            g_tok = sb("g_tok", [128, CPT * D], st=p1)
            k_fm = sb("k_fm", [128, DC * 128], st=p1)
            kw_tok = sb("kw_tok", [128, D], st=p1)
            attw = sb("attw", [128, 128], st=p1)
            WKB = sb("WKB", [128, D], st=p1)
            WBB = sb("WBB", [128, D], st=p1)
            WSB = sb("WSB", [128, (H // 2) * HN], st=p1)
            attw2 = sb("attw2", [128, 128], st=p1)
            attw3 = sb("attw3", [128, 128], st=p1)
            state = sb("state", [128, (H // 2) * HN], st=p1)
            ysb = sb("ysb", [128, D], st=p1)
            st6 = sb("st6", [128, 6 * H], st=p1)
            gn_m = sb("gn_m", [128, H], st=p1)
            gn_v = sb("gn_v", [128, H], st=p1)
            gn_t = sb("gn_t", [128, H], st=p1)
            gn_rs = sb("gn_rs", [128, H], st=p1)
            zf = sb("zf", [128, D], st=p1)
            zh = sb("zh", [128, D], BF16, st=p1)
            zl = sb("zl", [128, D], BF16, st=p1)
            zfmh = sb("zfmh", [128, DC * TT], BF16, st=p1)
            zfml = sb("zfml", [128, DC * TT], BF16, st=p1)

            wstg2 = sb("wstg2", [128, D], st=p1)
            wsth2 = sb("wsth2", [128, D], BF16, st=p1)
            wstl2 = sb("wstl2", [128, D], BF16, st=p1)
            wsp_stg = [(wstg, wsth, wstl), (wstg2, wsth2, wstl2)]
            wsp_busy = [prep_v, prep_v]
            wsp_st = [0, 0]
            wu_i = 0
            for wi, nm in enumerate(["k", "v", "g", "r", "o"]):
                src = WT["W" + nm + "T"]
                for c in range(DC):
                    par = wu_i % 2
                    wu_i += 1
                    stg_, sth_, stl_ = wsp_stg[par]

                    @blk.sync
                    def _(sy, src=src, c=c, uw=wsp_busy[par], uw2=wsp_st[par], stg_=stg_):
                        sy.wait_ge(e.v.s, uw)
                        sy.wait_ge(e.gq.s, uw2)
                        e.ld.inc(sy.dma_start(out=stg_[:, :], in_=src[c * 128:(c + 1) * 128, :]), 16)
                    w_ld = e.ld.n

                    @blk.scalar
                    def _(s, w_ld=w_ld, sth_=sth_, stg_=stg_, uw2=wsp_st[par]):
                        s.wait_ge(e.ld.s, w_ld)
                        s.wait_ge(e.gq.s, uw2)
                        e.s.inc(s.activation(sth_[:, :], stg_[:, :], AF.Copy))
                    w_s = e.s.n

                    @blk.vector
                    def _(v, w_s=w_s, stl_=stl_, stg_=stg_, sth_=sth_):
                        v.wait_ge(e.s.s, w_s)
                        e.v.inc(v.tensor_tensor(stl_[:, :], stg_[:, :], sth_[:, :], ALU.subtract))
                    w_v = e.v.n
                    wsp_busy[par] = w_v

                    @blk.gpsimd
                    def _(g, nm=nm, c=c, w_v=w_v, wsth=sth_, wstl=stl_):
                        g.wait_ge(e.v.s, w_v)
                        if nm in ("k", "v", "g"):
                            for oh in range(2):
                                e.gq.inc(g.dma_start(out=wsplit[nm][0, oh, :, c * 512:(c + 1) * 512],
                                                     in_=wsth[:, oh * 512:(oh + 1) * 512]), 16)
                                e.gq.inc(g.dma_start(out=wsplit[nm][1, oh, :, c * 512:(c + 1) * 512],
                                                     in_=wstl[:, oh * 512:(oh + 1) * 512]), 16)
                        else:
                            for oc in range(DC):
                                e.gq.inc(g.dma_start(out=wsplit[nm][0, oc, :, c * 128:(c + 1) * 128],
                                                     in_=wsth[:, oc * 128:(oc + 1) * 128]), 16)
                                e.gq.inc(g.dma_start(out=wsplit[nm][1, oc, :, c * 128:(c + 1) * 128],
                                                     in_=wstl[:, oc * 128:(oc + 1) * 128]), 16)
                    wsp_st[par] = e.gq.n
            wsplit_done = e.gq.n

            @blk.vector
            def _(v):
                v.memset(state[:, :], 0.0)
                v.memset(ln[:, :], 0.0)
                v.memset(k_tok[:, :], 0.0)
                v.wait_ge(e.v.s, prep_v)
                e.v.inc(v.memset(zf[:, 0:HN], 1.0))
                v.wait_ge(e.v.s, e.v.n)
                for hh in range(H):
                    hr3, hc3 = (hh % 2) * 64, (hh // 2) * HN
                    v.tensor_scalar(WKB[:, hh * HN:(hh + 1) * HN], zf[:, 0:HN],
                                    e.wkd[:, hh:hh + 1], None, ALU.mult)
                    v.tensor_scalar(WBB[:, hh * HN:(hh + 1) * HN], zf[:, 0:HN],
                                    e.wb[:, hh:hh + 1], None, ALU.mult)
                    e.v.inc(v.tensor_scalar(WSB[hr3:hr3 + 64, hc3:hc3 + HN], zf[hr3:hr3 + 64, 0:HN],
                                            e.wsd[hr3:hr3 + 64, hh:hh + 1], None, ALU.mult))
            p1_init_v = e.v.n

            slot_users = {"wkvg": 0, "wkvg2": 0, "wro": 0, "wro2": 0}
            mix_users = [0, 0]

            def load_kvg(slot, sname, nm, oh):
                @blk.sync
                def _(sy, slot=slot, nm=nm, oh=oh, uw=slot_users[sname]):
                    sy.wait_ge(e.gq.s, wsplit_done)
                    sy.wait_ge(e.t.s, uw)
                    for half in range(2):
                        e.ld.inc(sy.dma_start(out=slot[:, half * DC * 512:(half + 1) * DC * 512],
                                              in_=wsplit[nm][half, oh, :, :]), 16)
                return e.ld.n

            def load_ro_oc(slot, sname, nm, oc):
                @blk.sync
                def _(sy, slot=slot, nm=nm, oc=oc, uw=slot_users[sname]):
                    sy.wait_ge(e.gq.s, wsplit_done)
                    sy.wait_ge(e.t.s, uw)
                    for half in range(2):
                        e.ld.inc(sy.dma_start(out=slot[:, half * DC * 128:(half + 1) * DC * 128],
                                              in_=wsplit[nm][half, oc, :, :]), 16)
                return e.ld.n

            def emit_mix(mcol, ln_ready, mh, ml, uw):
                @blk.vector
                def _(v):
                    v.wait_ge(e.v.s, ln_ready)
                    v.wait_ge(e.t.s, uw)
                    W1 = TT + 1
                    for c in range(DC):
                        lncur = ln[:, c * W1 + 1:(c + 1) * W1]
                        lnsh = ln[:, c * W1:(c + 1) * W1 - 1]
                        xx = sq[:, c * TT:(c + 1) * TT]
                        v.tensor_tensor(xx, lnsh, lncur, ALU.subtract)
                        v.scalar_tensor_tensor(mh[:, c * TT:(c + 1) * TT], xx,
                                               e.maa[:, mcol * DC + c:mcol * DC + c + 1],
                                               lncur, ALU.mult, ALU.add)
                        v.scalar_tensor_tensor(xx, xx,
                                               e.maa[:, mcol * DC + c:mcol * DC + c + 1],
                                               lncur, ALU.mult, ALU.add)
                        e.v.inc(v.tensor_tensor(ml[:, c * TT:(c + 1) * TT], xx,
                                                mh[:, c * TT:(c + 1) * TT], ALU.subtract))
                return e.v.n

            evac_guard = [0, 0]
            x1_store = 0
            for ti in range(1 if dev_dump == 'probe' else NT):
                t0 = ti * TT

                @blk.gpsimd
                def _(g, t0=t0, x1_store=x1_store):
                    g.wait_ge(e.gq.s, max(x1_store, wsplit_done))
                    g.wait_ge(e.v.s, e.v.n)
                    for c in range(DC):
                        e.gq.inc(g.dma_start(out=xt[:, c * TT:(c + 1) * TT],
                                             in_=xin[c * 128:(c + 1) * 128, t0:t0 + TT]), 16)
                x_ld = e.gq.n

                @blk.vector
                def _(v, ti=ti):
                    v.wait_ge(e.gq.s, x_ld)
                    if ti > 0:
                        W1 = TT + 1
                        for c in range(DC):
                            e.v.inc(v.tensor_copy(ln[:, c * W1:c * W1 + 1], ln[:, c * W1 + TT:c * W1 + TT + 1]))
                    else:
                        W1 = TT + 1
                        for c in range(DC):
                            e.v.inc(v.memset(ln[:, c * W1:c * W1 + 1], 0.0))
                guard_v = e.v.n

                ln_done = emit_ln(xt, ln, e.v, guard_v, 0)

                for i, pname in enumerate(["k", "v", "g"]):
                    dst = {"k": k_tok, "v": v_tok, "g": g_tok}[pname]
                    mcol = {"k": 0, "v": 1, "r": 2, "g": 3}[pname]
                    mh, ml = (mxh, mxl) if i % 2 == 0 else (mxh2, mxl2)
                    mix_done = emit_mix(mcol, ln_done, mh, ml, mix_users[i % 2])
                    for oh in range(2):
                        sname = "wkvg" if oh == 0 else "wkvg2"
                        slot = wkvg if oh == 0 else wkvg2
                        w_rdy = load_kvg(slot, sname, pname, oh)
                        for cg in range(CPT):
                            ps = (e.psA if cg % 2 == 0 else e.psB)[:, 0:512]
                            pi = cg % 2

                            @blk.tensor
                            def _(t, slot=slot, w_rdy=w_rdy, mix_done=mix_done, ps=ps,
                                  pi=pi, cg=cg, oh=oh, mh=mh, ml=ml):
                                t.wait_ge(e.v.s, mix_done)
                                t.wait_ge(e.ld.s, w_rdy)
                                t.wait_ge(e.s.s, evac_guard[pi])
                                for dc in range(DC):
                                    xh = mh[:, dc * TT + cg * 128:dc * TT + (cg + 1) * 128]
                                    xl = ml[:, dc * TT + cg * 128:dc * TT + (cg + 1) * 128]
                                    wh_ = slot[:, dc * 512:(dc + 1) * 512]
                                    wl_ = slot[:, DC * 512 + dc * 512:DC * 512 + (dc + 1) * 512]
                                    t.matmul(ps, xh, wh_, start=(dc == 0), stop=False)
                                    t.matmul(ps, xh, wl_, start=False, stop=False)
                                    ins = t.matmul(ps, xl, wh_, start=False, stop=(dc == DC - 1))
                                e.t.inc(ins)
                            tn = e.t.n
                            slot_users[sname] = tn

                            @blk.scalar
                            def _(s, cg=cg, oh=oh, tn=tn, ps=ps, dst=dst, pname=pname):
                                s.wait_ge(e.t.s, tn)
                                fn = AF.Gelu if pname == "g" else AF.Copy
                                e.s.inc(s.activation(dst[:, cg * D + oh * 512:cg * D + oh * 512 + 512], ps, fn))
                            evac_guard[pi] = e.s.n
                    mix_users[i % 2] = e.t.n

                mh, ml = mxh2, mxl2
                mix_done = emit_mix(2, ln_done, mh, ml, mix_users[1])
                for oc in range(DC):
                    sname = "wro" if oc % 2 == 0 else "wro2"
                    slot = wro if oc % 2 == 0 else wro2
                    w_rdy = load_ro_oc(slot, sname, "r", oc)
                    ps = (e.psA if oc % 2 == 0 else e.psB)[:, 0:TT]
                    pi = oc % 2

                    @blk.tensor
                    def _(t, slot=slot, w_rdy=w_rdy, mix_done=mix_done, ps=ps, pi=pi, oc=oc,
                          mh=mh, ml=ml):
                        t.wait_ge(e.v.s, mix_done)
                        t.wait_ge(e.ld.s, w_rdy)
                        t.wait_ge(e.s.s, evac_guard[pi])
                        for dc in range(DC):
                            xh = mh[:, dc * TT:(dc + 1) * TT]
                            xl = ml[:, dc * TT:(dc + 1) * TT]
                            wh_ = slot[:, dc * 128:(dc + 1) * 128]
                            wl_ = slot[:, DC * 128 + dc * 128:DC * 128 + (dc + 1) * 128]
                            t.matmul(ps, wh_, xh, start=(dc == 0), stop=False)
                            t.matmul(ps, wl_, xh, start=False, stop=False)
                            ins = t.matmul(ps, wh_, xl, start=False, stop=(dc == DC - 1))
                        e.t.inc(ins)
                    tn = e.t.n
                    slot_users[sname] = tn

                    @blk.scalar
                    def _(s, oc=oc, tn=tn, ps=ps):
                        s.wait_ge(e.t.s, tn)
                        e.s.inc(s.activation(r_fm[:, oc * TT:(oc + 1) * TT], ps, AF.Copy))
                    evac_guard[pi] = e.s.n
                mix_users[1] = e.t.n

                proj_done = e.s.n

                for cg in range(CPT):
                    ci = ti * CPT + cg
                    ks = k_tok[:, cg * D:(cg + 1) * D]
                    vs = v_tok[:, cg * D:(cg + 1) * D]
                    gs = g_tok[:, cg * D:(cg + 1) * D]

                    kf_half_done = []
                    for half in range(2):
                        @blk.tensor
                        def _(t, ks=ks, proj_done=proj_done, half=half):
                            if half == 0:
                                t.wait_ge(e.s.s, proj_done)
                                t.wait_ge(e.v.s, e.v.n)
                            else:
                                t.wait_ge(e.v.s, kf_half_done[0])
                            for c4 in range(4):
                                c = half * 4 + c4
                                ins = t.transpose(e.psA[:, c4 * 128:(c4 + 1) * 128],
                                                  ks[:, c * 128:(c + 1) * 128], e.eye[:, :])
                            e.t.inc(ins)
                        tn = e.t.n

                        @blk.vector
                        def _(v, half=half, tn=tn):
                            v.wait_ge(e.t.s, tn)
                            e.v.inc(v.tensor_copy(k_fm[:, half * 512:(half + 1) * 512], e.psA[:, 0:512]))
                        kf_half_done.append(e.v.n)
                    kfm_done = e.v.n

                    @blk.vector
                    def _(v, ks=ks, proj_done=proj_done):
                        v.wait_ge(e.s.s, proj_done)
                        e.v.inc(v.tensor_tensor(kw_tok[:, :], ks, WKB[:, :], ALU.mult))
                    kw_done = e.v.n

                    def att_bank(hh):
                        return (e.psD, e.psB, e.psA)[hh % 3][:, 0:128]

                    def att_buf(hh):
                        return (attw, attw2, attw3)[hh % 3]

                    att_t_arr = [0] * H
                    attw_v_arr = [0] * H
                    y_t_arr = [0] * H
                    for hh in range(H + 1):
                        if hh < H:
                            hc, hr = hh // 2, (hh % 2) * 64
                            kfm_h = k_fm[hr:hr + 64, hc * 128:(hc + 1) * 128]
                            rfm_h = r_fm[hr:hr + 64, hc * TT + cg * 128:hc * TT + (cg + 1) * 128]

                            @blk.tensor
                            def _(t, hh=hh, kfm_h=kfm_h, rfm_h=rfm_h, kfm_done=kfm_done,
                                  kw_done=kw_done, proj_done=proj_done):
                                if hh == 0:
                                    t.wait_ge(e.v.s, max(kfm_done, kw_done))
                                elif hh == 1:
                                    t.wait_ge(e.s.s, proj_done)
                                elif hh == 2:
                                    pass
                                else:
                                    t.wait_ge(e.v.s, attw_v_arr[hh - 3])
                                e.t.inc(t.matmul(att_bank(hh), kfm_h, rfm_h, start=True, stop=True))
                            att_t_arr[hh] = e.t.n

                        if hh >= 1:
                            h0 = hh - 1
                            hc0, hr0 = h0 // 2, (h0 % 2) * 64
                            rfm_h0 = r_fm[hr0:hr0 + 64, hc0 * TT + cg * 128:hc0 * TT + (cg + 1) * 128]

                            @blk.vector
                            def _(v, h0=h0):
                                v.wait_ge(e.t.s, att_t_arr[h0])
                                if h0 >= 3:
                                    v.wait_ge(e.t.s, y_t_arr[h0 - 3])
                                e.v.inc(v.tensor_tensor(att_buf(h0)[:, :], att_bank(h0),
                                                        e.wmatT[:, h0 * 128:(h0 + 1) * 128], ALU.mult))
                            attw_v_arr[h0] = e.v.n

                            @blk.tensor
                            def _(t, h0=h0, vs=vs, rfm_h0=rfm_h0, hr0=hr0):
                                t.wait_ge(e.v.s, attw_v_arr[h0])
                                hcol = (h0 // 2) * HN
                                t.matmul(e.psY[:, h0 * HN:(h0 + 1) * HN], att_buf(h0)[:, :],
                                         vs[:, h0 * HN:(h0 + 1) * HN], start=True, stop=True)
                                t.matmul(e.psY2[:, h0 * HN:(h0 + 1) * HN], rfm_h0,
                                         state[hr0:hr0 + 64, hcol:hcol + HN], start=True, stop=True)
                                e.t.inc(t.matmul(e.psC[hr0:hr0 + 64, hcol:hcol + HN],
                                                 kw_tok[:, h0 * HN:(h0 + 1) * HN],
                                                 vs[:, h0 * HN:(h0 + 1) * HN], start=True, stop=True,
                                                 tile_position=(0, hr0)))
                            y_t_arr[h0] = e.t.n

                    all_y_t = e.t.n

                    @blk.scalar
                    def _(s):
                        s.wait_ge(e.t.s, all_y_t)
                        e.s.inc(s.activation(ysb[:, :], e.psY[:, 0:1024], AF.Copy))
                    y1_s = e.s.n

                    @blk.vector
                    def _(v, y1_s=y1_s):
                        v.wait_ge(e.s.s, y1_s)
                        v.tensor_tensor(zf[:, :], e.psY2[:, 0:1024], WBB[:, :], ALU.mult)
                        v.tensor_tensor(ysb[:, :], ysb[:, :], zf[:, :], ALU.add)
                        v.tensor_tensor(state[:, :], state[:, :], WSB[:, :], ALU.mult)
                        e.v.inc(v.tensor_tensor(state[:, :], state[:, :], e.psC[:, 0:(H // 2) * HN], ALU.add))
                    heads_done = e.v.n

                    @blk.vector
                    def _(v, heads_done=heads_done):
                        v.wait_ge(e.v.s, heads_done)
                        for hh in range(H):
                            ins = v.tensor_reduce(gn_m[:, hh:hh + 1], ysb[:, hh * HN:(hh + 1) * HN], AX.X, ALU.add)
                        e.v.inc(ins)
                        v.wait_ge(e.v.s, e.v.n)
                        e.v.inc(v.tensor_scalar(gn_m[:, :], gn_m[:, :], 1.0 / HN, None, ALU.mult))
                        v.wait_ge(e.v.s, e.v.n)
                        for hh in range(H):
                            e.v.inc(v.tensor_scalar(zf[:, hh * HN:(hh + 1) * HN],
                                                    ysb[:, hh * HN:(hh + 1) * HN],
                                                    gn_m[:, hh:hh + 1], None, ALU.subtract))
                    ycen_v = e.v.n

                    @blk.vector
                    def _(v, ycen_v=ycen_v):
                        v.wait_ge(e.v.s, ycen_v)
                        e.v.inc(v.tensor_tensor(k_fm[:, 0:D], zf[:, :], zf[:, :], ALU.mult))
                    ysq_s = e.v.n

                    @blk.vector
                    def _(v, ysq_s=ysq_s):
                        v.wait_ge(e.v.s, ysq_s)
                        for hh in range(H):
                            ins = v.tensor_reduce(gn_v[:, hh:hh + 1], k_fm[:, hh * HN:(hh + 1) * HN], AX.X, ALU.add)
                        e.v.inc(ins)
                        v.wait_ge(e.v.s, e.v.n)
                        e.v.inc(v.tensor_scalar(gn_v[:, :], gn_v[:, :], 1.0 / HN, None, ALU.mult))
                    gnv = e.v.n

                    @blk.scalar
                    def _(s, gnv=gnv):
                        s.wait_ge(e.v.s, gnv)
                        e.s.inc(s.activation(gn_t[:, :], gn_v[:, :], AF.Sqrt, bias=e.epsgn[:, :]))
                    gns = e.s.n

                    @blk.vector
                    def _(v, gns=gns, gs=gs):
                        v.wait_ge(e.s.s, gns)
                        e.v.inc(v.reciprocal(gn_rs[:, :], gn_t[:, :]))
                        v.wait_ge(e.v.s, e.v.n)
                        for hh in range(H):
                            v.tensor_scalar(zf[:, hh * HN:(hh + 1) * HN],
                                            zf[:, hh * HN:(hh + 1) * HN],
                                            gn_rs[:, hh:hh + 1], None, ALU.mult)
                        v.tensor_tensor(zf[:, :], zf[:, :], e.lnwb[:, :], ALU.mult)
                        v.tensor_tensor(zf[:, :], zf[:, :], e.lnbb[:, :], ALU.add)
                        e.v.inc(v.tensor_tensor(zf[:, :], zf[:, :], gs, ALU.mult))
                    zv = e.v.n

                    @blk.scalar
                    def _(s, zv=zv):
                        s.wait_ge(e.v.s, zv)
                        s.wait_ge(e.sc.s, e.sc.n)
                        e.s.inc(s.activation(zh[:, :], zf[:, :], AF.Copy))
                    zhs = e.s.n

                    @blk.vector
                    def _(v, zhs=zhs):
                        v.wait_ge(e.s.s, zhs)
                        e.v.inc(v.tensor_tensor(zl[:, :], zf[:, :], zh[:, :], ALU.subtract))
                    zlv = e.v.n

                    @blk.scalar
                    def _(s2, zlv=zlv, cg=cg):
                        s2.wait_ge(e.v.s, zlv)
                        for c in range(DC):
                            e.sc.inc(s2.dma_start(out=zfmh[:, c * TT + cg * 128:c * TT + (cg + 1) * 128],
                                                  in_=zh[:, c * 128:(c + 1) * 128], transpose=True), 16)
                            e.sc.inc(s2.dma_start(out=zfml[:, c * TT + cg * 128:c * TT + (cg + 1) * 128],
                                                  in_=zl[:, c * 128:(c + 1) * 128], transpose=True), 16)
                zt_done = e.sc.n

                for oc in range(DC):
                    sname = "wro" if oc % 2 == 0 else "wro2"
                    slot = wro if oc % 2 == 0 else wro2
                    w_rdy = load_ro_oc(slot, sname, "o", oc)
                    ps = (e.psA if oc % 2 == 0 else e.psB)[:, 0:TT]
                    pi = oc % 2

                    @blk.tensor
                    def _(t, slot=slot, w_rdy=w_rdy, zt_done=zt_done, ps=ps, pi=pi, oc=oc):
                        t.wait_ge(e.ld.s, w_rdy)
                        t.wait_ge(e.sc.s, zt_done)
                        t.wait_ge(e.v.s, e.v.n)
                        for dc in range(DC):
                            xh = zfmh[:, dc * TT:(dc + 1) * TT]
                            xl = zfml[:, dc * TT:(dc + 1) * TT]
                            wh_ = slot[:, dc * 128:(dc + 1) * 128]
                            wl_ = slot[:, DC * 128 + dc * 128:DC * 128 + (dc + 1) * 128]
                            t.matmul(ps, wh_, xh, start=(dc == 0), stop=False)
                            t.matmul(ps, wl_, xh, start=False, stop=False)
                            ins = t.matmul(ps, wh_, xl, start=False, stop=(dc == DC - 1))
                        e.t.inc(ins)
                    tn = e.t.n
                    slot_users[sname] = tn

                    @blk.vector
                    def _(v, oc=oc, tn=tn, ps=ps):
                        v.wait_ge(e.t.s, tn)
                        e.v.inc(v.tensor_tensor(xt[:, oc * TT:(oc + 1) * TT],
                                                xt[:, oc * TT:(oc + 1) * TT], ps, ALU.add))
                x1_v = e.v.n
                evac_guard = [0, 0]

                @blk.gpsimd
                def _(g, t0=t0, x1_v=x1_v):
                    g.wait_ge(e.v.s, x1_v)
                    for c in range(DC):
                        e.gq.inc(g.dma_start(out=xs1[c * 128:(c + 1) * 128, t0:t0 + TT],
                                             in_=xt[:, c * TT:(c + 1) * TT]), 16)
                x1_store = e.gq.n

                if dev_dump == "probe" and ti == 0:
                    @blk.gpsimd
                    def _(g):
                        g.wait_ge(e.gq.s, e.gq.n)
                        g.wait_ge(e.v.s, e.v.n)
                        g.wait_ge(e.s.s, e.s.n)
                        
                        e.fin.inc(g.dma_start(out=dbg[0:128, 2048:4096], in_=r_fm[:, :]), 16)
                        e.fin.inc(g.dma_start(out=dbg[640:768, 0:2056], in_=ln[:, :]), 16)
                        e.fin.inc(g.dma_start(out=dbg[128:256, 0:2048], in_=k_tok[:, :]), 16)
                        e.fin.inc(g.dma_start(out=dbg[128:256, 2048:4096], in_=v_tok[:, :]), 16)
                        e.fin.inc(g.dma_start(out=dbg[256:384, 0:2048], in_=g_tok[:, :]), 16)
                        e.fin.inc(g.dma_start(out=dbg[256:384, 2048:2048 + 1024], in_=ysb[:, :]), 16)
                        e.fin.inc(g.dma_start(out=dbg[384:512, 0:1024], in_=zf[:, :]), 16)
                        e.fin.inc(g.dma_start(out=dbg[384:512, 1024:1024 + 512], in_=state[:, :]), 16)
                        e.fin.inc(g.dma_start(out=dbg[512:640, 0:2048], in_=zfmh[:, :], ), 16)
                        e.fin.inc(g.dma_start(out=dbg[512:640, 2048:2048 + 96], in_=st6[:, :]), 16)
                        e.fin.inc(g.dma_start(out=dbg[512:640, 2144:2144 + 16], in_=gn_m[:, :]), 16)
                        e.fin.inc(g.dma_start(out=dbg[512:640, 2160:2160 + 16], in_=gn_v[:, :]), 16)
                        e.fin.inc(g.dma_start(out=dbg[512:640, 2176:2176 + 16], in_=gn_rs[:, :]), 16)
                        g.wait_ge(e.fin.s, e.fin.n)

            if dev_dump == "x1":
                @blk.gpsimd
                def _(g):
                    g.wait_ge(e.gq.s, e.gq.n)
                    e.fin.inc(g.dma_start(out=dbg, in_=xs1), 16)
                    g.wait_ge(e.fin.s, e.fin.n)
            p1.close()

        if n_phases >= 2:
            p2 = ExitStack()
            Ah = sb("Ah", [128, DC * D], BF16, st=p2)
            Al = sb("Al", [128, DC * D], BF16, st=p2)
            Bh = sb("Bh", [128, DC * D], BF16, st=p2)
            Bl = sb("Bl", [128, DC * D], BF16, st=p2)
            tstg = sb("tstg", [128, D], st=p2)
            tstg2 = sb("tstg2", [128, D], st=p2)
            fsum = sb("fsum", [128, DC * TT], st=p2)
            halfpi = sb("halfpi", [128, 1], st=p2)
            bv1 = barrier_vals()

            @blk.vector
            def _(v):
                wait_barrier(v, bv1)
                e.v.inc(v.memset(halfpi[:, :], HALFPI))
            p2v0 = e.v.n

            tstg3 = sb("tstg3", [128, D], st=p2)
            tstg4 = sb("tstg4", [128, D], st=p2)
            ab_stg = [(tstg, tstg2), (tstg3, tstg4)]
            ab_busy = [p2v0, p2v0]
            abi = 0
            for (srcA, srcN, dsth, dstl, use_cos) in [
                (theta_in, theta_n_in, Ah, Al, True),
                (phi_in, phi_n_in, Bh, Bl, False),
            ]:
                for c in range(DC):
                    par = abi % 2
                    abi += 1
                    sA, sB = ab_stg[par]

                    @blk.sync
                    def _(sy, srcA=srcA, srcN=srcN, c=c, bt=bv1["t"], uw=ab_busy[par], sA=sA, sB=sB):
                        sy.wait_ge(e.t.s, bt)
                        sy.wait_ge(e.v.s, uw)
                        e.ld.inc(sy.dma_start(out=sA[:, :], in_=srcA[c * 128:(c + 1) * 128, :]), 16)
                        e.ld.inc(sy.dma_start(out=sB[:, :], in_=srcN[c * 128:(c + 1) * 128, :]), 16)
                    tl = e.ld.n

                    @blk.vector
                    def _(v, tl=tl, sA=sA, sB=sB):
                        v.wait_ge(e.ld.s, tl)
                        e.v.inc(v.tensor_tensor(sA[:, :], sA[:, :], sB[:, :], ALU.add))
                    tv = e.v.n

                    @blk.scalar
                    def _(s, tv=tv, use_cos=use_cos, sA=sA, sB=sB):
                        s.wait_ge(e.v.s, tv)
                        if use_cos:
                            e.s.inc(s.activation(sB[:, :], sA[:, :], AF.Sin, bias=halfpi[:, :]))
                        else:
                            e.s.inc(s.activation(sB[:, :], sA[:, :], AF.Sin))
                    ts_ = e.s.n

                    @blk.vector
                    def _(v, ts_=ts_, c=c, dsth=dsth, dstl=dstl, sB=sB):
                        v.wait_ge(e.s.s, ts_)
                        v.tensor_copy(dsth[:, c * D:(c + 1) * D], sB[:, :])
                        e.v.inc(v.tensor_tensor(dstl[:, c * D:(c + 1) * D], sB[:, :],
                                                dsth[:, c * D:(c + 1) * D], ALU.subtract))
                    ab_busy[par] = e.v.n
            ab_done = e.v.n

            x2_store = 0
            for ti in range(NT):
                t0 = ti * TT

                @blk.gpsimd
                def _(g, t0=t0, x2_store=max(x2_store, 0) if ti > 0 else None):
                    g.wait_ge(e.gq.s, x2_store if x2_store is not None else e.gq.n)
                    g.wait_ge(e.v.s, e.v.n)
                    for c in range(DC):
                        e.gq.inc(g.dma_start(out=xt[:, c * TT:(c + 1) * TT],
                                             in_=xs1[c * 128:(c + 1) * 128, t0:t0 + TT]), 16)
                x_ld = e.gq.n

                @blk.vector
                def _(v):
                    v.wait_ge(e.gq.s, x_ld)
                    e.v.inc(v.memset(ln[:, 0:1], 0.0))
                gv_ = e.v.n

                ln_done = emit_ln(xt, ln, e.v, gv_, 0)

                @blk.scalar
                def _(s, ln_done=ln_done):
                    s.wait_ge(e.v.s, ln_done)
                    s.wait_ge(e.t.s, e.t.n)
                    W1 = TT + 1
                    for c in range(DC):
                        e.s.inc(s.activation(mxh[:, c * TT:(c + 1) * TT],
                                             ln[:, c * W1 + 1:(c + 1) * W1], AF.Copy))
                sp_ = e.s.n

                @blk.vector
                def _(v, sp_=sp_):
                    v.wait_ge(e.s.s, sp_)
                    W1 = TT + 1
                    for c in range(DC):
                        e.v.inc(v.tensor_tensor(mxl[:, c * TT:(c + 1) * TT],
                                                ln[:, c * W1 + 1:(c + 1) * W1],
                                                mxh[:, c * TT:(c + 1) * TT], ALU.subtract))
                mix_done = e.v.n

                evac_v2 = [e.v.n]
                sq_guard = [0, 0]
                for oc in range(DC):
                    bp = ((e.psA, e.psB), (e.psC, e.psD))[oc % 2]

                    @blk.tensor
                    def _(t, oc=oc, mix_done=mix_done, bp=bp, gw=None):
                        t.wait_ge(e.v.s, max(mix_done, ab_done))
                        t.wait_ge(e.s.s, sq_guard[oc % 2])
                        for (ps, wh_, wl_) in ((bp[0][:, 0:TT], Ah, Al), (bp[1][:, 0:TT], Bh, Bl)):
                            for dc in range(DC):
                                xh = mxh[:, dc * TT:(dc + 1) * TT]
                                xl = mxl[:, dc * TT:(dc + 1) * TT]
                                ww_h = wh_[:, dc * D + oc * 128: dc * D + oc * 128 + 128]
                                ww_l = wl_[:, dc * D + oc * 128: dc * D + oc * 128 + 128]
                                t.matmul(ps, ww_h, xh, start=(dc == 0), stop=False)
                                t.matmul(ps, ww_l, xh, start=False, stop=False)
                                ins = t.matmul(ps, ww_h, xl, start=False, stop=(dc == DC - 1))
                            e.t.inc(ins)
                    tn = e.t.n

                    @blk.scalar
                    def _(s, tn=tn, oc=oc, bp=bp):
                        s.wait_ge(e.t.s, tn)
                        s.wait_ge(e.v.s, e.v.n)
                        s.activation(sq[:, 0:TT], bp[0][:, 0:TT], AF.Square)
                        e.s.inc(s.activation(sq[:, TT:2 * TT], bp[1][:, 0:TT], AF.Square))
                    sq_s2 = e.s.n
                    sq_guard[oc % 2] = sq_s2

                    @blk.vector
                    def _(v, sq_s2=sq_s2, oc=oc):
                        v.wait_ge(e.s.s, sq_s2)
                        e.v.inc(v.tensor_tensor(fsum[:, oc * TT:(oc + 1) * TT],
                                                sq[:, 0:TT], sq[:, TT:2 * TT], ALU.add))
                    evac_v2[0] = e.v.n

                @blk.scalar
                def _(s):
                    s.wait_ge(e.v.s, evac_v2[0])
                    e.s.inc(s.activation(fsum[:, :], fsum[:, :], AF.Sqrt))
                fs = e.s.n

                @blk.vector
                def _(v, fs=fs):
                    v.wait_ge(e.s.s, fs)
                    for oc in range(DC):
                        e.v.inc(v.tensor_tensor(xt[:, oc * TT:(oc + 1) * TT],
                                                xt[:, oc * TT:(oc + 1) * TT],
                                                fsum[:, oc * TT:(oc + 1) * TT], ALU.add))
                x2_v = e.v.n

                @blk.gpsimd
                def _(g, t0=t0, x2_v=x2_v):
                    g.wait_ge(e.v.s, x2_v)
                    for c in range(DC):
                        e.gq.inc(g.dma_start(out=xs2[c * 128:(c + 1) * 128, t0:t0 + TT],
                                             in_=xt[:, c * TT:(c + 1) * TT]), 16)
                x2_store = e.gq.n

            if dev_dump == "x2":
                @blk.gpsimd
                def _(g):
                    g.wait_ge(e.gq.s, e.gq.n)
                    e.fin.inc(g.dma_start(out=dbg, in_=xs2), 16)
                    g.wait_ge(e.fin.s, e.fin.n)
            p2.close()

        if n_phases >= 3:
            p3 = ExitStack()
            WwPh = sb("WwPh", [128, DC * D], BF16, st=p3)
            WwPl = sb("WwPl", [128, DC * D], BF16, st=p3)
            WrdPh = sb("WrdPh", [128, DC * D], BF16, st=p3)
            WrdPl = sb("WrdPl", [128, DC * D], BF16, st=p3)
            ST = sb("ST", [128, DC * D], st=p3)
            v_tok3 = sb("v_tok3", [128, CPT * D], st=p3)
            w_tok3 = sb("w_tok3", [128, D], st=p3)
            t_tok3 = sb("t_tok3", [128, D], st=p3)
            vgw = sb("vgw", [128, D], st=p3)
            stm = sb("stm", [128, 128], st=p3)
            intr = sb("intr", [128, D], st=p3)
            reads_tok = sb("reads_tok", [128, D], st=p3)
            reads_fm = sb("reads_fm", [128, DC * 128], st=p3)
            rdh = sb("rdh", [128, DC * 128], BF16, st=p3)
            rdl = sb("rdl", [128, DC * 128], BF16, st=p3)
            x3f = sb("x3f", [128, D], st=p3)
            x3bh = sb("x3bh", [128, D], BF16, st=p3)
            x3bl = sb("x3bl", [128, D], BF16, st=p3)
            MT = sb("MT", [128, 128], st=p3)
            l1row = sb("l1row", [1, 8], st=p3)
            lg_col = sb("lg_col", [128, 1], st=p3)
            al_col = sb("al_col", [128, 1], st=p3)
            dec_col = sb("dec_col", [128, 1], st=p3)
            gw_col = sb("gw_col", [128, 1], st=p3)
            gC_col = sb("gC_col", [128, 1], st=p3)
            bv2 = barrier_vals()

            for (srcW, dsth, dstl) in [(WwT_in, WwPh, WwPl), (WrdT_in, WrdPh, WrdPl)]:
                for c in range(DC):
                    @blk.sync
                    def _(sy, srcW=srcW, c=c, bt=bv2["t"]):
                        sy.wait_ge(e.t.s, bt)
                        sy.wait_ge(e.v.s, e.v.n)
                        sy.wait_ge(e.s.s, e.s.n)
                        e.ld.inc(sy.dma_start(out=wstg[:, :], in_=srcW[c * 128:(c + 1) * 128, :]), 16)
                    tl = e.ld.n

                    @blk.scalar
                    def _(s, tl=tl, c=c, dsth=dsth):
                        s.wait_ge(e.ld.s, tl)
                        e.s.inc(s.activation(dsth[:, c * D:(c + 1) * D], wstg[:, :], AF.Copy))
                    ts_ = e.s.n

                    @blk.vector
                    def _(v, ts_=ts_, c=c, dsth=dsth, dstl=dstl):
                        v.wait_ge(e.s.s, ts_)
                        e.v.inc(v.tensor_tensor(dstl[:, c * D:(c + 1) * D], wstg[:, :],
                                                dsth[:, c * D:(c + 1) * D], ALU.subtract))
            wsplit3 = e.v.n

            @blk.sync
            def _(sy, bt=bv2["t"]):
                sy.wait_ge(e.t.s, bt)
                e.ld.inc(sy.dma_start(out=l1row[0:1, 0:4], in_=l1s_in), 16)
                e.ld.inc(sy.dma_start(out=MT[:, :], in_=c2t_in), 16)
            sc_ld = e.ld.n

            @blk.scalar
            def _(s):
                s.wait_ge(e.ld.s, sc_ld)
                s.activation(l1row[0:1, 4:5], l1row[0:1, 0:1], AF.Sigmoid)
                e.s.inc(s.activation(l1row[0:1, 5:6], l1row[0:1, 4:5], AF.Ln))
            sc_s = e.s.n

            @blk.gpsimd
            def _(g):
                g.wait_ge(e.s.s, sc_s)
                e.gq.inc(g.dma_start(out=scal_d[0:1, 0:8], in_=l1row[0:1, 0:8]), 16)
            sc_g = e.gq.n

            @blk.sync
            def _(sy):
                sy.wait_ge(e.gq.s, sc_g)
                e.ld.inc(sy.dma_start(out=lg_col[:, :], in_=scal_d[0:1, 5:6].broadcast_to([128, 1])), 16)
                e.ld.inc(sy.dma_start(out=al_col[:, :], in_=scal_d[0:1, 1:2].broadcast_to([128, 1])), 16)
            sc_b = e.ld.n

            @blk.vector
            def _(v):
                v.wait_ge(e.ld.s, sc_b)
                e.v.inc(v.tensor_scalar(dec_col[:, :], e.ioi[:, :], lg_col[:, :], None, ALU.mult))
                e.v.inc(v.tensor_scalar(gw_col[:, :], e.ior[:, :], lg_col[:, :], None, ALU.mult))
                e.v.inc(v.tensor_scalar(gC_col[:, :], lg_col[:, :], float(Q), None, ALU.mult))
                e.v.inc(v.tensor_scalar(MT[:, :], MT[:, :], lg_col[:, :], None, ALU.mult))
                v.wait_ge(e.v.s, e.v.n)
            sc_v = e.v.n

            @blk.scalar
            def _(s):
                s.wait_ge(e.v.s, sc_v)
                s.activation(dec_col[:, :], dec_col[:, :], AF.Exp)
                s.activation(gw_col[:, :], gw_col[:, :], AF.Exp)
                s.activation(gC_col[:, :], gC_col[:, :], AF.Exp)
                s.activation(al_col[:, :], al_col[:, :], AF.Exp)
                e.s.inc(s.activation(MT[:, :], MT[:, :], AF.Exp))
            sc_s2 = e.s.n

            @blk.vector
            def _(v):
                v.wait_ge(e.s.s, sc_s2)
                v.memset(ST[:, :], 0.0)
                for c in range(DC):
                    e.v.inc(v.memset(ln[:, c * (TT + 1):c * (TT + 1) + 1], 0.0))
            p3_init = e.v.n

            x3_store = 0
            st_upd_v = p3_init
            for ti in range(NT):
                t0 = ti * TT

                @blk.gpsimd
                def _(g, t0=t0, w=x3_store if ti > 0 else None):
                    g.wait_ge(e.gq.s, w if w is not None else e.gq.n)
                    g.wait_ge(e.v.s, e.v.n)
                    for c in range(DC):
                        e.gq.inc(g.dma_start(out=xt[:, c * TT:(c + 1) * TT],
                                             in_=xs2[c * 128:(c + 1) * 128, t0:t0 + TT]), 16)
                x_ld = e.gq.n

                @blk.vector
                def _(v, ti=ti):
                    v.wait_ge(e.gq.s, x_ld)
                    W1 = TT + 1
                    if ti > 0:
                        for c in range(DC):
                            e.v.inc(v.tensor_copy(ln[:, c * W1:c * W1 + 1], ln[:, c * W1 + TT:c * W1 + TT + 1]))
                    else:
                        for c in range(DC):
                            e.v.inc(v.memset(ln[:, c * W1:c * W1 + 1], 0.0))
                guard_v = e.v.n

                ln_done = emit_ln(xt, ln, e.v, guard_v, 2)

                @blk.scalar
                def _(s, ln_done=ln_done):
                    s.wait_ge(e.v.s, ln_done)
                    s.wait_ge(e.t.s, e.t.n)
                    W1 = TT + 1
                    for c in range(DC):
                        e.s.inc(s.activation(mxh[:, c * TT:(c + 1) * TT],
                                             ln[:, c * W1 + 1:(c + 1) * W1], AF.Copy))
                sp_ = e.s.n

                @blk.vector
                def _(v, sp_=sp_):
                    v.wait_ge(e.s.s, sp_)
                    W1 = TT + 1
                    for c in range(DC):
                        e.v.inc(v.tensor_tensor(mxl[:, c * TT:(c + 1) * TT],
                                                ln[:, c * W1 + 1:(c + 1) * W1],
                                                mxh[:, c * TT:(c + 1) * TT], ALU.subtract))
                mix_done = e.v.n

                for cg in range(CPT):
                    ps = (e.psA if cg % 2 == 0 else e.psB)[:, 0:512]
                    for oh in range(2):
                        @blk.tensor
                        def _(t, cg=cg, oh=oh, mix_done=mix_done, ps=ps):
                            t.wait_ge(e.v.s, max(mix_done, wsplit3))
                            t.wait_ge(e.s.s, e.s.n)
                            for dc in range(DC):
                                xh = mxh[:, dc * TT + cg * 128:dc * TT + (cg + 1) * 128]
                                xl = mxl[:, dc * TT + cg * 128:dc * TT + (cg + 1) * 128]
                                wh_ = WwPh[:, dc * D + oh * 512:dc * D + oh * 512 + 512]
                                wl_ = WwPl[:, dc * D + oh * 512:dc * D + oh * 512 + 512]
                                t.matmul(ps[:, 0:512] if False else ps, xh, wh_, start=(dc == 0), stop=False)
                                t.matmul(ps, xh, wl_, start=False, stop=False)
                                ins = t.matmul(ps, xl, wh_, start=False, stop=(dc == DC - 1))
                            e.t.inc(ins)
                        tn = e.t.n

                        @blk.scalar
                        def _(s, cg=cg, oh=oh, tn=tn, ps=ps):
                            s.wait_ge(e.t.s, tn)
                            e.s.inc(s.activation(v_tok3[:, cg * D + oh * 512:cg * D + oh * 512 + 512],
                                                 ps, AF.Copy))
                vproj_done = e.s.n

                for cg in range(CPT):
                    ci = ti * CPT + cg
                    W1 = TT + 1
                    vs3 = v_tok3[:, cg * D:(cg + 1) * D]

                    trdone = []
                    for half in range(2):
                        @blk.tensor
                        def _(t, half=half, cg=cg, vproj_done=vproj_done):
                            if half == 0:
                                t.wait_ge(e.s.s, vproj_done)
                                t.wait_ge(e.v.s, e.v.n)
                            else:
                                t.wait_ge(e.v.s, trdone[0])
                            for c4 in range(4):
                                c = half * 4 + c4
                                ins = t.transpose(e.psC[:, c4 * 128:(c4 + 1) * 128],
                                                  ln[:, c * W1 + cg * 128:c * W1 + cg * 128 + 128],
                                                  e.eye[:, :])
                            e.t.inc(ins)
                        tn = e.t.n

                        @blk.vector
                        def _(v, half=half, tn=tn):
                            v.wait_ge(e.t.s, tn)
                            e.v.inc(v.tensor_copy(w_tok3[:, half * 512:(half + 1) * 512], e.psC[:, 0:512]))
                        trdone.append(e.v.n)

                    @blk.vector
                    def _(v, cg=cg):
                        v.wait_ge(e.v.s, trdone[1])
                        for c in range(DC):
                            e.v.inc(v.tensor_tensor(sq[:, c * TT + cg * 128:c * TT + (cg + 1) * 128],
                                                    xt[:, c * TT + cg * 128:c * TT + (cg + 1) * 128],
                                                    ln[:, c * W1 + cg * 128 + 1:c * W1 + (cg + 1) * 128 + 1],
                                                    ALU.add))
                    tfm_v = e.v.n

                    trdone2 = []
                    for half in range(2):
                        @blk.tensor
                        def _(t, half=half, cg=cg, tfm_v=tfm_v):
                            if half == 0:
                                t.wait_ge(e.v.s, tfm_v)
                            else:
                                t.wait_ge(e.v.s, trdone2[0])
                            for c4 in range(4):
                                c = half * 4 + c4
                                ins = t.transpose(e.psC[:, c4 * 128:(c4 + 1) * 128],
                                                  sq[:, c * TT + cg * 128:c * TT + (cg + 1) * 128],
                                                  e.eye[:, :])
                            e.t.inc(ins)
                        tn = e.t.n

                        @blk.vector
                        def _(v, half=half, tn=tn):
                            v.wait_ge(e.t.s, tn)
                            e.v.inc(v.tensor_copy(t_tok3[:, half * 512:(half + 1) * 512], e.psC[:, 0:512]))
                        trdone2.append(e.v.n)

                    @blk.vector
                    def _(v, vs3=vs3):
                        v.wait_ge(e.s.s, vproj_done)
                        e.v.inc(v.tensor_scalar(vgw[:, :], vs3, gw_col[:, :], None, ALU.mult))
                    vgw_v = e.v.n

                    @blk.tensor
                    def _(t, cg=cg, st_upd_v=st_upd_v):
                        t.wait_ge(e.v.s, max(trdone[1], st_upd_v))
                        for c in range(DC):
                            ins = t.matmul(e.psD[:, 0:128],
                                           ln[:, c * W1 + cg * 128:c * W1 + cg * 128 + 128],
                                           ln[:, c * W1 + cg * 128 + 1:c * W1 + (cg + 1) * 128 + 1],
                                           start=(c == 0), stop=(c == DC - 1))
                        e.t.inc(ins)
                    stp_t = e.t.n

                    @blk.vector
                    def _(v, stp_t=stp_t):
                        v.wait_ge(e.t.s, stp_t)
                        e.v.inc(v.tensor_tensor(stm[:, :], e.psD[:, 0:128], MT[:, :], ALU.mult))
                    stm_v = e.v.n

                    @blk.tensor
                    def _(t, cg=cg, stm_v=stm_v, vs3=vs3):
                        t.wait_ge(e.v.s, stm_v)
                        for nh in range(2):
                            ins = t.matmul(e.psY2[:, nh * 512:(nh + 1) * 512], stm[:, :],
                                           vs3[:, nh * 512:(nh + 1) * 512], start=True, stop=True)
                        e.t.inc(ins)
                        for nh in range(2):
                            for c in range(DC):
                                ins = t.matmul(e.psY[:, nh * 512:(nh + 1) * 512],
                                               ln[:, c * W1 + cg * 128 + 1:c * W1 + (cg + 1) * 128 + 1],
                                               ST[:, c * D + nh * 512:c * D + nh * 512 + 512],
                                               start=(c == 0), stop=(c == DC - 1))
                            e.t.inc(ins)
                    ii_t = e.t.n

                    @blk.scalar
                    def _(s, ii_t=ii_t):
                        s.wait_ge(e.t.s, ii_t)
                        e.s.inc(s.activation(intr[:, :], e.psY2[:, 0:1024], AF.Copy))
                    intr_s = e.s.n

                    upd_t = []
                    for ec in range(DC):
                        @blk.tensor
                        def _(t, ec=ec, vgw_v=vgw_v, intr_s=intr_s):
                            if ec == 0:
                                t.wait_ge(e.v.s, vgw_v)
                                t.wait_ge(e.s.s, max(e.s.n, intr_s))
                            if ec >= 2:
                                t.wait_ge(e.v.s, upd_t[ec - 2])
                            ps = (e.psA if ec % 2 == 0 else e.psB)[:, 0:512]
                            for nh in range(2):
                                ins = t.matmul(ps[:, 0:512], w_tok3[:, ec * 128:(ec + 1) * 128],
                                               vgw[:, nh * 512:(nh + 1) * 512],
                                               start=(nh == 0), stop=False) if False else None
                            ins = t.matmul(ps[:, 0:512], w_tok3[:, ec * 128:(ec + 1) * 128],
                                           vgw[:, 0:512], start=True, stop=True)
                            e.t.inc(ins)
                            ins = t.matmul((e.psD if ec % 2 == 0 else e.psY2)[:, 0:512],
                                           w_tok3[:, ec * 128:(ec + 1) * 128],
                                           vgw[:, 512:1024], start=True, stop=True)
                            e.t.inc(ins)
                        tn2 = e.t.n

                        @blk.vector
                        def _(v, ec=ec, tn2=tn2):
                            v.wait_ge(e.t.s, tn2)
                            ps1 = (e.psA if ec % 2 == 0 else e.psB)[:, 0:512]
                            ps2 = (e.psD if ec % 2 == 0 else e.psY2)[:, 0:512]
                            v.scalar_tensor_tensor(ST[:, ec * D:ec * D + 512],
                                                   ST[:, ec * D:ec * D + 512],
                                                   gC_col[:, :], ps1, ALU.mult, ALU.add)
                            e.v.inc(v.scalar_tensor_tensor(ST[:, ec * D + 512:ec * D + 1024],
                                                           ST[:, ec * D + 512:ec * D + 1024],
                                                           gC_col[:, :], ps2, ALU.mult, ALU.add))
                        upd_t.append(e.v.n)
                    st_upd_v = e.v.n

                    @blk.vector
                    def _(v, intr_s=intr_s):
                        v.wait_ge(e.s.s, intr_s)
                        e.v.inc(v.scalar_tensor_tensor(reads_tok[:, :], e.psY[:, 0:1024],
                                                       dec_col[:, :], intr[:, :],
                                                       ALU.mult, ALU.add))
                    reads_v = e.v.n

                    trdone3 = []
                    for half in range(2):
                        @blk.tensor
                        def _(t, half=half, reads_v=reads_v):
                            if half == 0:
                                t.wait_ge(e.v.s, reads_v)
                            else:
                                t.wait_ge(e.v.s, trdone3[0])
                            for c4 in range(4):
                                c = half * 4 + c4
                                ins = t.transpose(e.psC[:, c4 * 128:(c4 + 1) * 128],
                                                  reads_tok[:, c * 128:(c + 1) * 128], e.eye[:, :])
                            e.t.inc(ins)
                        tn = e.t.n

                        @blk.vector
                        def _(v, half=half, tn=tn):
                            v.wait_ge(e.t.s, tn)
                            e.v.inc(v.tensor_copy(reads_fm[:, half * 512:(half + 1) * 512], e.psC[:, 0:512]))
                        trdone3.append(e.v.n)

                    @blk.scalar
                    def _(s):
                        s.wait_ge(e.v.s, trdone3[1])
                        e.s.inc(s.activation(rdh[:, :], reads_fm[:, :], AF.Copy))
                    rdh_s = e.s.n

                    @blk.vector
                    def _(v, rdh_s=rdh_s):
                        v.wait_ge(e.s.s, rdh_s)
                        e.v.inc(v.tensor_tensor(rdl[:, :], reads_fm[:, :], rdh[:, :], ALU.subtract))
                    rdl_v = e.v.n

                    @blk.tensor
                    def _(t, rdl_v=rdl_v):
                        t.wait_ge(e.v.s, rdl_v)
                        for nh in range(2):
                            for dc in range(DC):
                                wh_ = WrdPh[:, dc * D + nh * 512:dc * D + nh * 512 + 512]
                                wl_ = WrdPl[:, dc * D + nh * 512:dc * D + nh * 512 + 512]
                                xh = rdh[:, dc * 128:(dc + 1) * 128]
                                xl = rdl[:, dc * 128:(dc + 1) * 128]
                                t.matmul(e.psY[:, nh * 512:(nh + 1) * 512], xh, wh_, start=(dc == 0), stop=False)
                                t.matmul(e.psY[:, nh * 512:(nh + 1) * 512], xh, wl_, start=False, stop=False)
                                ins = t.matmul(e.psY[:, nh * 512:(nh + 1) * 512], xl, wh_, start=False,
                                               stop=(dc == DC - 1))
                            e.t.inc(ins)
                    rp_t = e.t.n

                    @blk.vector
                    def _(v, rp_t=rp_t):
                        v.wait_ge(e.t.s, rp_t)
                        e.v.inc(v.scalar_tensor_tensor(x3f[:, :], e.psY[:, 0:1024], al_col[:, :],
                                                       t_tok3[:, :], ALU.mult, ALU.add))
                    x3_v = e.v.n

                    @blk.vector
                    def _(v, ci=ci, x3_v=x3_v):
                        v.wait_ge(e.v.s, x3_v)
                        e.v.inc(v.tensor_reduce(e.m4[:, ci:ci + 1], x3f[:, :], AX.X, ALU.add))
                        v.wait_ge(e.v.s, e.v.n)
                        e.v.inc(v.tensor_scalar(e.m4[:, ci:ci + 1], e.m4[:, ci:ci + 1], 1.0 / D, None, ALU.mult))
                        v.wait_ge(e.v.s, e.v.n)
                        e.v.inc(v.tensor_scalar(vgw[:, :], x3f[:, :], e.m4[:, ci:ci + 1], None, ALU.subtract))
                        e.v.inc(v.tensor_reduce(e.cmx4[:, ci:ci + 1], vgw[:, :], AX.X, ALU.max,
                                                apply_absolute_value=True))
                        e.v.inc(v.tensor_tensor(vgw[:, :], vgw[:, :], vgw[:, :], ALU.mult))
                        e.v.inc(v.tensor_reduce(e.var4[:, ci:ci + 1], vgw[:, :], AX.X, ALU.add))
                    s4b = e.v.n

                    @blk.scalar
                    def _(s, s4b=s4b):
                        s.wait_ge(e.v.s, s4b)
                        s.wait_ge(e.gq.s, e.gq.n)
                        e.s.inc(s.activation(x3bh[:, :], x3f[:, :], AF.Copy))
                    x3h_s = e.s.n

                    @blk.vector
                    def _(v, x3h_s=x3h_s):
                        v.wait_ge(e.s.s, x3h_s)
                        e.v.inc(v.tensor_tensor(x3bl[:, :], x3f[:, :], x3bh[:, :], ALU.subtract))
                    x3l_v = e.v.n

                    @blk.gpsimd
                    def _(g, ci=ci, x3l_v=x3l_v):
                        g.wait_ge(e.v.s, x3l_v)
                        e.gq.inc(g.dma_start(out=x3h_d[ci * 128:(ci + 1) * 128, :], in_=x3bh[:, :]), 16)
                        e.gq.inc(g.dma_start(out=x3l_d[ci * 128:(ci + 1) * 128, :], in_=x3bl[:, :]), 16)
                        if dev_dump == "x3" or n_phases == 3:
                            e.gq.inc(g.dma_start(out=out[ci * 128:(ci + 1) * 128, :], in_=x3f[:, :]), 16)
                    x3_store = e.gq.n

            if n_phases == 3:
                @blk.gpsimd
                def _(g):
                    g.wait_ge(e.gq.s, e.gq.n)
                    e.fin.inc(g.dma_start(out=scal_d[0:1, 8:9], in_=l1row[0:1, 0:1]), 16)
                    g.wait_ge(e.fin.s, e.fin.n)
            p3.close()

        psh.close()

        if n_phases >= 4:
            p4 = ExitStack()
            kwgT = sb("kwgT", [128, DC * DFF], BF16, st=p4)
            kwuT = sb("kwuT", [128, DC * DFF], BF16, st=p4)
            kwdT = sb("kwdT", [128, IC * D], BF16, st=p4)
            qstg = sb("qstg", [128, DFF], st=p4)
            qu = sb("qu", [128, DFF], st=p4)
            qk = sb("qk", [128, DFF], I32, st=p4)
            qb = sb("qb", [128, DFF], BF16, st=p4)
            wsg = sb("wsg", [128, IC], st=p4)
            wsu = sb("wsu", [128, IC], st=p4)
            wsd = sb("wsd", [128, DC], st=p4)
            wsgx = sb("wsgx", [128, IC], st=p4)
            wsux = sb("wsux", [128, IC], st=p4)
            m4 = e.m4
            var4 = e.var4
            cmx4 = e.cmx4
            rs4 = sb("rs4", [128, NCH], st=p4)
            rsi4 = sb("rsi4", [128, NCH], st=p4)
            sb4 = sb("sb4", [128, NCH], st=p4)
            x3th = sb("x3th", [128, D], BF16, st=p4)
            x3tl = sb("x3tl", [128, D], BF16, st=p4)
            x3f4 = sb("x3f4", [128, D], st=p4)
            scr4 = sb("scr4", [128, D], st=p4)
            kxf = sb("kxf", [128, DC * 512], BF16, st=p4)
            sgt = sb("sgt", [128, 512], st=p4)
            sut = sb("sut", [128, 512], st=p4)
            htile = sb("htile", [128, 512], st=p4)
            hmax = sb("hmax", [128, 1], st=p4)
            hab = sb("hab", [128, IC], st=p4)
            wsdq = sb("wsdq", [128, D], st=p4)
            p4row = sb("p4row", [1, 16], st=p4)
            mcol1 = sb("mcol1", [128, 1], st=p4)
            mcol2 = sb("mcol2", [128, 1], st=p4)
            colbuf = nc.dram_tensor("colbuf", [1, 256], F32).ap()
            bv3 = barrier_vals()

            winv_g = sb("winv_g", [128, IC], st=p4)
            winv_u = sb("winv_u", [128, IC], st=p4)
            winv_d = sb("winv_d", [128, DC], st=p4)
            wsd2 = sb("wsd2", [128, DC], st=p4)
            wsd3 = sb("wsd3", [128, DC], st=p4)
            units = []
            for rc in range(IC):
                units.append((wg_in, rc, 0, D, kwgT, DFF, (wsg, rc)))
            for rc in range(IC):
                units.append((wu_in, rc, 0, D, kwuT, DFF, (wsu, rc)))
            for rc in range(DC):
                units.append((wd_in, rc, 0, 1024, kwdT, D, (wsd, rc)))
                units.append((wd_in, rc, 1024, 1024, kwdT, D, (wsd2, rc)))
                units.append((wd_in, rc, 2048, DFF - 2048, kwdT, D, (wsd3, rc)))

            wbusy = [bv3["v"], bv3["v"]]
            for ui, (sap, rc, co, wd_, dstT, dstr, (wscol, wsi)) in enumerate(units):
                par = ui % 2
                wbuf = (qstg, qu)[par]

                @blk.sync
                def _(sy, sap=sap, rc=rc, co=co, wd_=wd_, wbuf=wbuf, bt=bv3["t"], uw=wbusy[par]):
                    sy.wait_ge(e.t.s, bt)
                    sy.wait_ge(e.v.s, uw)
                    e.ld.inc(sy.dma_start(out=wbuf[:, 0:wd_],
                                          in_=sap[rc * 128:(rc + 1) * 128, co:co + wd_]), 16)
                wl = e.ld.n

                @blk.vector
                def _(v, wl=wl, wbuf=wbuf, wd_=wd_, wscol=wscol, wsi=wsi):
                    v.wait_ge(e.ld.s, wl)
                    e.v.inc(v.tensor_reduce(wscol[:, wsi:wsi + 1], wbuf[:, 0:wd_],
                                            AX.X, ALU.max, apply_absolute_value=True))
                wbusy[par] = e.v.n

            @blk.vector
            def _(v):
                v.wait_ge(e.v.s, e.v.n)
                v.tensor_tensor(wsd[:, :], wsd[:, :], wsd2[:, :], ALU.max)
                e.v.inc(v.tensor_tensor(wsd[:, :], wsd[:, :], wsd3[:, :], ALU.max))
                v.wait_ge(e.v.s, e.v.n)
                v.reciprocal(winv_g[:, :], wsg[:, :])
                v.reciprocal(winv_u[:, :], wsu[:, :])
                e.v.inc(v.reciprocal(winv_d[:, :], wsd[:, :]))
                v.wait_ge(e.v.s, e.v.n)
            winv_v = e.v.n

            WINV = {id(wg_in): winv_g, id(wu_in): winv_u, id(wd_in): winv_d}
            wbusy = [winv_v, winv_v]
            ubusy = [winv_v, winv_v]
            tr_prev = e.sc.n
            for ui, (sap, rc, co, wd_, dstT, dstr, _ws) in enumerate(units):
                par = ui % 2
                wbuf = (qstg, qu)[par]
                ups = (e.psY, e.psY2)[par]

                @blk.sync
                def _(sy, sap=sap, rc=rc, co=co, wd_=wd_, wbuf=wbuf, uw=wbusy[par]):
                    sy.wait_ge(e.v.s, uw)
                    e.ld.inc(sy.dma_start(out=wbuf[:, 0:wd_],
                                          in_=sap[rc * 128:(rc + 1) * 128, co:co + wd_]), 16)
                wl = e.ld.n

                @blk.vector
                def _(v, wl=wl, wbuf=wbuf, ups=ups, wd_=wd_, rc=rc, winv=WINV[id(sap)],
                      trp=tr_prev):
                    v.wait_ge(e.ld.s, wl)
                    v.wait_ge(e.sc.s, trp)
                    v.tensor_scalar(ups[:, 0:wd_], wbuf[:, 0:wd_], winv[:, rc:rc + 1], None, ALU.mult)
                    v.tensor_scalar(ups[:, 0:wd_], ups[:, 0:wd_], 127.0, 0.5, ALU.mult, ALU.add)
                    v.tensor_copy(qk[:, 0:wd_], ups[:, 0:wd_])
                    v.tensor_copy(wbuf[:, 0:wd_], qk[:, 0:wd_])
                    v.tensor_tensor(ups[:, 0:wd_], ups[:, 0:wd_], wbuf[:, 0:wd_], ALU.subtract)
                    v.tensor_scalar(ups[:, 0:wd_], ups[:, 0:wd_], 0.0, None, ALU.is_lt)
                    v.tensor_tensor(qb[:, 0:wd_], wbuf[:, 0:wd_], ups[:, 0:wd_], ALU.subtract)
                    v.tensor_scalar(qb[:, 0:wd_], qb[:, 0:wd_], -128.0, None, ALU.max)
                    e.v.inc(v.tensor_scalar(qb[:, 0:wd_], qb[:, 0:wd_], 127.0, None, ALU.min))
                qv = e.v.n
                wbusy[par] = qv
                ubusy[par] = qv

                @blk.scalar
                def _(s2, rc=rc, co=co, qv=qv, dstT=dstT, wd_=wd_, dstr=dstr):
                    s2.wait_ge(e.v.s, qv)
                    for tc in range(wd_ // 128):
                        e.sc.inc(s2.dma_start(
                            out=dstT[:, (co // 128 + tc) * dstr + rc * 128:(co // 128 + tc) * dstr + (rc + 1) * 128],
                            in_=qb[:, tc * 128:(tc + 1) * 128], transpose=True), 16)
                tr_prev = e.sc.n
            wq_done_ld = e.ld.n
            wq_done_sc = e.sc.n
            wq_done_v = e.v.n

            @blk.sync
            def _(sy, bt=bv3["t"]):
                sy.wait_ge(e.t.s, bt)
                e.ld.inc(sy.dma_start(out=p4row[0:1, 0:4], in_=l1s_in), 16)
            p4l = e.ld.n

            @blk.scalar
            def _(s):
                s.wait_ge(e.ld.s, p4l)
                e.s.inc(s.activation(p4row[0:1, 4:5], p4row[0:1, 2:3], AF.Sigmoid))
                s.wait_ge(e.s.s, e.s.n)
                e.s.inc(s.activation(p4row[0:1, 5:6], p4row[0:1, 3:4], AF.Sigmoid))
                s.wait_ge(e.s.s, e.s.n)
            p4s = e.s.n

            @blk.vector
            def _(v, bv=bv3["v"]):
                v.wait_ge(e.v.s, bv)
                v.wait_ge(e.s.s, p4s)
                e.v.inc(v.tensor_scalar(var4[:, :], var4[:, :], 1.0 / D, None, ALU.mult))
            v4a = e.v.n

            @blk.scalar
            def _(s):
                s.wait_ge(e.v.s, v4a)
                e.s.inc(s.activation(var4[:, :], var4[:, :], AF.Sqrt, bias=e.epsln[:, :]))
            v4b = e.s.n

            @blk.vector
            def _(v):
                v.wait_ge(e.s.s, v4b)
                e.v.inc(v.reciprocal(rs4[:, :], var4[:, :]))
                v.wait_ge(e.v.s, e.v.n)
                e.v.inc(v.tensor_tensor(cmx4[:, :], cmx4[:, :], rs4[:, :], ALU.mult))
                v.wait_ge(e.v.s, e.v.n)
                e.v.inc(v.tensor_reduce(hmax[:, :], cmx4[:, :], AX.X, ALU.max))
            v4c = e.v.n

            @blk.gpsimd
            def _(g):
                g.wait_ge(e.v.s, v4c)
                e.gq.inc(g.dma_start(out=colbuf[0:1, 0:128], in_=hmax[:, 0:1]), 16)
            c1g = e.gq.n

            @blk.sync
            def _(sy):
                sy.wait_ge(e.gq.s, c1g)
                e.ld.inc(sy.dma_start(out=qstg[0:1, 0:128], in_=colbuf[0:1, 0:128]), 16)
            c1l = e.ld.n

            @blk.vector
            def _(v):
                v.wait_ge(e.ld.s, c1l)
                v.wait_ge(e.v.s, wq_done_v)
                e.v.inc(v.tensor_reduce(qstg[0:1, 128:129], qstg[0:1, 0:128], AX.X, ALU.max))
            c1v = e.v.n

            @blk.gpsimd
            def _(g):
                g.wait_ge(e.v.s, c1v)
                e.gq.inc(g.dma_start(out=cc_in1.ap(), in_=qstg[0:1, 128:129]), 16)
                g.wait_ge(e.gq.s, e.gq.n)
                e.cc.inc(g.collective_compute("AllReduce", ALU.max,
                                              replica_groups=[list(range(B))],
                                              ins=[cc_in1.ap().opt()],
                                              outs=[cc_out1.ap().opt()]))
                g.wait_ge(e.cc.s, e.cc.n)
                e.gq.inc(g.dma_start(out=qstg[0:1, 130:131], in_=cc_out1.ap()), 16)
            cc1_g = e.gq.n

            @blk.vector
            def _(v):
                v.wait_ge(e.gq.s, cc1_g)
                e.v.inc(v.tensor_tensor(qstg[0:1, 131:132], qstg[0:1, 130:131], p4row[0:1, 4:5], ALU.mult))
                v.wait_ge(e.v.s, e.v.n)
                e.v.inc(v.tensor_scalar(qstg[0:1, 131:132], qstg[0:1, 131:132], 1.0 / 127.0, None, ALU.mult))
                v.wait_ge(e.v.s, e.v.n)
                e.v.inc(v.reciprocal(qstg[0:1, 132:133], qstg[0:1, 131:132]))
                v.wait_ge(e.v.s, e.v.n)
            sc1_v = e.v.n

            @blk.gpsimd
            def _(g):
                g.wait_ge(e.v.s, sc1_v)
                e.gq.inc(g.dma_start(out=scal_d[0:1, 0:4], in_=qstg[0:1, 130:134]), 16)
            sc1_g = e.gq.n

            @blk.sync
            def _(sy):
                sy.wait_ge(e.gq.s, sc1_g)
                e.ld.inc(sy.dma_start(out=mcol1[:, :], in_=scal_d[0:1, 2:3].broadcast_to([128, 1])), 16)
            sc1_b = e.ld.n

            @blk.vector
            def _(v):
                v.wait_ge(e.ld.s, sc1_b)
                e.v.inc(v.tensor_scalar(rsi4[:, :], rs4[:, :], mcol1[:, :], None, ALU.mult))
                v.wait_ge(e.v.s, e.v.n)
                e.v.inc(v.tensor_tensor(sb4[:, :], m4[:, :], rsi4[:, :], ALU.mult))
                v.wait_ge(e.v.s, e.v.n)
                e.v.inc(v.tensor_scalar(sb4[:, :], sb4[:, :], -1.0, 0.5, ALU.mult, ALU.add))
                v.wait_ge(e.v.s, e.v.n)
            sc2_v = e.v.n

            @blk.sync
            def _(sy):
                sy.wait_ge(e.gq.s, sc1_g)
                e.ld.inc(sy.dma_start(out=mcol2[:, :], in_=scal_d[0:1, 1:2].broadcast_to([128, 1])), 16)
            scx_b = e.ld.n

            @blk.vector
            def _(v):
                v.wait_ge(e.ld.s, scx_b)
                v.wait_ge(e.v.s, sc2_v)
                e.v.inc(v.tensor_scalar(wsgx[:, :], wsg[:, :], mcol2[:, :], None, ALU.mult))
                e.v.inc(v.tensor_scalar(wsux[:, :], wsu[:, :], mcol2[:, :], None, ALU.mult))
                v.wait_ge(e.v.s, e.v.n)
                e.v.inc(v.tensor_scalar(wsgx[:, :], wsgx[:, :], 1.0 / 127.0, None, ALU.mult))
                e.v.inc(v.tensor_scalar(wsux[:, :], wsux[:, :], 1.0 / 127.0, None, ALU.mult))
                v.wait_ge(e.v.s, e.v.n)
            wsx_v = e.v.n

            @blk.vector
            def _(v):
                v.wait_ge(e.v.s, wsx_v)
                e.v.inc(v.memset(hmax[:, :], 0.0))
            hm0 = e.v.n

            h_store = 0
            for gi in range(T // 512):
                for cg in range(4):
                    ci = gi * 4 + cg

                    @blk.gpsimd
                    def _(g, ci=ci, hm0=hm0):
                        g.wait_ge(e.v.s, e.v.n)
                        g.wait_ge(e.sc.s, e.sc.n)
                        e.gq.inc(g.dma_start(out=x3th[:, :], in_=x3h_d[ci * 128:(ci + 1) * 128, :]), 16)
                        e.gq.inc(g.dma_start(out=x3tl[:, :], in_=x3l_d[ci * 128:(ci + 1) * 128, :]), 16)
                    xl_ = e.gq.n

                    @blk.vector
                    def _(v, ci=ci, xl_=xl_):
                        v.wait_ge(e.gq.s, xl_)
                        e.v.inc(v.tensor_tensor(x3f4[:, :], x3th[:, :], x3tl[:, :], ALU.add))
                        v.wait_ge(e.v.s, e.v.n)
                        e.v.inc(v.tensor_scalar(scr4[:, :], x3f4[:, :], rsi4[:, ci:ci + 1],
                                                sb4[:, ci:ci + 1], ALU.mult, ALU.add))
                        v.wait_ge(e.v.s, e.v.n)
                        v.tensor_copy(qk[:, 0:D], scr4[:, :])
                        e.v.inc(v.tensor_copy(x3f4[:, :], qk[:, 0:D]))
                        v.wait_ge(e.v.s, e.v.n)
                        v.tensor_tensor(scr4[:, :], scr4[:, :], x3f4[:, :], ALU.subtract)
                        e.v.inc(v.tensor_scalar(scr4[:, :], scr4[:, :], 0.0, None, ALU.is_lt))
                        v.wait_ge(e.v.s, e.v.n)
                        v.tensor_tensor(qb[:, 0:D], x3f4[:, :], scr4[:, :], ALU.subtract)
                        v.tensor_scalar(qb[:, 0:D], qb[:, 0:D], -128.0, None, ALU.max)
                        e.v.inc(v.tensor_scalar(qb[:, 0:D], qb[:, 0:D], 127.0, None, ALU.min))
                    kx_v = e.v.n

                    @blk.scalar
                    def _(s2, cg=cg, kx_v=kx_v):
                        s2.wait_ge(e.v.s, kx_v)
                        s2.wait_ge(e.t.s, e.t.n)
                        for dc in range(DC):
                            e.sc.inc(s2.dma_start(out=kxf[:, dc * 512 + cg * 128:dc * 512 + (cg + 1) * 128],
                                                  in_=qb[:, dc * 128:(dc + 1) * 128], transpose=True), 16)
                kx_t = e.sc.n

                if dev_dump == "p4" and gi == 0:
                    @blk.gpsimd
                    def _(g, kx_t=kx_t):
                        g.wait_ge(e.sc.s, kx_t)
                        e.fin.inc(g.dma_start(out=dbg[0:128, 0:4096], in_=kxf[:, :]), 16)

                for ic in range(IC):
                    @blk.tensor
                    def _(t, ic=ic, kx_t=kx_t, wq=wq_done_sc, wql=wq_done_ld):
                        t.wait_ge(e.sc.s, max(kx_t, wq))
                        t.wait_ge(e.ld.s, wql)
                        t.wait_ge(e.s.s, e.s.n)
                        t.wait_ge(e.v.s, e.v.n)
                        for dc in range(DC):
                            ins = t.matmul(e.psA[:, 0:512], kwgT[:, dc * DFF + ic * 128:dc * DFF + (ic + 1) * 128],
                                           kxf[:, dc * 512:(dc + 1) * 512], start=(dc == 0), stop=(dc == DC - 1))
                        e.t.inc(ins)
                        for dc in range(DC):
                            ins = t.matmul(e.psB[:, 0:512], kwuT[:, dc * DFF + ic * 128:dc * DFF + (ic + 1) * 128],
                                           kxf[:, dc * 512:(dc + 1) * 512], start=(dc == 0), stop=(dc == DC - 1))
                        e.t.inc(ins)
                    gu_t = e.t.n

                    @blk.scalar
                    def _(s, ic=ic, gu_t=gu_t):
                        s.wait_ge(e.t.s, gu_t)
                        e.s.inc(s.activation(sgt[:, :], e.psA[:, 0:512], AF.Silu, scale=wsgx[:, ic:ic + 1]))
                    gu_s = e.s.n

                    @blk.vector
                    def _(v, ic=ic, gu_s=gu_s, gu_t=gu_t):
                        v.wait_ge(e.t.s, gu_t)
                        v.wait_ge(e.gq.s, e.gq.n)
                        e.v.inc(v.tensor_scalar(sut[:, :], e.psB[:, 0:512], wsux[:, ic:ic + 1], None, ALU.mult))
                        v.wait_ge(e.s.s, gu_s)
                        e.v.inc(v.tensor_tensor(htile[:, :], sgt[:, :], sut[:, :], ALU.mult))
                        v.wait_ge(e.v.s, e.v.n)
                        e.v.inc(v.tensor_reduce(hab[:, ic:ic + 1], htile[:, :], AX.X, ALU.max,
                                                apply_absolute_value=True))
                    h_v = e.v.n

                    @blk.gpsimd
                    def _(g, ic=ic, gi=gi, h_v=h_v):
                        g.wait_ge(e.v.s, h_v)
                        e.gq.inc(g.dma_start(out=h_sp[ic * 128:(ic + 1) * 128, gi * 512:(gi + 1) * 512],
                                             in_=htile[:, :]), 16)
                        if dev_dump == "p4" and gi == 0 and ic == 0:
                            e.fin.inc(g.dma_start(out=dbg[128:256, 0:512], in_=sgt[:, :]), 16)
                            e.fin.inc(g.dma_start(out=dbg[128:256, 512:1024], in_=sut[:, :]), 16)
                            e.fin.inc(g.dma_start(out=dbg[128:256, 1024:1536], in_=htile[:, :]), 16)
                h_store = e.gq.n

                @blk.vector
                def _(v, h_store=h_store):
                    v.wait_ge(e.v.s, e.v.n)
                    e.v.inc(v.tensor_reduce(qu[:, 0:1], hab[:, :], AX.X, ALU.max))
                    v.wait_ge(e.v.s, e.v.n)
                    e.v.inc(v.tensor_tensor(hmax[:, :], hmax[:, :], qu[:, 0:1], ALU.max))
            hmax_v = e.v.n

            @blk.gpsimd
            def _(g):
                g.wait_ge(e.v.s, hmax_v)
                e.gq.inc(g.dma_start(out=colbuf[0:1, 128:256], in_=hmax[:, 0:1]), 16)
            c2g = e.gq.n

            @blk.sync
            def _(sy):
                sy.wait_ge(e.gq.s, c2g)
                e.ld.inc(sy.dma_start(out=qstg[0:1, 0:128], in_=colbuf[0:1, 128:256]), 16)
            c2l = e.ld.n

            @blk.vector
            def _(v):
                v.wait_ge(e.ld.s, c2l)
                e.v.inc(v.tensor_reduce(qstg[0:1, 128:129], qstg[0:1, 0:128], AX.X, ALU.max))
            c2v = e.v.n

            @blk.gpsimd
            def _(g):
                g.wait_ge(e.v.s, c2v)
                e.gq.inc(g.dma_start(out=cc_in2.ap(), in_=qstg[0:1, 128:129]), 16)
                g.wait_ge(e.gq.s, e.gq.n)
                e.cc.inc(g.collective_compute("AllReduce", ALU.max,
                                              replica_groups=[list(range(B))],
                                              ins=[cc_in2.ap().opt()],
                                              outs=[cc_out2.ap().opt()]))
                g.wait_ge(e.cc.s, e.cc.n)
                e.gq.inc(g.dma_start(out=qstg[0:1, 130:131], in_=cc_out2.ap()), 16)
            cc2_g = e.gq.n

            @blk.vector
            def _(v):
                v.wait_ge(e.gq.s, cc2_g)
                e.v.inc(v.tensor_tensor(qstg[0:1, 131:132], qstg[0:1, 130:131], p4row[0:1, 5:6], ALU.mult))
                v.wait_ge(e.v.s, e.v.n)
                e.v.inc(v.tensor_scalar(qstg[0:1, 131:132], qstg[0:1, 131:132], 1.0 / 127.0, None, ALU.mult))
                v.wait_ge(e.v.s, e.v.n)
                e.v.inc(v.reciprocal(qstg[0:1, 132:133], qstg[0:1, 131:132]))
                v.wait_ge(e.v.s, e.v.n)
            sc3_v = e.v.n

            @blk.gpsimd
            def _(g):
                g.wait_ge(e.v.s, sc3_v)
                e.gq.inc(g.dma_start(out=scal_d[0:1, 4:8], in_=qstg[0:1, 130:134]), 16)
            sc3_g = e.gq.n

            @blk.sync
            def _(sy):
                sy.wait_ge(e.gq.s, sc3_g)
                e.ld.inc(sy.dma_start(out=mcol2[:, :], in_=scal_d[0:1, 6:7].broadcast_to([128, 1])), 16)
                e.ld.inc(sy.dma_start(out=mcol1[:, :], in_=scal_d[0:1, 5:6].broadcast_to([128, 1])), 16)
            sc3_b = e.ld.n

            @blk.gpsimd
            def _(g, wq=wq_done_v):
                g.wait_ge(e.v.s, wq)
                for hc in range(DC):
                    e.gq.inc(g.dma_start(out=rowb[4:5, hc * 128:(hc + 1) * 128],
                                         in_=wsd[:, hc:hc + 1]), 16)
            wsd_g = e.gq.n

            @blk.sync
            def _(sy):
                sy.wait_ge(e.gq.s, wsd_g)
                e.ld.inc(sy.dma_start(out=wsdq[:, :], in_=rowb[4:5, 0:D].broadcast_to([128, D])), 16)
            wsd_b = e.ld.n

            @blk.vector
            def _(v):
                v.wait_ge(e.ld.s, wsd_b)
                v.wait_ge(e.ld.s, sc3_b)
                v.tensor_scalar(wsdq[:, :], wsdq[:, :], mcol1[:, :], None, ALU.mult)
                e.v.inc(v.tensor_scalar(wsdq[:, :], wsdq[:, :], 1.0 / 127.0, None, ALU.mult))
            wsdq_v = e.v.n

            out_st = 0
            for ci in range(NCH):
                @blk.gpsimd
                def _(g, ci=ci, wsdq_v=wsdq_v):
                    if ci == 0:
                        g.wait_ge(e.v.s, wsdq_v)
                    g.wait_ge(e.v.s, e.v.n)
                    g.wait_ge(e.t.s, e.t.n)
                    src3 = h_sp[:, ci * 128:(ci + 1) * 128].rearrange("(c p) w -> p c w", p=128)
                    dst3 = qstg[:, 0:IC * 128].rearrange("p (c w) -> p c w", w=128)
                    e.gq.inc(g.dma_start(out=dst3, in_=src3), 16)
                    e.gq.inc(g.dma_start(out=x3th[:, :], in_=x3h_d[ci * 128:(ci + 1) * 128, :]), 16)
                    e.gq.inc(g.dma_start(out=x3tl[:, :], in_=x3l_d[ci * 128:(ci + 1) * 128, :]), 16)
                hr_g = e.gq.n

                @blk.vector
                def _(v, hr_g=hr_g):
                    v.wait_ge(e.gq.s, hr_g)
                    e.v.inc(v.tensor_scalar(qu[:, 0:IC * 128], qstg[:, 0:IC * 128], mcol2[:, :], 0.5, ALU.mult, ALU.add))
                    v.wait_ge(e.v.s, e.v.n)
                    v.tensor_copy(qk[:, 0:IC * 128], qu[:, 0:IC * 128])
                    e.v.inc(v.tensor_copy(qstg[:, 0:IC * 128], qk[:, 0:IC * 128]))
                    v.wait_ge(e.v.s, e.v.n)
                    v.tensor_tensor(qu[:, 0:IC * 128], qu[:, 0:IC * 128], qstg[:, 0:IC * 128], ALU.subtract)
                    e.v.inc(v.tensor_scalar(qu[:, 0:IC * 128], qu[:, 0:IC * 128], 0.0, None, ALU.is_lt))
                    v.wait_ge(e.v.s, e.v.n)
                    v.tensor_tensor(qb[:, 0:IC * 128], qstg[:, 0:IC * 128], qu[:, 0:IC * 128], ALU.subtract)
                    v.tensor_scalar(qb[:, 0:IC * 128], qb[:, 0:IC * 128], -128.0, None, ALU.max)
                    e.v.inc(v.tensor_scalar(qb[:, 0:IC * 128], qb[:, 0:IC * 128], 127.0, None, ALU.min))
                kh_v = e.v.n

                @blk.tensor
                def _(t, kh_v=kh_v):
                    t.wait_ge(e.v.s, kh_v)
                    for nh in range(2):
                        for ic in range(IC):
                            ins = t.matmul(e.psY[:, nh * 512:(nh + 1) * 512],
                                           qb[:, ic * 128:(ic + 1) * 128],
                                           kwdT[:, ic * D + nh * 512:ic * D + nh * 512 + 512],
                                           start=(ic == 0), stop=(ic == IC - 1))
                        e.t.inc(ins)
                dn_t = e.t.n

                @blk.vector
                def _(v, dn_t=dn_t, ci=ci):
                    v.wait_ge(e.t.s, dn_t)
                    v.wait_ge(e.fin.s, e.fin.n)
                    v.tensor_tensor(scr4[:, :], e.psY[:, 0:1024], wsdq[:, :], ALU.mult)
                    v.tensor_tensor(x3f4[:, :], x3th[:, :], x3tl[:, :], ALU.add)
                    e.v.inc(v.tensor_tensor(scr4[:, :], scr4[:, :], x3f4[:, :], ALU.add))
                o_v = e.v.n

                @blk.gpsimd
                def _(g, ci=ci, o_v=o_v):
                    g.wait_ge(e.v.s, o_v)
                    e.fin.inc(g.dma_start(out=out[ci * 128:(ci + 1) * 128, :], in_=scr4[:, :]), 16)

            if dev_dump == "p4":
                @blk.gpsimd
                def _(g):
                    g.wait_ge(e.fin.s, e.fin.n)
                    g.wait_ge(e.gq.s, e.gq.n)
                    g.wait_ge(e.v.s, e.v.n)
                    e.fin.inc(g.dma_start(out=dbg[256:384, 0:32], in_=m4[:, :]), 16)
                    e.fin.inc(g.dma_start(out=dbg[256:384, 32:64], in_=rs4[:, :]), 16)
                    e.fin.inc(g.dma_start(out=dbg[256:384, 64:86], in_=wsg[:, :]), 16)
                    e.fin.inc(g.dma_start(out=dbg[256:384, 86:108], in_=wsu[:, :]), 16)
                    e.fin.inc(g.dma_start(out=dbg[256:384, 108:116], in_=wsd[:, :]), 16)
                    e.fin.inc(g.dma_start(out=dbg[256:384, 116:138], in_=wsgx[:, :]), 16)
                    e.fin.inc(g.dma_start(out=dbg[256:384, 138:160], in_=wsux[:, :]), 16)
                    e.fin.inc(g.dma_start(out=dbg[256:257, 160:168], in_=scal_d[0:1, 0:8]), 16)
                    e.fin.inc(g.dma_start(out=dbg[256:384, 168:169], in_=hmax[:, :], allow_slow_non_contiguous=True), 16)
                    e.fin.inc(g.dma_start(out=dbg[384:512, 0:2816], in_=kwgT[:, 0:2816]), 16)
                    e.fin.inc(g.dma_start(out=dbg[384:512, 2816:2816 + 1024], in_=wsdq[:, :]), 16)
                    g.wait_ge(e.fin.s, e.fin.n)

            @blk.gpsimd
            def _(g):
                g.wait_ge(e.fin.s, e.fin.n)
            p4.close()

    return nc



def _constants():
    q = np.arange(128, dtype=np.float64)
    diff = q[None, :] - q[:, None] - 1.0
    c1t = np.where(q[None, :] > q[:, None], diff, BIGNEG).astype(np.float32)
    c2t = np.where(q[None, :] > q[:, None], np.maximum(diff, 0.0), BIGNEG).astype(np.float32)
    return {
        "c1t": np.ascontiguousarray(c1t),
        "c2t": np.ascontiguousarray(c2t),
        "eye": np.eye(128, dtype=np.float32),
        "iota_i": np.arange(128, dtype=np.float32).reshape(128, 1),
        "iota_r": (127.0 - np.arange(128, dtype=np.float32)).reshape(128, 1),
        "ones_in": np.ones((128, 1), dtype=np.float32),
    }


def make_in_maps(ii):
    consts = _constants()
    maa_all = np.stack([ii["L0_attn_time_maa_k"].reshape(-1),
                        ii["L0_attn_time_maa_v"].reshape(-1),
                        ii["L0_attn_time_maa_r"].reshape(-1),
                        ii["L0_attn_time_maa_g"].reshape(-1)], axis=0)
    maa_dev = np.ascontiguousarray(
        maa_all.reshape(4, DC, 128).transpose(2, 0, 1).reshape(128, 4 * DC)).astype(np.float32)
    common = {
        "WrT": np.ascontiguousarray(ii["L0_attn_Wr"].T),
        "WkT": np.ascontiguousarray(ii["L0_attn_Wk"].T),
        "WvT": np.ascontiguousarray(ii["L0_attn_Wv"].T),
        "WgT": np.ascontiguousarray(ii["L0_attn_Wg"].T),
        "WoT": np.ascontiguousarray(ii["L0_attn_Wo"].T),
        "maa": maa_dev,
        "decay_row": np.ascontiguousarray(ii["L0_attn_time_decay"].reshape(1, H)),
        "faaaa_row": np.ascontiguousarray(ii["L0_attn_time_faaaa"].reshape(1, H)),
        "lnw_row": np.ascontiguousarray(ii["L0_attn_ln_w"].reshape(1, D)),
        "lnb_row": np.ascontiguousarray(ii["L0_attn_ln_b"].reshape(1, D)),
        "theta": np.ascontiguousarray(ii["L0_ffn_theta"]),
        "theta_n": np.ascontiguousarray(ii["L0_ffn_theta_noise"]),
        "phi": np.ascontiguousarray(ii["L0_ffn_phi"]),
        "phi_n": np.ascontiguousarray(ii["L0_ffn_phi_noise"]),
        "WwT": np.ascontiguousarray(ii["L1_attn_W_write"].T),
        "WrdT": np.ascontiguousarray(ii["L1_attn_W_read"].T),
        "l1_scal": np.array([[float(ii["L1_attn_decay"]), float(ii["L1_attn_log_alpha"]),
                              float(np.asarray(ii["L1_ffn_clip_x"]).reshape(-1)[0]),
                              float(np.asarray(ii["L1_ffn_clip_down"]).reshape(-1)[0])]],
                            dtype=np.float32),
        "w_gate": np.ascontiguousarray(ii["L1_ffn_w_gate"]),
        "w_up": np.ascontiguousarray(ii["L1_ffn_w_up"]),
        "w_down": np.ascontiguousarray(ii["L1_ffn_w_down"]),
        **consts,
    }
    x = ii["L0_attn_x"]
    in_maps = []
    for b in range(B):
        m = dict(common)
        m["xin"] = np.ascontiguousarray(x[b].T)
        in_maps.append(m)
    return in_maps


def kernel(**inputs):
    ii = {k: np.asarray(v) for k, v in inputs.items()}
    nc = build(N_PHASES)
    in_maps = make_in_maps(ii)
    res = run_bass_kernel_spmd(nc, in_maps, core_ids=list(range(B)), trace_cores=[0])
    return np.stack([np.asarray(res.results[b]["out"]) for b in range(B)], axis=0)


# version 10 (final)
# speedup vs baseline: 1.0462x; 1.0293x over previous
import numpy as np
import ml_dtypes
from contextlib import ExitStack
import concourse.bass as bass
import concourse.mybir as mybir
from concourse.bass_utils import run_bass_kernel_spmd

F32 = mybir.dt.float32
BF16 = mybir.dt.bfloat16
I32 = mybir.dt.int32
AF = mybir.ActivationFunctionType
ALU = mybir.AluOpType
AX = mybir.AxisListType

B, T, D, H, HN, DFF = 8, 4096, 1024, 16, 64, 2816
DC = D // 128
Q = 128
NCH = T // Q
TT = 256
NT = T // TT
CPT = TT // Q
IC = DFF // 128
EPS_LN = 1e-5
EPS_GN = 1e-5 * 64
BIGNEG = 1.0e9
HALFPI = 1.5707963267948966

N_PHASES = 4


class Sem:
    def __init__(self, nc, ctx, name):
        self.s = ctx.enter_context(nc.semaphore(name))
        self.n = 0

    def inc(self, instr, k=1):
        instr.then_inc(self.s, k)
        self.n += k
        return self.n


class E:
    pass


def build(n_phases=N_PHASES, dev_dump=None):
    nc = bass.Bass("TRN2", target_bir_lowering=False)
    e = E()
    e.nc = nc

    xin = nc.dram_tensor("xin", [D, T], F32, kind="ExternalInput").ap()
    WT = {}
    for nm in ["WrT", "WkT", "WvT", "WgT", "WoT"]:
        WT[nm] = nc.dram_tensor(nm, [D, D], F32, kind="ExternalInput").ap()
    maa_in = nc.dram_tensor("maa", [128, DC * 4], F32, kind="ExternalInput").ap()
    decay_row = nc.dram_tensor("decay_row", [1, H], F32, kind="ExternalInput").ap()
    faaaa_row = nc.dram_tensor("faaaa_row", [1, H], F32, kind="ExternalInput").ap()
    lnw_row = nc.dram_tensor("lnw_row", [1, D], F32, kind="ExternalInput").ap()
    lnb_row = nc.dram_tensor("lnb_row", [1, D], F32, kind="ExternalInput").ap()
    c1t_in = nc.dram_tensor("c1t", [128, 128], F32, kind="ExternalInput").ap()
    c2t_in = nc.dram_tensor("c2t", [128, 128], F32, kind="ExternalInput").ap()
    eye_in = nc.dram_tensor("eye", [128, 128], F32, kind="ExternalInput").ap()
    iota_i_in = nc.dram_tensor("iota_i", [128, 1], F32, kind="ExternalInput").ap()
    iota_r_in = nc.dram_tensor("iota_r", [128, 1], F32, kind="ExternalInput").ap()
    ones_in = nc.dram_tensor("ones_in", [128, 1], F32, kind="ExternalInput").ap()
    theta_in = nc.dram_tensor("theta", [D, D], F32, kind="ExternalInput").ap()
    theta_n_in = nc.dram_tensor("theta_n", [D, D], F32, kind="ExternalInput").ap()
    phi_in = nc.dram_tensor("phi", [D, D], F32, kind="ExternalInput").ap()
    phi_n_in = nc.dram_tensor("phi_n", [D, D], F32, kind="ExternalInput").ap()
    WwT_in = nc.dram_tensor("WwT", [D, D], F32, kind="ExternalInput").ap()
    WrdT_in = nc.dram_tensor("WrdT", [D, D], F32, kind="ExternalInput").ap()
    l1s_in = nc.dram_tensor("l1_scal", [1, 4], F32, kind="ExternalInput").ap()
    wg_in = nc.dram_tensor("w_gate", [DFF, D], F32, kind="ExternalInput").ap()
    wu_in = nc.dram_tensor("w_up", [DFF, D], F32, kind="ExternalInput").ap()
    wd_in = nc.dram_tensor("w_down", [D, DFF], F32, kind="ExternalInput").ap()

    out = nc.dram_tensor("out", [T, D], F32, kind="ExternalOutput").ap()
    if dev_dump:
        dbg = nc.dram_tensor("dbg", [D, T], F32, kind="ExternalOutput").ap()

    xs1 = nc.dram_tensor("xs1", [D, T], F32).ap()
    xs2 = nc.dram_tensor("xs2", [D, T], F32).ap()
    x3_d = nc.dram_tensor("x3_d", [T, D], F32).ap()
    h_sp = nc.dram_tensor("h_spill", [DFF, T], F32).ap()
    rowb = nc.dram_tensor("rowbuf", [16, T], F32).ap()
    wsplit = {}
    for nm in ["k", "v", "g"]:
        wsplit[nm] = nc.dram_tensor("ws_" + nm, [2, 2, 128, DC * 512], BF16).ap()
    for nm in ["r", "o"]:
        wsplit[nm] = nc.dram_tensor("ws_" + nm, [2, DC, 128, DC * 128], BF16).ap()
    cc_in1 = nc.dram_tensor("cc_in1", [1, 1], F32)
    cc_out1 = nc.dram_tensor("cc_out1", [1, 1], F32)
    cc_in2 = nc.dram_tensor("cc_in2", [1, 1], F32)
    cc_out2 = nc.dram_tensor("cc_out2", [1, 1], F32)
    scal_d = nc.dram_tensor("scal_d", [1, 16], F32).ap()

    stk = ExitStack()
    with stk:
        def sb(nm, shape, dt=F32, st=stk):
            return st.enter_context(nc.sbuf_tensor(nm, shape, dt))

        e.ones = sb("ones_c", [128, 1])
        e.ioi = sb("ioi", [128, 1])
        e.ior = sb("ior", [128, 1])
        e.eye = sb("eye_t", [128, 128])
        e.maa = sb("maa_t", [128, DC * 4])
        e.Ecol = sb("Ecol", [128, H])
        e.Ucol = sb("Ucol", [128, H])
        e.wb = sb("wb_t", [128, H])
        e.wkd = sb("wkd_t", [128, H])
        e.wsd = sb("wsd_t", [128, H])
        e.scal = sb("scal", [128, 16])
        e.epsln = sb("epsln", [128, 1])
        e.epsgn = sb("epsgn", [128, 1])
        e.m4 = sb("m4_g", [128, NCH])
        e.var4 = sb("var4_g", [128, NCH])
        e.cmx4 = sb("cmx4_g", [128, NCH])
        e.row1 = sb("row1", [1, 512])
        e.ones_row = sb("ones_row", [1, 128])
        e.psA = stk.enter_context(nc.psum_tensor("psA", [128, 512], F32))
        e.psB = stk.enter_context(nc.psum_tensor("psB", [128, 512], F32))
        e.psY = stk.enter_context(nc.psum_tensor("psY", [128, 1024], F32))
        e.psY2 = stk.enter_context(nc.psum_tensor("psY2", [128, 1024], F32))
        e.psC = stk.enter_context(nc.psum_tensor("psC", [128, 512], F32))
        e.psD = stk.enter_context(nc.psum_tensor("psD", [128, 512], F32))

        e.wmatT = None
        e.lnwb = None
        e.lnbb = None
        e.ld = Sem(nc, stk, "ld")
        e.sc = Sem(nc, stk, "scq")
        e.gq = Sem(nc, stk, "gq")
        e.t = Sem(nc, stk, "tse")
        e.v = Sem(nc, stk, "vse")
        e.s = Sem(nc, stk, "sse")
        e.cc = Sem(nc, stk, "ccse")
        e.fin = Sem(nc, stk, "fin")

        blk = stk.enter_context(nc.Block())
        e.blk = blk

        def barrier_vals():
            return dict(ld=e.ld.n, sc=e.sc.n, gq=e.gq.n, t=e.t.n, v=e.v.n, s=e.s.n)

        def wait_barrier(engine, bv):
            engine.wait_ge(e.ld.s, bv["ld"])
            engine.wait_ge(e.sc.s, bv["sc"])
            engine.wait_ge(e.gq.s, bv["gq"])
            engine.wait_ge(e.t.s, bv["t"])
            engine.wait_ge(e.v.s, bv["v"])
            engine.wait_ge(e.s.s, bv["s"])

        psh = ExitStack()
        e.wmatT = sb("wmatT", [128, H * 128], st=psh)
        e.lnwb = sb("lnwb", [128, D], st=psh)
        e.lnbb = sb("lnbb", [128, D], st=psh)
        xt = sb("xt", [128, DC * TT], st=psh)
        ln = sb("ln", [128, DC * (TT + 1)], st=psh)
        sq = sb("sq", [128, DC * TT], st=psh)
        mxh = sb("mxh", [128, DC * TT], BF16, st=psh)
        mxl = sb("mxl", [128, DC * TT], BF16, st=psh)
        wstg = sb("wstg", [128, D], st=psh)
        wsth = sb("wsth", [128, D], BF16, st=psh)
        wstl = sb("wstl", [128, D], BF16, st=psh)
        rw_m = sb("rw_m", [1, TT], st=psh)
        rw_v = sb("rw_v", [1, TT], st=psh)
        rw_t = sb("rw_t", [1, TT], st=psh)
        rw_rs = sb("rw_rs", [1, TT], st=psh)
        rw_mrs = sb("rw_mrs", [1, TT], st=psh)

        def emit_ln(xtile, dst_ln, wait_sem, wait_val, roff):
            @blk.scalar
            def _(s):
                s.wait_ge(wait_sem.s, wait_val)
                for c in range(DC):
                    e.s.inc(s.activation(sq[:, c * TT:(c + 1) * TT], xtile[:, c * TT:(c + 1) * TT], AF.Square))
            sqd = e.s.n

            @blk.tensor
            def _(t):
                t.wait_ge(e.s.s, sqd)
                t.wait_ge(e.v.s, e.v.n)
                for c in range(DC):
                    ins = t.matmul(e.psC[0:1, 0:TT], e.ones[:, :], xtile[:, c * TT:(c + 1) * TT],
                                   start=(c == 0), stop=(c == DC - 1))
                e.t.inc(ins)
                for c in range(DC):
                    ins = t.matmul(e.psD[0:1, 0:TT], e.ones[:, :], sq[:, c * TT:(c + 1) * TT],
                                   start=(c == 0), stop=(c == DC - 1))
                e.t.inc(ins)
            mmd = e.t.n

            @blk.vector
            def _(v):
                v.wait_ge(e.t.s, mmd)
                v.tensor_scalar(rw_m[0:1, 0:TT], e.psC[0:1, 0:TT], 1.0 / D, None, ALU.mult)
                e.v.inc(v.tensor_scalar(rw_v[0:1, 0:TT], e.psD[0:1, 0:TT], 1.0 / D, None, ALU.mult))
                v.wait_ge(e.v.s, e.v.n)
                v.tensor_tensor(rw_t[0:1, 0:TT], rw_m[0:1, 0:TT], rw_m[0:1, 0:TT], ALU.mult)
                e.v.inc(v.tensor_tensor(rw_v[0:1, 0:TT], rw_v[0:1, 0:TT], rw_t[0:1, 0:TT], ALU.subtract))
            vd = e.v.n

            @blk.scalar
            def _(s):
                s.wait_ge(e.v.s, vd)
                e.s.inc(s.activation(rw_t[0:1, 0:TT], rw_v[0:1, 0:TT], AF.Sqrt, bias=e.epsln[0:1, :]))
            sd = e.s.n

            @blk.vector
            def _(v):
                v.wait_ge(e.s.s, sd)
                e.v.inc(v.reciprocal(rw_rs[0:1, 0:TT], rw_t[0:1, 0:TT]))
                v.wait_ge(e.v.s, e.v.n)
                e.v.inc(v.tensor_tensor(rw_mrs[0:1, 0:TT], rw_m[0:1, 0:TT], rw_rs[0:1, 0:TT], ALU.mult))
            vd2 = e.v.n

            @blk.tensor
            def _(t):
                t.wait_ge(e.v.s, vd2)
                e.t.inc(t.matmul(e.psC[:, 0:TT], e.ones_row[0:1, :], rw_rs[0:1, 0:TT],
                                 start=True, stop=True))
                e.t.inc(t.matmul(e.psD[:, 0:TT], e.ones_row[0:1, :], rw_mrs[0:1, 0:TT],
                                 start=True, stop=True))
            bd = e.t.n

            @blk.vector
            def _(v):
                v.wait_ge(e.t.s, bd)
                for c in range(DC):
                    v.tensor_tensor(dst_ln[:, c * (TT + 1) + 1:(c + 1) * (TT + 1)],
                                    xtile[:, c * TT:(c + 1) * TT], e.psC[:, 0:TT], ALU.mult)
                    e.v.inc(v.tensor_tensor(dst_ln[:, c * (TT + 1) + 1:(c + 1) * (TT + 1)],
                                            dst_ln[:, c * (TT + 1) + 1:(c + 1) * (TT + 1)],
                                            e.psD[:, 0:TT], ALU.subtract))
            return e.v.n



        @blk.sync
        def _(sy):
            e.ld.inc(sy.dma_start(out=e.ones[:, :], in_=ones_in), 16)
            e.ld.inc(sy.dma_start(out=e.ones_row[0:1, :], in_=ones_in), 16)
            e.ld.inc(sy.dma_start(out=e.ioi[:, :], in_=iota_i_in), 16)
            e.ld.inc(sy.dma_start(out=e.ior[:, :], in_=iota_r_in), 16)
            e.ld.inc(sy.dma_start(out=e.eye[:, :], in_=eye_in), 16)
            e.ld.inc(sy.dma_start(out=e.maa[:, :], in_=maa_in), 16)
            e.ld.inc(sy.dma_start(out=e.lnwb[:, :], in_=lnw_row.broadcast_to([128, D])), 16)
            e.ld.inc(sy.dma_start(out=e.lnbb[:, :], in_=lnb_row.broadcast_to([128, D])), 16)
            e.ld.inc(sy.dma_start(out=e.Ecol[:, 0:H], in_=decay_row.broadcast_to([128, H])), 16)
            e.ld.inc(sy.dma_start(out=e.Ucol[:, 0:H], in_=faaaa_row.broadcast_to([128, H])), 16)
            for hh in range(H):
                e.ld.inc(sy.dma_start(out=e.wmatT[:, hh * 128:(hh + 1) * 128], in_=c1t_in), 16)
        n0 = e.ld.n

        @blk.scalar
        def _(s):
            s.wait_ge(e.ld.s, n0)
            e.s.inc(s.activation(e.Ecol[:, 0:H], e.Ecol[:, 0:H], AF.Exp))
        n1 = e.s.n

        @blk.vector
        def _(v):
            v.wait_ge(e.s.s, n1)
            v.memset(e.epsln[:, :], EPS_LN)
            v.memset(e.epsgn[:, :], EPS_GN)
            for hh in range(H):
                v.tensor_scalar(e.wb[:, hh:hh + 1], e.ioi[:, :], e.Ecol[:, hh:hh + 1], None, ALU.mult)
                v.tensor_scalar(e.wkd[:, hh:hh + 1], e.ior[:, :], e.Ecol[:, hh:hh + 1], None, ALU.mult)
            e.v.inc(v.tensor_scalar(e.wsd[:, 0:H], e.Ecol[:, 0:H], float(Q), None, ALU.mult))
        n2 = e.v.n

        @blk.scalar
        def _(s):
            s.wait_ge(e.v.s, n2)
            s.activation(e.wb[:, 0:H], e.wb[:, 0:H], AF.Exp, scale=-1.0)
            s.activation(e.wkd[:, 0:H], e.wkd[:, 0:H], AF.Exp, scale=-1.0)
            e.s.inc(s.activation(e.wsd[:, 0:H], e.wsd[:, 0:H], AF.Exp, scale=-1.0))
        n3 = e.s.n

        @blk.vector
        def _(v):
            v.wait_ge(e.s.s, n3)
            for hh in range(H):
                e.v.inc(v.tensor_scalar(e.wmatT[:, hh * 128:(hh + 1) * 128],
                                        e.wmatT[:, hh * 128:(hh + 1) * 128],
                                        e.Ecol[:, hh:hh + 1], None, ALU.mult))
        n4 = e.v.n

        @blk.scalar
        def _(s):
            s.wait_ge(e.v.s, n4)
            for hh in range(H):
                e.s.inc(s.activation(e.wmatT[:, hh * 128:(hh + 1) * 128],
                                     e.wmatT[:, hh * 128:(hh + 1) * 128], AF.Exp, scale=-1.0))
        n5 = e.s.n

        @blk.vector
        def _(v):
            v.wait_ge(e.s.s, n5)
            for hh in range(H):
                e.v.inc(v.scalar_tensor_tensor(e.wmatT[:, hh * 128:(hh + 1) * 128],
                                               e.eye[:, :], e.Ucol[:, hh:hh + 1],
                                               e.wmatT[:, hh * 128:(hh + 1) * 128],
                                               ALU.mult, ALU.add))
        prep_v = e.v.n

        if n_phases >= 1:
            p1 = ExitStack()
            mxh2 = sb("mxh2", [128, DC * TT], BF16, st=p1)
            mxl2 = sb("mxl2", [128, DC * TT], BF16, st=p1)
            wkvg = sb("wkvg", [128, 2 * DC * 512], BF16, st=p1)
            wkvg2 = sb("wkvg2", [128, 2 * DC * 512], BF16, st=p1)
            wro = sb("wro", [128, 2 * DC * 128], BF16, st=p1)
            wro2 = sb("wro2", [128, 2 * DC * 128], BF16, st=p1)
            r_fm = sb("r_fm", [128, DC * TT], st=p1)
            k_tok = sb("k_tok", [128, CPT * D], st=p1)
            v_tok = sb("v_tok", [128, CPT * D], st=p1)
            g_tok = sb("g_tok", [128, CPT * D], st=p1)
            k_fm = sb("k_fm", [128, DC * 128], st=p1)
            kw_tok = sb("kw_tok", [128, D], st=p1)
            attw = sb("attw", [128, 128], st=p1)
            WKB = sb("WKB", [128, D], st=p1)
            WBB = sb("WBB", [128, D], st=p1)
            WSB = sb("WSB", [128, (H // 2) * HN], st=p1)
            attw2 = sb("attw2", [128, 128], st=p1)
            attw3 = sb("attw3", [128, 128], st=p1)
            state = sb("state", [128, (H // 2) * HN], st=p1)
            ysb = sb("ysb", [128, D], st=p1)
            st6 = sb("st6", [128, 6 * H], st=p1)
            gn_m = sb("gn_m", [128, H], st=p1)
            gn_v = sb("gn_v", [128, H], st=p1)
            gn_t = sb("gn_t", [128, H], st=p1)
            gn_rs = sb("gn_rs", [128, H], st=p1)
            zf = sb("zf", [128, D], st=p1)
            zh = sb("zh", [128, D], BF16, st=p1)
            zl = sb("zl", [128, D], BF16, st=p1)
            zfmh = sb("zfmh", [128, DC * TT], BF16, st=p1)
            zfml = sb("zfml", [128, DC * TT], BF16, st=p1)

            wstg2 = sb("wstg2", [128, D], st=p1)
            wsth2 = sb("wsth2", [128, D], BF16, st=p1)
            wstl2 = sb("wstl2", [128, D], BF16, st=p1)
            wsp_stg = [(wstg, wsth, wstl), (wstg2, wsth2, wstl2)]
            wsp_busy = [prep_v, prep_v]
            wsp_st = [0, 0]
            wu_i = 0
            for wi, nm in enumerate(["k", "v", "g", "r", "o"]):
                src = WT["W" + nm + "T"]
                for c in range(DC):
                    par = wu_i % 2
                    wu_i += 1
                    stg_, sth_, stl_ = wsp_stg[par]

                    @blk.sync
                    def _(sy, src=src, c=c, uw=wsp_busy[par], uw2=wsp_st[par], stg_=stg_):
                        sy.wait_ge(e.v.s, uw)
                        sy.wait_ge(e.gq.s, uw2)
                        e.ld.inc(sy.dma_start(out=stg_[:, :], in_=src[c * 128:(c + 1) * 128, :]), 16)
                    w_ld = e.ld.n

                    @blk.scalar
                    def _(s, w_ld=w_ld, sth_=sth_, stg_=stg_, uw2=wsp_st[par]):
                        s.wait_ge(e.ld.s, w_ld)
                        s.wait_ge(e.gq.s, uw2)
                        e.s.inc(s.activation(sth_[:, :], stg_[:, :], AF.Copy))
                    w_s = e.s.n

                    @blk.vector
                    def _(v, w_s=w_s, stl_=stl_, stg_=stg_, sth_=sth_):
                        v.wait_ge(e.s.s, w_s)
                        e.v.inc(v.tensor_tensor(stl_[:, :], stg_[:, :], sth_[:, :], ALU.subtract))
                    w_v = e.v.n
                    wsp_busy[par] = w_v

                    @blk.gpsimd
                    def _(g, nm=nm, c=c, w_v=w_v, wsth=sth_, wstl=stl_):
                        g.wait_ge(e.v.s, w_v)
                        if nm in ("k", "v", "g"):
                            for oh in range(2):
                                e.gq.inc(g.dma_start(out=wsplit[nm][0, oh, :, c * 512:(c + 1) * 512],
                                                     in_=wsth[:, oh * 512:(oh + 1) * 512]), 16)
                                e.gq.inc(g.dma_start(out=wsplit[nm][1, oh, :, c * 512:(c + 1) * 512],
                                                     in_=wstl[:, oh * 512:(oh + 1) * 512]), 16)
                        else:
                            for oc in range(DC):
                                e.gq.inc(g.dma_start(out=wsplit[nm][0, oc, :, c * 128:(c + 1) * 128],
                                                     in_=wsth[:, oc * 128:(oc + 1) * 128]), 16)
                                e.gq.inc(g.dma_start(out=wsplit[nm][1, oc, :, c * 128:(c + 1) * 128],
                                                     in_=wstl[:, oc * 128:(oc + 1) * 128]), 16)
                    wsp_st[par] = e.gq.n
            wsplit_done = e.gq.n

            @blk.vector
            def _(v):
                v.memset(state[:, :], 0.0)
                v.memset(ln[:, :], 0.0)
                v.memset(k_tok[:, :], 0.0)
                v.wait_ge(e.v.s, prep_v)
                e.v.inc(v.memset(zf[:, 0:HN], 1.0))
                v.wait_ge(e.v.s, e.v.n)
                for hh in range(H):
                    hr3, hc3 = (hh % 2) * 64, (hh // 2) * HN
                    v.tensor_scalar(WKB[:, hh * HN:(hh + 1) * HN], zf[:, 0:HN],
                                    e.wkd[:, hh:hh + 1], None, ALU.mult)
                    v.tensor_scalar(WBB[:, hh * HN:(hh + 1) * HN], zf[:, 0:HN],
                                    e.wb[:, hh:hh + 1], None, ALU.mult)
                    e.v.inc(v.tensor_scalar(WSB[hr3:hr3 + 64, hc3:hc3 + HN], zf[hr3:hr3 + 64, 0:HN],
                                            e.wsd[hr3:hr3 + 64, hh:hh + 1], None, ALU.mult))
            p1_init_v = e.v.n

            slot_users = {"wkvg": 0, "wkvg2": 0, "wro": 0, "wro2": 0}
            mix_users = [0, 0]

            def load_kvg(slot, sname, nm, oh):
                @blk.sync
                def _(sy, slot=slot, nm=nm, oh=oh, uw=slot_users[sname]):
                    sy.wait_ge(e.gq.s, wsplit_done)
                    sy.wait_ge(e.t.s, uw)
                    for half in range(2):
                        e.ld.inc(sy.dma_start(out=slot[:, half * DC * 512:(half + 1) * DC * 512],
                                              in_=wsplit[nm][half, oh, :, :]), 16)
                return e.ld.n

            def load_ro_oc(slot, sname, nm, oc):
                @blk.sync
                def _(sy, slot=slot, nm=nm, oc=oc, uw=slot_users[sname]):
                    sy.wait_ge(e.gq.s, wsplit_done)
                    sy.wait_ge(e.t.s, uw)
                    for half in range(2):
                        e.ld.inc(sy.dma_start(out=slot[:, half * DC * 128:(half + 1) * DC * 128],
                                              in_=wsplit[nm][half, oc, :, :]), 16)
                return e.ld.n

            def emit_mix(mcol, ln_ready, mh, ml, uw):
                @blk.vector
                def _(v):
                    v.wait_ge(e.v.s, ln_ready)
                    v.wait_ge(e.t.s, uw)
                    W1 = TT + 1
                    for c in range(DC):
                        lncur = ln[:, c * W1 + 1:(c + 1) * W1]
                        lnsh = ln[:, c * W1:(c + 1) * W1 - 1]
                        xx = sq[:, c * TT:(c + 1) * TT]
                        v.tensor_tensor(xx, lnsh, lncur, ALU.subtract)
                        v.scalar_tensor_tensor(mh[:, c * TT:(c + 1) * TT], xx,
                                               e.maa[:, mcol * DC + c:mcol * DC + c + 1],
                                               lncur, ALU.mult, ALU.add)
                        v.scalar_tensor_tensor(xx, xx,
                                               e.maa[:, mcol * DC + c:mcol * DC + c + 1],
                                               lncur, ALU.mult, ALU.add)
                        e.v.inc(v.tensor_tensor(ml[:, c * TT:(c + 1) * TT], xx,
                                                mh[:, c * TT:(c + 1) * TT], ALU.subtract))
                return e.v.n

            evac_guard = [0, 0]
            x1_store = 0
            for ti in range(1 if dev_dump == 'probe' else NT):
                t0 = ti * TT

                @blk.gpsimd
                def _(g, t0=t0, x1_store=x1_store):
                    g.wait_ge(e.gq.s, max(x1_store, wsplit_done))
                    g.wait_ge(e.v.s, e.v.n)
                    for c in range(DC):
                        e.gq.inc(g.dma_start(out=xt[:, c * TT:(c + 1) * TT],
                                             in_=xin[c * 128:(c + 1) * 128, t0:t0 + TT]), 16)
                x_ld = e.gq.n

                @blk.vector
                def _(v, ti=ti):
                    v.wait_ge(e.gq.s, x_ld)
                    if ti > 0:
                        W1 = TT + 1
                        for c in range(DC):
                            e.v.inc(v.tensor_copy(ln[:, c * W1:c * W1 + 1], ln[:, c * W1 + TT:c * W1 + TT + 1]))
                    else:
                        W1 = TT + 1
                        for c in range(DC):
                            e.v.inc(v.memset(ln[:, c * W1:c * W1 + 1], 0.0))
                guard_v = e.v.n

                ln_done = emit_ln(xt, ln, e.v, guard_v, 0)

                for i, pname in enumerate(["k", "v", "g"]):
                    dst = {"k": k_tok, "v": v_tok, "g": g_tok}[pname]
                    mcol = {"k": 0, "v": 1, "r": 2, "g": 3}[pname]
                    mh, ml = (mxh, mxl) if i % 2 == 0 else (mxh2, mxl2)
                    mix_done = emit_mix(mcol, ln_done, mh, ml, mix_users[i % 2])
                    for oh in range(2):
                        sname = "wkvg" if oh == 0 else "wkvg2"
                        slot = wkvg if oh == 0 else wkvg2
                        w_rdy = load_kvg(slot, sname, pname, oh)
                        for cg in range(CPT):
                            ps = (e.psA if cg % 2 == 0 else e.psB)[:, 0:512]
                            pi = cg % 2

                            @blk.tensor
                            def _(t, slot=slot, w_rdy=w_rdy, mix_done=mix_done, ps=ps,
                                  pi=pi, cg=cg, oh=oh, mh=mh, ml=ml):
                                t.wait_ge(e.v.s, mix_done)
                                t.wait_ge(e.ld.s, w_rdy)
                                t.wait_ge(e.s.s, evac_guard[pi])
                                for dc in range(DC):
                                    xh = mh[:, dc * TT + cg * 128:dc * TT + (cg + 1) * 128]
                                    xl = ml[:, dc * TT + cg * 128:dc * TT + (cg + 1) * 128]
                                    wh_ = slot[:, dc * 512:(dc + 1) * 512]
                                    wl_ = slot[:, DC * 512 + dc * 512:DC * 512 + (dc + 1) * 512]
                                    t.matmul(ps, xh, wh_, start=(dc == 0), stop=False)
                                    t.matmul(ps, xh, wl_, start=False, stop=False)
                                    ins = t.matmul(ps, xl, wh_, start=False, stop=(dc == DC - 1))
                                e.t.inc(ins)
                            tn = e.t.n
                            slot_users[sname] = tn

                            @blk.scalar
                            def _(s, cg=cg, oh=oh, tn=tn, ps=ps, dst=dst, pname=pname):
                                s.wait_ge(e.t.s, tn)
                                fn = AF.Gelu if pname == "g" else AF.Copy
                                e.s.inc(s.activation(dst[:, cg * D + oh * 512:cg * D + oh * 512 + 512], ps, fn))
                            evac_guard[pi] = e.s.n
                    mix_users[i % 2] = e.t.n

                mh, ml = mxh2, mxl2
                mix_done = emit_mix(2, ln_done, mh, ml, mix_users[1])
                for oc in range(DC):
                    sname = "wro" if oc % 2 == 0 else "wro2"
                    slot = wro if oc % 2 == 0 else wro2
                    w_rdy = load_ro_oc(slot, sname, "r", oc)
                    ps = (e.psA if oc % 2 == 0 else e.psB)[:, 0:TT]
                    pi = oc % 2

                    @blk.tensor
                    def _(t, slot=slot, w_rdy=w_rdy, mix_done=mix_done, ps=ps, pi=pi, oc=oc,
                          mh=mh, ml=ml):
                        t.wait_ge(e.v.s, mix_done)
                        t.wait_ge(e.ld.s, w_rdy)
                        t.wait_ge(e.s.s, evac_guard[pi])
                        for dc in range(DC):
                            xh = mh[:, dc * TT:(dc + 1) * TT]
                            xl = ml[:, dc * TT:(dc + 1) * TT]
                            wh_ = slot[:, dc * 128:(dc + 1) * 128]
                            wl_ = slot[:, DC * 128 + dc * 128:DC * 128 + (dc + 1) * 128]
                            t.matmul(ps, wh_, xh, start=(dc == 0), stop=False)
                            t.matmul(ps, wl_, xh, start=False, stop=False)
                            ins = t.matmul(ps, wh_, xl, start=False, stop=(dc == DC - 1))
                        e.t.inc(ins)
                    tn = e.t.n
                    slot_users[sname] = tn

                    @blk.scalar
                    def _(s, oc=oc, tn=tn, ps=ps):
                        s.wait_ge(e.t.s, tn)
                        e.s.inc(s.activation(r_fm[:, oc * TT:(oc + 1) * TT], ps, AF.Copy))
                    evac_guard[pi] = e.s.n
                mix_users[1] = e.t.n

                proj_done = e.s.n

                for cg in range(CPT):
                    ci = ti * CPT + cg
                    ks = k_tok[:, cg * D:(cg + 1) * D]
                    vs = v_tok[:, cg * D:(cg + 1) * D]
                    gs = g_tok[:, cg * D:(cg + 1) * D]

                    kf_half_done = []
                    for half in range(2):
                        @blk.tensor
                        def _(t, ks=ks, proj_done=proj_done, half=half):
                            if half == 0:
                                t.wait_ge(e.s.s, proj_done)
                                t.wait_ge(e.v.s, e.v.n)
                            else:
                                t.wait_ge(e.v.s, kf_half_done[0])
                            for c4 in range(4):
                                c = half * 4 + c4
                                ins = t.transpose(e.psA[:, c4 * 128:(c4 + 1) * 128],
                                                  ks[:, c * 128:(c + 1) * 128], e.eye[:, :])
                            e.t.inc(ins)
                        tn = e.t.n

                        @blk.vector
                        def _(v, half=half, tn=tn):
                            v.wait_ge(e.t.s, tn)
                            e.v.inc(v.tensor_copy(k_fm[:, half * 512:(half + 1) * 512], e.psA[:, 0:512]))
                        kf_half_done.append(e.v.n)
                    kfm_done = e.v.n

                    @blk.vector
                    def _(v, ks=ks, proj_done=proj_done):
                        v.wait_ge(e.s.s, proj_done)
                        e.v.inc(v.tensor_tensor(kw_tok[:, :], ks, WKB[:, :], ALU.mult))
                    kw_done = e.v.n

                    def att_bank(hh):
                        return (e.psD, e.psB, e.psA)[hh % 3][:, 0:128]

                    def att_buf(hh):
                        return (attw, attw2, attw3)[hh % 3]

                    att_t_arr = [0] * H
                    attw_v_arr = [0] * H
                    y_t_arr = [0] * H
                    for hh in range(H + 1):
                        if hh < H:
                            hc, hr = hh // 2, (hh % 2) * 64
                            kfm_h = k_fm[hr:hr + 64, hc * 128:(hc + 1) * 128]
                            rfm_h = r_fm[hr:hr + 64, hc * TT + cg * 128:hc * TT + (cg + 1) * 128]

                            @blk.tensor
                            def _(t, hh=hh, kfm_h=kfm_h, rfm_h=rfm_h, kfm_done=kfm_done,
                                  kw_done=kw_done, proj_done=proj_done):
                                if hh == 0:
                                    t.wait_ge(e.v.s, max(kfm_done, kw_done))
                                elif hh == 1:
                                    t.wait_ge(e.s.s, proj_done)
                                elif hh == 2:
                                    pass
                                else:
                                    t.wait_ge(e.v.s, attw_v_arr[hh - 3])
                                e.t.inc(t.matmul(att_bank(hh), kfm_h, rfm_h, start=True, stop=True))
                            att_t_arr[hh] = e.t.n

                        if hh >= 1:
                            h0 = hh - 1
                            hc0, hr0 = h0 // 2, (h0 % 2) * 64
                            rfm_h0 = r_fm[hr0:hr0 + 64, hc0 * TT + cg * 128:hc0 * TT + (cg + 1) * 128]

                            @blk.vector
                            def _(v, h0=h0):
                                v.wait_ge(e.t.s, att_t_arr[h0])
                                if h0 >= 3:
                                    v.wait_ge(e.t.s, y_t_arr[h0 - 3])
                                e.v.inc(v.tensor_tensor(att_buf(h0)[:, :], att_bank(h0),
                                                        e.wmatT[:, h0 * 128:(h0 + 1) * 128], ALU.mult))
                            attw_v_arr[h0] = e.v.n

                            @blk.tensor
                            def _(t, h0=h0, vs=vs, rfm_h0=rfm_h0, hr0=hr0):
                                t.wait_ge(e.v.s, attw_v_arr[h0])
                                hcol = (h0 // 2) * HN
                                t.matmul(e.psY[:, h0 * HN:(h0 + 1) * HN], att_buf(h0)[:, :],
                                         vs[:, h0 * HN:(h0 + 1) * HN], start=True, stop=True)
                                t.matmul(e.psY2[:, h0 * HN:(h0 + 1) * HN], rfm_h0,
                                         state[hr0:hr0 + 64, hcol:hcol + HN], start=True, stop=True)
                                e.t.inc(t.matmul(e.psC[hr0:hr0 + 64, hcol:hcol + HN],
                                                 kw_tok[:, h0 * HN:(h0 + 1) * HN],
                                                 vs[:, h0 * HN:(h0 + 1) * HN], start=True, stop=True,
                                                 tile_position=(0, hr0)))
                            y_t_arr[h0] = e.t.n

                    all_y_t = e.t.n

                    @blk.scalar
                    def _(s):
                        s.wait_ge(e.t.s, all_y_t)
                        e.s.inc(s.activation(ysb[:, :], e.psY[:, 0:1024], AF.Copy))
                    y1_s = e.s.n

                    @blk.vector
                    def _(v, y1_s=y1_s):
                        v.wait_ge(e.s.s, y1_s)
                        v.tensor_tensor(zf[:, :], e.psY2[:, 0:1024], WBB[:, :], ALU.mult)
                        v.tensor_tensor(ysb[:, :], ysb[:, :], zf[:, :], ALU.add)
                        v.tensor_tensor(state[:, :], state[:, :], WSB[:, :], ALU.mult)
                        e.v.inc(v.tensor_tensor(state[:, :], state[:, :], e.psC[:, 0:(H // 2) * HN], ALU.add))
                    heads_done = e.v.n

                    @blk.vector
                    def _(v, heads_done=heads_done):
                        v.wait_ge(e.v.s, heads_done)
                        for hh in range(H):
                            ins = v.tensor_reduce(gn_m[:, hh:hh + 1], ysb[:, hh * HN:(hh + 1) * HN], AX.X, ALU.add)
                        e.v.inc(ins)
                        v.wait_ge(e.v.s, e.v.n)
                        e.v.inc(v.tensor_scalar(gn_m[:, :], gn_m[:, :], 1.0 / HN, None, ALU.mult))
                        v.wait_ge(e.v.s, e.v.n)
                        for hh in range(H):
                            e.v.inc(v.tensor_scalar(zf[:, hh * HN:(hh + 1) * HN],
                                                    ysb[:, hh * HN:(hh + 1) * HN],
                                                    gn_m[:, hh:hh + 1], None, ALU.subtract))
                    ycen_v = e.v.n

                    @blk.vector
                    def _(v, ycen_v=ycen_v):
                        v.wait_ge(e.v.s, ycen_v)
                        e.v.inc(v.tensor_tensor(k_fm[:, 0:D], zf[:, :], zf[:, :], ALU.mult))
                    ysq_s = e.v.n

                    @blk.vector
                    def _(v, ysq_s=ysq_s):
                        v.wait_ge(e.v.s, ysq_s)
                        for hh in range(H):
                            ins = v.tensor_reduce(gn_v[:, hh:hh + 1], k_fm[:, hh * HN:(hh + 1) * HN], AX.X, ALU.add)
                        e.v.inc(ins)
                        v.wait_ge(e.v.s, e.v.n)
                        e.v.inc(v.tensor_scalar(gn_v[:, :], gn_v[:, :], 1.0 / HN, None, ALU.mult))
                    gnv = e.v.n

                    @blk.scalar
                    def _(s, gnv=gnv):
                        s.wait_ge(e.v.s, gnv)
                        e.s.inc(s.activation(gn_t[:, :], gn_v[:, :], AF.Sqrt, bias=e.epsgn[:, :]))
                    gns = e.s.n

                    @blk.vector
                    def _(v, gns=gns, gs=gs):
                        v.wait_ge(e.s.s, gns)
                        e.v.inc(v.reciprocal(gn_rs[:, :], gn_t[:, :]))
                        v.wait_ge(e.v.s, e.v.n)
                        for hh in range(H):
                            v.tensor_scalar(zf[:, hh * HN:(hh + 1) * HN],
                                            zf[:, hh * HN:(hh + 1) * HN],
                                            gn_rs[:, hh:hh + 1], None, ALU.mult)
                        v.tensor_tensor(zf[:, :], zf[:, :], e.lnwb[:, :], ALU.mult)
                        v.tensor_tensor(zf[:, :], zf[:, :], e.lnbb[:, :], ALU.add)
                        e.v.inc(v.tensor_tensor(zf[:, :], zf[:, :], gs, ALU.mult))
                    zv = e.v.n

                    @blk.scalar
                    def _(s, zv=zv):
                        s.wait_ge(e.v.s, zv)
                        s.wait_ge(e.sc.s, e.sc.n)
                        e.s.inc(s.activation(zh[:, :], zf[:, :], AF.Copy))
                    zhs = e.s.n

                    @blk.vector
                    def _(v, zhs=zhs):
                        v.wait_ge(e.s.s, zhs)
                        e.v.inc(v.tensor_tensor(zl[:, :], zf[:, :], zh[:, :], ALU.subtract))
                    zlv = e.v.n

                    @blk.scalar
                    def _(s2, zlv=zlv, cg=cg):
                        s2.wait_ge(e.v.s, zlv)
                        for c in range(DC):
                            e.sc.inc(s2.dma_start(out=zfmh[:, c * TT + cg * 128:c * TT + (cg + 1) * 128],
                                                  in_=zh[:, c * 128:(c + 1) * 128], transpose=True), 16)
                            e.sc.inc(s2.dma_start(out=zfml[:, c * TT + cg * 128:c * TT + (cg + 1) * 128],
                                                  in_=zl[:, c * 128:(c + 1) * 128], transpose=True), 16)
                zt_done = e.sc.n

                for oc in range(DC):
                    sname = "wro" if oc % 2 == 0 else "wro2"
                    slot = wro if oc % 2 == 0 else wro2
                    w_rdy = load_ro_oc(slot, sname, "o", oc)
                    ps = (e.psA if oc % 2 == 0 else e.psB)[:, 0:TT]
                    pi = oc % 2

                    @blk.tensor
                    def _(t, slot=slot, w_rdy=w_rdy, zt_done=zt_done, ps=ps, pi=pi, oc=oc):
                        t.wait_ge(e.ld.s, w_rdy)
                        t.wait_ge(e.sc.s, zt_done)
                        t.wait_ge(e.v.s, e.v.n)
                        for dc in range(DC):
                            xh = zfmh[:, dc * TT:(dc + 1) * TT]
                            xl = zfml[:, dc * TT:(dc + 1) * TT]
                            wh_ = slot[:, dc * 128:(dc + 1) * 128]
                            wl_ = slot[:, DC * 128 + dc * 128:DC * 128 + (dc + 1) * 128]
                            t.matmul(ps, wh_, xh, start=(dc == 0), stop=False)
                            t.matmul(ps, wl_, xh, start=False, stop=False)
                            ins = t.matmul(ps, wh_, xl, start=False, stop=(dc == DC - 1))
                        e.t.inc(ins)
                    tn = e.t.n
                    slot_users[sname] = tn

                    @blk.vector
                    def _(v, oc=oc, tn=tn, ps=ps):
                        v.wait_ge(e.t.s, tn)
                        e.v.inc(v.tensor_tensor(xt[:, oc * TT:(oc + 1) * TT],
                                                xt[:, oc * TT:(oc + 1) * TT], ps, ALU.add))
                x1_v = e.v.n
                evac_guard = [0, 0]

                @blk.gpsimd
                def _(g, t0=t0, x1_v=x1_v):
                    g.wait_ge(e.v.s, x1_v)
                    for c in range(DC):
                        e.gq.inc(g.dma_start(out=xs1[c * 128:(c + 1) * 128, t0:t0 + TT],
                                             in_=xt[:, c * TT:(c + 1) * TT]), 16)
                x1_store = e.gq.n

                if dev_dump == "probe" and ti == 0:
                    @blk.gpsimd
                    def _(g):
                        g.wait_ge(e.gq.s, e.gq.n)
                        g.wait_ge(e.v.s, e.v.n)
                        g.wait_ge(e.s.s, e.s.n)
                        
                        e.fin.inc(g.dma_start(out=dbg[0:128, 2048:4096], in_=r_fm[:, :]), 16)
                        e.fin.inc(g.dma_start(out=dbg[640:768, 0:2056], in_=ln[:, :]), 16)
                        e.fin.inc(g.dma_start(out=dbg[128:256, 0:2048], in_=k_tok[:, :]), 16)
                        e.fin.inc(g.dma_start(out=dbg[128:256, 2048:4096], in_=v_tok[:, :]), 16)
                        e.fin.inc(g.dma_start(out=dbg[256:384, 0:2048], in_=g_tok[:, :]), 16)
                        e.fin.inc(g.dma_start(out=dbg[256:384, 2048:2048 + 1024], in_=ysb[:, :]), 16)
                        e.fin.inc(g.dma_start(out=dbg[384:512, 0:1024], in_=zf[:, :]), 16)
                        e.fin.inc(g.dma_start(out=dbg[384:512, 1024:1024 + 512], in_=state[:, :]), 16)
                        e.fin.inc(g.dma_start(out=dbg[512:640, 0:2048], in_=zfmh[:, :], ), 16)
                        e.fin.inc(g.dma_start(out=dbg[512:640, 2048:2048 + 96], in_=st6[:, :]), 16)
                        e.fin.inc(g.dma_start(out=dbg[512:640, 2144:2144 + 16], in_=gn_m[:, :]), 16)
                        e.fin.inc(g.dma_start(out=dbg[512:640, 2160:2160 + 16], in_=gn_v[:, :]), 16)
                        e.fin.inc(g.dma_start(out=dbg[512:640, 2176:2176 + 16], in_=gn_rs[:, :]), 16)
                        g.wait_ge(e.fin.s, e.fin.n)

            if dev_dump == "x1":
                @blk.gpsimd
                def _(g):
                    g.wait_ge(e.gq.s, e.gq.n)
                    e.fin.inc(g.dma_start(out=dbg, in_=xs1), 16)
                    g.wait_ge(e.fin.s, e.fin.n)
            p1.close()

        if n_phases >= 2:
            p2 = ExitStack()
            Ah = sb("Ah", [128, DC * D], BF16, st=p2)
            Al = sb("Al", [128, DC * D], BF16, st=p2)
            Bh = sb("Bh", [128, DC * D], BF16, st=p2)
            Bl = sb("Bl", [128, DC * D], BF16, st=p2)
            tstg = sb("tstg", [128, D], st=p2)
            tstg2 = sb("tstg2", [128, D], st=p2)
            fsum = sb("fsum", [128, DC * TT], st=p2)
            halfpi = sb("halfpi", [128, 1], st=p2)
            bv1 = barrier_vals()

            @blk.vector
            def _(v):
                wait_barrier(v, bv1)
                e.v.inc(v.memset(halfpi[:, :], HALFPI))
            p2v0 = e.v.n

            tstg3 = sb("tstg3", [128, D], st=p2)
            tstg4 = sb("tstg4", [128, D], st=p2)
            ab_stg = [(tstg, tstg2), (tstg3, tstg4)]
            ab_busy = [p2v0, p2v0]
            abi = 0
            for (srcA, srcN, dsth, dstl, use_cos) in [
                (theta_in, theta_n_in, Ah, Al, True),
                (phi_in, phi_n_in, Bh, Bl, False),
            ]:
                for c in range(DC):
                    par = abi % 2
                    abi += 1
                    sA, sB = ab_stg[par]

                    @blk.sync
                    def _(sy, srcA=srcA, srcN=srcN, c=c, bt=bv1["t"], uw=ab_busy[par], sA=sA, sB=sB):
                        sy.wait_ge(e.t.s, bt)
                        sy.wait_ge(e.v.s, uw)
                        e.ld.inc(sy.dma_start(out=sA[:, :], in_=srcA[c * 128:(c + 1) * 128, :]), 16)
                        e.ld.inc(sy.dma_start(out=sB[:, :], in_=srcN[c * 128:(c + 1) * 128, :]), 16)
                    tl = e.ld.n

                    @blk.vector
                    def _(v, tl=tl, sA=sA, sB=sB):
                        v.wait_ge(e.ld.s, tl)
                        e.v.inc(v.tensor_tensor(sA[:, :], sA[:, :], sB[:, :], ALU.add))
                    tv = e.v.n

                    @blk.scalar
                    def _(s, tv=tv, use_cos=use_cos, sA=sA, sB=sB):
                        s.wait_ge(e.v.s, tv)
                        if use_cos:
                            e.s.inc(s.activation(sB[:, :], sA[:, :], AF.Sin, bias=halfpi[:, :]))
                        else:
                            e.s.inc(s.activation(sB[:, :], sA[:, :], AF.Sin))
                    ts_ = e.s.n

                    @blk.vector
                    def _(v, ts_=ts_, c=c, dsth=dsth, dstl=dstl, sB=sB):
                        v.wait_ge(e.s.s, ts_)
                        v.tensor_copy(dsth[:, c * D:(c + 1) * D], sB[:, :])
                        e.v.inc(v.tensor_tensor(dstl[:, c * D:(c + 1) * D], sB[:, :],
                                                dsth[:, c * D:(c + 1) * D], ALU.subtract))
                    ab_busy[par] = e.v.n
            ab_done = e.v.n

            x2_store = 0
            for ti in range(NT):
                t0 = ti * TT

                @blk.gpsimd
                def _(g, t0=t0, x2_store=max(x2_store, 0) if ti > 0 else None):
                    g.wait_ge(e.gq.s, x2_store if x2_store is not None else e.gq.n)
                    g.wait_ge(e.v.s, e.v.n)
                    for c in range(DC):
                        e.gq.inc(g.dma_start(out=xt[:, c * TT:(c + 1) * TT],
                                             in_=xs1[c * 128:(c + 1) * 128, t0:t0 + TT]), 16)
                x_ld = e.gq.n

                @blk.vector
                def _(v):
                    v.wait_ge(e.gq.s, x_ld)
                    e.v.inc(v.memset(ln[:, 0:1], 0.0))
                gv_ = e.v.n

                ln_done = emit_ln(xt, ln, e.v, gv_, 0)

                @blk.scalar
                def _(s, ln_done=ln_done):
                    s.wait_ge(e.v.s, ln_done)
                    s.wait_ge(e.t.s, e.t.n)
                    W1 = TT + 1
                    for c in range(DC):
                        e.s.inc(s.activation(mxh[:, c * TT:(c + 1) * TT],
                                             ln[:, c * W1 + 1:(c + 1) * W1], AF.Copy))
                sp_ = e.s.n

                @blk.vector
                def _(v, sp_=sp_):
                    v.wait_ge(e.s.s, sp_)
                    W1 = TT + 1
                    for c in range(DC):
                        e.v.inc(v.tensor_tensor(mxl[:, c * TT:(c + 1) * TT],
                                                ln[:, c * W1 + 1:(c + 1) * W1],
                                                mxh[:, c * TT:(c + 1) * TT], ALU.subtract))
                mix_done = e.v.n

                evac_v2 = [e.v.n]
                sq_guard = [0, 0]
                for oc in range(DC):
                    bp = ((e.psA, e.psB), (e.psC, e.psD))[oc % 2]

                    @blk.tensor
                    def _(t, oc=oc, mix_done=mix_done, bp=bp, gw=None):
                        t.wait_ge(e.v.s, max(mix_done, ab_done))
                        t.wait_ge(e.s.s, sq_guard[oc % 2])
                        for (ps, wh_, wl_) in ((bp[0][:, 0:TT], Ah, Al), (bp[1][:, 0:TT], Bh, Bl)):
                            for dc in range(DC):
                                xh = mxh[:, dc * TT:(dc + 1) * TT]
                                xl = mxl[:, dc * TT:(dc + 1) * TT]
                                ww_h = wh_[:, dc * D + oc * 128: dc * D + oc * 128 + 128]
                                ww_l = wl_[:, dc * D + oc * 128: dc * D + oc * 128 + 128]
                                t.matmul(ps, ww_h, xh, start=(dc == 0), stop=False)
                                t.matmul(ps, ww_l, xh, start=False, stop=False)
                                ins = t.matmul(ps, ww_h, xl, start=False, stop=(dc == DC - 1))
                            e.t.inc(ins)
                    tn = e.t.n

                    @blk.scalar
                    def _(s, tn=tn, oc=oc, bp=bp):
                        s.wait_ge(e.t.s, tn)
                        s.wait_ge(e.v.s, e.v.n)
                        s.activation(sq[:, 0:TT], bp[0][:, 0:TT], AF.Square)
                        e.s.inc(s.activation(sq[:, TT:2 * TT], bp[1][:, 0:TT], AF.Square))
                    sq_s2 = e.s.n
                    sq_guard[oc % 2] = sq_s2

                    @blk.vector
                    def _(v, sq_s2=sq_s2, oc=oc):
                        v.wait_ge(e.s.s, sq_s2)
                        e.v.inc(v.tensor_tensor(fsum[:, oc * TT:(oc + 1) * TT],
                                                sq[:, 0:TT], sq[:, TT:2 * TT], ALU.add))
                    evac_v2[0] = e.v.n

                @blk.scalar
                def _(s):
                    s.wait_ge(e.v.s, evac_v2[0])
                    e.s.inc(s.activation(fsum[:, :], fsum[:, :], AF.Sqrt))
                fs = e.s.n

                @blk.vector
                def _(v, fs=fs):
                    v.wait_ge(e.s.s, fs)
                    for oc in range(DC):
                        e.v.inc(v.tensor_tensor(xt[:, oc * TT:(oc + 1) * TT],
                                                xt[:, oc * TT:(oc + 1) * TT],
                                                fsum[:, oc * TT:(oc + 1) * TT], ALU.add))
                x2_v = e.v.n

                @blk.gpsimd
                def _(g, t0=t0, x2_v=x2_v):
                    g.wait_ge(e.v.s, x2_v)
                    for c in range(DC):
                        e.gq.inc(g.dma_start(out=xs2[c * 128:(c + 1) * 128, t0:t0 + TT],
                                             in_=xt[:, c * TT:(c + 1) * TT]), 16)
                x2_store = e.gq.n

            if dev_dump == "x2":
                @blk.gpsimd
                def _(g):
                    g.wait_ge(e.gq.s, e.gq.n)
                    e.fin.inc(g.dma_start(out=dbg, in_=xs2), 16)
                    g.wait_ge(e.fin.s, e.fin.n)
            p2.close()

        if n_phases >= 3:
            p3 = ExitStack()
            WwPh = sb("WwPh", [128, DC * D], BF16, st=p3)
            WwPl = sb("WwPl", [128, DC * D], BF16, st=p3)
            WrdPh = sb("WrdPh", [128, DC * D], BF16, st=p3)
            WrdPl = sb("WrdPl", [128, DC * D], BF16, st=p3)
            ST = sb("ST", [128, DC * D], st=p3)
            v_tok3 = sb("v_tok3", [128, CPT * D], st=p3)
            w_tok3 = sb("w_tok3", [128, D], st=p3)
            t_tok3 = sb("t_tok3", [128, D], st=p3)
            vgw = sb("vgw", [128, D], st=p3)
            stm = sb("stm", [128, 128], st=p3)
            intr = sb("intr", [128, D], st=p3)
            reads_tok = sb("reads_tok", [128, D], st=p3)
            reads_fm = sb("reads_fm", [128, DC * 128], st=p3)
            rdh = sb("rdh", [128, DC * 128], BF16, st=p3)
            rdl = sb("rdl", [128, DC * 128], BF16, st=p3)
            x3f = sb("x3f", [128, D], st=p3)
            MT = sb("MT", [128, 128], st=p3)
            l1row = sb("l1row", [1, 8], st=p3)
            lg_col = sb("lg_col", [128, 1], st=p3)
            al_col = sb("al_col", [128, 1], st=p3)
            dec_col = sb("dec_col", [128, 1], st=p3)
            gw_col = sb("gw_col", [128, 1], st=p3)
            gC_col = sb("gC_col", [128, 1], st=p3)
            bv2 = barrier_vals()

            for (srcW, dsth, dstl) in [(WwT_in, WwPh, WwPl), (WrdT_in, WrdPh, WrdPl)]:
                for c in range(DC):
                    @blk.sync
                    def _(sy, srcW=srcW, c=c, bt=bv2["t"]):
                        sy.wait_ge(e.t.s, bt)
                        sy.wait_ge(e.v.s, e.v.n)
                        sy.wait_ge(e.s.s, e.s.n)
                        e.ld.inc(sy.dma_start(out=wstg[:, :], in_=srcW[c * 128:(c + 1) * 128, :]), 16)
                    tl = e.ld.n

                    @blk.scalar
                    def _(s, tl=tl, c=c, dsth=dsth):
                        s.wait_ge(e.ld.s, tl)
                        e.s.inc(s.activation(dsth[:, c * D:(c + 1) * D], wstg[:, :], AF.Copy))
                    ts_ = e.s.n

                    @blk.vector
                    def _(v, ts_=ts_, c=c, dsth=dsth, dstl=dstl):
                        v.wait_ge(e.s.s, ts_)
                        e.v.inc(v.tensor_tensor(dstl[:, c * D:(c + 1) * D], wstg[:, :],
                                                dsth[:, c * D:(c + 1) * D], ALU.subtract))
            wsplit3 = e.v.n

            @blk.sync
            def _(sy, bt=bv2["t"]):
                sy.wait_ge(e.t.s, bt)
                e.ld.inc(sy.dma_start(out=l1row[0:1, 0:4], in_=l1s_in), 16)
                e.ld.inc(sy.dma_start(out=MT[:, :], in_=c2t_in), 16)
            sc_ld = e.ld.n

            @blk.scalar
            def _(s):
                s.wait_ge(e.ld.s, sc_ld)
                s.activation(l1row[0:1, 4:5], l1row[0:1, 0:1], AF.Sigmoid)
                e.s.inc(s.activation(l1row[0:1, 5:6], l1row[0:1, 4:5], AF.Ln))
            sc_s = e.s.n

            @blk.gpsimd
            def _(g):
                g.wait_ge(e.s.s, sc_s)
                e.gq.inc(g.dma_start(out=scal_d[0:1, 0:8], in_=l1row[0:1, 0:8]), 16)
            sc_g = e.gq.n

            @blk.sync
            def _(sy):
                sy.wait_ge(e.gq.s, sc_g)
                e.ld.inc(sy.dma_start(out=lg_col[:, :], in_=scal_d[0:1, 5:6].broadcast_to([128, 1])), 16)
                e.ld.inc(sy.dma_start(out=al_col[:, :], in_=scal_d[0:1, 1:2].broadcast_to([128, 1])), 16)
            sc_b = e.ld.n

            @blk.vector
            def _(v):
                v.wait_ge(e.ld.s, sc_b)
                e.v.inc(v.tensor_scalar(dec_col[:, :], e.ioi[:, :], lg_col[:, :], None, ALU.mult))
                e.v.inc(v.tensor_scalar(gw_col[:, :], e.ior[:, :], lg_col[:, :], None, ALU.mult))
                e.v.inc(v.tensor_scalar(gC_col[:, :], lg_col[:, :], float(Q), None, ALU.mult))
                e.v.inc(v.tensor_scalar(MT[:, :], MT[:, :], lg_col[:, :], None, ALU.mult))
                v.wait_ge(e.v.s, e.v.n)
            sc_v = e.v.n

            @blk.scalar
            def _(s):
                s.wait_ge(e.v.s, sc_v)
                s.activation(dec_col[:, :], dec_col[:, :], AF.Exp)
                s.activation(gw_col[:, :], gw_col[:, :], AF.Exp)
                s.activation(gC_col[:, :], gC_col[:, :], AF.Exp)
                s.activation(al_col[:, :], al_col[:, :], AF.Exp)
                e.s.inc(s.activation(MT[:, :], MT[:, :], AF.Exp))
            sc_s2 = e.s.n

            @blk.vector
            def _(v):
                v.wait_ge(e.s.s, sc_s2)
                v.memset(ST[:, :], 0.0)
                for c in range(DC):
                    e.v.inc(v.memset(ln[:, c * (TT + 1):c * (TT + 1) + 1], 0.0))
            p3_init = e.v.n

            x3_store = 0
            st_upd_v = p3_init
            for ti in range(NT):
                t0 = ti * TT

                @blk.gpsimd
                def _(g, t0=t0, w=x3_store if ti > 0 else None):
                    g.wait_ge(e.gq.s, w if w is not None else e.gq.n)
                    g.wait_ge(e.v.s, e.v.n)
                    for c in range(DC):
                        e.gq.inc(g.dma_start(out=xt[:, c * TT:(c + 1) * TT],
                                             in_=xs2[c * 128:(c + 1) * 128, t0:t0 + TT]), 16)
                x_ld = e.gq.n

                @blk.vector
                def _(v, ti=ti):
                    v.wait_ge(e.gq.s, x_ld)
                    W1 = TT + 1
                    if ti > 0:
                        for c in range(DC):
                            e.v.inc(v.tensor_copy(ln[:, c * W1:c * W1 + 1], ln[:, c * W1 + TT:c * W1 + TT + 1]))
                    else:
                        for c in range(DC):
                            e.v.inc(v.memset(ln[:, c * W1:c * W1 + 1], 0.0))
                guard_v = e.v.n

                ln_done = emit_ln(xt, ln, e.v, guard_v, 2)

                @blk.scalar
                def _(s, ln_done=ln_done):
                    s.wait_ge(e.v.s, ln_done)
                    s.wait_ge(e.t.s, e.t.n)
                    W1 = TT + 1
                    for c in range(DC):
                        e.s.inc(s.activation(mxh[:, c * TT:(c + 1) * TT],
                                             ln[:, c * W1 + 1:(c + 1) * W1], AF.Copy))
                sp_ = e.s.n

                @blk.vector
                def _(v, sp_=sp_):
                    v.wait_ge(e.s.s, sp_)
                    W1 = TT + 1
                    for c in range(DC):
                        e.v.inc(v.tensor_tensor(mxl[:, c * TT:(c + 1) * TT],
                                                ln[:, c * W1 + 1:(c + 1) * W1],
                                                mxh[:, c * TT:(c + 1) * TT], ALU.subtract))
                mix_done = e.v.n

                for cg in range(CPT):
                    ps = (e.psA if cg % 2 == 0 else e.psB)[:, 0:512]
                    for oh in range(2):
                        @blk.tensor
                        def _(t, cg=cg, oh=oh, mix_done=mix_done, ps=ps):
                            t.wait_ge(e.v.s, max(mix_done, wsplit3))
                            t.wait_ge(e.s.s, e.s.n)
                            for dc in range(DC):
                                xh = mxh[:, dc * TT + cg * 128:dc * TT + (cg + 1) * 128]
                                xl = mxl[:, dc * TT + cg * 128:dc * TT + (cg + 1) * 128]
                                wh_ = WwPh[:, dc * D + oh * 512:dc * D + oh * 512 + 512]
                                wl_ = WwPl[:, dc * D + oh * 512:dc * D + oh * 512 + 512]
                                t.matmul(ps[:, 0:512] if False else ps, xh, wh_, start=(dc == 0), stop=False)
                                t.matmul(ps, xh, wl_, start=False, stop=False)
                                ins = t.matmul(ps, xl, wh_, start=False, stop=(dc == DC - 1))
                            e.t.inc(ins)
                        tn = e.t.n

                        @blk.scalar
                        def _(s, cg=cg, oh=oh, tn=tn, ps=ps):
                            s.wait_ge(e.t.s, tn)
                            e.s.inc(s.activation(v_tok3[:, cg * D + oh * 512:cg * D + oh * 512 + 512],
                                                 ps, AF.Copy))
                vproj_done = e.s.n

                for cg in range(CPT):
                    ci = ti * CPT + cg
                    W1 = TT + 1
                    vs3 = v_tok3[:, cg * D:(cg + 1) * D]

                    trdone = []
                    for half in range(2):
                        @blk.tensor
                        def _(t, half=half, cg=cg, vproj_done=vproj_done):
                            if half == 0:
                                t.wait_ge(e.s.s, vproj_done)
                                t.wait_ge(e.v.s, e.v.n)
                            else:
                                t.wait_ge(e.v.s, trdone[0])
                            for c4 in range(4):
                                c = half * 4 + c4
                                ins = t.transpose(e.psC[:, c4 * 128:(c4 + 1) * 128],
                                                  ln[:, c * W1 + cg * 128:c * W1 + cg * 128 + 128],
                                                  e.eye[:, :])
                            e.t.inc(ins)
                        tn = e.t.n

                        @blk.vector
                        def _(v, half=half, tn=tn):
                            v.wait_ge(e.t.s, tn)
                            e.v.inc(v.tensor_copy(w_tok3[:, half * 512:(half + 1) * 512], e.psC[:, 0:512]))
                        trdone.append(e.v.n)

                    @blk.vector
                    def _(v, cg=cg):
                        v.wait_ge(e.v.s, trdone[1])
                        for c in range(DC):
                            e.v.inc(v.tensor_tensor(sq[:, c * TT + cg * 128:c * TT + (cg + 1) * 128],
                                                    xt[:, c * TT + cg * 128:c * TT + (cg + 1) * 128],
                                                    ln[:, c * W1 + cg * 128 + 1:c * W1 + (cg + 1) * 128 + 1],
                                                    ALU.add))
                    tfm_v = e.v.n

                    trdone2 = []
                    for half in range(2):
                        @blk.tensor
                        def _(t, half=half, cg=cg, tfm_v=tfm_v):
                            if half == 0:
                                t.wait_ge(e.v.s, tfm_v)
                            else:
                                t.wait_ge(e.v.s, trdone2[0])
                            for c4 in range(4):
                                c = half * 4 + c4
                                ins = t.transpose(e.psC[:, c4 * 128:(c4 + 1) * 128],
                                                  sq[:, c * TT + cg * 128:c * TT + (cg + 1) * 128],
                                                  e.eye[:, :])
                            e.t.inc(ins)
                        tn = e.t.n

                        @blk.vector
                        def _(v, half=half, tn=tn):
                            v.wait_ge(e.t.s, tn)
                            e.v.inc(v.tensor_copy(t_tok3[:, half * 512:(half + 1) * 512], e.psC[:, 0:512]))
                        trdone2.append(e.v.n)

                    @blk.vector
                    def _(v, vs3=vs3):
                        v.wait_ge(e.s.s, vproj_done)
                        e.v.inc(v.tensor_scalar(vgw[:, :], vs3, gw_col[:, :], None, ALU.mult))
                    vgw_v = e.v.n

                    @blk.tensor
                    def _(t, cg=cg, st_upd_v=st_upd_v):
                        t.wait_ge(e.v.s, max(trdone[1], st_upd_v))
                        for c in range(DC):
                            ins = t.matmul(e.psD[:, 0:128],
                                           ln[:, c * W1 + cg * 128:c * W1 + cg * 128 + 128],
                                           ln[:, c * W1 + cg * 128 + 1:c * W1 + (cg + 1) * 128 + 1],
                                           start=(c == 0), stop=(c == DC - 1))
                        e.t.inc(ins)
                    stp_t = e.t.n

                    @blk.vector
                    def _(v, stp_t=stp_t):
                        v.wait_ge(e.t.s, stp_t)
                        e.v.inc(v.tensor_tensor(stm[:, :], e.psD[:, 0:128], MT[:, :], ALU.mult))
                    stm_v = e.v.n

                    @blk.tensor
                    def _(t, cg=cg, stm_v=stm_v, vs3=vs3):
                        t.wait_ge(e.v.s, stm_v)
                        for nh in range(2):
                            ins = t.matmul(e.psY2[:, nh * 512:(nh + 1) * 512], stm[:, :],
                                           vs3[:, nh * 512:(nh + 1) * 512], start=True, stop=True)
                        e.t.inc(ins)
                        for nh in range(2):
                            for c in range(DC):
                                ins = t.matmul(e.psY[:, nh * 512:(nh + 1) * 512],
                                               ln[:, c * W1 + cg * 128 + 1:c * W1 + (cg + 1) * 128 + 1],
                                               ST[:, c * D + nh * 512:c * D + nh * 512 + 512],
                                               start=(c == 0), stop=(c == DC - 1))
                            e.t.inc(ins)
                    ii_t = e.t.n

                    @blk.scalar
                    def _(s, ii_t=ii_t):
                        s.wait_ge(e.t.s, ii_t)
                        e.s.inc(s.activation(intr[:, :], e.psY2[:, 0:1024], AF.Copy))
                    intr_s = e.s.n

                    upd_t = []
                    for ec in range(DC):
                        @blk.tensor
                        def _(t, ec=ec, vgw_v=vgw_v, intr_s=intr_s):
                            if ec == 0:
                                t.wait_ge(e.v.s, vgw_v)
                                t.wait_ge(e.s.s, max(e.s.n, intr_s))
                            if ec >= 2:
                                t.wait_ge(e.v.s, upd_t[ec - 2])
                            ps = (e.psA if ec % 2 == 0 else e.psB)[:, 0:512]
                            for nh in range(2):
                                ins = t.matmul(ps[:, 0:512], w_tok3[:, ec * 128:(ec + 1) * 128],
                                               vgw[:, nh * 512:(nh + 1) * 512],
                                               start=(nh == 0), stop=False) if False else None
                            ins = t.matmul(ps[:, 0:512], w_tok3[:, ec * 128:(ec + 1) * 128],
                                           vgw[:, 0:512], start=True, stop=True)
                            e.t.inc(ins)
                            ins = t.matmul((e.psD if ec % 2 == 0 else e.psY2)[:, 0:512],
                                           w_tok3[:, ec * 128:(ec + 1) * 128],
                                           vgw[:, 512:1024], start=True, stop=True)
                            e.t.inc(ins)
                        tn2 = e.t.n

                        @blk.vector
                        def _(v, ec=ec, tn2=tn2):
                            v.wait_ge(e.t.s, tn2)
                            ps1 = (e.psA if ec % 2 == 0 else e.psB)[:, 0:512]
                            ps2 = (e.psD if ec % 2 == 0 else e.psY2)[:, 0:512]
                            v.scalar_tensor_tensor(ST[:, ec * D:ec * D + 512],
                                                   ST[:, ec * D:ec * D + 512],
                                                   gC_col[:, :], ps1, ALU.mult, ALU.add)
                            e.v.inc(v.scalar_tensor_tensor(ST[:, ec * D + 512:ec * D + 1024],
                                                           ST[:, ec * D + 512:ec * D + 1024],
                                                           gC_col[:, :], ps2, ALU.mult, ALU.add))
                        upd_t.append(e.v.n)
                    st_upd_v = e.v.n

                    @blk.vector
                    def _(v, intr_s=intr_s):
                        v.wait_ge(e.s.s, intr_s)
                        e.v.inc(v.scalar_tensor_tensor(reads_tok[:, :], e.psY[:, 0:1024],
                                                       dec_col[:, :], intr[:, :],
                                                       ALU.mult, ALU.add))
                    reads_v = e.v.n

                    trdone3 = []
                    for half in range(2):
                        @blk.tensor
                        def _(t, half=half, reads_v=reads_v):
                            if half == 0:
                                t.wait_ge(e.v.s, reads_v)
                            else:
                                t.wait_ge(e.v.s, trdone3[0])
                            for c4 in range(4):
                                c = half * 4 + c4
                                ins = t.transpose(e.psC[:, c4 * 128:(c4 + 1) * 128],
                                                  reads_tok[:, c * 128:(c + 1) * 128], e.eye[:, :])
                            e.t.inc(ins)
                        tn = e.t.n

                        @blk.vector
                        def _(v, half=half, tn=tn):
                            v.wait_ge(e.t.s, tn)
                            e.v.inc(v.tensor_copy(reads_fm[:, half * 512:(half + 1) * 512], e.psC[:, 0:512]))
                        trdone3.append(e.v.n)

                    @blk.scalar
                    def _(s):
                        s.wait_ge(e.v.s, trdone3[1])
                        e.s.inc(s.activation(rdh[:, :], reads_fm[:, :], AF.Copy))
                    rdh_s = e.s.n

                    @blk.vector
                    def _(v, rdh_s=rdh_s):
                        v.wait_ge(e.s.s, rdh_s)
                        e.v.inc(v.tensor_tensor(rdl[:, :], reads_fm[:, :], rdh[:, :], ALU.subtract))
                    rdl_v = e.v.n

                    @blk.tensor
                    def _(t, rdl_v=rdl_v):
                        t.wait_ge(e.v.s, rdl_v)
                        for nh in range(2):
                            for dc in range(DC):
                                wh_ = WrdPh[:, dc * D + nh * 512:dc * D + nh * 512 + 512]
                                wl_ = WrdPl[:, dc * D + nh * 512:dc * D + nh * 512 + 512]
                                xh = rdh[:, dc * 128:(dc + 1) * 128]
                                xl = rdl[:, dc * 128:(dc + 1) * 128]
                                t.matmul(e.psY[:, nh * 512:(nh + 1) * 512], xh, wh_, start=(dc == 0), stop=False)
                                t.matmul(e.psY[:, nh * 512:(nh + 1) * 512], xh, wl_, start=False, stop=False)
                                ins = t.matmul(e.psY[:, nh * 512:(nh + 1) * 512], xl, wh_, start=False,
                                               stop=(dc == DC - 1))
                            e.t.inc(ins)
                    rp_t = e.t.n

                    @blk.vector
                    def _(v, rp_t=rp_t, x3sp=x3_store):
                        v.wait_ge(e.t.s, rp_t)
                        v.wait_ge(e.gq.s, x3sp)
                        e.v.inc(v.scalar_tensor_tensor(x3f[:, :], e.psY[:, 0:1024], al_col[:, :],
                                                       t_tok3[:, :], ALU.mult, ALU.add))
                    x3_v = e.v.n

                    @blk.vector
                    def _(v, ci=ci, x3_v=x3_v):
                        v.wait_ge(e.v.s, x3_v)
                        e.v.inc(v.tensor_reduce(e.m4[:, ci:ci + 1], x3f[:, :], AX.X, ALU.add))
                        v.wait_ge(e.v.s, e.v.n)
                        e.v.inc(v.tensor_scalar(e.m4[:, ci:ci + 1], e.m4[:, ci:ci + 1], 1.0 / D, None, ALU.mult))
                        v.wait_ge(e.v.s, e.v.n)
                        e.v.inc(v.tensor_scalar(vgw[:, :], x3f[:, :], e.m4[:, ci:ci + 1], None, ALU.subtract))
                        e.v.inc(v.tensor_reduce(e.cmx4[:, ci:ci + 1], vgw[:, :], AX.X, ALU.max,
                                                apply_absolute_value=True))
                        e.v.inc(v.tensor_tensor(vgw[:, :], vgw[:, :], vgw[:, :], ALU.mult))
                        e.v.inc(v.tensor_reduce(e.var4[:, ci:ci + 1], vgw[:, :], AX.X, ALU.add))
                    s4b = e.v.n

                    @blk.gpsimd
                    def _(g, ci=ci, s4b=s4b):
                        g.wait_ge(e.v.s, s4b)
                        e.gq.inc(g.dma_start(out=x3_d[ci * 128:(ci + 1) * 128, :], in_=x3f[:, :]), 16)
                        if dev_dump == "x3" or n_phases == 3:
                            e.gq.inc(g.dma_start(out=out[ci * 128:(ci + 1) * 128, :], in_=x3f[:, :]), 16)
                    x3_store = e.gq.n

            if n_phases == 3:
                @blk.gpsimd
                def _(g):
                    g.wait_ge(e.gq.s, e.gq.n)
                    e.fin.inc(g.dma_start(out=scal_d[0:1, 8:9], in_=l1row[0:1, 0:1]), 16)
                    g.wait_ge(e.fin.s, e.fin.n)
            p3.close()

        psh.close()

        if n_phases >= 4:
            p4 = ExitStack()
            kwgT = sb("kwgT", [128, DC * DFF], BF16, st=p4)
            kwuT = sb("kwuT", [128, DC * DFF], BF16, st=p4)
            kwdT = sb("kwdT", [128, IC * D], BF16, st=p4)
            qstg = sb("qstg", [128, DFF], st=p4)
            qu = sb("qu", [128, DFF], st=p4)
            qk = sb("qk", [128, DFF], I32, st=p4)
            qb = sb("qb", [128, DFF], BF16, st=p4)
            wsg = sb("wsg", [128, IC], st=p4)
            wsu = sb("wsu", [128, IC], st=p4)
            wsd = sb("wsd", [128, DC], st=p4)
            wsgx = sb("wsgx", [128, IC], st=p4)
            wsux = sb("wsux", [128, IC], st=p4)
            m4 = e.m4
            var4 = e.var4
            cmx4 = e.cmx4
            rs4 = sb("rs4", [128, NCH], st=p4)
            rsi4 = sb("rsi4", [128, NCH], st=p4)
            sb4 = sb("sb4", [128, NCH], st=p4)
            x3t = sb("x3t", [128, D], F32, st=p4)
            x3f4 = sb("x3f4", [128, D], st=p4)
            scr4 = sb("scr4", [128, D], st=p4)
            kxf = sb("kxf", [128, DC * 512], BF16, st=p4)
            sgt = sb("sgt", [128, 512], st=p4)
            sut = sb("sut", [128, 512], st=p4)
            htile = sb("htile", [128, 512], st=p4)
            hmax = sb("hmax", [128, 1], st=p4)
            hab = sb("hab", [128, IC], st=p4)
            wsdq = sb("wsdq", [128, D], st=p4)
            p4row = sb("p4row", [1, 16], st=p4)
            mcol1 = sb("mcol1", [128, 1], st=p4)
            mcol2 = sb("mcol2", [128, 1], st=p4)
            colbuf = nc.dram_tensor("colbuf", [1, 256], F32).ap()
            bv3 = barrier_vals()

            winv_g = sb("winv_g", [128, IC], st=p4)
            winv_u = sb("winv_u", [128, IC], st=p4)
            winv_d = sb("winv_d", [128, DC], st=p4)
            wsd2 = sb("wsd2", [128, DC], st=p4)
            wsd3 = sb("wsd3", [128, DC], st=p4)
            units = []
            for rc in range(IC):
                units.append((wg_in, rc, 0, D, kwgT, DFF, (wsg, rc)))
            for rc in range(IC):
                units.append((wu_in, rc, 0, D, kwuT, DFF, (wsu, rc)))
            for rc in range(DC):
                units.append((wd_in, rc, 0, 1024, kwdT, D, (wsd, rc)))
                units.append((wd_in, rc, 1024, 1024, kwdT, D, (wsd2, rc)))
                units.append((wd_in, rc, 2048, DFF - 2048, kwdT, D, (wsd3, rc)))

            wbusy = [bv3["v"], bv3["v"]]
            for ui, (sap, rc, co, wd_, dstT, dstr, (wscol, wsi)) in enumerate(units):
                par = ui % 2
                wbuf = (qstg, qu)[par]

                @blk.sync
                def _(sy, sap=sap, rc=rc, co=co, wd_=wd_, wbuf=wbuf, bt=bv3["t"], uw=wbusy[par]):
                    sy.wait_ge(e.t.s, bt)
                    sy.wait_ge(e.v.s, uw)
                    e.ld.inc(sy.dma_start(out=wbuf[:, 0:wd_],
                                          in_=sap[rc * 128:(rc + 1) * 128, co:co + wd_]), 16)
                wl = e.ld.n

                @blk.vector
                def _(v, wl=wl, wbuf=wbuf, wd_=wd_, wscol=wscol, wsi=wsi):
                    v.wait_ge(e.ld.s, wl)
                    e.v.inc(v.tensor_reduce(wscol[:, wsi:wsi + 1], wbuf[:, 0:wd_],
                                            AX.X, ALU.max, apply_absolute_value=True))
                wbusy[par] = e.v.n

            @blk.vector
            def _(v):
                v.wait_ge(e.v.s, e.v.n)
                v.tensor_tensor(wsd[:, :], wsd[:, :], wsd2[:, :], ALU.max)
                e.v.inc(v.tensor_tensor(wsd[:, :], wsd[:, :], wsd3[:, :], ALU.max))
                v.wait_ge(e.v.s, e.v.n)
                v.reciprocal(winv_g[:, :], wsg[:, :])
                v.reciprocal(winv_u[:, :], wsu[:, :])
                e.v.inc(v.reciprocal(winv_d[:, :], wsd[:, :]))
                v.wait_ge(e.v.s, e.v.n)
            winv_v = e.v.n

            WINV = {id(wg_in): winv_g, id(wu_in): winv_u, id(wd_in): winv_d}
            wbusy = [winv_v, winv_v]
            ubusy = [winv_v, winv_v]
            tr_prev = e.sc.n
            for ui, (sap, rc, co, wd_, dstT, dstr, _ws) in enumerate(units):
                par = ui % 2
                wbuf = (qstg, qu)[par]
                ups = (e.psY, e.psY2)[par]

                @blk.sync
                def _(sy, sap=sap, rc=rc, co=co, wd_=wd_, wbuf=wbuf, uw=wbusy[par]):
                    sy.wait_ge(e.v.s, uw)
                    e.ld.inc(sy.dma_start(out=wbuf[:, 0:wd_],
                                          in_=sap[rc * 128:(rc + 1) * 128, co:co + wd_]), 16)
                wl = e.ld.n

                @blk.vector
                def _(v, wl=wl, wbuf=wbuf, ups=ups, wd_=wd_, rc=rc, winv=WINV[id(sap)],
                      trp=tr_prev):
                    v.wait_ge(e.ld.s, wl)
                    v.wait_ge(e.sc.s, trp)
                    v.tensor_scalar(ups[:, 0:wd_], wbuf[:, 0:wd_], winv[:, rc:rc + 1], None, ALU.mult)
                    v.tensor_scalar(ups[:, 0:wd_], ups[:, 0:wd_], 127.0, 0.5, ALU.mult, ALU.add)
                    v.tensor_copy(qk[:, 0:wd_], ups[:, 0:wd_])
                    v.tensor_copy(wbuf[:, 0:wd_], qk[:, 0:wd_])
                    v.tensor_tensor(ups[:, 0:wd_], ups[:, 0:wd_], wbuf[:, 0:wd_], ALU.subtract)
                    v.tensor_scalar(ups[:, 0:wd_], ups[:, 0:wd_], 0.0, None, ALU.is_lt)
                    v.tensor_tensor(qb[:, 0:wd_], wbuf[:, 0:wd_], ups[:, 0:wd_], ALU.subtract)
                    e.v.inc(v.tensor_scalar(qb[:, 0:wd_], qb[:, 0:wd_], -128.0, 127.0, ALU.max, ALU.min))
                qv = e.v.n
                wbusy[par] = qv
                ubusy[par] = qv

                @blk.scalar
                def _(s2, rc=rc, co=co, qv=qv, dstT=dstT, wd_=wd_, dstr=dstr):
                    s2.wait_ge(e.v.s, qv)
                    for tc in range(wd_ // 128):
                        e.sc.inc(s2.dma_start(
                            out=dstT[:, (co // 128 + tc) * dstr + rc * 128:(co // 128 + tc) * dstr + (rc + 1) * 128],
                            in_=qb[:, tc * 128:(tc + 1) * 128], transpose=True), 16)
                tr_prev = e.sc.n
            wq_done_ld = e.ld.n
            wq_done_sc = e.sc.n
            wq_done_v = e.v.n

            @blk.sync
            def _(sy, bt=bv3["t"]):
                sy.wait_ge(e.t.s, bt)
                e.ld.inc(sy.dma_start(out=p4row[0:1, 0:4], in_=l1s_in), 16)
            p4l = e.ld.n

            @blk.scalar
            def _(s):
                s.wait_ge(e.ld.s, p4l)
                e.s.inc(s.activation(p4row[0:1, 4:5], p4row[0:1, 2:3], AF.Sigmoid))
                s.wait_ge(e.s.s, e.s.n)
                e.s.inc(s.activation(p4row[0:1, 5:6], p4row[0:1, 3:4], AF.Sigmoid))
                s.wait_ge(e.s.s, e.s.n)
            p4s = e.s.n

            @blk.vector
            def _(v, bv=bv3["v"]):
                v.wait_ge(e.v.s, bv)
                v.wait_ge(e.s.s, p4s)
                e.v.inc(v.tensor_scalar(var4[:, :], var4[:, :], 1.0 / D, None, ALU.mult))
            v4a = e.v.n

            @blk.scalar
            def _(s):
                s.wait_ge(e.v.s, v4a)
                e.s.inc(s.activation(var4[:, :], var4[:, :], AF.Sqrt, bias=e.epsln[:, :]))
            v4b = e.s.n

            @blk.vector
            def _(v):
                v.wait_ge(e.s.s, v4b)
                e.v.inc(v.reciprocal(rs4[:, :], var4[:, :]))
                v.wait_ge(e.v.s, e.v.n)
                e.v.inc(v.tensor_tensor(cmx4[:, :], cmx4[:, :], rs4[:, :], ALU.mult))
                v.wait_ge(e.v.s, e.v.n)
                e.v.inc(v.tensor_reduce(hmax[:, :], cmx4[:, :], AX.X, ALU.max))
            v4c = e.v.n

            @blk.gpsimd
            def _(g):
                g.wait_ge(e.v.s, v4c)
                e.gq.inc(g.dma_start(out=colbuf[0:1, 0:128], in_=hmax[:, 0:1]), 16)
            c1g = e.gq.n

            @blk.sync
            def _(sy):
                sy.wait_ge(e.gq.s, c1g)
                e.ld.inc(sy.dma_start(out=qstg[0:1, 0:128], in_=colbuf[0:1, 0:128]), 16)
            c1l = e.ld.n

            @blk.vector
            def _(v):
                v.wait_ge(e.ld.s, c1l)
                v.wait_ge(e.v.s, wq_done_v)
                e.v.inc(v.tensor_reduce(qstg[0:1, 128:129], qstg[0:1, 0:128], AX.X, ALU.max))
            c1v = e.v.n

            @blk.gpsimd
            def _(g):
                g.wait_ge(e.v.s, c1v)
                e.gq.inc(g.dma_start(out=cc_in1.ap(), in_=qstg[0:1, 128:129]), 16)
                g.wait_ge(e.gq.s, e.gq.n)
                e.cc.inc(g.collective_compute("AllReduce", ALU.max,
                                              replica_groups=[list(range(B))],
                                              ins=[cc_in1.ap().opt()],
                                              outs=[cc_out1.ap().opt()]))
                g.wait_ge(e.cc.s, e.cc.n)
                e.gq.inc(g.dma_start(out=qstg[0:1, 130:131], in_=cc_out1.ap()), 16)
            cc1_g = e.gq.n

            @blk.vector
            def _(v):
                v.wait_ge(e.gq.s, cc1_g)
                e.v.inc(v.tensor_tensor(qstg[0:1, 131:132], qstg[0:1, 130:131], p4row[0:1, 4:5], ALU.mult))
                v.wait_ge(e.v.s, e.v.n)
                e.v.inc(v.tensor_scalar(qstg[0:1, 131:132], qstg[0:1, 131:132], 1.0 / 127.0, None, ALU.mult))
                v.wait_ge(e.v.s, e.v.n)
                e.v.inc(v.reciprocal(qstg[0:1, 132:133], qstg[0:1, 131:132]))
                v.wait_ge(e.v.s, e.v.n)
            sc1_v = e.v.n

            @blk.gpsimd
            def _(g):
                g.wait_ge(e.v.s, sc1_v)
                e.gq.inc(g.dma_start(out=scal_d[0:1, 0:4], in_=qstg[0:1, 130:134]), 16)
            sc1_g = e.gq.n

            @blk.sync
            def _(sy):
                sy.wait_ge(e.gq.s, sc1_g)
                e.ld.inc(sy.dma_start(out=mcol1[:, :], in_=scal_d[0:1, 2:3].broadcast_to([128, 1])), 16)
            sc1_b = e.ld.n

            @blk.vector
            def _(v):
                v.wait_ge(e.ld.s, sc1_b)
                e.v.inc(v.tensor_scalar(rsi4[:, :], rs4[:, :], mcol1[:, :], None, ALU.mult))
                v.wait_ge(e.v.s, e.v.n)
                e.v.inc(v.tensor_tensor(sb4[:, :], m4[:, :], rsi4[:, :], ALU.mult))
                v.wait_ge(e.v.s, e.v.n)
                e.v.inc(v.tensor_scalar(sb4[:, :], sb4[:, :], -1.0, 0.5, ALU.mult, ALU.add))
                v.wait_ge(e.v.s, e.v.n)
            sc2_v = e.v.n

            @blk.sync
            def _(sy):
                sy.wait_ge(e.gq.s, sc1_g)
                e.ld.inc(sy.dma_start(out=mcol2[:, :], in_=scal_d[0:1, 1:2].broadcast_to([128, 1])), 16)
            scx_b = e.ld.n

            @blk.vector
            def _(v):
                v.wait_ge(e.ld.s, scx_b)
                v.wait_ge(e.v.s, sc2_v)
                e.v.inc(v.tensor_scalar(wsgx[:, :], wsg[:, :], mcol2[:, :], None, ALU.mult))
                e.v.inc(v.tensor_scalar(wsux[:, :], wsu[:, :], mcol2[:, :], None, ALU.mult))
                v.wait_ge(e.v.s, e.v.n)
                e.v.inc(v.tensor_scalar(wsgx[:, :], wsgx[:, :], 1.0 / 127.0, None, ALU.mult))
                e.v.inc(v.tensor_scalar(wsux[:, :], wsux[:, :], 1.0 / 127.0, None, ALU.mult))
                v.wait_ge(e.v.s, e.v.n)
            wsx_v = e.v.n

            @blk.vector
            def _(v):
                v.wait_ge(e.v.s, wsx_v)
                e.v.inc(v.memset(hmax[:, :], 0.0))
            hm0 = e.v.n

            h_store = 0
            for gi in range(T // 512):
                for cg in range(4):
                    ci = gi * 4 + cg

                    @blk.gpsimd
                    def _(g, ci=ci, hm0=hm0):
                        g.wait_ge(e.v.s, e.v.n)
                        g.wait_ge(e.sc.s, e.sc.n)
                        e.gq.inc(g.dma_start(out=x3t[:, :], in_=x3_d[ci * 128:(ci + 1) * 128, :]), 16)
                    xl_ = e.gq.n

                    @blk.vector
                    def _(v, ci=ci, xl_=xl_):
                        v.wait_ge(e.gq.s, xl_)
                        e.v.inc(v.tensor_scalar(scr4[:, :], x3t[:, :], rsi4[:, ci:ci + 1],
                                                sb4[:, ci:ci + 1], ALU.mult, ALU.add))
                        v.wait_ge(e.v.s, e.v.n)
                        v.tensor_copy(qk[:, 0:D], scr4[:, :])
                        e.v.inc(v.tensor_copy(x3f4[:, :], qk[:, 0:D]))
                        v.wait_ge(e.v.s, e.v.n)
                        v.tensor_tensor(scr4[:, :], scr4[:, :], x3f4[:, :], ALU.subtract)
                        e.v.inc(v.tensor_scalar(scr4[:, :], scr4[:, :], 0.0, None, ALU.is_lt))
                        v.wait_ge(e.v.s, e.v.n)
                        v.tensor_tensor(qb[:, 0:D], x3f4[:, :], scr4[:, :], ALU.subtract)
                        e.v.inc(v.tensor_scalar(qb[:, 0:D], qb[:, 0:D], -128.0, 127.0, ALU.max, ALU.min))
                    kx_v = e.v.n

                    @blk.scalar
                    def _(s2, cg=cg, kx_v=kx_v):
                        s2.wait_ge(e.v.s, kx_v)
                        s2.wait_ge(e.t.s, e.t.n)
                        for dc in range(DC):
                            e.sc.inc(s2.dma_start(out=kxf[:, dc * 512 + cg * 128:dc * 512 + (cg + 1) * 128],
                                                  in_=qb[:, dc * 128:(dc + 1) * 128], transpose=True), 16)
                kx_t = e.sc.n

                if dev_dump == "p4" and gi == 0:
                    @blk.gpsimd
                    def _(g, kx_t=kx_t):
                        g.wait_ge(e.sc.s, kx_t)
                        e.fin.inc(g.dma_start(out=dbg[0:128, 0:4096], in_=kxf[:, :]), 16)

                for ic in range(IC):
                    @blk.tensor
                    def _(t, ic=ic, kx_t=kx_t, wq=wq_done_sc, wql=wq_done_ld):
                        t.wait_ge(e.sc.s, max(kx_t, wq))
                        t.wait_ge(e.ld.s, wql)
                        t.wait_ge(e.s.s, e.s.n)
                        t.wait_ge(e.v.s, e.v.n)
                        for dc in range(DC):
                            ins = t.matmul(e.psA[:, 0:512], kwgT[:, dc * DFF + ic * 128:dc * DFF + (ic + 1) * 128],
                                           kxf[:, dc * 512:(dc + 1) * 512], start=(dc == 0), stop=(dc == DC - 1))
                        e.t.inc(ins)
                        for dc in range(DC):
                            ins = t.matmul(e.psB[:, 0:512], kwuT[:, dc * DFF + ic * 128:dc * DFF + (ic + 1) * 128],
                                           kxf[:, dc * 512:(dc + 1) * 512], start=(dc == 0), stop=(dc == DC - 1))
                        e.t.inc(ins)
                    gu_t = e.t.n

                    @blk.scalar
                    def _(s, ic=ic, gu_t=gu_t):
                        s.wait_ge(e.t.s, gu_t)
                        e.s.inc(s.activation(sgt[:, :], e.psA[:, 0:512], AF.Silu, scale=wsgx[:, ic:ic + 1]))
                    gu_s = e.s.n

                    @blk.vector
                    def _(v, ic=ic, gu_s=gu_s, gu_t=gu_t):
                        v.wait_ge(e.t.s, gu_t)
                        v.wait_ge(e.gq.s, e.gq.n)
                        e.v.inc(v.tensor_scalar(sut[:, :], e.psB[:, 0:512], wsux[:, ic:ic + 1], None, ALU.mult))
                        v.wait_ge(e.s.s, gu_s)
                        e.v.inc(v.tensor_tensor(htile[:, :], sgt[:, :], sut[:, :], ALU.mult))
                        v.wait_ge(e.v.s, e.v.n)
                        e.v.inc(v.tensor_reduce(hab[:, ic:ic + 1], htile[:, :], AX.X, ALU.max,
                                                apply_absolute_value=True))
                    h_v = e.v.n

                    @blk.gpsimd
                    def _(g, ic=ic, gi=gi, h_v=h_v):
                        g.wait_ge(e.v.s, h_v)
                        e.gq.inc(g.dma_start(out=h_sp[ic * 128:(ic + 1) * 128, gi * 512:(gi + 1) * 512],
                                             in_=htile[:, :]), 16)
                        if dev_dump == "p4" and gi == 0 and ic == 0:
                            e.fin.inc(g.dma_start(out=dbg[128:256, 0:512], in_=sgt[:, :]), 16)
                            e.fin.inc(g.dma_start(out=dbg[128:256, 512:1024], in_=sut[:, :]), 16)
                            e.fin.inc(g.dma_start(out=dbg[128:256, 1024:1536], in_=htile[:, :]), 16)
                h_store = e.gq.n

                @blk.vector
                def _(v, h_store=h_store):
                    v.wait_ge(e.v.s, e.v.n)
                    e.v.inc(v.tensor_reduce(qu[:, 0:1], hab[:, :], AX.X, ALU.max))
                    v.wait_ge(e.v.s, e.v.n)
                    e.v.inc(v.tensor_tensor(hmax[:, :], hmax[:, :], qu[:, 0:1], ALU.max))
            hmax_v = e.v.n

            @blk.gpsimd
            def _(g):
                g.wait_ge(e.v.s, hmax_v)
                e.gq.inc(g.dma_start(out=colbuf[0:1, 128:256], in_=hmax[:, 0:1]), 16)
            c2g = e.gq.n

            @blk.sync
            def _(sy):
                sy.wait_ge(e.gq.s, c2g)
                e.ld.inc(sy.dma_start(out=qstg[0:1, 0:128], in_=colbuf[0:1, 128:256]), 16)
            c2l = e.ld.n

            @blk.vector
            def _(v):
                v.wait_ge(e.ld.s, c2l)
                e.v.inc(v.tensor_reduce(qstg[0:1, 128:129], qstg[0:1, 0:128], AX.X, ALU.max))
            c2v = e.v.n

            @blk.gpsimd
            def _(g):
                g.wait_ge(e.v.s, c2v)
                e.gq.inc(g.dma_start(out=cc_in2.ap(), in_=qstg[0:1, 128:129]), 16)
                g.wait_ge(e.gq.s, e.gq.n)
                e.cc.inc(g.collective_compute("AllReduce", ALU.max,
                                              replica_groups=[list(range(B))],
                                              ins=[cc_in2.ap().opt()],
                                              outs=[cc_out2.ap().opt()]))
                g.wait_ge(e.cc.s, e.cc.n)
                e.gq.inc(g.dma_start(out=qstg[0:1, 130:131], in_=cc_out2.ap()), 16)
            cc2_g = e.gq.n

            @blk.vector
            def _(v):
                v.wait_ge(e.gq.s, cc2_g)
                e.v.inc(v.tensor_tensor(qstg[0:1, 131:132], qstg[0:1, 130:131], p4row[0:1, 5:6], ALU.mult))
                v.wait_ge(e.v.s, e.v.n)
                e.v.inc(v.tensor_scalar(qstg[0:1, 131:132], qstg[0:1, 131:132], 1.0 / 127.0, None, ALU.mult))
                v.wait_ge(e.v.s, e.v.n)
                e.v.inc(v.reciprocal(qstg[0:1, 132:133], qstg[0:1, 131:132]))
                v.wait_ge(e.v.s, e.v.n)
            sc3_v = e.v.n

            @blk.gpsimd
            def _(g):
                g.wait_ge(e.v.s, sc3_v)
                e.gq.inc(g.dma_start(out=scal_d[0:1, 4:8], in_=qstg[0:1, 130:134]), 16)
            sc3_g = e.gq.n

            @blk.sync
            def _(sy):
                sy.wait_ge(e.gq.s, sc3_g)
                e.ld.inc(sy.dma_start(out=mcol2[:, :], in_=scal_d[0:1, 6:7].broadcast_to([128, 1])), 16)
                e.ld.inc(sy.dma_start(out=mcol1[:, :], in_=scal_d[0:1, 5:6].broadcast_to([128, 1])), 16)
            sc3_b = e.ld.n

            @blk.gpsimd
            def _(g, wq=wq_done_v):
                g.wait_ge(e.v.s, wq)
                for hc in range(DC):
                    e.gq.inc(g.dma_start(out=rowb[4:5, hc * 128:(hc + 1) * 128],
                                         in_=wsd[:, hc:hc + 1]), 16)
            wsd_g = e.gq.n

            @blk.sync
            def _(sy):
                sy.wait_ge(e.gq.s, wsd_g)
                e.ld.inc(sy.dma_start(out=wsdq[:, :], in_=rowb[4:5, 0:D].broadcast_to([128, D])), 16)
            wsd_b = e.ld.n

            @blk.vector
            def _(v):
                v.wait_ge(e.ld.s, wsd_b)
                v.wait_ge(e.ld.s, sc3_b)
                v.tensor_scalar(wsdq[:, :], wsdq[:, :], mcol1[:, :], None, ALU.mult)
                e.v.inc(v.tensor_scalar(wsdq[:, :], wsdq[:, :], 1.0 / 127.0, None, ALU.mult))
            wsdq_v = e.v.n

            out_st = 0
            for ci in range(NCH):
                @blk.gpsimd
                def _(g, ci=ci, wsdq_v=wsdq_v):
                    if ci == 0:
                        g.wait_ge(e.v.s, wsdq_v)
                    g.wait_ge(e.v.s, e.v.n)
                    g.wait_ge(e.t.s, e.t.n)
                    src3 = h_sp[:, ci * 128:(ci + 1) * 128].rearrange("(c p) w -> p c w", p=128)
                    dst3 = qstg[:, 0:IC * 128].rearrange("p (c w) -> p c w", w=128)
                    e.gq.inc(g.dma_start(out=dst3, in_=src3), 16)
                    e.gq.inc(g.dma_start(out=x3t[:, :], in_=x3_d[ci * 128:(ci + 1) * 128, :]), 16)
                hr_g = e.gq.n

                @blk.vector
                def _(v, hr_g=hr_g):
                    v.wait_ge(e.gq.s, hr_g)
                    e.v.inc(v.tensor_scalar(qu[:, 0:IC * 128], qstg[:, 0:IC * 128], mcol2[:, :], 0.5, ALU.mult, ALU.add))
                    v.wait_ge(e.v.s, e.v.n)
                    v.tensor_copy(qk[:, 0:IC * 128], qu[:, 0:IC * 128])
                    e.v.inc(v.tensor_copy(qstg[:, 0:IC * 128], qk[:, 0:IC * 128]))
                    v.wait_ge(e.v.s, e.v.n)
                    v.tensor_tensor(qu[:, 0:IC * 128], qu[:, 0:IC * 128], qstg[:, 0:IC * 128], ALU.subtract)
                    e.v.inc(v.tensor_scalar(qu[:, 0:IC * 128], qu[:, 0:IC * 128], 0.0, None, ALU.is_lt))
                    v.wait_ge(e.v.s, e.v.n)
                    v.tensor_tensor(qb[:, 0:IC * 128], qstg[:, 0:IC * 128], qu[:, 0:IC * 128], ALU.subtract)
                    e.v.inc(v.tensor_scalar(qb[:, 0:IC * 128], qb[:, 0:IC * 128], -128.0, 127.0, ALU.max, ALU.min))
                kh_v = e.v.n

                @blk.tensor
                def _(t, kh_v=kh_v):
                    t.wait_ge(e.v.s, kh_v)
                    for nh in range(2):
                        for ic in range(IC):
                            ins = t.matmul(e.psY[:, nh * 512:(nh + 1) * 512],
                                           qb[:, ic * 128:(ic + 1) * 128],
                                           kwdT[:, ic * D + nh * 512:ic * D + nh * 512 + 512],
                                           start=(ic == 0), stop=(ic == IC - 1))
                        e.t.inc(ins)
                dn_t = e.t.n

                @blk.vector
                def _(v, dn_t=dn_t, ci=ci):
                    v.wait_ge(e.t.s, dn_t)
                    v.wait_ge(e.fin.s, e.fin.n)
                    v.tensor_tensor(scr4[:, :], e.psY[:, 0:1024], wsdq[:, :], ALU.mult)
                    e.v.inc(v.tensor_tensor(scr4[:, :], scr4[:, :], x3t[:, :], ALU.add))
                o_v = e.v.n

                @blk.gpsimd
                def _(g, ci=ci, o_v=o_v):
                    g.wait_ge(e.v.s, o_v)
                    e.fin.inc(g.dma_start(out=out[ci * 128:(ci + 1) * 128, :], in_=scr4[:, :]), 16)

            if dev_dump == "p4":
                @blk.gpsimd
                def _(g):
                    g.wait_ge(e.fin.s, e.fin.n)
                    g.wait_ge(e.gq.s, e.gq.n)
                    g.wait_ge(e.v.s, e.v.n)
                    e.fin.inc(g.dma_start(out=dbg[256:384, 0:32], in_=m4[:, :]), 16)
                    e.fin.inc(g.dma_start(out=dbg[256:384, 32:64], in_=rs4[:, :]), 16)
                    e.fin.inc(g.dma_start(out=dbg[256:384, 64:86], in_=wsg[:, :]), 16)
                    e.fin.inc(g.dma_start(out=dbg[256:384, 86:108], in_=wsu[:, :]), 16)
                    e.fin.inc(g.dma_start(out=dbg[256:384, 108:116], in_=wsd[:, :]), 16)
                    e.fin.inc(g.dma_start(out=dbg[256:384, 116:138], in_=wsgx[:, :]), 16)
                    e.fin.inc(g.dma_start(out=dbg[256:384, 138:160], in_=wsux[:, :]), 16)
                    e.fin.inc(g.dma_start(out=dbg[256:257, 160:168], in_=scal_d[0:1, 0:8]), 16)
                    e.fin.inc(g.dma_start(out=dbg[256:384, 168:169], in_=hmax[:, :], allow_slow_non_contiguous=True), 16)
                    e.fin.inc(g.dma_start(out=dbg[384:512, 0:2816], in_=kwgT[:, 0:2816]), 16)
                    e.fin.inc(g.dma_start(out=dbg[384:512, 2816:2816 + 1024], in_=wsdq[:, :]), 16)
                    g.wait_ge(e.fin.s, e.fin.n)

            @blk.gpsimd
            def _(g):
                g.wait_ge(e.fin.s, e.fin.n)
            p4.close()

    return nc



def _constants():
    q = np.arange(128, dtype=np.float64)
    diff = q[None, :] - q[:, None] - 1.0
    c1t = np.where(q[None, :] > q[:, None], diff, BIGNEG).astype(np.float32)
    c2t = np.where(q[None, :] > q[:, None], np.maximum(diff, 0.0), BIGNEG).astype(np.float32)
    return {
        "c1t": np.ascontiguousarray(c1t),
        "c2t": np.ascontiguousarray(c2t),
        "eye": np.eye(128, dtype=np.float32),
        "iota_i": np.arange(128, dtype=np.float32).reshape(128, 1),
        "iota_r": (127.0 - np.arange(128, dtype=np.float32)).reshape(128, 1),
        "ones_in": np.ones((128, 1), dtype=np.float32),
    }


def make_in_maps(ii):
    consts = _constants()
    maa_all = np.stack([ii["L0_attn_time_maa_k"].reshape(-1),
                        ii["L0_attn_time_maa_v"].reshape(-1),
                        ii["L0_attn_time_maa_r"].reshape(-1),
                        ii["L0_attn_time_maa_g"].reshape(-1)], axis=0)
    maa_dev = np.ascontiguousarray(
        maa_all.reshape(4, DC, 128).transpose(2, 0, 1).reshape(128, 4 * DC)).astype(np.float32)
    common = {
        "WrT": np.ascontiguousarray(ii["L0_attn_Wr"].T),
        "WkT": np.ascontiguousarray(ii["L0_attn_Wk"].T),
        "WvT": np.ascontiguousarray(ii["L0_attn_Wv"].T),
        "WgT": np.ascontiguousarray(ii["L0_attn_Wg"].T),
        "WoT": np.ascontiguousarray(ii["L0_attn_Wo"].T),
        "maa": maa_dev,
        "decay_row": np.ascontiguousarray(ii["L0_attn_time_decay"].reshape(1, H)),
        "faaaa_row": np.ascontiguousarray(ii["L0_attn_time_faaaa"].reshape(1, H)),
        "lnw_row": np.ascontiguousarray(ii["L0_attn_ln_w"].reshape(1, D)),
        "lnb_row": np.ascontiguousarray(ii["L0_attn_ln_b"].reshape(1, D)),
        "theta": np.ascontiguousarray(ii["L0_ffn_theta"]),
        "theta_n": np.ascontiguousarray(ii["L0_ffn_theta_noise"]),
        "phi": np.ascontiguousarray(ii["L0_ffn_phi"]),
        "phi_n": np.ascontiguousarray(ii["L0_ffn_phi_noise"]),
        "WwT": np.ascontiguousarray(ii["L1_attn_W_write"].T),
        "WrdT": np.ascontiguousarray(ii["L1_attn_W_read"].T),
        "l1_scal": np.array([[float(ii["L1_attn_decay"]), float(ii["L1_attn_log_alpha"]),
                              float(np.asarray(ii["L1_ffn_clip_x"]).reshape(-1)[0]),
                              float(np.asarray(ii["L1_ffn_clip_down"]).reshape(-1)[0])]],
                            dtype=np.float32),
        "w_gate": np.ascontiguousarray(ii["L1_ffn_w_gate"]),
        "w_up": np.ascontiguousarray(ii["L1_ffn_w_up"]),
        "w_down": np.ascontiguousarray(ii["L1_ffn_w_down"]),
        **consts,
    }
    x = ii["L0_attn_x"]
    in_maps = []
    for b in range(B):
        m = dict(common)
        m["xin"] = np.ascontiguousarray(x[b].T)
        in_maps.append(m)
    return in_maps


def kernel(**inputs):
    ii = {k: np.asarray(v) for k, v in inputs.items()}
    nc = build(N_PHASES)
    in_maps = make_in_maps(ii)
    res = run_bass_kernel_spmd(nc, in_maps, core_ids=list(range(B)), trace_cores=[0])
    return np.stack([np.asarray(res.results[b]["out"]) for b in range(B)], axis=0)
